# version 04
# baseline (speedup 1.0000x reference)
import os
import numpy as np
import ml_dtypes
import concourse.bass as bass
import concourse.mybir as mybir
from concourse.bass_utils import run_bass_kernel_spmd

F32 = mybir.dt.float32
BF16 = mybir.dt.bfloat16
I32 = mybir.dt.int32
AF = mybir.ActivationFunctionType
ALU = mybir.AluOpType
AX = mybir.AxisListType

D = 1024
S = 4096
NCH = 8
NT = int(os.environ.get("KNT", "32"))
CHUNK = 64
EPS = 1e-5
EPS_IN = 1024 * 1e-5
CLAMP_MAX = float(np.log(1.0 / 0.01))
RANKS = [1, 2, 4, 8]
ROFF = [0, 1, 3, 7]
RTOT = 15
PROJ = 256

S_SCALE = float(np.nextafter(np.float32(2 * np.pi), np.float32(0)))
HALF_PI = float(np.float32(np.pi / 2))
INV_2PI = float(np.float32(1.0 / (2 * np.pi)))
KOFF = 64.0

ENGINES = ("pe", "act", "dve", "pool", "sp")
DMA_IN = tuple(f"in{i}" for i in range(4))
DMA_OUT = tuple(f"out{i}" for i in range(2))
DMA_CH = DMA_IN + DMA_OUT


class Sched:
    def __init__(self):
        self.ops = []
        self.cnt = {e: 0 for e in ENGINES + DMA_CH}
        self.writer = {}
        self.readers = {}
        self.need_inc = set()
        self.din = 0
        self.dout = 0

    def op(self, eng, emit, reads=(), writes=(), extra_waits=()):
        if eng == "in":
            eng = DMA_IN[self.din % len(DMA_IN)]
            self.din += 1
        elif eng == "out":
            eng = DMA_OUT[self.dout % len(DMA_OUT)]
            self.dout += 1
        step = 16 if eng in DMA_CH else 1
        waits = set(extra_waits)
        if eng in DMA_CH and self.cnt[eng] > 0:
            waits.add((eng, self.cnt[eng]))
        for k in reads:
            w = self.writer.get(k)
            if w is not None:
                waits.add(w)
        for k in writes:
            w = self.writer.get(k)
            if w is not None:
                waits.add(w)
            for r in self.readers.get(k, ()):
                waits.add(r)
        self.cnt[eng] += step
        me = (eng, self.cnt[eng])
        for k in reads:
            self.readers.setdefault(k, []).append(me)
        for k in writes:
            self.writer[k] = me
            self.readers[k] = []
        for w in waits:
            self.need_inc.add(w)
        if eng in DMA_CH:
            self.need_inc.add(me)
        self.ops.append((eng, emit, sorted(waits), me))
        return me

    def emit(self, sems, block, qmap):
        import bisect
        inc_list = {}
        for (e, c) in self.need_inc:
            inc_list.setdefault(e, []).append(c)
        for e in inc_list:
            inc_list[e].sort()

        def remap(src, val):
            lst = inc_list.get(src, ())
            r = bisect.bisect_right(lst, val)
            assert r > 0 and lst[r - 1] == val, (src, val)
            return r * (16 if src in DMA_CH else 1)

        per_q = {e: [] for e in ENGINES}
        for eng, emitfn, waits, me in self.ops:
            q = eng if eng in per_q else ("sp" if eng.startswith("in") else "pool")
            per_q[q].append((emitfn, waits, me if me in self.need_inc else None, eng))
        for q, items in per_q.items():
            if not items:
                continue

            def make_body(items):
                def body(e):
                    last = {}
                    for emitfn, waits, inc, semeng in items:
                        for (src, val) in waits:
                            v = remap(src, val)
                            if last.get(src, -1) >= v:
                                continue
                            e.wait_ge(sems[src], v)
                            last[src] = v
                        inst = emitfn(e)
                        if inc is not None:
                            inst.then_inc(sems[semeng],
                                          16 if semeng in DMA_CH else 1)
                return body
            getattr(block, qmap[q])(make_body(items))


def host_prep(inp):
    f = np.float32
    T = lambda a: np.ascontiguousarray(np.asarray(a, f).T)
    chunk2d = lambda v: np.ascontiguousarray(np.asarray(v, f).reshape(NCH, 128).T)
    bcast = lambda v, n: np.ascontiguousarray(
        np.broadcast_to(np.asarray(v, f).reshape(1, n), (128, n)))

    d = {}
    d["WkT"] = T(inp["L0_attn_Wk"])
    d["WvT"] = T(inp["L0_attn_Wv"])
    d["WqT"] = T(inp["L0_attn_Wq"])
    d["WkpT"] = T(inp["L0_attn_Wkp"])
    d["WqpT"] = T(inp["L0_attn_Wqp"])
    d["WoT"] = T(inp["L0_attn_Wo"])
    d["WfT"] = T(inp["L0_ffn_W"])
    d["WpT"] = T(inp["L0_ffn_Wp"])
    wd = np.concatenate([np.asarray(inp[f"L0_ffn_Wd{i}"], f) for i in range(4)], 0)
    d["WdT"] = T(wd)
    wu = np.concatenate([np.asarray(inp[f"L0_ffn_Wu{i}"], f) for i in range(4)], 1)
    d["WuT"] = T(wu)

    d["bk2d"] = chunk2d(inp["L0_attn_bk"])
    d["bq2d"] = chunk2d(inp["L0_attn_bq"])
    d["lnb2d"] = chunk2d(inp["L0_attn_ln_b"])
    d["g2d"] = chunk2d(inp["L0_attn_ln_g"])
    d["bv_bc"] = bcast(inp["L0_attn_bv"], D)
    d["ps_bc"] = bcast(inp["L0_attn_phase_scale"], D)
    d["bp_bc"] = bcast(inp["L0_ffn_bp"], PROJ)
    d["bkp_bc"] = bcast(inp["L0_attn_bkp"], D)
    d["bqp_bc"] = bcast(inp["L0_attn_bqp"], D)
    d["bo_bc"] = bcast(inp["L0_attn_bo"], D)
    sim = np.asarray(inp["L0_ffn_sim"], f).reshape(2, 128, 4).transpose(1, 0, 2)
    d["sim2d"] = np.ascontiguousarray(sim.reshape(128, 8))
    d["temp"] = np.asarray(inp["L0_ffn_temp"], f).reshape(1, 1).copy()

    L = np.zeros((128, 128), f)
    for b in range(2):
        L[b * CHUNK:(b + 1) * CHUNK, b * CHUNK:(b + 1) * CHUNK] = \
            np.tril(np.ones((CHUNK, CHUNK), f))
    d["LTb"] = np.ascontiguousarray(L.T).astype(ml_dtypes.bfloat16)
    d["I128b"] = np.eye(128, dtype=ml_dtypes.bfloat16)
    d["ones_f"] = np.ones((1, 128), f)
    d["ones_c"] = np.ones((128, 1), f)
    d["magic"] = np.full((128, 1), 0x5F3759DF, np.int32)
    consts = np.zeros((128, 8), f)
    consts[:, 0] = S_SCALE
    consts[:, 1] = HALF_PI
    consts[:, 2] = 0.5
    consts[:, 3] = -S_SCALE
    d["consts"] = consts
    return d


def build_core_program():
    nc = bass.Bass("TRN2", target_bir_lowering=False)

    hx = nc.dram_tensor("hx", [S, D], F32, kind="ExternalInput").ap()
    hy = nc.dram_tensor("hy", [S, D], F32, kind="ExternalOutput").ap()
    W_NAMES = ["WkT", "WvT", "WqT", "WkpT", "WqpT", "WoT", "WfT"]
    h = {}
    for n in W_NAMES:
        h[n] = nc.dram_tensor(n, [D, D], F32, kind="ExternalInput").ap()
    h["WpT"] = nc.dram_tensor("WpT", [D, PROJ], F32, kind="ExternalInput").ap()
    h["WdT"] = nc.dram_tensor("WdT", [D, RTOT], F32, kind="ExternalInput").ap()
    h["WuT"] = nc.dram_tensor("WuT", [RTOT, D], F32, kind="ExternalInput").ap()
    for n, shp in [("bk2d", [128, 8]), ("bq2d", [128, 8]), ("lnb2d", [128, 8]),
                   ("g2d", [128, 8]), ("bv_bc", [128, D]), ("ps_bc", [128, D]),
                   ("bp_bc", [128, PROJ]), ("bkp_bc", [128, D]),
                   ("bqp_bc", [128, D]), ("bo_bc", [128, D]), ("sim2d", [128, 8]),
                   ("temp", [1, 1]), ("ones_f", [1, 128]),
                   ("ones_c", [128, 1]), ("consts", [128, 8])]:
        h[n] = nc.dram_tensor(n, shp, F32, kind="ExternalInput").ap()
    h["LTb"] = nc.dram_tensor("LTb", [128, 128], BF16, kind="ExternalInput").ap()
    h["I128b"] = nc.dram_tensor("I128b", [128, 128], BF16, kind="ExternalInput").ap()
    h["magic"] = nc.dram_tensor("magic", [128, 1], I32, kind="ExternalInput").ap()

    from contextlib import ExitStack
    ctx = ExitStack()
    sb = lambda name, shape, dt: ctx.enter_context(nc.sbuf_tensor(name, shape, dt))
    with ctx:
        wb = {n: sb(n + "_b", [128, NCH * D], BF16) for n in W_NAMES}
        WpTb = sb("WpT_b", [128, NCH * PROJ], BF16)
        WdTb = sb("WdT_b", [128, NCH * RTOT], BF16)
        WuTb = sb("WuT_b", [RTOT, D], BF16)
        LTs = sb("LT_b", [128, 128], BF16)
        I128 = sb("I128_b", [128, 128], BF16)

        bk2d = sb("bk2d_s", [128, 8], F32)
        bq2d = sb("bq2d_s", [128, 8], F32)
        lnb2d = sb("lnb2d_s", [128, 8], F32)
        g2d = sb("g2d_s", [128, 8], F32)
        b2db = sb("b2d_bb", [128, 24], BF16)
        bv_bc = sb("bv_bc_s", [128, D], F32)
        ps_bc = sb("ps_bc_s", [128, D], F32)
        bp_bc = sb("bp_bc_s", [128, PROJ], F32)
        rowtmp_t = sb("rowtmp", [1, D], F32)
        bkp_bc = sb("bkp_bc_s", [128, D], F32)
        bqp_bc = sb("bqp_bc_s", [128, D], F32)
        bo_bc = sb("bo_bc_s", [128, D], F32)
        sim2d = sb("sim2d_s", [128, 8], F32)
        simnb = sb("simn_b", [128, 8], BF16)
        temp = sb("temp_s", [1, 1], F32)
        ones_f = sb("ones_f_s", [1, 128], F32)
        ones_c = sb("ones_c_s", [128, 1], F32)
        consts = sb("consts_s", [128, 8], F32)
        magic = sb("magic_s", [128, 1], I32)
        sc_bc = sb("sc_bc_s", [128, 1], F32)

        x_in = sb("x_in", [128, D], F32)
        A = sb("scrA", [128, D], F32)
        B = sb("scrB", [128, D], F32)
        C = sb("scrC", [128, D], F32)
        U = sb("scrU", [128, D], I32)
        SKb = sb("sinK", [128, D], BF16)
        CKb = sb("cosK", [128, D], BF16)
        VAL = sb("val", [128, D], F32)
        X2 = sb("x2", [128, D], F32)
        xbb = sb("xb_b", [128, D], BF16)
        bsh = sb("bshared", [128, D], BF16)
        tsh = sb("tshared", [128, D], BF16)
        kqT = sb("kqT", [128, D], BF16)
        junkb = sb("junk_b", [128, D], BF16)
        proj = sb("proj_s", [128, PROJ], F32)
        projb = sb("proj_bb", [128, PROJ], BF16)
        projT = sb("projT_b", [128, 2 * 128], BF16)
        hsml = sb("h_small", [128, RTOT], F32)
        hpb = sb("hp_b", [128, RTOT], BF16)
        hpT = sb("hpT_b", [RTOT, 128], BF16)
        lg = sb("lg_s", [128, 4], F32)
        gsm = sb("g_small", [128, 16], F32)
        STAT = [[sb(f"stat{p}_{i}", [128, 1], F32) for i in range(4)]
                for p in range(2)]
        NRS = [sb(f"nr_small{p}", [128, 4], F32) for p in range(2)]
        SEEDI = [sb(f"seed_i{p}", [128, 1], I32) for p in range(2)]
        seed4 = sb("seed4_i", [1, 8], I32)
        rowtmp = rowtmp_t
        srow = sb("srow", [1, 16], F32)
        st1, st2, st3, st4 = STAT[0]
        nrs = NRS[0]
        seedi = SEEDI[0]

        P = [ctx.enter_context(nc.psum_tensor(f"P{i}", [128, 1024], F32))
             for i in range(4)]
        PS = [[P[i][:, 0:512], P[i][:, 512:1024]] for i in range(4)]

        sems = {}
        for e in ENGINES + DMA_CH:
            sems[e] = ctx.enter_context(nc.semaphore("sem_" + e))
        chain_sems = {e: ctx.enter_context(nc.semaphore("chain_" + e))
                      for e in ("dve", "act", "pool")}
        chain_cnt = {e: 0 for e in chain_sems}

        def mkstep(eng):
            def step(e, inst):
                chain_cnt[eng] += 1
                inst.then_inc(chain_sems[eng], 1)
                e.wait_ge(chain_sems[eng], chain_cnt[eng])
            return step
        stepd = mkstep("dve")
        stepp = mkstep("pool")

        s = Sched()
        wkeys = lambda n: tuple((n, c) for c in range(NCH))

        def dma_in(dst_ap, src_ap, writes, reads=()):
            return s.op("in", lambda e, d=dst_ap, sr=src_ap: e.dma_start(out=d, in_=sr),
                        reads=reads, writes=writes)

        def mmseq(jobs, reads, writes, stop_last=True):
            jobs = list(jobs)

            def emit(e, jobs=jobs):
                inst = None
                for (o, l, r, st, sp) in jobs:
                    inst = e.matmul(out=o, lhsT=l, rhs=r, start=st, stop=sp)
                return inst
            return s.op("pe", emit, reads=reads, writes=writes)

        def proj_tok(slot, lhsT_key, lhsT_ap, wtile, wkey_t, stop_last=True):
            jobs = []
            for nh in range(2):
                for dc in range(NCH):
                    jobs.append((PS[slot][nh][:, 0:512],
                                 lhsT_ap[:, dc * 128:(dc + 1) * 128],
                                 wtile[:, dc * D + nh * 512:dc * D + (nh + 1) * 512],
                                 dc == 0, (dc == NCH - 1) and stop_last))
            return mmseq(jobs, reads=(lhsT_key,) + wkey_t,
                         writes=(("ps", slot),))

        def feat_proj(slot, wtile, wkey_t):
            jobs = []
            for ec in range(NCH):
                bank = PS[slot][ec // 4]
                for dc in range(NCH):
                    jobs.append((bank[:, (ec % 4) * 128:(ec % 4 + 1) * 128],
                                 wtile[:, dc * D + ec * 128:dc * D + (ec + 1) * 128],
                                 tsh[:, dc * 128:(dc + 1) * 128],
                                 dc == 0, dc == NCH - 1))
            return mmseq(jobs, reads=("tsh",) + wkey_t, writes=(("ps", slot),))

        def transpose8(slot, src_key, src_ap):
            jobs = []
            for c in range(NCH):
                bank = PS[slot][c // 4]
                jobs.append((bank[:, (c % 4) * 128:(c % 4 + 1) * 128],
                             src_ap[:, c * 128:(c + 1) * 128], I128[:], True, True))
            return mmseq(jobs, reads=(src_key, "I128"), writes=(("ps", slot),))

        def evac_pair(slot, dst_ap, dst_key, eng="act"):
            if eng == "act":
                def emit(e, sl=slot, d=dst_ap):
                    return e.activation(out=d, in_=P[sl][:, :], func=AF.Identity)
            else:
                def emit(e, sl=slot, d=dst_ap):
                    return e.tensor_copy(out=d, in_=P[sl][:, :])
            return s.op(eng, emit, reads=(("ps", slot),), writes=(dst_key,))

        def evac_pair_add(slot, bcast_tile, bkey, dst, dst_key):
            def emit(e, sl=slot, bc=bcast_tile, d=dst):
                return e.tensor_tensor(out=d[:, 0:1024], in0=P[sl][:, :],
                                       in1=bc[:, 0:1024], op=ALU.add)
            return s.op("dve", emit, reads=(("ps", slot), bkey), writes=(dst_key,))

        def rsqrt_nr(src, src_key, dst, dst_key, sdi, sdi_key, nr, nr_key,
                     clamp=None, iters=4):
            def emit_seed(e, src=src, dst=dst, sdi=sdi):
                sbits = src[:].bitcast(I32)
                stepd(e, e.tensor_scalar(out=sdi[:], in0=sbits, scalar1=1,
                                         scalar2=None,
                                         op0=ALU.arith_shift_right))
                stepd(e, e.tensor_tensor(out=sdi[:], in0=magic[:],
                                         in1=sdi[:], op=ALU.subtract))
                return e.tensor_copy(out=dst[:], in_=sdi[:].bitcast(F32))
            s.op("dve", emit_seed, reads=(src_key, "magic"),
                 writes=(dst_key, sdi_key))

            def emit_nr(e, src=src, dst=dst, nr=nr):
                inst = None
                for it in range(iters):
                    stepp(e, e.tensor_tensor(out=nr[:, 0:1], in0=dst[:],
                                             in1=dst[:], op=ALU.mult))
                    stepp(e, e.tensor_tensor(out=nr[:, 0:1], in0=nr[:, 0:1],
                                             in1=src[:], op=ALU.mult))
                    stepp(e, e.tensor_scalar(out=nr[:, 0:1], in0=nr[:, 0:1],
                                             scalar1=-0.5, scalar2=1.5,
                                             op0=ALU.mult, op1=ALU.add))
                    inst = e.tensor_tensor(out=dst[:], in0=dst[:],
                                           in1=nr[:, 0:1], op=ALU.mult)
                    if it < iters - 1 or clamp is not None:
                        stepp(e, inst)
                if clamp is not None:
                    inst = e.tensor_scalar(out=dst[:], in0=dst[:], scalar1=clamp,
                                           scalar2=None, op0=ALU.min)
                return inst
            return s.op("pool", emit_nr, reads=(src_key, dst_key),
                        writes=(dst_key, nr_key))

        def _ap(t):
            return t if isinstance(t, bass.AP) else t[:]
        for name, tile in [("bk2d", bk2d), ("bq2d", bq2d), ("lnb2d", lnb2d),
                           ("g2d", g2d), ("bv_bc", bv_bc), ("ps_bc", ps_bc),
                           ("bp_bc", bp_bc), ("bkp_bc", bkp_bc),
                           ("bqp_bc", bqp_bc), ("bo_bc", bo_bc),
                           ("sim2d", sim2d), ("temp", temp), ("ones_f", ones_f),
                           ("ones_c", ones_c), ("consts", consts)]:
            dma_in(_ap(tile), h[name], writes=(name,))
        dma_in(LTs[:], h["LTb"], writes=("LT",))
        dma_in(I128[:], h["I128b"], writes=("I128",))
        dma_in(magic[:], h["magic"], writes=("magic",))
        dma_in(x_in[:], hx[0:128, :], writes=("x_in",))

        stage = [(A, "scrA"), (B, "scrB")]
        cast_engines = ["dve", "act", "pool"]
        ei = 0
        for wi, wname in enumerate(W_NAMES + ["WpT", "WdT"]):
            src = h[wname]
            ncols = {"WpT": PROJ, "WdT": RTOT}.get(wname, D)
            dstt = {"WpT": WpTb, "WdT": WdTb}.get(wname, wb.get(wname))
            for c in range(NCH):
                stg, skey = stage[(wi * NCH + c) % 2]
                dma_in(stg[:, 0:ncols], src[c * 128:(c + 1) * 128, :],
                       writes=(skey,))
                eng = cast_engines[ei % 3]
                ei += 1
                dst_ap = dstt[:, c * ncols:(c + 1) * ncols]
                if wname == "WoT":
                    def emit(e, stg=stg, dst_ap=dst_ap, gc=g2d[:, c:c + 1], n=ncols):
                        return e.tensor_scalar(out=dst_ap, in0=stg[:, 0:n],
                                               scalar1=gc, scalar2=None,
                                               op0=ALU.mult)
                    s.op("dve", emit, reads=(skey, "g2d"), writes=((wname, c),))
                elif eng == "act":
                    def emit(e, stg=stg, dst_ap=dst_ap, n=ncols):
                        return e.activation(out=dst_ap, in_=stg[:, 0:n],
                                            func=AF.Identity)
                    s.op("act", emit, reads=(skey,), writes=((wname, c),))
                else:
                    def emit(e, stg=stg, dst_ap=dst_ap, n=ncols):
                        return e.tensor_copy(out=dst_ap, in_=stg[:, 0:n])
                    s.op(eng, emit, reads=(skey,), writes=((wname, c),))
        dma_in(C[0:RTOT, 0:D], h["WuT"], writes=("scrC",))
        s.op("dve", lambda e: e.tensor_copy(out=WuTb[:], in_=C[0:RTOT, 0:D]),
             reads=("scrC",), writes=("WuT", 0))
        wuk = (("WuT", 0),)

        def small_cast(e):
            e.tensor_copy(out=b2db[:, 0:8], in_=bk2d[:])
            e.tensor_copy(out=b2db[:, 8:16], in_=bq2d[:])
            return e.tensor_copy(out=b2db[:, 16:24], in_=lnb2d[:])
        s.op("pool", small_cast, reads=("bk2d", "bq2d", "lnb2d"), writes=("b2db",))

        def row_fold(boff, wname, wkey_t, bcdst, bc_key, slot_row, slot_bc):
            jobs = []
            for c in range(NCH):
                for nh in range(2):
                    jobs.append((PS[slot_row][nh][0:1, 0:512],
                                 b2db[:, boff + c:boff + c + 1],
                                 wb[wname][:, c * D + nh * 512:c * D + (nh + 1) * 512],
                                 c == 0, c == NCH - 1))
            mmseq(jobs, reads=("b2db",) + wkey_t, writes=(("ps", slot_row),))

            def ev(e, sl=slot_row):
                e.tensor_copy(out=rowtmp[0:1, 0:512], in_=PS[sl][0][0:1, :])
                return e.tensor_copy(out=rowtmp[0:1, 512:1024],
                                     in_=PS[sl][1][0:1, :])
            s.op("dve", ev, reads=(("ps", slot_row),), writes=("rowtmp",))
            jobs2 = [(PS[slot_bc][nh][:, 0:512], ones_f[:],
                      rowtmp[0:1, nh * 512:(nh + 1) * 512], True, True)
                     for nh in range(2)]
            mmseq(jobs2, reads=("rowtmp", "ones_f"), writes=(("ps", slot_bc),))

            def ev2(e, sl=slot_bc, d=bcdst):
                e.tensor_tensor(out=d[:, 0:512], in0=PS[sl][0][:, :],
                                in1=d[:, 0:512], op=ALU.add)
                return e.tensor_tensor(out=d[:, 512:1024], in0=PS[sl][1][:, :],
                                       in1=d[:, 512:1024], op=ALU.add)
            s.op("dve", ev2, reads=(("ps", slot_bc), bc_key), writes=(bc_key,))

        row_fold(0, "WkpT", wkeys("WkpT"), bkp_bc, "bkp_bc", 0, 1)
        row_fold(8, "WqpT", wkeys("WqpT"), bqp_bc, "bqp_bc", 2, 3)

        for c in range(NCH):
            stg, skey = stage[c % 2]
            dma_in(stg[:, 0:D], h["WoT"][c * 128:(c + 1) * 128, :], writes=(skey,))

            def cem(e, stg=stg):
                return e.tensor_copy(out=bsh[:, 0:D], in_=stg[:, 0:D])
            s.op("dve", cem, reads=(skey,), writes=("bsh",))
            jb = []
            for nh in range(2):
                jb.append((PS[0][nh][0:1, 0:512], b2db[:, 16 + c:17 + c],
                           bsh[:, nh * 512:(nh + 1) * 512],
                           c == 0, c == NCH - 1))
            mmseq(jb, reads=("bsh", "b2db"), writes=(("ps", 0),))

        def bo_ev(e):
            e.tensor_copy(out=rowtmp[0:1, 0:512], in_=PS[0][0][0:1, :])
            return e.tensor_copy(out=rowtmp[0:1, 512:1024], in_=PS[0][1][0:1, :])
        s.op("dve", bo_ev, reads=(("ps", 0),), writes=("rowtmp",))
        jobs2 = [(PS[1][nh][:, 0:512], ones_f[:], rowtmp[0:1, nh * 512:(nh + 1) * 512],
                  True, True) for nh in range(2)]
        mmseq(jobs2, reads=("rowtmp", "ones_f"), writes=(("ps", 1),))

        def bo_bc_ev(e):
            e.tensor_tensor(out=bo_bc[:, 0:512], in0=PS[1][0][:, :],
                            in1=bo_bc[:, 0:512], op=ALU.add)
            return e.tensor_tensor(out=bo_bc[:, 512:1024], in0=PS[1][1][:, :],
                                   in1=bo_bc[:, 512:1024], op=ALU.add)
        s.op("dve", bo_bc_ev, reads=(("ps", 1), "bo_bc"), writes=("bo_bc",))

        s.op("act", lambda e: e.activation(out=gsm[:, 0:8], in_=sim2d[:],
                                           func=AF.Square),
             reads=("sim2d",), writes=("gsm",))
        jb = [(PS[2][0][0:1, 0:8], ones_c[:], gsm[:, 0:8], True, True)]
        mmseq(jb, reads=("gsm", "ones_c"), writes=(("ps", 2),))

        def sim_norm(e):
            stepd(e, e.tensor_copy(out=srow[0:1, 8:16], in_=PS[2][0][0:1, 0:8]))
            stepd(e, e.tensor_tensor(out=srow[0:1, 0:4], in0=srow[0:1, 8:12],
                                     in1=srow[0:1, 12:16], op=ALU.add))
            sbits = srow[0:1, 0:4].bitcast(I32)
            stepd(e, e.tensor_scalar(out=seed4[0:1, 0:4], in0=sbits, scalar1=1,
                                     scalar2=None, op0=ALU.arith_shift_right))
            stepd(e, e.tensor_scalar(out=seed4[0:1, 4:8], in0=seed4[0:1, 0:4],
                                     scalar1=-1, scalar2=0x5F3759DF,
                                     op0=ALU.mult, op1=ALU.add))
            stepd(e, e.tensor_copy(out=srow[0:1, 4:8],
                                   in_=seed4[0:1, 4:8].bitcast(F32)))
            for _ in range(4):
                stepd(e, e.tensor_tensor(out=srow[0:1, 8:12], in0=srow[0:1, 4:8],
                                         in1=srow[0:1, 4:8], op=ALU.mult))
                stepd(e, e.tensor_tensor(out=srow[0:1, 8:12], in0=srow[0:1, 8:12],
                                         in1=srow[0:1, 0:4], op=ALU.mult))
                stepd(e, e.tensor_scalar(out=srow[0:1, 8:12], in0=srow[0:1, 8:12],
                                         scalar1=-0.5, scalar2=1.5, op0=ALU.mult,
                                         op1=ALU.add))
                stepd(e, e.tensor_tensor(out=srow[0:1, 4:8], in0=srow[0:1, 4:8],
                                         in1=srow[0:1, 8:12], op=ALU.mult))
            return e.tensor_scalar(out=srow[0:1, 4:8], in0=srow[0:1, 4:8],
                                   scalar1=1e12, scalar2=None, op0=ALU.min)
        s.op("dve", sim_norm, reads=(("ps", 2),), writes=("srow", "seed4"))
        jb = [(PS[2][0][:, 0:4], ones_f[:], srow[0:1, 4:8], True, True)]
        mmseq(jb, reads=("srow", "ones_f"), writes=(("ps", 2),))

        def simn_apply(e):
            stepd(e, e.tensor_copy(out=gsm[:, 8:12], in_=PS[2][0][:, 0:4]))
            stepd(e, e.tensor_tensor(out=gsm[:, 0:4], in0=sim2d[:, 0:4],
                                     in1=gsm[:, 8:12], op=ALU.mult))
            stepd(e, e.tensor_tensor(out=gsm[:, 4:8], in0=sim2d[:, 4:8],
                                     in1=gsm[:, 8:12], op=ALU.mult))
            return e.tensor_copy(out=simnb[:], in_=gsm[:, 0:8])
        s.op("dve", simn_apply, reads=(("ps", 2), "sim2d"),
             writes=("simnb", "gsm"))

        s.op("dve", lambda e: e.tensor_scalar(
            out=srow[0:1, 12:13], in0=temp[:], scalar1=CLAMP_MAX, scalar2=0.5,
            op0=ALU.min, op1=ALU.mult),
            reads=("temp",), writes=("srow",))
        s.op("act", lambda e: e.activation(out=srow[0:1, 13:14],
                                           in_=srow[0:1, 12:13], func=AF.Tanh),
             reads=("srow",), writes=("srow",))

        def sc_fin(e):
            stepd(e, e.tensor_scalar(out=srow[0:1, 14:15], in0=srow[0:1, 13:14],
                                     scalar1=-1.0, scalar2=1.0, op0=ALU.mult,
                                     op1=ALU.add))
            stepd(e, e.reciprocal(out=srow[0:1, 14:15], in_=srow[0:1, 14:15]))
            stepd(e, e.tensor_scalar(out=srow[0:1, 13:14], in0=srow[0:1, 13:14],
                                     scalar1=1.0, scalar2=None, op0=ALU.add))
            return e.tensor_tensor(out=srow[0:1, 15:16], in0=srow[0:1, 13:14],
                                   in1=srow[0:1, 14:15], op=ALU.mult)
        s.op("dve", sc_fin, reads=("srow",), writes=("srow",))
        jb = [(PS[2][0][:, 4:5], ones_f[:], srow[0:1, 15:16], True, True)]
        mmseq(jb, reads=("srow", "ones_f"), writes=(("ps", 2),))
        s.op("dve", lambda e: e.tensor_copy(out=sc_bc[:], in_=PS[2][0][:, 4:5]),
             reads=(("ps", 2),), writes=("sc_bc",))

        c_S = consts[:, 0:1]
        c_HPI = consts[:, 1:2]
        c_HALF = consts[:, 2:3]
        c_NS = consts[:, 3:4]

        xTb = sb("xT_b", [128, D], BF16)
        OUTB = sb("out_b", [128, D], F32)
        MIB = sb("mi_b", [128, D], F32)
        KQT = [kqT, sb("kqT2", [128, D], BF16)]
        XIN = [x_in, sb("x_in2", [128, D], F32)]

        def tctx(ti):
            pp = ti % 2
            stp = STAT[pp]
            ks = (f"st1_{pp}", f"st2_{pp}", f"st3_{pp}", f"st4_{pp}")
            return dict(pp=pp, stp=stp, ks=ks, nrp=NRS[pp], knr=f"nrs_{pp}",
                        sdp=SEEDI[pp], ksd=f"seedi_{pp}",
                        xin=XIN[pp], kx=f"x_in_{pp}",
                        kqt=KQT[pp], kk=f"kqT_{pp}")

        def ln_stats(t, eps):
            stp1, stp2, stp3, stp4 = t["stp"]
            k1, k2, k3, k4 = t["ks"]
            nrp, knr = t["nrp"], t["knr"]

            def emit(e):
                stepp(e, e.tensor_scalar(out=stp1[:], in0=stp3[:],
                                         scalar1=1.0 / D, scalar2=None,
                                         op0=ALU.mult))
                e.tensor_scalar(out=nrp[:, 1:2], in0=stp4[:], scalar1=1.0 / D,
                                scalar2=eps, op0=ALU.mult, op1=ALU.add)
                stepp(e, e.tensor_tensor(out=nrp[:, 2:3], in0=stp1[:],
                                         in1=stp1[:], op=ALU.mult))
                return e.tensor_tensor(out=stp2[:], in0=nrp[:, 1:2],
                                       in1=nrp[:, 2:3], op=ALU.subtract)
            s.op("pool", emit, reads=(k3, k4), writes=(k1, k2, knr))
            rsqrt_nr(stp2, k2, stp3, k3, t["sdp"], t["ksd"], nrp, knr, iters=3)

            def emit_nb(e):
                stepp(e, e.tensor_tensor(out=stp4[:], in0=stp1[:], in1=stp3[:],
                                         op=ALU.mult))
                return e.tensor_scalar(out=stp4[:], in0=stp4[:], scalar1=-1.0,
                                       scalar2=None, op0=ALU.mult)
            s.op("pool", emit_nb, reads=(k1, k3), writes=(k4,))

        def wrap_and_trig(src_key):
            s.op("act", lambda e: e.activation(out=A[:], in_=C[:], func=AF.Tanh),
                 reads=(src_key,), writes=("scrA",))
            s.op("dve", lambda e: e.scalar_tensor_tensor(
                out=A[:], in0=A[:], scalar=INV_2PI, in1=ps_bc[:],
                op0=ALU.mult, op1=ALU.mult),
                reads=("scrA", "ps_bc"), writes=("scrA",))
            s.op("dve", lambda e: e.tensor_scalar(
                out=U[:], in0=A[:], scalar1=KOFF, scalar2=None, op0=ALU.add),
                reads=("scrA",), writes=("scrU",))
            s.op("dve", lambda e: e.scalar_tensor_tensor(
                out=C[:], in0=U[:], scalar=-KOFF, in1=A[:],
                op0=ALU.add, op1=ALU.subtract),
                reads=("scrU", "scrA"), writes=("scrC",))
            s.op("dve", lambda e: e.tensor_scalar(
                out=A[:], in0=C[:], scalar1=-0.5, scalar2=None, op0=ALU.is_lt),
                reads=("scrC",), writes=("scrA",))
            s.op("dve", lambda e: e.tensor_tensor(
                out=C[:], in0=C[:], in1=A[:], op=ALU.add),
                reads=("scrC", "scrA"), writes=("scrC",))
            s.op("act", lambda e: e.activation(
                out=SKb[:], in_=C[:], func=AF.Sin, scale=c_NS),
                reads=("scrC", "consts"), writes=("SK",))
            s.op("dve", lambda e: e.tensor_scalar(
                out=A[:], in0=C[:], scalar1=-0.25, scalar2=None, op0=ALU.is_le),
                reads=("scrC",), writes=("scrA",))
            s.op("dve", lambda e: e.tensor_tensor(
                out=A[:], in0=C[:], in1=A[:], op=ALU.add),
                reads=("scrC", "scrA"), writes=("scrA",))
            s.op("act", lambda e: e.activation(
                out=CKb[:], in_=A[:], func=AF.Sin, scale=c_NS, bias=c_HPI),
                reads=("scrA", "consts"), writes=("CK",))

        def f_xkey(ti):
            t = tctx(ti)
            s.op("act", lambda e, x=t["xin"]: e.activation(
                out=xbb[:], in_=x[:], func=AF.Identity),
                reads=(t["kx"],), writes=("xbb",))
            transpose8(0, "xbb", xbb[:])
            evac_pair(0, xTb[:, 0:1024], "xTb")
            jobs = []
            for ec in range(NCH):
                for dc in range(NCH):
                    jobs.append((P[1][:, ec * 128:(ec + 1) * 128],
                                 wb["WkT"][:, dc * D + ec * 128:dc * D + (ec + 1) * 128],
                                 xTb[:, dc * 128:(dc + 1) * 128],
                                 dc == 0, dc == NCH - 1))
            mmseq(jobs, reads=("xTb",) + wkeys("WkT"), writes=(("ps", 1),))
            evac_pair(1, t["kqt"][:, 0:1024], t["kk"])

        def f_val_kp(ti):
            t = tctx(ti)
            kq = t["kqt"]
            proj_tok(0, "xTb", xTb[:], wb["WvT"], wkeys("WvT"))
            evac_pair_add(0, bv_bc, "bv_bc", VAL, "VAL")
            jobs = []
            for nh in range(2):
                for ec in range(NCH):
                    jobs.append((PS[1][nh][:, 0:512],
                                 kq[:, ec * 128:(ec + 1) * 128],
                                 wb["WkpT"][:, ec * D + nh * 512:ec * D + (nh + 1) * 512],
                                 ec == 0, ec == NCH - 1))
            mmseq(jobs, reads=(t["kk"],) + wkeys("WkpT"), writes=(("ps", 1),))
            evac_pair_add(1, bkp_bc, "bkp_bc", C, "scrC")
            wrap_and_trig("scrC")

        def f_qbm(ti):
            t = tctx(ti)
            jobs = []
            for ec in range(NCH):
                for dc in range(NCH):
                    jobs.append((P[1][:, ec * 128:(ec + 1) * 128],
                                 wb["WqT"][:, dc * D + ec * 128:dc * D + (ec + 1) * 128],
                                 xTb[:, dc * 128:(dc + 1) * 128],
                                 dc == 0, dc == NCH - 1))
            mmseq(jobs, reads=("xTb",) + wkeys("WqT"), writes=(("ps", 1),))
            s.op("dve", lambda e: e.tensor_tensor(out=xbb[:], in0=VAL[:],
                                                  in1=CKb[:], op=ALU.mult),
                 reads=("VAL", "CK"), writes=("xbb",))
            s.op("dve", lambda e: e.tensor_tensor(out=bsh[:], in0=VAL[:],
                                                  in1=SKb[:], op=ALU.mult),
                 reads=("VAL", "SK"), writes=("bsh",))
            evac_pair(1, t["kqt"][:, 0:1024], t["kk"])
            jobs = [(PS[0][nh][:, 0:512], LTs[:], xbb[:, nh * 512:(nh + 1) * 512],
                     True, True) for nh in range(2)]
            mmseq(jobs, reads=("xbb", "LT"), writes=(("ps", 0),))
            jobs = [(PS[1][nh][:, 0:512], LTs[:], bsh[:, nh * 512:(nh + 1) * 512],
                     True, True) for nh in range(2)]
            mmseq(jobs, reads=("bsh", "LT"), writes=(("ps", 1),))
            s.op("act", lambda e: e.activation(out=VAL[:], in_=P[0][:, :],
                                               func=AF.Identity),
                 reads=(("ps", 0),), writes=("VAL",))
            s.op("act", lambda e: e.activation(out=MIB[:], in_=P[1][:, :],
                                               func=AF.Identity),
                 reads=(("ps", 1),), writes=("MIB",))

        def f_qp(ti):
            t = tctx(ti)
            kq = t["kqt"]
            jobs = []
            for nh in range(2):
                for ec in range(NCH):
                    jobs.append((PS[2][nh][:, 0:512],
                                 kq[:, ec * 128:(ec + 1) * 128],
                                 wb["WqpT"][:, ec * D + nh * 512:ec * D + (nh + 1) * 512],
                                 ec == 0, ec == NCH - 1))
            mmseq(jobs, reads=(t["kk"],) + wkeys("WqpT"), writes=(("ps", 2),))
            evac_pair_add(2, bqp_bc, "bqp_bc", C, "scrC")
            wrap_and_trig("scrC")

        def f_retr(ti):
            t = tctx(ti)
            stp3, stp4 = t["stp"][2], t["stp"][3]
            k3, k4 = t["ks"][2], t["ks"][3]
            s.op("dve", lambda e: e.tensor_tensor(
                out=VAL[:], in0=VAL[:], in1=CKb[:], op=ALU.mult),
                reads=("VAL", "CK"), writes=("VAL",))
            s.op("dve", lambda e: e.tensor_tensor(
                out=MIB[:], in0=MIB[:], in1=SKb[:], op=ALU.mult),
                reads=("MIB", "SK"), writes=("MIB",))
            s.op("dve", lambda e, stp3=stp3: e.scalar_tensor_tensor(
                out=VAL[:], in0=VAL[:], scalar=1.0, in1=MIB[:], op0=ALU.mult,
                op1=ALU.add, accum_out=stp3[:]),
                reads=("VAL", "MIB"), writes=("VAL", k3))
            s.op("act", lambda e, stp4=stp4: e.activation(
                out=junkb[:], in_=VAL[:], func=AF.Square, accum_out=stp4[:]),
                reads=("VAL",), writes=("junkb", k4))
            ln_stats(t, EPS_IN)

        def f_lnT(ti):
            t = tctx(ti)
            stp3, stp4 = t["stp"][2], t["stp"][3]
            k3, k4 = t["ks"][2], t["ks"][3]
            s.op("act", lambda e, stp3=stp3, stp4=stp4: e.activation(
                out=bsh[:], in_=VAL[:], func=AF.Identity, scale=stp3[:],
                bias=stp4[:]),
                reads=("VAL", k3, k4), writes=("bsh",))
            transpose8(3, "bsh", bsh[:])
            evac_pair(3, tsh[:, 0:1024], "tsh")

        def f_wo(ti):
            t = tctx(ti)
            stp3 = t["stp"][2]
            k3 = t["ks"][2]
            proj_tok(2, "tsh", tsh[:], wb["WoT"], wkeys("WoT"))
            evac_pair_add(2, bo_bc, "bo_bc", C, "scrC")
            s.op("dve", lambda e, x=t["xin"], stp3=stp3: e.scalar_tensor_tensor(
                out=B[:], in0=x[:], scalar=2.0, in1=C[:], op0=ALU.mult,
                op1=ALU.add, accum_out=stp3[:]),
                reads=(t["kx"], "scrC"), writes=("scrB", k3))

        def f_ln1x2(ti):
            t = tctx(ti)
            stp3, stp4 = t["stp"][2], t["stp"][3]
            k3, k4 = t["ks"][2], t["ks"][3]
            s.op("act", lambda e, stp4=stp4: e.activation(
                out=junkb[:], in_=B[:], func=AF.Square, accum_out=stp4[:]),
                reads=("scrB",), writes=("junkb", k4))
            ln_stats(t, EPS)
            s.op("act", lambda e, stp3=stp3, stp4=stp4: e.activation(
                out=X2[:], in_=B[:], func=AF.Identity, scale=stp3[:],
                bias=stp4[:]),
                reads=("scrB", k3, k4), writes=("X2",))
            s.op("act", lambda e, stp3=stp3, stp4=stp4: e.activation(
                out=bsh[:], in_=B[:], func=AF.Identity, scale=stp3[:],
                bias=stp4[:]),
                reads=("scrB", k3, k4), writes=("bsh",))
            transpose8(3, "bsh", bsh[:])
            evac_pair(3, tsh[:, 0:1024], "tsh")

        def f_ffn1(ti):
            t = tctx(ti)
            stp1, stp2 = t["stp"][0], t["stp"][1]
            k1, k2 = t["ks"][0], t["ks"][1]
            proj_tok(2, "tsh", tsh[:], wb["WfT"], wkeys("WfT"), stop_last=False)
            jobs = [(PS[3][0][:, 0:PROJ], tsh[:, dc * 128:(dc + 1) * 128],
                     WpTb[:, dc * PROJ:(dc + 1) * PROJ], dc == 0, dc == NCH - 1)
                    for dc in range(NCH)]
            mmseq(jobs, reads=("tsh",) + wkeys("WpT"), writes=(("ps", 3),))
            jobs = [(PS[3][1][:, 0:RTOT], tsh[:, dc * 128:(dc + 1) * 128],
                     WdTb[:, dc * RTOT:(dc + 1) * RTOT], dc == 0, dc == NCH - 1)
                    for dc in range(NCH)]
            mmseq(jobs, reads=("tsh",) + wkeys("WdT"), writes=(("ps", 3),))
            s.op("dve", lambda e: e.tensor_tensor(
                out=proj[:], in0=PS[3][0][:, 0:PROJ], in1=bp_bc[:], op=ALU.add),
                reads=(("ps", 3), "bp_bc"), writes=("proj",))
            s.op("dve", lambda e: e.tensor_copy(out=hsml[:],
                                                in_=PS[3][1][:, 0:RTOT]),
                 reads=(("ps", 3),), writes=("hsml",))
            s.op("act", lambda e, stp1=stp1: e.activation(
                out=junkb[:, 0:PROJ], in_=proj[:], func=AF.Square,
                accum_out=stp1[:]),
                reads=("proj",), writes=("junkb", k1))
            rsqrt_nr(stp1, k1, stp2, k2, t["sdp"], t["ksd"], t["nrp"], t["knr"],
                     clamp=1e12, iters=3)
            s.op("pool", lambda e, stp1=stp1, stp2=stp2: e.tensor_tensor(
                out=stp1[:], in0=stp2[:], in1=sc_bc[:], op=ALU.mult),
                reads=(k2, "sc_bc"), writes=(k1,))
            s.op("act", lambda e: e.activation(out=projb[:], in_=proj[:],
                                               func=AF.Identity),
                 reads=("proj",), writes=("projb",))
            jobs = [(PS[3][0][:, PROJ + c * 128:PROJ + (c + 1) * 128],
                     projb[:, c * 128:(c + 1) * 128], I128[:], True, True)
                    for c in range(2)]
            mmseq(jobs, reads=("projb", "I128"), writes=(("ps", 3),))
            s.op("dve", lambda e: e.tensor_copy(out=projT[:],
                                                in_=PS[3][0][:, PROJ:PROJ + 256]),
                 reads=(("ps", 3),), writes=("projT",))
            jobs = [(PS[3][0][:, 0:4], projT[:, c * 128:(c + 1) * 128],
                     simnb[:, c * 4:(c + 1) * 4], c == 0, c == 1)
                    for c in range(2)]
            mmseq(jobs, reads=("projT", "simnb"), writes=(("ps", 3),))
            s.op("dve", lambda e, stp1=stp1: e.tensor_scalar(
                out=lg[:], in0=PS[3][0][:, 0:4], scalar1=stp1[:], scalar2=None,
                op0=ALU.mult),
                reads=(("ps", 3), k1), writes=("lg",))

        def f_gate(ti):
            def gate1(e):
                stepd(e, e.tensor_reduce(gsm[:, 0:1], lg[:], AX.X, ALU.max))
                return e.tensor_scalar(out=gsm[:, 4:8], in0=lg[:], scalar1=-1.0,
                                       scalar2=gsm[:, 0:1], op0=ALU.mult,
                                       op1=ALU.add)
            s.op("dve", gate1, reads=("lg",), writes=("gsm",))
            s.op("act", lambda e: e.activation(out=gsm[:, 8:12], in_=gsm[:, 4:8],
                                               func=AF.Tanh, scale=c_HALF),
                 reads=("gsm", "consts"), writes=("gsm",))

            def gate2(e):
                stepd(e, e.tensor_scalar(out=gsm[:, 4:8], in0=gsm[:, 8:12],
                                         scalar1=-1.0, scalar2=1.0,
                                         op0=ALU.mult, op1=ALU.add))
                stepd(e, e.tensor_scalar(out=gsm[:, 8:12], in0=gsm[:, 8:12],
                                         scalar1=1.0, scalar2=None, op0=ALU.add))
                stepd(e, e.reciprocal(out=gsm[:, 8:12], in_=gsm[:, 8:12]))
                stepd(e, e.tensor_tensor(out=gsm[:, 4:8], in0=gsm[:, 4:8],
                                         in1=gsm[:, 8:12], op=ALU.mult))
                stepd(e, e.tensor_reduce(gsm[:, 1:2], gsm[:, 4:8], AX.X, ALU.add))
                stepd(e, e.reciprocal(out=gsm[:, 1:2], in_=gsm[:, 1:2]))
                stepd(e, e.tensor_scalar(out=gsm[:, 8:12], in0=lg[:],
                                         scalar1=gsm[:, 0:1], scalar2=None,
                                         op0=ALU.is_ge))
                stepd(e, e.tensor_scalar(out=gsm[:, 12:13], in0=gsm[:, 8:9],
                                         scalar1=-1.0, scalar2=1.0,
                                         op0=ALU.mult, op1=ALU.add))
                stepd(e, e.tensor_tensor(out=gsm[:, 9:10], in0=gsm[:, 9:10],
                                         in1=gsm[:, 12:13], op=ALU.mult))
                stepd(e, e.tensor_scalar(out=gsm[:, 13:14], in0=gsm[:, 9:10],
                                         scalar1=-1.0, scalar2=1.0,
                                         op0=ALU.mult, op1=ALU.add))
                stepd(e, e.tensor_tensor(out=gsm[:, 12:13], in0=gsm[:, 12:13],
                                         in1=gsm[:, 13:14], op=ALU.mult))
                stepd(e, e.tensor_tensor(out=gsm[:, 10:11], in0=gsm[:, 10:11],
                                         in1=gsm[:, 12:13], op=ALU.mult))
                stepd(e, e.tensor_scalar(out=gsm[:, 13:14], in0=gsm[:, 10:11],
                                         scalar1=-1.0, scalar2=1.0,
                                         op0=ALU.mult, op1=ALU.add))
                stepd(e, e.tensor_tensor(out=gsm[:, 12:13], in0=gsm[:, 12:13],
                                         in1=gsm[:, 13:14], op=ALU.mult))
                stepd(e, e.tensor_tensor(out=gsm[:, 11:12], in0=gsm[:, 11:12],
                                         in1=gsm[:, 12:13], op=ALU.mult))
                return e.tensor_scalar(out=gsm[:, 8:12], in0=gsm[:, 8:12],
                                       scalar1=gsm[:, 1:2], scalar2=None,
                                       op0=ALU.mult)
            s.op("dve", gate2, reads=("gsm", "lg"), writes=("gsm",))

            def hgate(e):
                inst = None
                for i, (off, r) in enumerate(zip(ROFF, RANKS)):
                    inst = e.tensor_scalar(out=hpb[:, off:off + r],
                                           in0=hsml[:, off:off + r],
                                           scalar1=gsm[:, 8 + i:9 + i],
                                           scalar2=None, op0=ALU.mult)
                return inst
            s.op("dve", hgate, reads=("hsml", "gsm"), writes=("hpb",))

        def f_ffn2(ti):
            t = tctx(ti)
            stp3, stp4 = t["stp"][2], t["stp"][3]
            k3, k4 = t["ks"][2], t["ks"][3]
            nrp, knr = t["nrp"], t["knr"]
            jobs = [(PS[3][1][0:RTOT, 128:256], hpb[:], I128[:], True, True)]
            mmseq(jobs, reads=("hpb", "I128"), writes=(("ps", 3),))
            s.op("dve", lambda e: e.tensor_copy(out=hpT[:],
                                                in_=PS[3][1][0:RTOT, 128:256]),
                 reads=(("ps", 3),), writes=("hpT",))
            jobs = [(PS[2][nh][:, 0:512], hpT[:],
                     WuTb[:, nh * 512:(nh + 1) * 512], False, True)
                    for nh in range(2)]
            mmseq(jobs, reads=("hpT",) + wuk, writes=(("ps", 2),))

            def pre3_ev(e, nrp=nrp, stp3=stp3):
                e.scalar_tensor_tensor(out=B[:, 0:512], in0=X2[:, 0:512],
                                       scalar=1.0, in1=PS[2][0][:, :],
                                       op0=ALU.mult, op1=ALU.add,
                                       accum_out=nrp[:, 1:2])
                stepd(e, e.scalar_tensor_tensor(out=B[:, 512:1024],
                                                in0=X2[:, 512:1024],
                                                scalar=1.0, in1=PS[2][1][:, :],
                                                op0=ALU.mult, op1=ALU.add,
                                                accum_out=nrp[:, 2:3]))
                return e.tensor_tensor(out=stp3[:], in0=nrp[:, 1:2],
                                       in1=nrp[:, 2:3], op=ALU.add)
            s.op("dve", pre3_ev, reads=(("ps", 2), "X2"),
                 writes=("scrB", k3, knr))
            s.op("act", lambda e, stp4=stp4: e.activation(
                out=junkb[:], in_=B[:], func=AF.Square, accum_out=stp4[:]),
                reads=("scrB",), writes=("junkb", k4))
            ln_stats(t, EPS)
            s.op("act", lambda e, stp3=stp3, stp4=stp4: e.activation(
                out=OUTB[:], in_=B[:], func=AF.Identity, scale=stp3[:],
                bias=stp4[:]),
                reads=("scrB", k3, k4), writes=("OUTB",))
            s.op("out", lambda e, r0=ti * 128: e.dma_start(
                out=hy[r0:r0 + 128, :], in_=OUTB[:]),
                reads=("OUTB",), writes=())

        f_xkey(0)
        f_val_kp(0)
        f_qbm(0)
        for i in range(NT):
            if i + 1 < NT:
                tn = tctx(i + 1)
                dma_in(tn["xin"][:], hx[(i + 1) * 128:(i + 2) * 128, :],
                       writes=(tn["kx"],))
                f_xkey(i + 1)
            f_qp(i)
            f_retr(i)
            f_lnT(i)
            f_wo(i)
            if i + 1 < NT:
                f_val_kp(i + 1)
            f_ln1x2(i)
            f_ffn1(i)
            if i + 1 < NT:
                f_qbm(i + 1)
            f_gate(i)
            f_ffn2(i)

        s.op("pool", lambda e: e.nop(),
             extra_waits=tuple((ch, s.cnt[ch]) for ch in DMA_OUT if s.cnt[ch]))

        qmap = {"pe": "tensor", "act": "scalar", "dve": "vector",
                "pool": "gpsimd", "sp": "sync"}
        with nc.Block() as block:
            s.emit(sems, block, qmap)

    return nc


_PROG = None


def kernel(**inputs):
    global _PROG
    x = np.ascontiguousarray(np.asarray(inputs["L0_attn_x"], np.float32))
    assert x.shape == (8, S, D)
    hp = host_prep(inputs)
    if _PROG is None:
        _PROG = build_core_program()
    in_maps = []
    for b in range(8):
        m = {"hx": x[b]}
        m.update(hp)
        in_maps.append(m)
    res = run_bass_kernel_spmd(_PROG, in_maps, core_ids=list(range(8)))
    out = np.stack([np.asarray(r["hy"]) for r in res.results], axis=0)
    return out.astype(np.float32)


# version 05
# speedup vs baseline: 1.1462x; 1.1462x over previous
import os
import numpy as np
import ml_dtypes
import concourse.bass as bass
import concourse.mybir as mybir
from concourse.bass_utils import run_bass_kernel_spmd

F32 = mybir.dt.float32
BF16 = mybir.dt.bfloat16
I32 = mybir.dt.int32
AF = mybir.ActivationFunctionType
ALU = mybir.AluOpType
AX = mybir.AxisListType

D = 1024
S = 4096
NCH = 8
NT = int(os.environ.get("KNT", "32"))
CHUNK = 64
EPS = 1e-5
EPS_IN = 1024 * 1e-5
CLAMP_MAX = float(np.log(1.0 / 0.01))
RANKS = [1, 2, 4, 8]
ROFF = [0, 1, 3, 7]
RTOT = 15
PROJ = 256

S_SCALE = float(np.nextafter(np.float32(2 * np.pi), np.float32(0)))
HALF_PI = float(np.float32(np.pi / 2))
INV_2PI = float(np.float32(1.0 / (2 * np.pi)))
KOFF = 64.0

ENGINES = ("pe", "act", "dve", "pool", "sp")
DMA_IN = tuple(f"in{i}" for i in range(4))
DMA_OUT = tuple(f"out{i}" for i in range(2))
DMA_CH = DMA_IN + DMA_OUT


class Sched:
    def __init__(self):
        self.ops = []
        self.cnt = {e: 0 for e in ENGINES + DMA_CH}
        self.writer = {}
        self.readers = {}
        self.need_inc = set()
        self.din = 0
        self.dout = 0

    def op(self, eng, emit, reads=(), writes=(), extra_waits=()):
        if eng == "in":
            eng = DMA_IN[self.din % len(DMA_IN)]
            self.din += 1
        elif eng == "out":
            eng = DMA_OUT[self.dout % len(DMA_OUT)]
            self.dout += 1
        step = 16 if eng in DMA_CH else 1
        waits = set(extra_waits)
        if eng in DMA_CH and self.cnt[eng] > 0:
            waits.add((eng, self.cnt[eng]))
        for k in reads:
            w = self.writer.get(k)
            if w is not None:
                waits.add(w)
        for k in writes:
            w = self.writer.get(k)
            if w is not None:
                waits.add(w)
            for r in self.readers.get(k, ()):
                waits.add(r)
        self.cnt[eng] += step
        me = (eng, self.cnt[eng])
        for k in reads:
            self.readers.setdefault(k, []).append(me)
        for k in writes:
            self.writer[k] = me
            self.readers[k] = []
        for w in waits:
            self.need_inc.add(w)
        if eng in DMA_CH:
            self.need_inc.add(me)
        self.ops.append((eng, emit, sorted(waits), me))
        return me

    def emit(self, sems, block, qmap):
        import bisect
        inc_list = {}
        for (e, c) in self.need_inc:
            inc_list.setdefault(e, []).append(c)
        for e in inc_list:
            inc_list[e].sort()

        def remap(src, val):
            lst = inc_list.get(src, ())
            r = bisect.bisect_right(lst, val)
            assert r > 0 and lst[r - 1] == val, (src, val)
            return r * (16 if src in DMA_CH else 1)

        per_q = {e: [] for e in ENGINES}
        for eng, emitfn, waits, me in self.ops:
            q = eng if eng in per_q else ("sp" if eng.startswith("in") else "pool")
            per_q[q].append((emitfn, waits, me if me in self.need_inc else None, eng))
        for q, items in per_q.items():
            if not items:
                continue

            def make_body(items):
                def body(e):
                    last = {}
                    for emitfn, waits, inc, semeng in items:
                        for (src, val) in waits:
                            v = remap(src, val)
                            if last.get(src, -1) >= v:
                                continue
                            e.wait_ge(sems[src], v)
                            last[src] = v
                        inst = emitfn(e)
                        if inc is not None:
                            inst.then_inc(sems[semeng],
                                          16 if semeng in DMA_CH else 1)
                return body
            getattr(block, qmap[q])(make_body(items))


def host_prep(inp):
    f = np.float32
    T = lambda a: np.ascontiguousarray(np.asarray(a, f).T)
    chunk2d = lambda v: np.ascontiguousarray(np.asarray(v, f).reshape(NCH, 128).T)
    bcast = lambda v, n: np.ascontiguousarray(
        np.broadcast_to(np.asarray(v, f).reshape(1, n), (128, n)))

    d = {}
    d["WkT"] = T(inp["L0_attn_Wk"])
    d["WvT"] = T(inp["L0_attn_Wv"])
    d["WqT"] = T(inp["L0_attn_Wq"])
    d["WkpT"] = T(inp["L0_attn_Wkp"])
    d["WqpT"] = T(inp["L0_attn_Wqp"])
    d["WoT"] = T(inp["L0_attn_Wo"])
    d["WfT"] = T(inp["L0_ffn_W"])
    d["WpT"] = T(inp["L0_ffn_Wp"])
    wd = np.concatenate([np.asarray(inp[f"L0_ffn_Wd{i}"], f) for i in range(4)], 0)
    d["WdT"] = T(wd)
    wu = np.concatenate([np.asarray(inp[f"L0_ffn_Wu{i}"], f) for i in range(4)], 1)
    d["WuT"] = T(wu)

    d["bk2d"] = chunk2d(inp["L0_attn_bk"])
    d["bq2d"] = chunk2d(inp["L0_attn_bq"])
    d["lnb2d"] = chunk2d(inp["L0_attn_ln_b"])
    d["g2d"] = chunk2d(inp["L0_attn_ln_g"])
    d["bv_bc"] = bcast(inp["L0_attn_bv"], D)
    d["ps_bc"] = bcast(inp["L0_attn_phase_scale"], D)
    d["bp_bc"] = bcast(inp["L0_ffn_bp"], PROJ)
    d["bkp_bc"] = bcast(inp["L0_attn_bkp"], D)
    d["bqp_bc"] = bcast(inp["L0_attn_bqp"], D)
    d["bo_bc"] = bcast(inp["L0_attn_bo"], D)
    sim = np.asarray(inp["L0_ffn_sim"], f).reshape(2, 128, 4).transpose(1, 0, 2)
    d["sim2d"] = np.ascontiguousarray(sim.reshape(128, 8))
    d["temp"] = np.asarray(inp["L0_ffn_temp"], f).reshape(1, 1).copy()

    L = np.zeros((128, 128), f)
    for b in range(2):
        L[b * CHUNK:(b + 1) * CHUNK, b * CHUNK:(b + 1) * CHUNK] = \
            np.tril(np.ones((CHUNK, CHUNK), f))
    d["LTb"] = np.ascontiguousarray(L.T).astype(ml_dtypes.bfloat16)
    d["I128b"] = np.eye(128, dtype=ml_dtypes.bfloat16)
    d["ones_f"] = np.ones((1, 128), f)
    d["ones_c"] = np.ones((128, 1), f)
    d["magic"] = np.full((128, 1), 0x5F3759DF, np.int32)
    consts = np.zeros((128, 8), f)
    consts[:, 0] = S_SCALE
    consts[:, 1] = HALF_PI
    consts[:, 2] = 0.5
    consts[:, 3] = -S_SCALE
    d["consts"] = consts
    return d


def build_core_program():
    nc = bass.Bass("TRN2", target_bir_lowering=False)

    hx = nc.dram_tensor("hx", [S, D], F32, kind="ExternalInput").ap()
    hy = nc.dram_tensor("hy", [S, D], F32, kind="ExternalOutput").ap()
    W_NAMES = ["WkT", "WvT", "WqT", "WkpT", "WqpT", "WoT", "WfT"]
    h = {}
    for n in W_NAMES:
        h[n] = nc.dram_tensor(n, [D, D], F32, kind="ExternalInput").ap()
    h["WpT"] = nc.dram_tensor("WpT", [D, PROJ], F32, kind="ExternalInput").ap()
    h["WdT"] = nc.dram_tensor("WdT", [D, RTOT], F32, kind="ExternalInput").ap()
    h["WuT"] = nc.dram_tensor("WuT", [RTOT, D], F32, kind="ExternalInput").ap()
    for n, shp in [("bk2d", [128, 8]), ("bq2d", [128, 8]), ("lnb2d", [128, 8]),
                   ("g2d", [128, 8]), ("bv_bc", [128, D]), ("ps_bc", [128, D]),
                   ("bp_bc", [128, PROJ]), ("bkp_bc", [128, D]),
                   ("bqp_bc", [128, D]), ("bo_bc", [128, D]), ("sim2d", [128, 8]),
                   ("temp", [1, 1]), ("ones_f", [1, 128]),
                   ("ones_c", [128, 1]), ("consts", [128, 8])]:
        h[n] = nc.dram_tensor(n, shp, F32, kind="ExternalInput").ap()
    h["LTb"] = nc.dram_tensor("LTb", [128, 128], BF16, kind="ExternalInput").ap()
    h["I128b"] = nc.dram_tensor("I128b", [128, 128], BF16, kind="ExternalInput").ap()
    h["magic"] = nc.dram_tensor("magic", [128, 1], I32, kind="ExternalInput").ap()

    from contextlib import ExitStack
    ctx = ExitStack()
    sb = lambda name, shape, dt: ctx.enter_context(nc.sbuf_tensor(name, shape, dt))
    with ctx:
        wb = {n: sb(n + "_b", [128, NCH * D], BF16) for n in W_NAMES}
        WpTb = sb("WpT_b", [128, NCH * PROJ], BF16)
        WdTb = sb("WdT_b", [128, NCH * RTOT], BF16)
        WuTb = sb("WuT_b", [RTOT, D], BF16)
        LTs = sb("LT_b", [128, 128], BF16)
        I128 = sb("I128_b", [128, 128], BF16)

        bk2d = sb("bk2d_s", [128, 8], F32)
        bq2d = sb("bq2d_s", [128, 8], F32)
        lnb2d = sb("lnb2d_s", [128, 8], F32)
        g2d = sb("g2d_s", [128, 8], F32)
        b2db = sb("b2d_bb", [128, 24], BF16)
        bv_bc = sb("bv_bc_s", [128, D], F32)
        ps_bc = sb("ps_bc_s", [128, D], F32)
        bp_bc = sb("bp_bc_s", [128, PROJ], F32)
        rowtmp_t = sb("rowtmp", [1, D], F32)
        bkp_bc = sb("bkp_bc_s", [128, D], F32)
        bqp_bc = sb("bqp_bc_s", [128, D], F32)
        bo_bc = sb("bo_bc_s", [128, D], F32)
        sim2d = sb("sim2d_s", [128, 8], F32)
        simnb = sb("simn_b", [128, 8], BF16)
        temp = sb("temp_s", [1, 1], F32)
        ones_f = sb("ones_f_s", [1, 128], F32)
        ones_c = sb("ones_c_s", [128, 1], F32)
        consts = sb("consts_s", [128, 8], F32)
        magic = sb("magic_s", [128, 1], I32)
        sc_bc = sb("sc_bc_s", [128, 1], F32)

        x_in = sb("x_in", [128, D], F32)
        A = sb("scrA", [128, D], F32)
        B = sb("scrB", [128, D], F32)
        C = sb("scrC", [128, D], F32)
        U = sb("scrU", [128, D], I32)
        SKb = sb("sinK", [128, D], BF16)
        CKb = sb("cosK", [128, D], BF16)
        VAL = sb("val", [128, D], F32)
        X2 = sb("x2", [128, D], F32)
        xbb = sb("xb_b", [128, D], BF16)
        bsh = sb("bshared", [128, D], BF16)
        tsh = sb("tshared", [128, D], BF16)
        kqT = sb("kqT", [128, D], BF16)
        junkb = sb("junk_b", [128, D], BF16)
        proj = sb("proj_s", [128, PROJ], F32)
        projb = sb("proj_bb", [128, PROJ], BF16)
        projT = sb("projT_b", [128, 2 * 128], BF16)
        hsml = sb("h_small", [128, RTOT], F32)
        hpb = sb("hp_b", [128, RTOT], BF16)
        hpT = sb("hpT_b", [RTOT, 128], BF16)
        lg = sb("lg_s", [128, 4], F32)
        gsm = sb("g_small", [128, 16], F32)
        STAT = [[sb(f"stat{p}_{i}", [128, 1], F32) for i in range(4)]
                for p in range(2)]
        NRS = [sb(f"nr_small{p}", [128, 4], F32) for p in range(2)]
        SEEDI = [sb(f"seed_i{p}", [128, 1], I32) for p in range(2)]
        seed4 = sb("seed4_i", [1, 8], I32)
        rowtmp = rowtmp_t
        srow = sb("srow", [1, 16], F32)
        st1, st2, st3, st4 = STAT[0]
        nrs = NRS[0]
        seedi = SEEDI[0]

        P = [ctx.enter_context(nc.psum_tensor(f"P{i}", [128, 1024], F32))
             for i in range(4)]
        PS = [[P[i][:, 0:512], P[i][:, 512:1024]] for i in range(4)]

        sems = {}
        for e in ENGINES + DMA_CH:
            sems[e] = ctx.enter_context(nc.semaphore("sem_" + e))
        chain_sems = {e: ctx.enter_context(nc.semaphore("chain_" + e))
                      for e in ("dve", "act", "pool")}
        chain_cnt = {e: 0 for e in chain_sems}

        def mkstep(eng):
            def step(e, inst):
                chain_cnt[eng] += 1
                inst.then_inc(chain_sems[eng], 1)
                e.wait_ge(chain_sems[eng], chain_cnt[eng])
            return step
        stepd = mkstep("dve")
        stepp = mkstep("pool")

        s = Sched()
        wkeys = lambda n: tuple((n, c) for c in range(NCH))

        def dma_in(dst_ap, src_ap, writes, reads=()):
            return s.op("in", lambda e, d=dst_ap, sr=src_ap: e.dma_start(out=d, in_=sr),
                        reads=reads, writes=writes)

        def mmseq(jobs, reads, writes, stop_last=True):
            jobs = list(jobs)

            def emit(e, jobs=jobs):
                inst = None
                for (o, l, r, st, sp) in jobs:
                    inst = e.matmul(out=o, lhsT=l, rhs=r, start=st, stop=sp)
                return inst
            return s.op("pe", emit, reads=reads, writes=writes)

        def proj_tok(slot, lhsT_key, lhsT_ap, wtile, wkey_t, stop_last=True):
            jobs = []
            for nh in range(2):
                for dc in range(NCH):
                    jobs.append((PS[slot][nh][:, 0:512],
                                 lhsT_ap[:, dc * 128:(dc + 1) * 128],
                                 wtile[:, dc * D + nh * 512:dc * D + (nh + 1) * 512],
                                 dc == 0, (dc == NCH - 1) and stop_last))
            return mmseq(jobs, reads=(lhsT_key,) + wkey_t,
                         writes=(("ps", slot),))

        def feat_proj(slot, wtile, wkey_t):
            jobs = []
            for ec in range(NCH):
                bank = PS[slot][ec // 4]
                for dc in range(NCH):
                    jobs.append((bank[:, (ec % 4) * 128:(ec % 4 + 1) * 128],
                                 wtile[:, dc * D + ec * 128:dc * D + (ec + 1) * 128],
                                 tsh[:, dc * 128:(dc + 1) * 128],
                                 dc == 0, dc == NCH - 1))
            return mmseq(jobs, reads=("tsh",) + wkey_t, writes=(("ps", slot),))

        def transpose8(slot, src_key, src_ap):
            jobs = []
            for c in range(NCH):
                bank = PS[slot][c // 4]
                jobs.append((bank[:, (c % 4) * 128:(c % 4 + 1) * 128],
                             src_ap[:, c * 128:(c + 1) * 128], I128[:], True, True))
            return mmseq(jobs, reads=(src_key, "I128"), writes=(("ps", slot),))

        def evac_pair(slot, dst_ap, dst_key, eng="act"):
            if eng == "act":
                def emit(e, sl=slot, d=dst_ap):
                    return e.activation(out=d, in_=P[sl][:, :], func=AF.Identity)
            else:
                def emit(e, sl=slot, d=dst_ap):
                    return e.tensor_copy(out=d, in_=P[sl][:, :])
            return s.op(eng, emit, reads=(("ps", slot),), writes=(dst_key,))

        def evac_pair_add(slot, bcast_tile, bkey, dst, dst_key):
            def emit(e, sl=slot, bc=bcast_tile, d=dst):
                return e.tensor_tensor(out=d[:, 0:1024], in0=P[sl][:, :],
                                       in1=bc[:, 0:1024], op=ALU.add)
            return s.op("dve", emit, reads=(("ps", slot), bkey), writes=(dst_key,))

        def rsqrt_nr(src, src_key, dst, dst_key, sdi, sdi_key, nr, nr_key,
                     clamp=None, iters=4):
            def emit_seed(e, src=src, dst=dst, sdi=sdi):
                sbits = src[:].bitcast(I32)
                stepd(e, e.tensor_scalar(out=sdi[:], in0=sbits, scalar1=1,
                                         scalar2=None,
                                         op0=ALU.arith_shift_right))
                stepd(e, e.tensor_tensor(out=sdi[:], in0=magic[:],
                                         in1=sdi[:], op=ALU.subtract))
                return e.tensor_copy(out=dst[:], in_=sdi[:].bitcast(F32))
            s.op("dve", emit_seed, reads=(src_key, "magic"),
                 writes=(dst_key, sdi_key))

            def emit_nr(e, src=src, dst=dst, nr=nr):
                inst = None
                for it in range(iters):
                    stepp(e, e.tensor_tensor(out=nr[:, 0:1], in0=dst[:],
                                             in1=dst[:], op=ALU.mult))
                    stepp(e, e.tensor_tensor(out=nr[:, 0:1], in0=nr[:, 0:1],
                                             in1=src[:], op=ALU.mult))
                    stepp(e, e.tensor_scalar(out=nr[:, 0:1], in0=nr[:, 0:1],
                                             scalar1=-0.5, scalar2=1.5,
                                             op0=ALU.mult, op1=ALU.add))
                    inst = e.tensor_tensor(out=dst[:], in0=dst[:],
                                           in1=nr[:, 0:1], op=ALU.mult)
                    if it < iters - 1 or clamp is not None:
                        stepp(e, inst)
                if clamp is not None:
                    inst = e.tensor_scalar(out=dst[:], in0=dst[:], scalar1=clamp,
                                           scalar2=None, op0=ALU.min)
                return inst
            return s.op("pool", emit_nr, reads=(src_key, dst_key),
                        writes=(dst_key, nr_key))

        def _ap(t):
            return t if isinstance(t, bass.AP) else t[:]
        for name, tile in [("bk2d", bk2d), ("bq2d", bq2d), ("lnb2d", lnb2d),
                           ("g2d", g2d), ("bv_bc", bv_bc), ("ps_bc", ps_bc),
                           ("bp_bc", bp_bc), ("bkp_bc", bkp_bc),
                           ("bqp_bc", bqp_bc), ("bo_bc", bo_bc),
                           ("sim2d", sim2d), ("temp", temp), ("ones_f", ones_f),
                           ("ones_c", ones_c), ("consts", consts)]:
            dma_in(_ap(tile), h[name], writes=(name,))
        dma_in(LTs[:], h["LTb"], writes=("LT",))
        dma_in(I128[:], h["I128b"], writes=("I128",))
        dma_in(magic[:], h["magic"], writes=("magic",))
        dma_in(x_in[:], hx[0:128, :], writes=("x_in",))

        stage = [(A, "scrA"), (B, "scrB")]
        cast_engines = ["dve", "act", "pool"]
        ei = 0
        for wi, wname in enumerate(W_NAMES + ["WpT", "WdT"]):
            src = h[wname]
            ncols = {"WpT": PROJ, "WdT": RTOT}.get(wname, D)
            dstt = {"WpT": WpTb, "WdT": WdTb}.get(wname, wb.get(wname))
            for c in range(NCH):
                stg, skey = stage[(wi * NCH + c) % 2]
                dma_in(stg[:, 0:ncols], src[c * 128:(c + 1) * 128, :],
                       writes=(skey,))
                eng = cast_engines[ei % 3]
                ei += 1
                dst_ap = dstt[:, c * ncols:(c + 1) * ncols]
                if wname == "WoT":
                    def emit(e, stg=stg, dst_ap=dst_ap, gc=g2d[:, c:c + 1], n=ncols):
                        return e.tensor_scalar(out=dst_ap, in0=stg[:, 0:n],
                                               scalar1=gc, scalar2=None,
                                               op0=ALU.mult)
                    s.op("dve", emit, reads=(skey, "g2d"), writes=((wname, c),))
                elif eng == "act":
                    def emit(e, stg=stg, dst_ap=dst_ap, n=ncols):
                        return e.activation(out=dst_ap, in_=stg[:, 0:n],
                                            func=AF.Identity)
                    s.op("act", emit, reads=(skey,), writes=((wname, c),))
                else:
                    def emit(e, stg=stg, dst_ap=dst_ap, n=ncols):
                        return e.tensor_copy(out=dst_ap, in_=stg[:, 0:n])
                    s.op(eng, emit, reads=(skey,), writes=((wname, c),))
        dma_in(C[0:RTOT, 0:D], h["WuT"], writes=("scrC",))
        s.op("dve", lambda e: e.tensor_copy(out=WuTb[:], in_=C[0:RTOT, 0:D]),
             reads=("scrC",), writes=("WuT", 0))
        wuk = (("WuT", 0),)

        def small_cast(e):
            e.tensor_copy(out=b2db[:, 0:8], in_=bk2d[:])
            e.tensor_copy(out=b2db[:, 8:16], in_=bq2d[:])
            return e.tensor_copy(out=b2db[:, 16:24], in_=lnb2d[:])
        s.op("pool", small_cast, reads=("bk2d", "bq2d", "lnb2d"), writes=("b2db",))

        def row_fold(boff, wname, wkey_t, bcdst, bc_key, slot_row, slot_bc):
            jobs = []
            for c in range(NCH):
                for nh in range(2):
                    jobs.append((PS[slot_row][nh][0:1, 0:512],
                                 b2db[:, boff + c:boff + c + 1],
                                 wb[wname][:, c * D + nh * 512:c * D + (nh + 1) * 512],
                                 c == 0, c == NCH - 1))
            mmseq(jobs, reads=("b2db",) + wkey_t, writes=(("ps", slot_row),))

            def ev(e, sl=slot_row):
                e.tensor_copy(out=rowtmp[0:1, 0:512], in_=PS[sl][0][0:1, :])
                return e.tensor_copy(out=rowtmp[0:1, 512:1024],
                                     in_=PS[sl][1][0:1, :])
            s.op("dve", ev, reads=(("ps", slot_row),), writes=("rowtmp",))
            jobs2 = [(PS[slot_bc][nh][:, 0:512], ones_f[:],
                      rowtmp[0:1, nh * 512:(nh + 1) * 512], True, True)
                     for nh in range(2)]
            mmseq(jobs2, reads=("rowtmp", "ones_f"), writes=(("ps", slot_bc),))

            def ev2(e, sl=slot_bc, d=bcdst):
                e.tensor_tensor(out=d[:, 0:512], in0=PS[sl][0][:, :],
                                in1=d[:, 0:512], op=ALU.add)
                return e.tensor_tensor(out=d[:, 512:1024], in0=PS[sl][1][:, :],
                                       in1=d[:, 512:1024], op=ALU.add)
            s.op("dve", ev2, reads=(("ps", slot_bc), bc_key), writes=(bc_key,))

        row_fold(0, "WkpT", wkeys("WkpT"), bkp_bc, "bkp_bc", 0, 1)
        row_fold(8, "WqpT", wkeys("WqpT"), bqp_bc, "bqp_bc", 2, 3)

        for c in range(NCH):
            stg, skey = stage[c % 2]
            dma_in(stg[:, 0:D], h["WoT"][c * 128:(c + 1) * 128, :], writes=(skey,))

            def cem(e, stg=stg):
                return e.tensor_copy(out=bsh[:, 0:D], in_=stg[:, 0:D])
            s.op("dve", cem, reads=(skey,), writes=("bsh",))
            jb = []
            for nh in range(2):
                jb.append((PS[0][nh][0:1, 0:512], b2db[:, 16 + c:17 + c],
                           bsh[:, nh * 512:(nh + 1) * 512],
                           c == 0, c == NCH - 1))
            mmseq(jb, reads=("bsh", "b2db"), writes=(("ps", 0),))

        def bo_ev(e):
            e.tensor_copy(out=rowtmp[0:1, 0:512], in_=PS[0][0][0:1, :])
            return e.tensor_copy(out=rowtmp[0:1, 512:1024], in_=PS[0][1][0:1, :])
        s.op("dve", bo_ev, reads=(("ps", 0),), writes=("rowtmp",))
        jobs2 = [(PS[1][nh][:, 0:512], ones_f[:], rowtmp[0:1, nh * 512:(nh + 1) * 512],
                  True, True) for nh in range(2)]
        mmseq(jobs2, reads=("rowtmp", "ones_f"), writes=(("ps", 1),))

        def bo_bc_ev(e):
            e.tensor_tensor(out=bo_bc[:, 0:512], in0=PS[1][0][:, :],
                            in1=bo_bc[:, 0:512], op=ALU.add)
            return e.tensor_tensor(out=bo_bc[:, 512:1024], in0=PS[1][1][:, :],
                                   in1=bo_bc[:, 512:1024], op=ALU.add)
        s.op("dve", bo_bc_ev, reads=(("ps", 1), "bo_bc"), writes=("bo_bc",))

        s.op("act", lambda e: e.activation(out=gsm[:, 0:8], in_=sim2d[:],
                                           func=AF.Square),
             reads=("sim2d",), writes=("gsm",))
        jb = [(PS[2][0][0:1, 0:8], ones_c[:], gsm[:, 0:8], True, True)]
        mmseq(jb, reads=("gsm", "ones_c"), writes=(("ps", 2),))

        def sim_norm(e):
            stepd(e, e.tensor_copy(out=srow[0:1, 8:16], in_=PS[2][0][0:1, 0:8]))
            stepd(e, e.tensor_tensor(out=srow[0:1, 0:4], in0=srow[0:1, 8:12],
                                     in1=srow[0:1, 12:16], op=ALU.add))
            sbits = srow[0:1, 0:4].bitcast(I32)
            stepd(e, e.tensor_scalar(out=seed4[0:1, 0:4], in0=sbits, scalar1=1,
                                     scalar2=None, op0=ALU.arith_shift_right))
            stepd(e, e.tensor_scalar(out=seed4[0:1, 4:8], in0=seed4[0:1, 0:4],
                                     scalar1=-1, scalar2=0x5F3759DF,
                                     op0=ALU.mult, op1=ALU.add))
            stepd(e, e.tensor_copy(out=srow[0:1, 4:8],
                                   in_=seed4[0:1, 4:8].bitcast(F32)))
            for _ in range(4):
                stepd(e, e.tensor_tensor(out=srow[0:1, 8:12], in0=srow[0:1, 4:8],
                                         in1=srow[0:1, 4:8], op=ALU.mult))
                stepd(e, e.tensor_tensor(out=srow[0:1, 8:12], in0=srow[0:1, 8:12],
                                         in1=srow[0:1, 0:4], op=ALU.mult))
                stepd(e, e.tensor_scalar(out=srow[0:1, 8:12], in0=srow[0:1, 8:12],
                                         scalar1=-0.5, scalar2=1.5, op0=ALU.mult,
                                         op1=ALU.add))
                stepd(e, e.tensor_tensor(out=srow[0:1, 4:8], in0=srow[0:1, 4:8],
                                         in1=srow[0:1, 8:12], op=ALU.mult))
            return e.tensor_scalar(out=srow[0:1, 4:8], in0=srow[0:1, 4:8],
                                   scalar1=1e12, scalar2=None, op0=ALU.min)
        s.op("dve", sim_norm, reads=(("ps", 2),), writes=("srow", "seed4"))
        jb = [(PS[2][0][:, 0:4], ones_f[:], srow[0:1, 4:8], True, True)]
        mmseq(jb, reads=("srow", "ones_f"), writes=(("ps", 2),))

        def simn_apply(e):
            stepd(e, e.tensor_copy(out=gsm[:, 8:12], in_=PS[2][0][:, 0:4]))
            stepd(e, e.tensor_tensor(out=gsm[:, 0:4], in0=sim2d[:, 0:4],
                                     in1=gsm[:, 8:12], op=ALU.mult))
            stepd(e, e.tensor_tensor(out=gsm[:, 4:8], in0=sim2d[:, 4:8],
                                     in1=gsm[:, 8:12], op=ALU.mult))
            return e.tensor_copy(out=simnb[:], in_=gsm[:, 0:8])
        s.op("dve", simn_apply, reads=(("ps", 2), "sim2d"),
             writes=("simnb", "gsm"))

        s.op("dve", lambda e: e.tensor_scalar(
            out=srow[0:1, 12:13], in0=temp[:], scalar1=CLAMP_MAX, scalar2=0.5,
            op0=ALU.min, op1=ALU.mult),
            reads=("temp",), writes=("srow",))
        s.op("act", lambda e: e.activation(out=srow[0:1, 13:14],
                                           in_=srow[0:1, 12:13], func=AF.Tanh),
             reads=("srow",), writes=("srow",))

        def sc_fin(e):
            stepd(e, e.tensor_scalar(out=srow[0:1, 14:15], in0=srow[0:1, 13:14],
                                     scalar1=-1.0, scalar2=1.0, op0=ALU.mult,
                                     op1=ALU.add))
            stepd(e, e.reciprocal(out=srow[0:1, 14:15], in_=srow[0:1, 14:15]))
            stepd(e, e.tensor_scalar(out=srow[0:1, 13:14], in0=srow[0:1, 13:14],
                                     scalar1=1.0, scalar2=None, op0=ALU.add))
            return e.tensor_tensor(out=srow[0:1, 15:16], in0=srow[0:1, 13:14],
                                   in1=srow[0:1, 14:15], op=ALU.mult)
        s.op("dve", sc_fin, reads=("srow",), writes=("srow",))
        jb = [(PS[2][0][:, 4:5], ones_f[:], srow[0:1, 15:16], True, True)]
        mmseq(jb, reads=("srow", "ones_f"), writes=(("ps", 2),))
        s.op("dve", lambda e: e.tensor_copy(out=sc_bc[:], in_=PS[2][0][:, 4:5]),
             reads=(("ps", 2),), writes=("sc_bc",))

        c_S = consts[:, 0:1]
        c_HPI = consts[:, 1:2]
        c_HALF = consts[:, 2:3]
        c_NS = consts[:, 3:4]

        xTb = sb("xT_b", [128, D], BF16)
        OUTB = sb("out_b", [128, D], F32)
        MIB = sb("mi_b", [128, D], F32)
        KQT = [kqT, sb("kqT2", [128, D], BF16)]
        XIN = [x_in, sb("x_in2", [128, D], F32)]

        def tctx(ti):
            pp = ti % 2
            stp = STAT[pp]
            ks = (f"st1_{pp}", f"st2_{pp}", f"st3_{pp}", f"st4_{pp}")
            return dict(pp=pp, stp=stp, ks=ks, nrp=NRS[pp], knr=f"nrs_{pp}",
                        sdp=SEEDI[pp], ksd=f"seedi_{pp}",
                        xin=XIN[pp], kx=f"x_in_{pp}",
                        kqt=KQT[pp], kk=f"kqT_{pp}")

        def ln_negmu(t):
            stp1, stp3 = t["stp"][0], t["stp"][2]
            k1, k3 = t["ks"][0], t["ks"][2]
            s.op("dve", lambda e, stp1=stp1, stp3=stp3: e.tensor_scalar(
                out=stp1[:], in0=stp3[:], scalar1=-1.0 / D, scalar2=None,
                op0=ALU.mult),
                reads=(k3,), writes=(k1,))

        def ln_rstd(t, eps):
            stp1, stp2, stp3, stp4 = t["stp"]
            k1, k2, k3, k4 = t["ks"]
            nrp, knr = t["nrp"], t["knr"]

            def emit(e):
                stepp(e, e.tensor_scalar(out=nrp[:, 1:2], in0=stp4[:],
                                         scalar1=1.0 / D, scalar2=eps,
                                         op0=ALU.mult, op1=ALU.add))
                stepp(e, e.tensor_tensor(out=nrp[:, 2:3], in0=stp1[:],
                                         in1=stp1[:], op=ALU.mult))
                return e.tensor_tensor(out=stp2[:], in0=nrp[:, 1:2],
                                       in1=nrp[:, 2:3], op=ALU.subtract)
            s.op("pool", emit, reads=(k1, k4), writes=(k2, knr))
            rsqrt_nr(stp2, k2, stp3, k3, t["sdp"], t["ksd"], nrp, knr, iters=3)

        def ln_stats(t, eps):
            stp1, stp2, stp3, stp4 = t["stp"]
            k1, k2, k3, k4 = t["ks"]
            nrp, knr = t["nrp"], t["knr"]

            def emit(e):
                stepp(e, e.tensor_scalar(out=stp1[:], in0=stp3[:],
                                         scalar1=1.0 / D, scalar2=None,
                                         op0=ALU.mult))
                e.tensor_scalar(out=nrp[:, 1:2], in0=stp4[:], scalar1=1.0 / D,
                                scalar2=eps, op0=ALU.mult, op1=ALU.add)
                stepp(e, e.tensor_tensor(out=nrp[:, 2:3], in0=stp1[:],
                                         in1=stp1[:], op=ALU.mult))
                return e.tensor_tensor(out=stp2[:], in0=nrp[:, 1:2],
                                       in1=nrp[:, 2:3], op=ALU.subtract)
            s.op("pool", emit, reads=(k3, k4), writes=(k1, k2, knr))
            rsqrt_nr(stp2, k2, stp3, k3, t["sdp"], t["ksd"], nrp, knr, iters=3)

            def emit_nb(e):
                stepp(e, e.tensor_tensor(out=stp4[:], in0=stp1[:], in1=stp3[:],
                                         op=ALU.mult))
                return e.tensor_scalar(out=stp4[:], in0=stp4[:], scalar1=-1.0,
                                       scalar2=None, op0=ALU.mult)
            s.op("pool", emit_nb, reads=(k1, k3), writes=(k4,))

        def wrap_and_trig(src_key):
            s.op("act", lambda e: e.activation(out=A[:], in_=C[:], func=AF.Tanh),
                 reads=(src_key,), writes=("scrA",))
            s.op("dve", lambda e: e.scalar_tensor_tensor(
                out=A[:], in0=A[:], scalar=INV_2PI, in1=ps_bc[:],
                op0=ALU.mult, op1=ALU.mult),
                reads=("scrA", "ps_bc"), writes=("scrA",))
            s.op("dve", lambda e: e.tensor_scalar(
                out=U[:], in0=A[:], scalar1=KOFF, scalar2=None, op0=ALU.add),
                reads=("scrA",), writes=("scrU",))
            s.op("dve", lambda e: e.scalar_tensor_tensor(
                out=C[:], in0=U[:], scalar=-KOFF, in1=A[:],
                op0=ALU.add, op1=ALU.subtract),
                reads=("scrU", "scrA"), writes=("scrC",))
            s.op("dve", lambda e: e.tensor_scalar(
                out=A[:], in0=C[:], scalar1=-0.5, scalar2=None, op0=ALU.is_lt),
                reads=("scrC",), writes=("scrA",))
            s.op("dve", lambda e: e.tensor_tensor(
                out=C[:], in0=C[:], in1=A[:], op=ALU.add),
                reads=("scrC", "scrA"), writes=("scrC",))
            s.op("act", lambda e: e.activation(
                out=SKb[:], in_=C[:], func=AF.Sin, scale=c_NS),
                reads=("scrC", "consts"), writes=("SK",))
            s.op("dve", lambda e: e.tensor_scalar(
                out=A[:], in0=C[:], scalar1=-0.25, scalar2=None, op0=ALU.is_le),
                reads=("scrC",), writes=("scrA",))
            s.op("dve", lambda e: e.tensor_tensor(
                out=A[:], in0=C[:], in1=A[:], op=ALU.add),
                reads=("scrC", "scrA"), writes=("scrA",))
            s.op("act", lambda e: e.activation(
                out=CKb[:], in_=A[:], func=AF.Sin, scale=c_NS, bias=c_HPI),
                reads=("scrA", "consts"), writes=("CK",))

        def f_xkey(ti):
            t = tctx(ti)
            s.op("act", lambda e, x=t["xin"]: e.activation(
                out=xbb[:], in_=x[:], func=AF.Identity),
                reads=(t["kx"],), writes=("xbb",))
            transpose8(0, "xbb", xbb[:])
            evac_pair(0, xTb[:, 0:1024], "xTb")
            jobs = []
            for ec in range(NCH):
                for dc in range(NCH):
                    jobs.append((P[1][:, ec * 128:(ec + 1) * 128],
                                 wb["WkT"][:, dc * D + ec * 128:dc * D + (ec + 1) * 128],
                                 xTb[:, dc * 128:(dc + 1) * 128],
                                 dc == 0, dc == NCH - 1))
            mmseq(jobs, reads=("xTb",) + wkeys("WkT"), writes=(("ps", 1),))
            evac_pair(1, t["kqt"][:, 0:1024], t["kk"])

        def f_val_kp(ti):
            t = tctx(ti)
            kq = t["kqt"]
            proj_tok(0, "xTb", xTb[:], wb["WvT"], wkeys("WvT"))
            evac_pair_add(0, bv_bc, "bv_bc", VAL, "VAL")
            jobs = []
            for nh in range(2):
                for ec in range(NCH):
                    jobs.append((PS[1][nh][:, 0:512],
                                 kq[:, ec * 128:(ec + 1) * 128],
                                 wb["WkpT"][:, ec * D + nh * 512:ec * D + (nh + 1) * 512],
                                 ec == 0, ec == NCH - 1))
            mmseq(jobs, reads=(t["kk"],) + wkeys("WkpT"), writes=(("ps", 1),))
            evac_pair_add(1, bkp_bc, "bkp_bc", C, "scrC")
            wrap_and_trig("scrC")

        def f_qbm(ti):
            t = tctx(ti)
            jobs = []
            for ec in range(NCH):
                for dc in range(NCH):
                    jobs.append((P[1][:, ec * 128:(ec + 1) * 128],
                                 wb["WqT"][:, dc * D + ec * 128:dc * D + (ec + 1) * 128],
                                 xTb[:, dc * 128:(dc + 1) * 128],
                                 dc == 0, dc == NCH - 1))
            mmseq(jobs, reads=("xTb",) + wkeys("WqT"), writes=(("ps", 1),))
            s.op("dve", lambda e: e.tensor_tensor(out=xbb[:], in0=VAL[:],
                                                  in1=CKb[:], op=ALU.mult),
                 reads=("VAL", "CK"), writes=("xbb",))
            s.op("dve", lambda e: e.tensor_tensor(out=bsh[:], in0=VAL[:],
                                                  in1=SKb[:], op=ALU.mult),
                 reads=("VAL", "SK"), writes=("bsh",))
            evac_pair(1, t["kqt"][:, 0:1024], t["kk"])
            jobs = [(PS[0][nh][:, 0:512], LTs[:], xbb[:, nh * 512:(nh + 1) * 512],
                     True, True) for nh in range(2)]
            mmseq(jobs, reads=("xbb", "LT"), writes=(("ps", 0),))
            jobs = [(PS[1][nh][:, 0:512], LTs[:], bsh[:, nh * 512:(nh + 1) * 512],
                     True, True) for nh in range(2)]
            mmseq(jobs, reads=("bsh", "LT"), writes=(("ps", 1),))
            s.op("act", lambda e: e.activation(out=VAL[:], in_=P[0][:, :],
                                               func=AF.Identity),
                 reads=(("ps", 0),), writes=("VAL",))
            s.op("act", lambda e: e.activation(out=MIB[:], in_=P[1][:, :],
                                               func=AF.Identity),
                 reads=(("ps", 1),), writes=("MIB",))

        def f_qp(ti):
            t = tctx(ti)
            kq = t["kqt"]
            jobs = []
            for nh in range(2):
                for ec in range(NCH):
                    jobs.append((PS[2][nh][:, 0:512],
                                 kq[:, ec * 128:(ec + 1) * 128],
                                 wb["WqpT"][:, ec * D + nh * 512:ec * D + (nh + 1) * 512],
                                 ec == 0, ec == NCH - 1))
            mmseq(jobs, reads=(t["kk"],) + wkeys("WqpT"), writes=(("ps", 2),))
            evac_pair_add(2, bqp_bc, "bqp_bc", C, "scrC")
            wrap_and_trig("scrC")

        def f_retr(ti):
            t = tctx(ti)
            stp3, stp4 = t["stp"][2], t["stp"][3]
            k3, k4 = t["ks"][2], t["ks"][3]
            s.op("dve", lambda e: e.tensor_tensor(
                out=VAL[:], in0=VAL[:], in1=CKb[:], op=ALU.mult),
                reads=("VAL", "CK"), writes=("VAL",))
            s.op("dve", lambda e: e.tensor_tensor(
                out=MIB[:], in0=MIB[:], in1=SKb[:], op=ALU.mult),
                reads=("MIB", "SK"), writes=("MIB",))
            s.op("dve", lambda e, stp3=stp3: e.scalar_tensor_tensor(
                out=MIB[:], in0=VAL[:], scalar=1.0, in1=MIB[:], op0=ALU.mult,
                op1=ALU.add, accum_out=stp3[:]),
                reads=("VAL", "MIB"), writes=("MIB", k3))
            ln_negmu(t)
            s.op("act", lambda e, stp4=stp4: e.activation(
                out=junkb[:], in_=MIB[:], func=AF.Square, accum_out=stp4[:]),
                reads=("MIB",), writes=("junkb", k4))
            ln_rstd(t, EPS_IN)

        def f_lnT(ti):
            t = tctx(ti)
            stp1 = t["stp"][0]
            k1 = t["ks"][0]
            s.op("act", lambda e, stp1=stp1: e.activation(
                out=bsh[:], in_=MIB[:], func=AF.Identity, bias=stp1[:]),
                reads=("MIB", k1), writes=("bsh",))
            transpose8(3, "bsh", bsh[:])
            evac_pair(3, tsh[:, 0:1024], "tsh")

        def f_wo(ti):
            t = tctx(ti)
            stp3 = t["stp"][2]
            k3 = t["ks"][2]
            proj_tok(2, "tsh", tsh[:], wb["WoT"], wkeys("WoT"))
            s.op("dve", lambda e, stp3=stp3: e.scalar_tensor_tensor(
                out=C[:], in0=P[2][:, :], scalar=stp3[:], in1=bo_bc[:, 0:1024],
                op0=ALU.mult, op1=ALU.add),
                reads=(("ps", 2), k3, "bo_bc"), writes=("scrC",))
            s.op("dve", lambda e, x=t["xin"], stp3=stp3: e.scalar_tensor_tensor(
                out=B[:], in0=x[:], scalar=2.0, in1=C[:], op0=ALU.mult,
                op1=ALU.add, accum_out=stp3[:]),
                reads=(t["kx"], "scrC"), writes=("scrB", k3))

        def f_ln1x2(ti):
            t = tctx(ti)
            stp3, stp4 = t["stp"][2], t["stp"][3]
            k3, k4 = t["ks"][2], t["ks"][3]
            s.op("act", lambda e, stp4=stp4: e.activation(
                out=junkb[:], in_=B[:], func=AF.Square, accum_out=stp4[:]),
                reads=("scrB",), writes=("junkb", k4))
            ln_stats(t, EPS)
            s.op("act", lambda e, stp3=stp3, stp4=stp4: e.activation(
                out=X2[:], in_=B[:], func=AF.Identity, scale=stp3[:],
                bias=stp4[:]),
                reads=("scrB", k3, k4), writes=("X2",))
            s.op("act", lambda e, stp3=stp3, stp4=stp4: e.activation(
                out=bsh[:], in_=B[:], func=AF.Identity, scale=stp3[:],
                bias=stp4[:]),
                reads=("scrB", k3, k4), writes=("bsh",))
            transpose8(3, "bsh", bsh[:])
            evac_pair(3, tsh[:, 0:1024], "tsh")

        def f_ffn1(ti):
            t = tctx(ti)
            stp1, stp2 = t["stp"][0], t["stp"][1]
            k1, k2 = t["ks"][0], t["ks"][1]
            proj_tok(2, "tsh", tsh[:], wb["WfT"], wkeys("WfT"), stop_last=False)
            jobs = [(PS[3][0][:, 0:PROJ], tsh[:, dc * 128:(dc + 1) * 128],
                     WpTb[:, dc * PROJ:(dc + 1) * PROJ], dc == 0, dc == NCH - 1)
                    for dc in range(NCH)]
            mmseq(jobs, reads=("tsh",) + wkeys("WpT"), writes=(("ps", 3),))
            jobs = [(PS[3][1][:, 0:RTOT], tsh[:, dc * 128:(dc + 1) * 128],
                     WdTb[:, dc * RTOT:(dc + 1) * RTOT], dc == 0, dc == NCH - 1)
                    for dc in range(NCH)]
            mmseq(jobs, reads=("tsh",) + wkeys("WdT"), writes=(("ps", 3),))
            s.op("dve", lambda e: e.tensor_tensor(
                out=proj[:], in0=PS[3][0][:, 0:PROJ], in1=bp_bc[:], op=ALU.add),
                reads=(("ps", 3), "bp_bc"), writes=("proj",))
            s.op("dve", lambda e: e.tensor_copy(out=hsml[:],
                                                in_=PS[3][1][:, 0:RTOT]),
                 reads=(("ps", 3),), writes=("hsml",))
            s.op("act", lambda e, stp1=stp1: e.activation(
                out=junkb[:, 0:PROJ], in_=proj[:], func=AF.Square,
                accum_out=stp1[:]),
                reads=("proj",), writes=("junkb", k1))
            rsqrt_nr(stp1, k1, stp2, k2, t["sdp"], t["ksd"], t["nrp"], t["knr"],
                     clamp=1e12, iters=3)
            s.op("pool", lambda e, stp1=stp1, stp2=stp2: e.tensor_tensor(
                out=stp1[:], in0=stp2[:], in1=sc_bc[:], op=ALU.mult),
                reads=(k2, "sc_bc"), writes=(k1,))
            s.op("act", lambda e: e.activation(out=projb[:], in_=proj[:],
                                               func=AF.Identity),
                 reads=("proj",), writes=("projb",))
            jobs = [(PS[3][0][:, PROJ + c * 128:PROJ + (c + 1) * 128],
                     projb[:, c * 128:(c + 1) * 128], I128[:], True, True)
                    for c in range(2)]
            mmseq(jobs, reads=("projb", "I128"), writes=(("ps", 3),))
            s.op("dve", lambda e: e.tensor_copy(out=projT[:],
                                                in_=PS[3][0][:, PROJ:PROJ + 256]),
                 reads=(("ps", 3),), writes=("projT",))
            jobs = [(PS[3][0][:, 0:4], projT[:, c * 128:(c + 1) * 128],
                     simnb[:, c * 4:(c + 1) * 4], c == 0, c == 1)
                    for c in range(2)]
            mmseq(jobs, reads=("projT", "simnb"), writes=(("ps", 3),))
            s.op("dve", lambda e, stp1=stp1: e.tensor_scalar(
                out=lg[:], in0=PS[3][0][:, 0:4], scalar1=stp1[:], scalar2=None,
                op0=ALU.mult),
                reads=(("ps", 3), k1), writes=("lg",))

        def f_gate(ti):
            def gate1(e):
                stepd(e, e.tensor_reduce(gsm[:, 0:1], lg[:], AX.X, ALU.max))
                return e.tensor_scalar(out=gsm[:, 4:8], in0=lg[:], scalar1=-1.0,
                                       scalar2=gsm[:, 0:1], op0=ALU.mult,
                                       op1=ALU.add)
            s.op("dve", gate1, reads=("lg",), writes=("gsm",))
            s.op("act", lambda e: e.activation(out=gsm[:, 8:12], in_=gsm[:, 4:8],
                                               func=AF.Tanh, scale=c_HALF),
                 reads=("gsm", "consts"), writes=("gsm",))

            def gate2(e):
                stepd(e, e.tensor_scalar(out=gsm[:, 4:8], in0=gsm[:, 8:12],
                                         scalar1=-1.0, scalar2=1.0,
                                         op0=ALU.mult, op1=ALU.add))
                stepd(e, e.tensor_scalar(out=gsm[:, 8:12], in0=gsm[:, 8:12],
                                         scalar1=1.0, scalar2=None, op0=ALU.add))
                stepd(e, e.reciprocal(out=gsm[:, 8:12], in_=gsm[:, 8:12]))
                stepd(e, e.tensor_tensor(out=gsm[:, 4:8], in0=gsm[:, 4:8],
                                         in1=gsm[:, 8:12], op=ALU.mult))
                stepd(e, e.tensor_reduce(gsm[:, 1:2], gsm[:, 4:8], AX.X, ALU.add))
                stepd(e, e.reciprocal(out=gsm[:, 1:2], in_=gsm[:, 1:2]))
                stepd(e, e.tensor_scalar(out=gsm[:, 8:12], in0=lg[:],
                                         scalar1=gsm[:, 0:1], scalar2=None,
                                         op0=ALU.is_ge))
                stepd(e, e.tensor_scalar(out=gsm[:, 12:13], in0=gsm[:, 8:9],
                                         scalar1=-1.0, scalar2=1.0,
                                         op0=ALU.mult, op1=ALU.add))
                stepd(e, e.tensor_tensor(out=gsm[:, 9:10], in0=gsm[:, 9:10],
                                         in1=gsm[:, 12:13], op=ALU.mult))
                stepd(e, e.tensor_scalar(out=gsm[:, 13:14], in0=gsm[:, 9:10],
                                         scalar1=-1.0, scalar2=1.0,
                                         op0=ALU.mult, op1=ALU.add))
                stepd(e, e.tensor_tensor(out=gsm[:, 12:13], in0=gsm[:, 12:13],
                                         in1=gsm[:, 13:14], op=ALU.mult))
                stepd(e, e.tensor_tensor(out=gsm[:, 10:11], in0=gsm[:, 10:11],
                                         in1=gsm[:, 12:13], op=ALU.mult))
                stepd(e, e.tensor_scalar(out=gsm[:, 13:14], in0=gsm[:, 10:11],
                                         scalar1=-1.0, scalar2=1.0,
                                         op0=ALU.mult, op1=ALU.add))
                stepd(e, e.tensor_tensor(out=gsm[:, 12:13], in0=gsm[:, 12:13],
                                         in1=gsm[:, 13:14], op=ALU.mult))
                stepd(e, e.tensor_tensor(out=gsm[:, 11:12], in0=gsm[:, 11:12],
                                         in1=gsm[:, 12:13], op=ALU.mult))
                return e.tensor_scalar(out=gsm[:, 8:12], in0=gsm[:, 8:12],
                                       scalar1=gsm[:, 1:2], scalar2=None,
                                       op0=ALU.mult)
            s.op("dve", gate2, reads=("gsm", "lg"), writes=("gsm",))

            def hgate(e):
                inst = None
                for i, (off, r) in enumerate(zip(ROFF, RANKS)):
                    inst = e.tensor_scalar(out=hpb[:, off:off + r],
                                           in0=hsml[:, off:off + r],
                                           scalar1=gsm[:, 8 + i:9 + i],
                                           scalar2=None, op0=ALU.mult)
                return inst
            s.op("dve", hgate, reads=("hsml", "gsm"), writes=("hpb",))

        def f_ffn2(ti):
            t = tctx(ti)
            stp3, stp4 = t["stp"][2], t["stp"][3]
            k3, k4 = t["ks"][2], t["ks"][3]
            nrp, knr = t["nrp"], t["knr"]
            jobs = [(PS[3][1][0:RTOT, 128:256], hpb[:], I128[:], True, True)]
            mmseq(jobs, reads=("hpb", "I128"), writes=(("ps", 3),))
            s.op("dve", lambda e: e.tensor_copy(out=hpT[:],
                                                in_=PS[3][1][0:RTOT, 128:256]),
                 reads=(("ps", 3),), writes=("hpT",))
            jobs = [(PS[2][nh][:, 0:512], hpT[:],
                     WuTb[:, nh * 512:(nh + 1) * 512], False, True)
                    for nh in range(2)]
            mmseq(jobs, reads=("hpT",) + wuk, writes=(("ps", 2),))

            def pre3_ev(e, nrp=nrp, stp3=stp3):
                e.scalar_tensor_tensor(out=B[:, 0:512], in0=X2[:, 0:512],
                                       scalar=1.0, in1=PS[2][0][:, :],
                                       op0=ALU.mult, op1=ALU.add,
                                       accum_out=nrp[:, 1:2])
                stepd(e, e.scalar_tensor_tensor(out=B[:, 512:1024],
                                                in0=X2[:, 512:1024],
                                                scalar=1.0, in1=PS[2][1][:, :],
                                                op0=ALU.mult, op1=ALU.add,
                                                accum_out=nrp[:, 2:3]))
                return e.tensor_tensor(out=stp3[:], in0=nrp[:, 1:2],
                                       in1=nrp[:, 2:3], op=ALU.add)
            s.op("dve", pre3_ev, reads=(("ps", 2), "X2"),
                 writes=("scrB", k3, knr))
            s.op("act", lambda e, stp4=stp4: e.activation(
                out=junkb[:], in_=B[:], func=AF.Square, accum_out=stp4[:]),
                reads=("scrB",), writes=("junkb", k4))
            ln_stats(t, EPS)

        def f_out(ti):
            t = tctx(ti)
            stp3, stp4 = t["stp"][2], t["stp"][3]
            k3, k4 = t["ks"][2], t["ks"][3]
            s.op("act", lambda e, stp3=stp3, stp4=stp4: e.activation(
                out=OUTB[:], in_=B[:], func=AF.Identity, scale=stp3[:],
                bias=stp4[:]),
                reads=("scrB", k3, k4), writes=("OUTB",))
            s.op("out", lambda e, r0=ti * 128: e.dma_start(
                out=hy[r0:r0 + 128, :], in_=OUTB[:]),
                reads=("OUTB",), writes=())

        f_xkey(0)
        f_val_kp(0)
        f_qbm(0)
        for i in range(NT):
            if i + 1 < NT:
                tn = tctx(i + 1)
                dma_in(tn["xin"][:], hx[(i + 1) * 128:(i + 2) * 128, :],
                       writes=(tn["kx"],))
                f_xkey(i + 1)
            if i > 0:
                f_out(i - 1)
            f_qp(i)
            f_retr(i)
            if i + 1 < NT:
                f_val_kp(i + 1)
            f_lnT(i)
            f_wo(i)
            if i + 1 < NT:
                f_qbm(i + 1)
            f_ln1x2(i)
            f_ffn1(i)
            f_gate(i)
            f_ffn2(i)
        f_out(NT - 1)

        s.op("pool", lambda e: e.nop(),
             extra_waits=tuple((ch, s.cnt[ch]) for ch in DMA_OUT if s.cnt[ch]))

        qmap = {"pe": "tensor", "act": "scalar", "dve": "vector",
                "pool": "gpsimd", "sp": "sync"}
        with nc.Block() as block:
            s.emit(sems, block, qmap)

    return nc


_PROG = None


def kernel(**inputs):
    global _PROG
    x = np.ascontiguousarray(np.asarray(inputs["L0_attn_x"], np.float32))
    assert x.shape == (8, S, D)
    hp = host_prep(inputs)
    if _PROG is None:
        _PROG = build_core_program()
    in_maps = []
    for b in range(8):
        m = {"hx": x[b]}
        m.update(hp)
        in_maps.append(m)
    res = run_bass_kernel_spmd(_PROG, in_maps, core_ids=list(range(8)))
    out = np.stack([np.asarray(r["hy"]) for r in res.results], axis=0)
    return out.astype(np.float32)


# version 06
# speedup vs baseline: 1.2457x; 1.0868x over previous
import os
import numpy as np
import ml_dtypes
import concourse.bass as bass
import concourse.mybir as mybir
from concourse.bass_utils import run_bass_kernel_spmd

F32 = mybir.dt.float32
BF16 = mybir.dt.bfloat16
I32 = mybir.dt.int32
AF = mybir.ActivationFunctionType
ALU = mybir.AluOpType
AX = mybir.AxisListType

D = 1024
S = 4096
NCH = 8
NT = int(os.environ.get("KNT", "32"))
CHUNK = 64
EPS = 1e-5
EPS_IN = 1024 * 1e-5
CLAMP_MAX = float(np.log(1.0 / 0.01))
RANKS = [1, 2, 4, 8]
ROFF = [0, 1, 3, 7]
RTOT = 15
PROJ = 256

S_SCALE = float(np.nextafter(np.float32(2 * np.pi), np.float32(0)))
HALF_PI = float(np.float32(np.pi / 2))
INV_2PI = float(np.float32(1.0 / (2 * np.pi)))
KOFF = 64.0

ENGINES = ("pe", "act", "dve", "pool", "sp")
DMA_IN = tuple(f"in{i}" for i in range(4))
DMA_OUT = tuple(f"out{i}" for i in range(2))
DMA_CH = DMA_IN + DMA_OUT


class Sched:
    def __init__(self):
        self.ops = []
        self.cnt = {e: 0 for e in ENGINES + DMA_CH}
        self.writer = {}
        self.readers = {}
        self.need_inc = set()
        self.meta = []
        self.din = 0
        self.dout = 0

    def op(self, eng, emit, reads=(), writes=(), extra_waits=(), cost=None,
           tag=""):
        if eng == "in":
            eng = DMA_IN[self.din % len(DMA_IN)]
            self.din += 1
        elif eng == "out":
            eng = DMA_OUT[self.dout % len(DMA_OUT)]
            self.dout += 1
        step = 16 if eng in DMA_CH else 1
        waits = set(extra_waits)
        if eng in DMA_CH and self.cnt[eng] > 0:
            waits.add((eng, self.cnt[eng]))
        for k in reads:
            w = self.writer.get(k)
            if w is not None:
                waits.add(w)
        for k in writes:
            w = self.writer.get(k)
            if w is not None:
                waits.add(w)
            for r in self.readers.get(k, ()):
                waits.add(r)
        self.cnt[eng] += step
        me = (eng, self.cnt[eng])
        for k in reads:
            self.readers.setdefault(k, []).append(me)
        for k in writes:
            self.writer[k] = me
            self.readers[k] = []
        for w in waits:
            self.need_inc.add(w)
        if eng in DMA_CH:
            self.need_inc.add(me)
        if cost is None:
            cost = {"pe": 0.3, "act": 1.2, "dve": 1.2, "pool": 0.25,
                    "sp": 0.05}.get(eng, 2.5)
            if eng.startswith("in") or eng.startswith("out"):
                cost = 2.5
        self.meta.append((cost, tag))
        self.ops.append((eng, emit, sorted(waits), me))
        return me

    def simulate(self, gap=0.15, xgap=0.12):
        qfree = {}
        done = {}
        rows = []
        for idx, (eng, _, waits, me) in enumerate(self.ops):
            cost, tag = self.meta[idx]
            q = eng if eng in ENGINES else ("dma" + eng[-1] if eng[-1].isdigit()
                                            else eng)
            t0 = qfree.get(q, 0.0)
            src_max = 0.0
            for w in waits:
                src_max = max(src_max, done.get(w, 0.0) + xgap)
            t0 = max(t0, src_max)
            t1 = t0 + cost
            qfree[q] = t1 + gap
            done[me] = t1
            rows.append((t0, t1, eng, tag, me, waits))
        busy = {}
        for idx, (eng, _, _, _) in enumerate(self.ops):
            busy[eng] = busy.get(eng, 0.0) + self.meta[idx][0]
        return rows, busy

    def emit(self, sems, block, qmap):
        import bisect
        inc_list = {}
        for (e, c) in self.need_inc:
            inc_list.setdefault(e, []).append(c)
        for e in inc_list:
            inc_list[e].sort()

        def remap(src, val):
            lst = inc_list.get(src, ())
            r = bisect.bisect_right(lst, val)
            assert r > 0 and lst[r - 1] == val, (src, val)
            return r * (16 if src in DMA_CH else 1)

        per_q = {e: [] for e in ENGINES}
        for eng, emitfn, waits, me in self.ops:
            q = eng if eng in per_q else ("sp" if eng.startswith("in") else "pool")
            per_q[q].append((emitfn, waits, me if me in self.need_inc else None, eng))
        for q, items in per_q.items():
            if not items:
                continue

            def make_body(items):
                def body(e):
                    last = {}
                    for emitfn, waits, inc, semeng in items:
                        for (src, val) in waits:
                            v = remap(src, val)
                            if last.get(src, -1) >= v:
                                continue
                            e.wait_ge(sems[src], v)
                            last[src] = v
                        inst = emitfn(e)
                        if inc is not None:
                            inst.then_inc(sems[semeng],
                                          16 if semeng in DMA_CH else 1)
                return body
            getattr(block, qmap[q])(make_body(items))


def host_prep(inp):
    f = np.float32
    T = lambda a: np.ascontiguousarray(np.asarray(a, f).T)
    chunk2d = lambda v: np.ascontiguousarray(np.asarray(v, f).reshape(NCH, 128).T)
    bcast = lambda v, n: np.ascontiguousarray(
        np.broadcast_to(np.asarray(v, f).reshape(1, n), (128, n)))

    d = {}
    d["WkT"] = T(inp["L0_attn_Wk"])
    d["WvT"] = T(inp["L0_attn_Wv"])
    d["WqT"] = T(inp["L0_attn_Wq"])
    d["WkpT"] = T(inp["L0_attn_Wkp"])
    d["WqpT"] = T(inp["L0_attn_Wqp"])
    d["WoT"] = T(inp["L0_attn_Wo"])
    d["WfT"] = T(inp["L0_ffn_W"])
    d["WpT"] = T(inp["L0_ffn_Wp"])
    wd = np.concatenate([np.asarray(inp[f"L0_ffn_Wd{i}"], f) for i in range(4)], 0)
    d["WdT"] = T(wd)
    wu = np.concatenate([np.asarray(inp[f"L0_ffn_Wu{i}"], f) for i in range(4)], 1)
    d["WuT"] = T(wu)

    d["bk2d"] = chunk2d(inp["L0_attn_bk"])
    d["bq2d"] = chunk2d(inp["L0_attn_bq"])
    d["lnb2d"] = chunk2d(inp["L0_attn_ln_b"])
    d["g2d"] = chunk2d(inp["L0_attn_ln_g"])
    d["bv_row"] = np.asarray(inp["L0_attn_bv"], f).reshape(1, D).copy()
    d["ps_bc"] = bcast(inp["L0_attn_phase_scale"], D)
    d["bp_bc"] = bcast(inp["L0_ffn_bp"], PROJ)
    d["bkp_row"] = np.asarray(inp["L0_attn_bkp"], f).reshape(1, D).copy()
    d["bqp_row"] = np.asarray(inp["L0_attn_bqp"], f).reshape(1, D).copy()
    d["bo_bc"] = bcast(inp["L0_attn_bo"], D)
    sim = np.asarray(inp["L0_ffn_sim"], f).reshape(2, 128, 4).transpose(1, 0, 2)
    d["sim2d"] = np.ascontiguousarray(sim.reshape(128, 8))
    d["temp"] = np.asarray(inp["L0_ffn_temp"], f).reshape(1, 1).copy()

    L = np.zeros((128, 128), f)
    for b in range(2):
        L[b * CHUNK:(b + 1) * CHUNK, b * CHUNK:(b + 1) * CHUNK] = \
            np.tril(np.ones((CHUNK, CHUNK), f))
    d["LTb"] = np.ascontiguousarray(L.T).astype(ml_dtypes.bfloat16)
    d["I128b"] = np.eye(128, dtype=ml_dtypes.bfloat16)
    d["ones_f"] = np.ones((1, 128), f)
    d["ones2b"] = np.ones((2, 128), dtype=ml_dtypes.bfloat16)
    d["ones_c"] = np.ones((128, 1), f)
    d["magic"] = np.full((128, 1), 0x5F3759DF, np.int32)
    consts = np.zeros((128, 8), f)
    consts[:, 0] = S_SCALE
    consts[:, 1] = HALF_PI
    consts[:, 2] = 0.5
    consts[:, 3] = -S_SCALE
    d["consts"] = consts
    d["iota4"] = np.ascontiguousarray(
        np.broadcast_to(np.arange(4, dtype=f), (128, 4)))
    gpad = np.full((128, 32), 0.0, f)
    gpad[:, 4:8] = -1e30
    d["gpad"] = gpad
    return d


def build_core_program():
    nc = bass.Bass("TRN2", target_bir_lowering=False)

    hx = nc.dram_tensor("hx", [S, D], F32, kind="ExternalInput").ap()
    hy = nc.dram_tensor("hy", [S, D], F32, kind="ExternalOutput").ap()
    W_NAMES = ["WkT", "WvT", "WqT", "WkpT", "WqpT", "WoT", "WfT"]
    h = {}
    for n in W_NAMES:
        h[n] = nc.dram_tensor(n, [D, D], F32, kind="ExternalInput").ap()
    h["WpT"] = nc.dram_tensor("WpT", [D, PROJ], F32, kind="ExternalInput").ap()
    h["WdT"] = nc.dram_tensor("WdT", [D, RTOT], F32, kind="ExternalInput").ap()
    h["WuT"] = nc.dram_tensor("WuT", [RTOT, D], F32, kind="ExternalInput").ap()
    for n, shp in [("bk2d", [128, 8]), ("bq2d", [128, 8]), ("lnb2d", [128, 8]),
                   ("g2d", [128, 8]), ("bv_row", [1, D]), ("ps_bc", [128, D]),
                   ("bp_bc", [128, PROJ]), ("bkp_row", [1, D]),
                   ("bqp_row", [1, D]), ("bo_bc", [128, D]), ("sim2d", [128, 8]),
                   ("temp", [1, 1]), ("ones_f", [1, 128]),
                   ("ones_c", [128, 1]), ("consts", [128, 8]),
                   ("gpad", [128, 32])]:
        h[n] = nc.dram_tensor(n, shp, F32, kind="ExternalInput").ap()
    h["iota4"] = nc.dram_tensor("iota4", [128, 4], F32,
                                kind="ExternalInput").ap()
    h["LTb"] = nc.dram_tensor("LTb", [128, 128], BF16, kind="ExternalInput").ap()
    h["ones2b"] = nc.dram_tensor("ones2b", [2, 128], BF16, kind="ExternalInput").ap()
    h["I128b"] = nc.dram_tensor("I128b", [128, 128], BF16, kind="ExternalInput").ap()
    h["magic"] = nc.dram_tensor("magic", [128, 1], I32, kind="ExternalInput").ap()
    hlscr = nc.dram_tensor("hlscr", [1, 2 * D], BF16, kind="Internal").ap()

    from contextlib import ExitStack
    ctx = ExitStack()
    sb = lambda name, shape, dt: ctx.enter_context(nc.sbuf_tensor(name, shape, dt))
    with ctx:
        wb = {n: sb(n + "_b", [128, NCH * D], BF16) for n in W_NAMES}
        WpTb = sb("WpT_b", [128, NCH * PROJ], BF16)
        WdTb = sb("WdT_b", [128, NCH * RTOT], BF16)
        WuTb = sb("WuT_b", [RTOT, D], BF16)
        LTs = sb("LT_b", [128, 128], BF16)
        I128 = sb("I128_b", [128, 128], BF16)

        bk2d = sb("bk2d_s", [128, 8], F32)
        bq2d = sb("bq2d_s", [128, 8], F32)
        lnb2d = sb("lnb2d_s", [128, 8], F32)
        g2d = sb("g2d_s", [128, 8], F32)
        b2db = sb("b2d_bb", [128, 24], BF16)
        ps_bc = sb("ps_bc_s", [128, D], F32)
        bp_bc = sb("bp_bc_s", [128, PROJ], F32)
        rowtmp_t = sb("rowtmp", [2, D], F32)
        rowbuf = sb("rowbuf", [1, D], F32)
        kp_hl = sb("kp_hl", [2, D], BF16)
        qp_hl = sb("qp_hl", [2, D], BF16)
        v_hl = sb("v_hl", [2, D], BF16)
        ones2 = sb("ones2_s", [2, 128], BF16)
        iota4 = sb("iota4_s", [128, 4], F32)
        gidx = sb("gidx_s", [128, 8], mybir.dt.uint32)
        bo_bc = sb("bo_bc_s", [128, D], F32)
        sim2d = sb("sim2d_s", [128, 8], F32)
        simnb = sb("simn_b", [128, 8], BF16)
        temp = sb("temp_s", [1, 1], F32)
        ones_f = sb("ones_f_s", [1, 128], F32)
        ones_c = sb("ones_c_s", [128, 1], F32)
        consts = sb("consts_s", [128, 8], F32)
        magic = sb("magic_s", [128, 1], I32)
        sc_bc = sb("sc_bc_s", [128, 1], F32)

        x_in = sb("x_in", [128, D], F32)
        A = sb("scrA", [128, D], F32)
        B = sb("scrB", [128, D], F32)
        C = sb("scrC", [128, D], F32)
        U = sb("scrU", [128, D], I32)
        SKb = sb("sinK", [128, D], BF16)
        CKb = sb("cosK", [128, D], BF16)
        VAL = sb("val", [128, D], F32)
        X2 = sb("x2", [128, D], F32)
        xbb = sb("xb_b", [128, D], BF16)
        bsh = sb("bshared", [128, D], BF16)
        tsh = sb("tshared", [128, D], BF16)
        kqT = sb("kqT", [128, D], BF16)
        junkb = sb("junk_b", [128, D], BF16)
        proj = sb("proj_s", [128, PROJ], F32)
        projb = sb("proj_bb", [128, PROJ], BF16)
        projT = sb("projT_b", [128, 2 * 128], BF16)
        hsml = sb("h_small", [128, RTOT], F32)
        hpb = sb("hp_b", [128, RTOT], BF16)
        hpT = sb("hpT_b", [RTOT, 128], BF16)
        gsm = sb("g_small", [128, 32], F32)
        lg = gsm[:, 0:4]
        STAT = [[sb(f"stat{p}_{i}", [128, 1], F32) for i in range(4)]
                for p in range(2)]
        NRS = [sb(f"nr_small{p}", [128, 4], F32) for p in range(2)]
        SEEDI = [sb(f"seed_i{p}", [128, 1], I32) for p in range(2)]
        seed4 = sb("seed4_i", [1, 8], I32)
        rowtmp = rowtmp_t[0:1, :]
        srow = sb("srow", [1, 16], F32)
        st1, st2, st3, st4 = STAT[0]
        nrs = NRS[0]
        seedi = SEEDI[0]

        xTb = sb("xT_b", [128, D], BF16)
        OUTB = sb("out_b", [128, D], F32)
        MIB = sb("mi_b", [128, D], F32)
        KQT = [kqT, sb("kqT2", [128, D], BF16)]
        XIN = [x_in, sb("x_in2", [128, D], F32)]

        P = [ctx.enter_context(nc.psum_tensor(f"P{i}", [128, 1024], F32))
             for i in range(4)]
        PS = [[P[i][:, 0:512], P[i][:, 512:1024]] for i in range(4)]

        sems = {}
        for e in ENGINES + DMA_CH:
            sems[e] = ctx.enter_context(nc.semaphore("sem_" + e))
        chain_sems = {e: ctx.enter_context(nc.semaphore("chain_" + e))
                      for e in ("dve", "act", "pool")}
        chain_cnt = {e: 0 for e in chain_sems}

        def mkstep(eng):
            def step(e, inst):
                chain_cnt[eng] += 1
                inst.then_inc(chain_sems[eng], 1)
                e.wait_ge(chain_sems[eng], chain_cnt[eng])
            return step
        stepd = mkstep("dve")
        stepp = mkstep("pool")

        s = Sched()
        wkeys = lambda n: tuple((n, c) for c in range(NCH))

        def dma_in(dst_ap, src_ap, writes, reads=()):
            return s.op("in", lambda e, d=dst_ap, sr=src_ap: e.dma_start(out=d, in_=sr),
                        reads=reads, writes=writes)

        def mmseq(jobs, reads, writes, stop_last=True, tag="mm"):
            jobs = list(jobs)
            cost = sum(max(j[2].shape[-1], 70) / 2400.0 + 0.015 for j in jobs)

            def emit(e, jobs=jobs):
                inst = None
                for (o, l, r, st, sp) in jobs:
                    inst = e.matmul(out=o, lhsT=l, rhs=r, start=st, stop=sp)
                return inst
            return s.op("pe", emit, reads=reads, writes=writes, cost=cost,
                        tag=tag)

        def proj_tok(slot, lhsT_key, lhsT_ap, wtile, wkey_t, stop_last=True):
            jobs = []
            for nh in range(2):
                for dc in range(NCH):
                    jobs.append((PS[slot][nh][:, 0:512],
                                 lhsT_ap[:, dc * 128:(dc + 1) * 128],
                                 wtile[:, dc * D + nh * 512:dc * D + (nh + 1) * 512],
                                 dc == 0, (dc == NCH - 1) and stop_last))
            return mmseq(jobs, reads=(lhsT_key,) + wkey_t,
                         writes=(("ps", slot),))

        def feat_proj(slot, wtile, wkey_t):
            jobs = []
            for ec in range(NCH):
                bank = PS[slot][ec // 4]
                for dc in range(NCH):
                    jobs.append((bank[:, (ec % 4) * 128:(ec % 4 + 1) * 128],
                                 wtile[:, dc * D + ec * 128:dc * D + (ec + 1) * 128],
                                 tsh[:, dc * 128:(dc + 1) * 128],
                                 dc == 0, dc == NCH - 1))
            return mmseq(jobs, reads=("tsh",) + wkey_t, writes=(("ps", slot),))

        def transpose8(slot, src_key, src_ap):
            jobs = []
            for c in range(NCH):
                bank = PS[slot][c // 4]
                jobs.append((bank[:, (c % 4) * 128:(c % 4 + 1) * 128],
                             src_ap[:, c * 128:(c + 1) * 128], I128[:], True, True))
            return mmseq(jobs, reads=(src_key, "I128"), writes=(("ps", slot),))

        def evac_pair(slot, dst_ap, dst_key, eng="act"):
            if eng == "act":
                def emit(e, sl=slot, d=dst_ap):
                    return e.activation(out=d, in_=P[sl][:, :], func=AF.Identity)
            else:
                def emit(e, sl=slot, d=dst_ap):
                    return e.tensor_copy(out=d, in_=P[sl][:, :])
            return s.op(eng, emit, reads=(("ps", slot),), writes=(dst_key,),
                        cost=1.2, tag=f"evac{slot}")

        def evac_pair_add(slot, bcast_tile, bkey, dst, dst_key):
            def emit(e, sl=slot, bc=bcast_tile, d=dst):
                return e.tensor_tensor(out=d[:, 0:1024], in0=P[sl][:, :],
                                       in1=bc[:, 0:1024], op=ALU.add)
            return s.op("dve", emit, reads=(("ps", slot), bkey),
                        writes=(dst_key,), cost=1.25, tag=f"evadd{slot}")

        def rsqrt_nr(src, src_key, dst, dst_key, sdi, sdi_key, nr, nr_key,
                     clamp=None, iters=4):
            def emit_seed(e, src=src, dst=dst, sdi=sdi):
                sbits = src[:].bitcast(I32)
                stepd(e, e.tensor_scalar(out=sdi[:], in0=sbits, scalar1=1,
                                         scalar2=None,
                                         op0=ALU.arith_shift_right))
                stepd(e, e.tensor_tensor(out=sdi[:], in0=magic[:],
                                         in1=sdi[:], op=ALU.subtract))
                return e.tensor_copy(out=dst[:], in_=sdi[:].bitcast(F32))
            s.op("dve", emit_seed, reads=(src_key, "magic"),
                 writes=(dst_key, sdi_key), cost=1.0, tag="nr_seed")

            def emit_nr(e, src=src, dst=dst, nr=nr):
                inst = None
                for it in range(iters):
                    stepp(e, e.tensor_tensor(out=nr[:, 0:1], in0=dst[:],
                                             in1=dst[:], op=ALU.mult))
                    stepp(e, e.tensor_tensor(out=nr[:, 0:1], in0=nr[:, 0:1],
                                             in1=src[:], op=ALU.mult))
                    stepp(e, e.tensor_scalar(out=nr[:, 0:1], in0=nr[:, 0:1],
                                             scalar1=-0.5, scalar2=1.5,
                                             op0=ALU.mult, op1=ALU.add))
                    inst = e.tensor_tensor(out=dst[:], in0=dst[:],
                                           in1=nr[:, 0:1], op=ALU.mult)
                    if it < iters - 1 or clamp is not None:
                        stepp(e, inst)
                if clamp is not None:
                    inst = e.tensor_scalar(out=dst[:], in0=dst[:], scalar1=clamp,
                                           scalar2=None, op0=ALU.min)
                return inst
            return s.op("pool", emit_nr, reads=(src_key, dst_key),
                        writes=(dst_key, nr_key), cost=0.36 * (4 * iters + 1),
                        tag="nr_loop")

        def _ap(t):
            return t if isinstance(t, bass.AP) else t[:]
        for name, tile in [("bk2d", bk2d), ("bq2d", bq2d), ("lnb2d", lnb2d),
                           ("g2d", g2d), ("ps_bc", ps_bc),
                           ("bp_bc", bp_bc), ("bo_bc", bo_bc),
                           ("iota4", iota4),
                           ("sim2d", sim2d), ("temp", temp), ("ones_f", ones_f),
                           ("ones_c", ones_c), ("consts", consts)]:
            dma_in(_ap(tile), h[name], writes=(name,))
        dma_in(LTs[:], h["LTb"], writes=("LT",))
        dma_in(ones2[:], h["ones2b"], writes=("ones2",))

        dma_in(I128[:], h["I128b"], writes=("I128",))
        dma_in(magic[:], h["magic"], writes=("magic",))
        dma_in(x_in[:], hx[0:128, :], writes=("x_in",))

        stage = [(A, "scrA"), (B, "scrB")]
        cast_engines = ["dve", "act", "pool"]
        ei = 0
        for wi, wname in enumerate(W_NAMES + ["WpT", "WdT"]):
            src = h[wname]
            ncols = {"WpT": PROJ, "WdT": RTOT}.get(wname, D)
            dstt = {"WpT": WpTb, "WdT": WdTb}.get(wname, wb.get(wname))
            for c in range(NCH):
                stg, skey = stage[(wi * NCH + c) % 2]
                dma_in(stg[:, 0:ncols], src[c * 128:(c + 1) * 128, :],
                       writes=(skey,))
                eng = cast_engines[ei % 3]
                ei += 1
                dst_ap = dstt[:, c * ncols:(c + 1) * ncols]
                if wname == "WoT":
                    def emit(e, stg=stg, dst_ap=dst_ap, gc=g2d[:, c:c + 1], n=ncols):
                        return e.tensor_scalar(out=dst_ap, in0=stg[:, 0:n],
                                               scalar1=gc, scalar2=None,
                                               op0=ALU.mult)
                    s.op("dve", emit, reads=(skey, "g2d"), writes=((wname, c),))
                elif eng == "act":
                    def emit(e, stg=stg, dst_ap=dst_ap, n=ncols):
                        return e.activation(out=dst_ap, in_=stg[:, 0:n],
                                            func=AF.Identity)
                    s.op("act", emit, reads=(skey,), writes=((wname, c),))
                else:
                    def emit(e, stg=stg, dst_ap=dst_ap, n=ncols):
                        return e.tensor_copy(out=dst_ap, in_=stg[:, 0:n])
                    s.op(eng, emit, reads=(skey,), writes=((wname, c),))
        dma_in(C[0:RTOT, 0:D], h["WuT"], writes=("scrC",))
        s.op("dve", lambda e: e.tensor_copy(out=WuTb[:], in_=C[0:RTOT, 0:D]),
             reads=("scrC",), writes=("WuT", 0))
        wuk = (("WuT", 0),)

        def small_cast(e):
            e.tensor_copy(out=b2db[:, 0:8], in_=bk2d[:])
            e.tensor_copy(out=b2db[:, 8:16], in_=bq2d[:])
            return e.tensor_copy(out=b2db[:, 16:24], in_=lnb2d[:])
        s.op("pool", small_cast, reads=("bk2d", "bq2d", "lnb2d"), writes=("b2db",))

        def hilo_from_row(dst, dkey):
            h0 = KQT[0][0:1, 0:D]
            h1 = KQT[1][0:1, 0:D]

            def emit(e):
                stepd(e, e.tensor_copy(out=h0, in_=rowtmp))
                stepd(e, e.tensor_copy(out=rowbuf[:], in_=h0))
                stepd(e, e.tensor_tensor(out=rowbuf[:], in0=rowtmp,
                                         in1=rowbuf[:], op=ALU.subtract))
                return e.tensor_copy(out=h1, in_=rowbuf[:])
            s.op("dve", emit, reads=("rowtmp",),
                 writes=("kqT_0", "kqT_1", "rowbuf"), cost=0.5, tag="hilo")
            s.op("out", lambda e: e.dma_start(out=hlscr[0:1, 0:D], in_=h0),
                 reads=("kqT_0",), writes=("hlscr",))
            s.op("out", lambda e: e.dma_start(out=hlscr[0:1, D:2 * D], in_=h1),
                 reads=("kqT_1",), writes=("hlscr",))
            s.op("in", lambda e, dst=dst: e.dma_start(
                out=dst[:], in_=hlscr.rearrange("a (b c) -> (a b) c", b=2)),
                reads=("hlscr",), writes=(dkey,))

        def row_fold(boff, wname, wkey_t, add_ap, addkey, dst, dkey, slot_row):
            jobs = []
            for c in range(NCH):
                for nh in range(2):
                    jobs.append((PS[slot_row][nh][0:1, 0:512],
                                 b2db[:, boff + c:boff + c + 1],
                                 wb[wname][:, c * D + nh * 512:c * D + (nh + 1) * 512],
                                 c == 0, c == NCH - 1))
            mmseq(jobs, reads=("b2db",) + wkey_t, writes=(("ps", slot_row),))

            def ev(e, sl=slot_row, ar=add_ap):
                stepd(e, e.tensor_tensor(out=rowtmp[0:1, 0:512],
                                         in0=PS[sl][0][0:1, :],
                                         in1=ar[0:1, 0:512], op=ALU.add))
                return e.tensor_tensor(out=rowtmp[0:1, 512:1024],
                                       in0=PS[sl][1][0:1, :],
                                       in1=ar[0:1, 512:1024], op=ALU.add)
            s.op("dve", ev, reads=(("ps", slot_row), addkey), writes=("rowtmp",))
            hilo_from_row(dst, dkey)

        dma_in(rowbuf[:], h["bkp_row"], writes=("rowbuf",))
        row_fold(0, "WkpT", wkeys("WkpT"), rowbuf[:], "rowbuf",
                 kp_hl, "kp_hl", 0)
        dma_in(rowbuf[:], h["bqp_row"], writes=("rowbuf",))
        row_fold(8, "WqpT", wkeys("WqpT"), rowbuf[:], "rowbuf",
                 qp_hl, "qp_hl", 2)
        dma_in(rowtmp, h["bv_row"], writes=("rowtmp",))
        hilo_from_row(v_hl, "v_hl")

        for c in range(NCH):
            stg, skey = stage[c % 2]
            dma_in(stg[:, 0:D], h["WoT"][c * 128:(c + 1) * 128, :], writes=(skey,))

            def cem(e, stg=stg):
                return e.tensor_copy(out=bsh[:, 0:D], in_=stg[:, 0:D])
            s.op("dve", cem, reads=(skey,), writes=("bsh",))
            jb = []
            for nh in range(2):
                jb.append((PS[0][nh][0:1, 0:512], b2db[:, 16 + c:17 + c],
                           bsh[:, nh * 512:(nh + 1) * 512],
                           c == 0, c == NCH - 1))
            mmseq(jb, reads=("bsh", "b2db"), writes=(("ps", 0),))

        def bo_ev(e):
            e.tensor_copy(out=rowtmp[0:1, 0:512], in_=PS[0][0][0:1, :])
            return e.tensor_copy(out=rowtmp[0:1, 512:1024], in_=PS[0][1][0:1, :])
        s.op("dve", bo_ev, reads=(("ps", 0),), writes=("rowtmp",))
        jobs2 = [(PS[1][nh][:, 0:512], ones_f[:], rowtmp[0:1, nh * 512:(nh + 1) * 512],
                  True, True) for nh in range(2)]
        mmseq(jobs2, reads=("rowtmp", "ones_f"), writes=(("ps", 1),))

        def bo_bc_ev(e):
            e.tensor_tensor(out=bo_bc[:, 0:512], in0=PS[1][0][:, :],
                            in1=bo_bc[:, 0:512], op=ALU.add)
            return e.tensor_tensor(out=bo_bc[:, 512:1024], in0=PS[1][1][:, :],
                                   in1=bo_bc[:, 512:1024], op=ALU.add)
        s.op("dve", bo_bc_ev, reads=(("ps", 1), "bo_bc"), writes=("bo_bc",))

        s.op("act", lambda e: e.activation(out=gsm[:, 0:8], in_=sim2d[:],
                                           func=AF.Square),
             reads=("sim2d",), writes=("gsm",))
        jb = [(PS[2][0][0:1, 0:8], ones_c[:], gsm[:, 0:8], True, True)]
        mmseq(jb, reads=("gsm", "ones_c"), writes=(("ps", 2),))

        def sim_norm(e):
            stepd(e, e.tensor_copy(out=srow[0:1, 8:16], in_=PS[2][0][0:1, 0:8]))
            stepd(e, e.tensor_tensor(out=srow[0:1, 0:4], in0=srow[0:1, 8:12],
                                     in1=srow[0:1, 12:16], op=ALU.add))
            sbits = srow[0:1, 0:4].bitcast(I32)
            stepd(e, e.tensor_scalar(out=seed4[0:1, 0:4], in0=sbits, scalar1=1,
                                     scalar2=None, op0=ALU.arith_shift_right))
            stepd(e, e.tensor_scalar(out=seed4[0:1, 4:8], in0=seed4[0:1, 0:4],
                                     scalar1=-1, scalar2=0x5F3759DF,
                                     op0=ALU.mult, op1=ALU.add))
            stepd(e, e.tensor_copy(out=srow[0:1, 4:8],
                                   in_=seed4[0:1, 4:8].bitcast(F32)))
            for _ in range(4):
                stepd(e, e.tensor_tensor(out=srow[0:1, 8:12], in0=srow[0:1, 4:8],
                                         in1=srow[0:1, 4:8], op=ALU.mult))
                stepd(e, e.tensor_tensor(out=srow[0:1, 8:12], in0=srow[0:1, 8:12],
                                         in1=srow[0:1, 0:4], op=ALU.mult))
                stepd(e, e.tensor_scalar(out=srow[0:1, 8:12], in0=srow[0:1, 8:12],
                                         scalar1=-0.5, scalar2=1.5, op0=ALU.mult,
                                         op1=ALU.add))
                stepd(e, e.tensor_tensor(out=srow[0:1, 4:8], in0=srow[0:1, 4:8],
                                         in1=srow[0:1, 8:12], op=ALU.mult))
            return e.tensor_scalar(out=srow[0:1, 4:8], in0=srow[0:1, 4:8],
                                   scalar1=1e12, scalar2=None, op0=ALU.min)
        s.op("dve", sim_norm, reads=(("ps", 2),), writes=("srow", "seed4"))
        jb = [(PS[2][0][:, 0:4], ones_f[:], srow[0:1, 4:8], True, True)]
        mmseq(jb, reads=("srow", "ones_f"), writes=(("ps", 2),))

        def simn_apply(e):
            stepd(e, e.tensor_copy(out=gsm[:, 8:12], in_=PS[2][0][:, 0:4]))
            stepd(e, e.tensor_tensor(out=gsm[:, 0:4], in0=sim2d[:, 0:4],
                                     in1=gsm[:, 8:12], op=ALU.mult))
            stepd(e, e.tensor_tensor(out=gsm[:, 4:8], in0=sim2d[:, 4:8],
                                     in1=gsm[:, 8:12], op=ALU.mult))
            return e.tensor_copy(out=simnb[:], in_=gsm[:, 0:8])
        s.op("dve", simn_apply, reads=(("ps", 2), "sim2d"),
             writes=("simnb", "gsm"))
        dma_in(gsm[:], h["gpad"], writes=("gsm",))

        s.op("dve", lambda e: e.tensor_scalar(
            out=srow[0:1, 12:13], in0=temp[:], scalar1=CLAMP_MAX, scalar2=0.5,
            op0=ALU.min, op1=ALU.mult),
            reads=("temp",), writes=("srow",))
        s.op("act", lambda e: e.activation(out=srow[0:1, 13:14],
                                           in_=srow[0:1, 12:13], func=AF.Tanh),
             reads=("srow",), writes=("srow",))

        def sc_fin(e):
            stepd(e, e.tensor_scalar(out=srow[0:1, 14:15], in0=srow[0:1, 13:14],
                                     scalar1=-1.0, scalar2=1.0, op0=ALU.mult,
                                     op1=ALU.add))
            stepd(e, e.reciprocal(out=srow[0:1, 14:15], in_=srow[0:1, 14:15]))
            stepd(e, e.tensor_scalar(out=srow[0:1, 13:14], in0=srow[0:1, 13:14],
                                     scalar1=1.0, scalar2=None, op0=ALU.add))
            return e.tensor_tensor(out=srow[0:1, 15:16], in0=srow[0:1, 13:14],
                                   in1=srow[0:1, 14:15], op=ALU.mult)
        s.op("dve", sc_fin, reads=("srow",), writes=("srow",))
        jb = [(PS[2][0][:, 4:5], ones_f[:], srow[0:1, 15:16], True, True)]
        mmseq(jb, reads=("srow", "ones_f"), writes=(("ps", 2),))
        s.op("dve", lambda e: e.tensor_copy(out=sc_bc[:], in_=PS[2][0][:, 4:5]),
             reads=(("ps", 2),), writes=("sc_bc",))

        c_S = consts[:, 0:1]
        c_HPI = consts[:, 1:2]
        c_HALF = consts[:, 2:3]
        c_NS = consts[:, 3:4]


        def tctx(ti):
            pp = ti % 2
            stp = STAT[pp]
            ks = (f"st1_{pp}", f"st2_{pp}", f"st3_{pp}", f"st4_{pp}")
            return dict(pp=pp, stp=stp, ks=ks, nrp=NRS[pp], knr=f"nrs_{pp}",
                        sdp=SEEDI[pp], ksd=f"seedi_{pp}",
                        xin=XIN[pp], kx=f"x_in_{pp}",
                        kqt=KQT[pp], kk=f"kqT_{pp}")

        def ln_negmu(t):
            stp1, stp3 = t["stp"][0], t["stp"][2]
            k1, k3 = t["ks"][0], t["ks"][2]
            s.op("dve", lambda e, stp1=stp1, stp3=stp3: e.tensor_scalar(
                out=stp1[:], in0=stp3[:], scalar1=-1.0 / D, scalar2=None,
                op0=ALU.mult),
                reads=(k3,), writes=(k1,))

        def ln_rstd(t, eps):
            stp1, stp2, stp3, stp4 = t["stp"]
            k1, k2, k3, k4 = t["ks"]
            nrp, knr = t["nrp"], t["knr"]

            def emit(e):
                stepp(e, e.tensor_scalar(out=nrp[:, 1:2], in0=stp4[:],
                                         scalar1=1.0 / D, scalar2=eps,
                                         op0=ALU.mult, op1=ALU.add))
                stepp(e, e.tensor_tensor(out=nrp[:, 2:3], in0=stp1[:],
                                         in1=stp1[:], op=ALU.mult))
                return e.tensor_tensor(out=stp2[:], in0=nrp[:, 1:2],
                                       in1=nrp[:, 2:3], op=ALU.subtract)
            s.op("pool", emit, reads=(k1, k4), writes=(k2, knr), cost=1.1,
                 tag="rstd_w")
            rsqrt_nr(stp2, k2, stp3, k3, t["sdp"], t["ksd"], nrp, knr, iters=3)

        def ln_stats(t, eps):
            stp1, stp2, stp3, stp4 = t["stp"]
            k1, k2, k3, k4 = t["ks"]
            nrp, knr = t["nrp"], t["knr"]

            def emit(e):
                stepp(e, e.tensor_scalar(out=stp1[:], in0=stp3[:],
                                         scalar1=1.0 / D, scalar2=None,
                                         op0=ALU.mult))
                e.tensor_scalar(out=nrp[:, 1:2], in0=stp4[:], scalar1=1.0 / D,
                                scalar2=eps, op0=ALU.mult, op1=ALU.add)
                stepp(e, e.tensor_tensor(out=nrp[:, 2:3], in0=stp1[:],
                                         in1=stp1[:], op=ALU.mult))
                return e.tensor_tensor(out=stp2[:], in0=nrp[:, 1:2],
                                       in1=nrp[:, 2:3], op=ALU.subtract)
            s.op("pool", emit, reads=(k3, k4), writes=(k1, k2, knr), cost=1.45,
                 tag="lnst_w")
            rsqrt_nr(stp2, k2, stp3, k3, t["sdp"], t["ksd"], nrp, knr, iters=3)

            def emit_nb(e):
                stepp(e, e.tensor_tensor(out=stp4[:], in0=stp1[:], in1=stp3[:],
                                         op=ALU.mult))
                return e.tensor_scalar(out=stp4[:], in0=stp4[:], scalar1=-1.0,
                                       scalar2=None, op0=ALU.mult)
            s.op("pool", emit_nb, reads=(k1, k3), writes=(k4,), cost=0.72,
                 tag="ln_nb")

        def wrap_and_trig(src_key, src_ap):
            s.op("act", lambda e, sa=src_ap: e.activation(
                out=A[:], in_=sa, func=AF.Tanh),
                reads=(src_key,), writes=("scrA",), cost=1.25, tag="wr_tanh")
            s.op("dve", lambda e: e.scalar_tensor_tensor(
                out=A[:], in0=A[:], scalar=INV_2PI, in1=ps_bc[:],
                op0=ALU.mult, op1=ALU.mult),
                reads=("scrA", "ps_bc"), writes=("scrA",), cost=1.28, tag="wr_t")
            s.op("dve", lambda e: e.tensor_scalar(
                out=U[:], in0=A[:], scalar1=KOFF, scalar2=None, op0=ALU.add),
                reads=("scrA",), writes=("scrU",), cost=0.7, tag="wr_u")
            s.op("dve", lambda e: e.scalar_tensor_tensor(
                out=C[:], in0=U[:], scalar=-KOFF, in1=A[:],
                op0=ALU.add, op1=ALU.subtract),
                reads=("scrU", "scrA"), writes=("scrC",), cost=1.28, tag="wr_nv")
            s.op("dve", lambda e: e.scalar_tensor_tensor(
                out=C[:], in0=C[:], scalar=-0.5, in1=C[:], op0=ALU.is_lt,
                op1=ALU.add),
                reads=("scrC",), writes=("scrC",), cost=1.28, tag="wr_w")
            s.op("act", lambda e: e.activation(
                out=SKb[:], in_=C[:], func=AF.Sin, scale=c_NS),
                reads=("scrC", "consts"), writes=("SK",), cost=1.25, tag="wr_sin")
            s.op("dve", lambda e: e.scalar_tensor_tensor(
                out=A[:], in0=C[:], scalar=-0.25, in1=C[:], op0=ALU.is_le,
                op1=ALU.add),
                reads=("scrC",), writes=("scrA",), cost=1.28, tag="wr_vc")
            s.op("act", lambda e: e.activation(
                out=CKb[:], in_=A[:], func=AF.Sin, scale=c_NS, bias=c_HPI),
                reads=("scrA", "consts"), writes=("CK",), cost=1.25, tag="wr_cos")

        def f_xkey(ti):
            t = tctx(ti)
            s.op("act", lambda e, x=t["xin"]: e.activation(
                out=xbb[:], in_=x[:], func=AF.Identity),
                reads=(t["kx"],), writes=("xbb",))
            transpose8(0, "xbb", xbb[:])
            evac_pair(0, xTb[:, 0:1024], "xTb")
            jobs = []
            for ec in range(NCH):
                for dc in range(NCH):
                    jobs.append((P[1][:, ec * 128:(ec + 1) * 128],
                                 wb["WkT"][:, dc * D + ec * 128:dc * D + (ec + 1) * 128],
                                 xTb[:, dc * 128:(dc + 1) * 128],
                                 dc == 0, dc == NCH - 1))
            mmseq(jobs, reads=("xTb",) + wkeys("WkT"), writes=(("ps", 1),))
            evac_pair(1, t["kqt"][:, 0:1024], t["kk"])

        def f_val_kp(ti):
            t = tctx(ti)
            kq = t["kqt"]
            jobs = []
            for nh in range(2):
                for dc in range(NCH):
                    jobs.append((PS[0][nh][:, 0:512],
                                 xTb[:, dc * 128:(dc + 1) * 128],
                                 wb["WvT"][:, dc * D + nh * 512:dc * D + (nh + 1) * 512],
                                 dc == 0, False))
                jobs.append((PS[0][nh][:, 0:512], ones2[:],
                             v_hl[0:2, nh * 512:(nh + 1) * 512], False, True))
            mmseq(jobs, reads=("xTb", "v_hl") + wkeys("WvT"),
                  writes=(("ps", 0),), tag="value")
            s.op("act", lambda e: e.activation(out=VAL[:], in_=P[0][:, :],
                                               func=AF.Identity),
                 reads=(("ps", 0),), writes=("VAL",), cost=1.2, tag="val_ev")
            jobs = []
            for nh in range(2):
                for ec in range(NCH):
                    jobs.append((PS[1][nh][:, 0:512],
                                 kq[:, ec * 128:(ec + 1) * 128],
                                 wb["WkpT"][:, ec * D + nh * 512:ec * D + (nh + 1) * 512],
                                 ec == 0, False))
                jobs.append((PS[1][nh][:, 0:512], ones2[:],
                             kp_hl[0:2, nh * 512:(nh + 1) * 512], False, True))
            mmseq(jobs, reads=(t["kk"], "kp_hl") + wkeys("WkpT"),
                  writes=(("ps", 1),), tag="kp")
            wrap_and_trig(("ps", 1), P[1][:, :])

        def f_qbm(ti):
            t = tctx(ti)
            jobs = []
            for ec in range(NCH):
                for dc in range(NCH):
                    jobs.append((P[1][:, ec * 128:(ec + 1) * 128],
                                 wb["WqT"][:, dc * D + ec * 128:dc * D + (ec + 1) * 128],
                                 xTb[:, dc * 128:(dc + 1) * 128],
                                 dc == 0, dc == NCH - 1))
            mmseq(jobs, reads=("xTb",) + wkeys("WqT"), writes=(("ps", 1),))
            s.op("dve", lambda e: e.tensor_tensor(out=xbb[:], in0=VAL[:],
                                                  in1=CKb[:], op=ALU.mult),
                 reads=("VAL", "CK"), writes=("xbb",))
            s.op("dve", lambda e: e.tensor_tensor(out=bsh[:], in0=VAL[:],
                                                  in1=SKb[:], op=ALU.mult),
                 reads=("VAL", "SK"), writes=("bsh",))
            evac_pair(1, t["kqt"][:, 0:1024], t["kk"])
            jobs = [(PS[0][nh][:, 0:512], LTs[:], xbb[:, nh * 512:(nh + 1) * 512],
                     True, True) for nh in range(2)]
            mmseq(jobs, reads=("xbb", "LT"), writes=(("ps", 0),))
            jobs = [(PS[1][nh][:, 0:512], LTs[:], bsh[:, nh * 512:(nh + 1) * 512],
                     True, True) for nh in range(2)]
            mmseq(jobs, reads=("bsh", "LT"), writes=(("ps", 1),))
            s.op("act", lambda e: e.activation(out=VAL[:], in_=P[0][:, :],
                                               func=AF.Identity),
                 reads=(("ps", 0),), writes=("VAL",))
            s.op("act", lambda e: e.activation(out=MIB[:], in_=P[1][:, :],
                                               func=AF.Identity),
                 reads=(("ps", 1),), writes=("MIB",))

        def f_qp(ti):
            t = tctx(ti)
            kq = t["kqt"]
            jobs = []
            for nh in range(2):
                for ec in range(NCH):
                    jobs.append((PS[2][nh][:, 0:512],
                                 kq[:, ec * 128:(ec + 1) * 128],
                                 wb["WqpT"][:, ec * D + nh * 512:ec * D + (nh + 1) * 512],
                                 ec == 0, False))
                jobs.append((PS[2][nh][:, 0:512], ones2[:],
                             qp_hl[0:2, nh * 512:(nh + 1) * 512], False, True))
            mmseq(jobs, reads=(t["kk"], "qp_hl") + wkeys("WqpT"),
                  writes=(("ps", 2),), tag="qp")
            wrap_and_trig(("ps", 2), P[2][:, :])

        def f_retr(ti):
            t = tctx(ti)
            stp3, stp4 = t["stp"][2], t["stp"][3]
            k3, k4 = t["ks"][2], t["ks"][3]
            s.op("dve", lambda e: e.tensor_tensor(
                out=VAL[:], in0=VAL[:], in1=CKb[:], op=ALU.mult),
                reads=("VAL", "CK"), writes=("VAL",))
            s.op("dve", lambda e: e.tensor_tensor(
                out=MIB[:], in0=MIB[:], in1=SKb[:], op=ALU.mult),
                reads=("MIB", "SK"), writes=("MIB",))
            s.op("dve", lambda e, stp3=stp3: e.scalar_tensor_tensor(
                out=MIB[:], in0=VAL[:], scalar=1.0, in1=MIB[:], op0=ALU.mult,
                op1=ALU.add, accum_out=stp3[:]),
                reads=("VAL", "MIB"), writes=("MIB", k3))
            ln_negmu(t)
            s.op("act", lambda e, stp4=stp4: e.activation(
                out=junkb[:], in_=MIB[:], func=AF.Square, accum_out=stp4[:]),
                reads=("MIB",), writes=("junkb", k4))
            ln_rstd(t, EPS_IN)

        def f_lnT(ti):
            t = tctx(ti)
            stp1 = t["stp"][0]
            k1 = t["ks"][0]
            s.op("act", lambda e, stp1=stp1: e.activation(
                out=bsh[:], in_=MIB[:], func=AF.Identity, bias=stp1[:]),
                reads=("MIB", k1), writes=("bsh",))
            transpose8(3, "bsh", bsh[:])
            evac_pair(3, tsh[:, 0:1024], "tsh")

        def f_wo(ti):
            t = tctx(ti)
            stp3 = t["stp"][2]
            k3 = t["ks"][2]
            proj_tok(2, "tsh", tsh[:], wb["WoT"], wkeys("WoT"))
            s.op("dve", lambda e, stp3=stp3: e.scalar_tensor_tensor(
                out=C[:], in0=P[2][:, :], scalar=stp3[:], in1=bo_bc[:, 0:1024],
                op0=ALU.mult, op1=ALU.add),
                reads=(("ps", 2), k3, "bo_bc"), writes=("scrC",))
            s.op("dve", lambda e, x=t["xin"], stp3=stp3: e.scalar_tensor_tensor(
                out=B[:], in0=x[:], scalar=2.0, in1=C[:], op0=ALU.mult,
                op1=ALU.add, accum_out=stp3[:]),
                reads=(t["kx"], "scrC"), writes=("scrB", k3))

        def f_ln1x2(ti):
            t = tctx(ti)
            stp3, stp4 = t["stp"][2], t["stp"][3]
            k3, k4 = t["ks"][2], t["ks"][3]
            s.op("act", lambda e, stp4=stp4: e.activation(
                out=junkb[:], in_=B[:], func=AF.Square, accum_out=stp4[:]),
                reads=("scrB",), writes=("junkb", k4))
            ln_stats(t, EPS)
            s.op("act", lambda e, stp3=stp3, stp4=stp4: e.activation(
                out=X2[:], in_=B[:], func=AF.Identity, scale=stp3[:],
                bias=stp4[:]),
                reads=("scrB", k3, k4), writes=("X2",))
            s.op("act", lambda e, stp3=stp3, stp4=stp4: e.activation(
                out=bsh[:], in_=B[:], func=AF.Identity, scale=stp3[:],
                bias=stp4[:]),
                reads=("scrB", k3, k4), writes=("bsh",))
            transpose8(3, "bsh", bsh[:])
            evac_pair(3, tsh[:, 0:1024], "tsh")

        def f_ffn1(ti):
            t = tctx(ti)
            stp1, stp2 = t["stp"][0], t["stp"][1]
            k1, k2 = t["ks"][0], t["ks"][1]
            proj_tok(2, "tsh", tsh[:], wb["WfT"], wkeys("WfT"), stop_last=False)
            jobs = [(PS[3][0][:, 0:PROJ], tsh[:, dc * 128:(dc + 1) * 128],
                     WpTb[:, dc * PROJ:(dc + 1) * PROJ], dc == 0, dc == NCH - 1)
                    for dc in range(NCH)]
            mmseq(jobs, reads=("tsh",) + wkeys("WpT"), writes=(("ps", 3),))
            jobs = [(PS[3][1][:, 0:RTOT], tsh[:, dc * 128:(dc + 1) * 128],
                     WdTb[:, dc * RTOT:(dc + 1) * RTOT], dc == 0, dc == NCH - 1)
                    for dc in range(NCH)]
            mmseq(jobs, reads=("tsh",) + wkeys("WdT"), writes=(("ps", 3),))
            s.op("dve", lambda e: e.tensor_tensor(
                out=proj[:], in0=PS[3][0][:, 0:PROJ], in1=bp_bc[:], op=ALU.add),
                reads=(("ps", 3), "bp_bc"), writes=("proj",))
            s.op("dve", lambda e: e.tensor_copy(out=hsml[:],
                                                in_=PS[3][1][:, 0:RTOT]),
                 reads=(("ps", 3),), writes=("hsml",))
            s.op("act", lambda e, stp1=stp1: e.activation(
                out=junkb[:, 0:PROJ], in_=proj[:], func=AF.Square,
                accum_out=stp1[:]),
                reads=("proj",), writes=("junkb", k1))
            rsqrt_nr(stp1, k1, stp2, k2, t["sdp"], t["ksd"], t["nrp"], t["knr"],
                     clamp=1e12, iters=3)
            s.op("pool", lambda e, stp1=stp1, stp2=stp2: e.tensor_tensor(
                out=stp1[:], in0=stp2[:], in1=sc_bc[:], op=ALU.mult),
                reads=(k2, "sc_bc"), writes=(k1,))
            s.op("act", lambda e: e.activation(out=projb[:], in_=proj[:],
                                               func=AF.Identity),
                 reads=("proj",), writes=("projb",))
            jobs = [(PS[3][0][:, PROJ + c * 128:PROJ + (c + 1) * 128],
                     projb[:, c * 128:(c + 1) * 128], I128[:], True, True)
                    for c in range(2)]
            mmseq(jobs, reads=("projb", "I128"), writes=(("ps", 3),))
            s.op("dve", lambda e: e.tensor_copy(out=projT[:],
                                                in_=PS[3][0][:, PROJ:PROJ + 256]),
                 reads=(("ps", 3),), writes=("projT",))
            jobs = [(PS[3][0][:, 0:4], projT[:, c * 128:(c + 1) * 128],
                     simnb[:, c * 4:(c + 1) * 4], c == 0, c == 1)
                    for c in range(2)]
            mmseq(jobs, reads=("projT", "simnb"), writes=(("ps", 3),))
            s.op("dve", lambda e, stp1=stp1: e.tensor_scalar(
                out=lg, in0=PS[3][0][:, 0:4], scalar1=stp1[:], scalar2=None,
                op0=ALU.mult),
                reads=(("ps", 3), k1), writes=("gsm",))

        def f_gate(ti):
            def gate1(e):
                stepd(e, e.max(gsm[:, 8:16], gsm[:, 0:8]))
                stepd(e, e.max_index(gidx[:], gsm[:, 8:16], gsm[:, 0:8]))
                return e.tensor_scalar(out=gsm[:, 24:28], in0=gsm[:, 0:4],
                                       scalar1=-1.0, scalar2=gsm[:, 8:9],
                                       op0=ALU.mult, op1=ALU.add)
            s.op("dve", gate1, reads=("gsm",), writes=("gsm",), cost=1.1,
                 tag="gate1")
            s.op("act", lambda e: e.activation(out=gsm[:, 24:28],
                                               in_=gsm[:, 24:28],
                                               func=AF.Tanh, scale=c_HALF),
                 reads=("gsm", "consts"), writes=("gsm",), cost=0.45,
                 tag="gate_tanh")

            def gate2(e):
                stepd(e, e.tensor_scalar(out=gsm[:, 0:4], in0=gsm[:, 24:28],
                                         scalar1=-1.0, scalar2=1.0,
                                         op0=ALU.mult, op1=ALU.add))
                stepd(e, e.tensor_scalar(out=gsm[:, 24:28], in0=gsm[:, 24:28],
                                         scalar1=1.0, scalar2=None, op0=ALU.add))
                stepd(e, e.reciprocal(out=gsm[:, 24:28], in_=gsm[:, 24:28]))
                stepd(e, e.tensor_tensor(out=gsm[:, 0:4], in0=gsm[:, 0:4],
                                         in1=gsm[:, 24:28], op=ALU.mult))
                stepd(e, e.tensor_reduce(gsm[:, 28:29], gsm[:, 0:4], AX.X,
                                         ALU.add))
                stepd(e, e.reciprocal(out=gsm[:, 28:29], in_=gsm[:, 28:29]))
                stepd(e, e.tensor_copy(out=gsm[:, 29:30], in_=gidx[:, 0:1]))
                stepd(e, e.tensor_scalar(out=gsm[:, 24:28], in0=iota4[:],
                                         scalar1=gsm[:, 29:30], scalar2=None,
                                         op0=ALU.is_equal))
                return e.tensor_scalar(out=gsm[:, 24:28], in0=gsm[:, 24:28],
                                       scalar1=gsm[:, 28:29], scalar2=None,
                                       op0=ALU.mult)
            s.op("dve", gate2, reads=("gsm", "iota4"), writes=("gsm",),
                 cost=2.9, tag="gate2")

            def hgate(e):
                inst = None
                for i, (off, r) in enumerate(zip(ROFF, RANKS)):
                    inst = e.tensor_scalar(out=hpb[:, off:off + r],
                                           in0=hsml[:, off:off + r],
                                           scalar1=gsm[:, 24 + i:25 + i],
                                           scalar2=None, op0=ALU.mult)
                return inst
            s.op("dve", hgate, reads=("hsml", "gsm"), writes=("hpb",), cost=0.8, tag="hgate")

        def f_ffn2(ti):
            t = tctx(ti)
            stp3, stp4 = t["stp"][2], t["stp"][3]
            k3, k4 = t["ks"][2], t["ks"][3]
            nrp, knr = t["nrp"], t["knr"]
            jobs = [(PS[3][1][0:RTOT, 128:256], hpb[:], I128[:], True, True)]
            mmseq(jobs, reads=("hpb", "I128"), writes=(("ps", 3),))
            s.op("dve", lambda e: e.tensor_copy(out=hpT[:],
                                                in_=PS[3][1][0:RTOT, 128:256]),
                 reads=(("ps", 3),), writes=("hpT",))
            jobs = [(PS[2][nh][:, 0:512], hpT[:],
                     WuTb[:, nh * 512:(nh + 1) * 512], False, True)
                    for nh in range(2)]
            mmseq(jobs, reads=("hpT",) + wuk, writes=(("ps", 2),))

            s.op("dve", lambda e, stp3=stp3: e.scalar_tensor_tensor(
                out=B[:], in0=X2[:], scalar=1.0, in1=P[2][:, :],
                op0=ALU.mult, op1=ALU.add, accum_out=stp3[:]),
                reads=(("ps", 2), "X2"), writes=("scrB", k3), cost=1.3,
                tag="pre3")
            s.op("act", lambda e, stp4=stp4: e.activation(
                out=junkb[:], in_=B[:], func=AF.Square, accum_out=stp4[:]),
                reads=("scrB",), writes=("junkb", k4))
            ln_stats(t, EPS)

        def f_out(ti):
            t = tctx(ti)
            stp3, stp4 = t["stp"][2], t["stp"][3]
            k3, k4 = t["ks"][2], t["ks"][3]
            s.op("act", lambda e, stp3=stp3, stp4=stp4: e.activation(
                out=OUTB[:], in_=B[:], func=AF.Identity, scale=stp3[:],
                bias=stp4[:]),
                reads=("scrB", k3, k4), writes=("OUTB",))
            s.op("out", lambda e, r0=ti * 128: e.dma_start(
                out=hy[r0:r0 + 128, :], in_=OUTB[:]),
                reads=("OUTB",), writes=())

        f_xkey(0)
        f_val_kp(0)
        f_qbm(0)
        for i in range(NT):
            if i + 1 < NT:
                tn = tctx(i + 1)
                dma_in(tn["xin"][:], hx[(i + 1) * 128:(i + 2) * 128, :],
                       writes=(tn["kx"],))
                f_xkey(i + 1)
            if i > 0:
                f_out(i - 1)
            f_qp(i)
            f_retr(i)
            if i + 1 < NT:
                f_val_kp(i + 1)
            f_lnT(i)
            f_wo(i)
            if i + 1 < NT:
                f_qbm(i + 1)
            f_ln1x2(i)
            f_ffn1(i)
            f_gate(i)
            f_ffn2(i)
        f_out(NT - 1)

        s.op("pool", lambda e: e.nop(),
             extra_waits=tuple((ch, s.cnt[ch]) for ch in DMA_OUT if s.cnt[ch]))

        qmap = {"pe": "tensor", "act": "scalar", "dve": "vector",
                "pool": "gpsimd", "sp": "sync"}
        with nc.Block() as block:
            s.emit(sems, block, qmap)

    return nc


_PROG = None


def kernel(**inputs):
    global _PROG
    x = np.ascontiguousarray(np.asarray(inputs["L0_attn_x"], np.float32))
    assert x.shape == (8, S, D)
    hp = host_prep(inputs)
    if _PROG is None:
        _PROG = build_core_program()
    in_maps = []
    for b in range(8):
        m = {"hx": x[b]}
        m.update(hp)
        in_maps.append(m)
    res = run_bass_kernel_spmd(_PROG, in_maps, core_ids=list(range(8)))
    out = np.stack([np.asarray(r["hy"]) for r in res.results], axis=0)
    return out.astype(np.float32)


# version 08
# speedup vs baseline: 1.4580x; 1.1704x over previous
import os
import numpy as np
import ml_dtypes
import concourse.bass as bass
import concourse.mybir as mybir
from concourse.bass_utils import run_bass_kernel_spmd

F32 = mybir.dt.float32
BF16 = mybir.dt.bfloat16
I32 = mybir.dt.int32
AF = mybir.ActivationFunctionType
ALU = mybir.AluOpType
AX = mybir.AxisListType

D = 1024
S = 4096
NCH = 8
NT = int(os.environ.get("KNT", "32"))
CHUNK = 64
EPS = 1e-5
EPS_IN = 1024 * 1e-5
CLAMP_MAX = float(np.log(1.0 / 0.01))
RANKS = [1, 2, 4, 8]
ROFF = [0, 1, 3, 7]
RTOT = 15
PROJ = 256

S_SCALE = float(np.nextafter(np.float32(2 * np.pi), np.float32(0)))
HALF_PI = float(np.float32(np.pi / 2))
INV_2PI = float(np.float32(1.0 / (2 * np.pi)))
KOFF = 64.0

ENGINES = ("pe", "act", "dve", "pool", "sp")
DMA_IN = tuple(f"in{i}" for i in range(4))
DMA_OUT = tuple(f"out{i}" for i in range(2))
DMA_CH = DMA_IN + DMA_OUT


class Sched:
    def __init__(self):
        self.ops = []
        self.cnt = {e: 0 for e in ENGINES + DMA_CH}
        self.writer = {}
        self.readers = {}
        self.need_inc = set()
        self.meta = []
        self.din = 0
        self.dout = 0

    def op(self, eng, emit, reads=(), writes=(), extra_waits=(), cost=None,
           tag=""):
        if eng == "in":
            eng = DMA_IN[self.din % len(DMA_IN)]
            self.din += 1
        elif eng == "out":
            eng = DMA_OUT[self.dout % len(DMA_OUT)]
            self.dout += 1
        step = 16 if eng in DMA_CH else 1
        waits = set(extra_waits)
        if eng in DMA_CH and self.cnt[eng] > 0:
            waits.add((eng, self.cnt[eng]))
        for k in reads:
            w = self.writer.get(k)
            if w is not None:
                waits.add(w)
        for k in writes:
            w = self.writer.get(k)
            if w is not None:
                waits.add(w)
            for r in self.readers.get(k, ()):
                waits.add(r)
        self.cnt[eng] += step
        me = (eng, self.cnt[eng])
        for k in reads:
            self.readers.setdefault(k, []).append(me)
        for k in writes:
            self.writer[k] = me
            self.readers[k] = []
        for w in waits:
            self.need_inc.add(w)
        if eng in DMA_CH:
            self.need_inc.add(me)
        if cost is None:
            cost = {"pe": 0.3, "act": 1.2, "dve": 1.2, "pool": 0.25,
                    "sp": 0.05}.get(eng, 2.5)
            if eng.startswith("in") or eng.startswith("out"):
                cost = 2.5
        self.meta.append((cost, tag))
        self.ops.append((eng, emit, sorted(waits), me))
        return me

    def simulate(self, gap=0.15, xgap=0.12):
        qfree = {}
        done = {}
        rows = []
        for idx, (eng, _, waits, me) in enumerate(self.ops):
            cost, tag = self.meta[idx]
            q = eng if eng in ENGINES else ("dma" + eng[-1] if eng[-1].isdigit()
                                            else eng)
            t0 = qfree.get(q, 0.0)
            src_max = 0.0
            for w in waits:
                src_max = max(src_max, done.get(w, 0.0) + xgap)
            t0 = max(t0, src_max)
            t1 = t0 + cost
            qfree[q] = t1 + gap
            done[me] = t1
            rows.append((t0, t1, eng, tag, me, waits))
        busy = {}
        for idx, (eng, _, _, _) in enumerate(self.ops):
            busy[eng] = busy.get(eng, 0.0) + self.meta[idx][0]
        return rows, busy

    def emit(self, sems, block, qmap):
        import bisect
        inc_list = {}
        for (e, c) in self.need_inc:
            inc_list.setdefault(e, []).append(c)
        for e in inc_list:
            inc_list[e].sort()

        def remap(src, val):
            lst = inc_list.get(src, ())
            r = bisect.bisect_right(lst, val)
            assert r > 0 and lst[r - 1] == val, (src, val)
            return r * (16 if src in DMA_CH else 1)

        per_q = {e: [] for e in ENGINES}
        for eng, emitfn, waits, me in self.ops:
            q = eng if eng in per_q else ("sp" if eng.startswith("in") else "pool")
            per_q[q].append((emitfn, waits, me if me in self.need_inc else None, eng))
        for q, items in per_q.items():
            if not items:
                continue

            def make_body(items):
                def body(e):
                    last = {}
                    for emitfn, waits, inc, semeng in items:
                        for (src, val) in waits:
                            v = remap(src, val)
                            if last.get(src, -1) >= v:
                                continue
                            e.wait_ge(sems[src], v)
                            last[src] = v
                        inst = emitfn(e)
                        if inc is not None:
                            inst.then_inc(sems[semeng],
                                          16 if semeng in DMA_CH else 1)
                return body
            getattr(block, qmap[q])(make_body(items))


def host_prep(inp):
    f = np.float32
    T = lambda a: np.ascontiguousarray(np.asarray(a, f).T)
    chunk2d = lambda v: np.ascontiguousarray(np.asarray(v, f).reshape(NCH, 128).T)
    bcast = lambda v, n: np.ascontiguousarray(
        np.broadcast_to(np.asarray(v, f).reshape(1, n), (128, n)))

    d = {}
    d["WkT"] = T(inp["L0_attn_Wk"])
    d["WvT"] = T(inp["L0_attn_Wv"])
    d["WqT"] = T(inp["L0_attn_Wq"])
    d["WkpT"] = T(inp["L0_attn_Wkp"])
    d["WqpT"] = T(inp["L0_attn_Wqp"])
    d["WoT"] = T(inp["L0_attn_Wo"])
    d["WfT"] = T(inp["L0_ffn_W"])
    d["WpT"] = T(inp["L0_ffn_Wp"])
    wd = np.concatenate([np.asarray(inp[f"L0_ffn_Wd{i}"], f) for i in range(4)], 0)
    d["WdT"] = T(wd)
    wu = np.concatenate([np.asarray(inp[f"L0_ffn_Wu{i}"], f) for i in range(4)], 1)
    d["WuT"] = T(wu)

    d["bk2d"] = chunk2d(inp["L0_attn_bk"])
    d["bq2d"] = chunk2d(inp["L0_attn_bq"])
    d["lnb2d"] = chunk2d(inp["L0_attn_ln_b"])
    d["g2d"] = chunk2d(inp["L0_attn_ln_g"])
    d["bv_row"] = np.asarray(inp["L0_attn_bv"], f).reshape(1, D).copy()
    d["ps_bc"] = bcast(inp["L0_attn_phase_scale"], D)
    d["bp_bc"] = bcast(inp["L0_ffn_bp"], PROJ)
    d["bp2d"] = np.ascontiguousarray(np.asarray(inp["L0_ffn_bp"], f).reshape(2, 128).T)
    d["bkp_row"] = np.asarray(inp["L0_attn_bkp"], f).reshape(1, D).copy()
    d["bqp_row"] = np.asarray(inp["L0_attn_bqp"], f).reshape(1, D).copy()
    d["bo_bc"] = bcast(inp["L0_attn_bo"], D)
    sim = np.asarray(inp["L0_ffn_sim"], f).reshape(2, 128, 4).transpose(1, 0, 2)
    d["sim2d"] = np.ascontiguousarray(sim.reshape(128, 8))
    d["temp"] = np.asarray(inp["L0_ffn_temp"], f).reshape(1, 1).copy()

    L = np.zeros((128, 128), f)
    for b in range(2):
        L[b * CHUNK:(b + 1) * CHUNK, b * CHUNK:(b + 1) * CHUNK] = \
            np.tril(np.ones((CHUNK, CHUNK), f))
    d["LTb"] = np.ascontiguousarray(L.T).astype(ml_dtypes.bfloat16)
    d["I128b"] = np.eye(128, dtype=ml_dtypes.bfloat16)
    d["ones_f"] = np.ones((1, 128), f)
    d["ones2b"] = np.ones((2, 128), dtype=ml_dtypes.bfloat16)
    d["ones_c"] = np.ones((128, 1), f)
    d["magic"] = np.full((128, 1), 0x5F3759DF, np.int32)
    consts = np.zeros((128, 8), f)
    consts[:, 0] = S_SCALE
    consts[:, 1] = HALF_PI
    consts[:, 2] = 0.5
    consts[:, 3] = -S_SCALE
    d["consts"] = consts
    d["iota4"] = np.ascontiguousarray(
        np.broadcast_to(np.arange(4, dtype=f), (128, 4)))
    gpad = np.full((128, 32), 0.0, f)
    gpad[:, 4:8] = -1e30
    d["gpad"] = gpad
    return d


def build_core_program():
    nc = bass.Bass("TRN2", target_bir_lowering=False)

    hx = nc.dram_tensor("hx", [S, D], F32, kind="ExternalInput").ap()
    hy = nc.dram_tensor("hy", [S, D], F32, kind="ExternalOutput").ap()
    W_NAMES = ["WkT", "WvT", "WqT", "WkpT", "WqpT", "WoT", "WfT"]
    h = {}
    for n in W_NAMES:
        h[n] = nc.dram_tensor(n, [D, D], F32, kind="ExternalInput").ap()
    h["WpT"] = nc.dram_tensor("WpT", [D, PROJ], F32, kind="ExternalInput").ap()
    h["WdT"] = nc.dram_tensor("WdT", [D, RTOT], F32, kind="ExternalInput").ap()
    h["WuT"] = nc.dram_tensor("WuT", [RTOT, D], F32, kind="ExternalInput").ap()
    for n, shp in [("bk2d", [128, 8]), ("bq2d", [128, 8]), ("lnb2d", [128, 8]),
                   ("g2d", [128, 8]), ("bv_row", [1, D]), ("ps_bc", [128, D]),
                   ("bp_bc", [128, PROJ]), ("bp2d", [128, 2]), ("bkp_row", [1, D]),
                   ("bqp_row", [1, D]), ("bo_bc", [128, D]), ("sim2d", [128, 8]),
                   ("temp", [1, 1]), ("ones_f", [1, 128]),
                   ("ones_c", [128, 1]), ("consts", [128, 8]),
                   ("gpad", [128, 32])]:
        h[n] = nc.dram_tensor(n, shp, F32, kind="ExternalInput").ap()
    h["iota4"] = nc.dram_tensor("iota4", [128, 4], F32,
                                kind="ExternalInput").ap()
    h["LTb"] = nc.dram_tensor("LTb", [128, 128], BF16, kind="ExternalInput").ap()
    h["ones2b"] = nc.dram_tensor("ones2b", [2, 128], BF16, kind="ExternalInput").ap()
    h["I128b"] = nc.dram_tensor("I128b", [128, 128], BF16, kind="ExternalInput").ap()
    h["magic"] = nc.dram_tensor("magic", [128, 1], I32, kind="ExternalInput").ap()
    hlscr = nc.dram_tensor("hlscr", [1, 2 * D], BF16, kind="Internal").ap()

    from contextlib import ExitStack
    ctx = ExitStack()
    sb = lambda name, shape, dt: ctx.enter_context(nc.sbuf_tensor(name, shape, dt))
    with ctx:
        wb = {n: sb(n + "_b", [128, NCH * D], BF16) for n in W_NAMES}
        WpTb = sb("WpT_b", [128, NCH * PROJ], BF16)
        WdTb = sb("WdT_b", [128, NCH * RTOT], BF16)
        WuTb = sb("WuT_b", [RTOT, D], BF16)
        LTs = sb("LT_b", [128, 128], BF16)
        I128 = sb("I128_b", [128, 128], BF16)

        bk2d = sb("bk2d_s", [128, 8], F32)
        bq2d = sb("bq2d_s", [128, 8], F32)
        lnb2d = sb("lnb2d_s", [128, 8], F32)
        g2d = sb("g2d_s", [128, 8], F32)
        b2db = sb("b2d_bb", [128, 24], BF16)
        ps_bc = sb("ps_bc_s", [128, D], F32)
        bp_bc = sb("bp_bc_s", [128, PROJ], F32)
        bp2d = sb("bp2d_s", [128, 2], F32)
        bp2db = sb("bp2d_bb", [128, 2], BF16)
        bps4 = sb("bps4_s", [128, 4], F32)
        cbc = sb("cbc_s", [128, 1], F32)
        rowtmp_t = sb("rowtmp", [2, D], F32)
        rowbuf = sb("rowbuf", [1, D], F32)
        kp_hl = sb("kp_hl", [2, D], BF16)
        qp_hl = sb("qp_hl", [2, D], BF16)
        v_hl = sb("v_hl", [2, D], BF16)
        ones2 = sb("ones2_s", [2, 128], BF16)
        iota4 = sb("iota4_s", [128, 4], F32)
        gidx = sb("gidx_s", [128, 8], mybir.dt.uint32)
        bo_bc = sb("bo_bc_s", [128, D], F32)
        sim2d = sb("sim2d_s", [128, 8], F32)
        simnb = sb("simn_b", [128, 8], BF16)
        temp = sb("temp_s", [1, 1], F32)
        ones_f = sb("ones_f_s", [1, 128], F32)
        ones_c = sb("ones_c_s", [128, 1], F32)
        consts = sb("consts_s", [128, 8], F32)
        magic = sb("magic_s", [128, 1], I32)
        sc_bc = sb("sc_bc_s", [128, 1], F32)

        x_in = sb("x_in", [128, D], F32)
        A = sb("scrA", [128, D], F32)
        B = sb("scrB", [128, D], F32)
        C = sb("scrC", [128, D], F32)
        U = sb("scrU", [128, D], I32)
        SKb = sb("sinK", [128, D], BF16)
        CKb = sb("cosK", [128, D], BF16)
        VAL = sb("val", [128, D], F32)
        X2 = sb("x2", [128, D], F32)
        xbb = sb("xb_b", [128, D], BF16)
        bsh = sb("bshared", [128, D], BF16)
        tsh = sb("tshared", [128, D], BF16)
        kqT = sb("kqT", [128, D], BF16)
        junkb = sb("junk_b", [128, D], BF16)
        proj = sb("proj_s", [128, PROJ], F32)
        projb = sb("proj_bb", [128, PROJ], BF16)
        projT = sb("projT_b", [128, 2 * 128], BF16)
        hsml = sb("h_small", [128, RTOT], F32)
        hpb = sb("hp_b", [128, RTOT], BF16)
        hpT = sb("hpT_b", [RTOT, 128], BF16)
        gsm = sb("g_small", [128, 32], F32)
        lg = gsm[:, 0:4]
        STAT = [[sb(f"stat{p}_{i}", [128, 1], F32) for i in range(4)]
                for p in range(2)]
        NRS = [sb(f"nr_small{p}", [128, 4], F32) for p in range(2)]
        PSTAT = [sb(f"pstat{p}", [128, 6], F32) for p in range(2)]
        SEEDI = [sb(f"seed_i{p}", [128, 2], I32) for p in range(2)]
        seed4 = sb("seed4_i", [1, 8], I32)
        rowtmp = rowtmp_t[0:1, :]
        srow = sb("srow", [1, 16], F32)
        st1, st2, st3, st4 = STAT[0]
        nrs = NRS[0]
        seedi = SEEDI[0]

        xTb = sb("xT_b", [128, D], BF16)
        OUTB = sb("out_b", [128, D], F32)
        MIB = sb("mi_b", [128, D], F32)
        KQT = [kqT, sb("kqT2", [128, D], BF16)]
        XIN = [x_in, sb("x_in2", [128, D], F32)]

        P = [ctx.enter_context(nc.psum_tensor(f"P{i}", [128, 1024], F32))
             for i in range(4)]
        PS = [[P[i][:, 0:512], P[i][:, 512:1024]] for i in range(4)]

        sems = {}
        for e in ENGINES + DMA_CH:
            sems[e] = ctx.enter_context(nc.semaphore("sem_" + e))
        chain_sems = {e: ctx.enter_context(nc.semaphore("chain_" + e))
                      for e in ("dve", "act", "pool")}
        chain_cnt = {e: 0 for e in chain_sems}

        def mkstep(eng):
            def step(e, inst):
                chain_cnt[eng] += 1
                inst.then_inc(chain_sems[eng], 1)
                e.wait_ge(chain_sems[eng], chain_cnt[eng])
            return step
        stepd = mkstep("dve")
        stepp = mkstep("pool")

        s = Sched()
        wkeys = lambda n: tuple((n, c) for c in range(NCH))

        def dma_in(dst_ap, src_ap, writes, reads=()):
            return s.op("in", lambda e, d=dst_ap, sr=src_ap: e.dma_start(out=d, in_=sr),
                        reads=reads, writes=writes)

        def mmseq(jobs, reads, writes, stop_last=True, tag="mm"):
            jobs = list(jobs)
            cost = sum(max(j[2].shape[-1], 70) / 2400.0 + 0.015 for j in jobs)

            def emit(e, jobs=jobs):
                inst = None
                for (o, l, r, st, sp) in jobs:
                    inst = e.matmul(out=o, lhsT=l, rhs=r, start=st, stop=sp)
                return inst
            return s.op("pe", emit, reads=reads, writes=writes, cost=cost,
                        tag=tag)

        def proj_tok(slot, lhsT_key, lhsT_ap, wtile, wkey_t, stop_last=True):
            jobs = []
            for nh in range(2):
                for dc in range(NCH):
                    jobs.append((PS[slot][nh][:, 0:512],
                                 lhsT_ap[:, dc * 128:(dc + 1) * 128],
                                 wtile[:, dc * D + nh * 512:dc * D + (nh + 1) * 512],
                                 dc == 0, (dc == NCH - 1) and stop_last))
            return mmseq(jobs, reads=(lhsT_key,) + wkey_t,
                         writes=(("ps", slot),))

        def feat_proj(slot, wtile, wkey_t):
            jobs = []
            for ec in range(NCH):
                bank = PS[slot][ec // 4]
                for dc in range(NCH):
                    jobs.append((bank[:, (ec % 4) * 128:(ec % 4 + 1) * 128],
                                 wtile[:, dc * D + ec * 128:dc * D + (ec + 1) * 128],
                                 tsh[:, dc * 128:(dc + 1) * 128],
                                 dc == 0, dc == NCH - 1))
            return mmseq(jobs, reads=("tsh",) + wkey_t, writes=(("ps", slot),))

        def transpose8(slot, src_key, src_ap):
            jobs = []
            for c in range(NCH):
                bank = PS[slot][c // 4]
                jobs.append((bank[:, (c % 4) * 128:(c % 4 + 1) * 128],
                             src_ap[:, c * 128:(c + 1) * 128], I128[:], True, True))
            return mmseq(jobs, reads=(src_key, "I128"), writes=(("ps", slot),))

        def evac_pair(slot, dst_ap, dst_key, eng="act"):
            if eng == "act":
                def emit(e, sl=slot, d=dst_ap):
                    return e.activation(out=d, in_=P[sl][:, :], func=AF.Identity)
            else:
                def emit(e, sl=slot, d=dst_ap):
                    return e.tensor_copy(out=d, in_=P[sl][:, :])
            return s.op(eng, emit, reads=(("ps", slot),), writes=(dst_key,),
                        cost=1.2, tag=f"evac{slot}")

        def evac_pair_add(slot, bcast_tile, bkey, dst, dst_key):
            def emit(e, sl=slot, bc=bcast_tile, d=dst):
                return e.tensor_tensor(out=d[:, 0:1024], in0=P[sl][:, :],
                                       in1=bc[:, 0:1024], op=ALU.add)
            return s.op("dve", emit, reads=(("ps", slot), bkey),
                        writes=(dst_key,), cost=1.25, tag=f"evadd{slot}")

        def rsqrt_nr(src, src_key, dst, dst_key, sdi, sdi_key, nr, nr_key,
                     clamp=None, iters=4, src_ap=None, dst_ap=None, nr_ap=None):
            if src_ap is None:
                src_ap = src[:]
            if dst_ap is None:
                dst_ap = dst[:]
            if nr_ap is None:
                nr_ap = nr[:, 0:1]

            def emit_seed(e, src_ap=src_ap, dst_ap=dst_ap, sdi=sdi):
                sbits = src_ap.bitcast(I32)
                stepd(e, e.tensor_scalar(out=sdi, in0=sbits, scalar1=1,
                                         scalar2=None,
                                         op0=ALU.arith_shift_right))
                stepd(e, e.tensor_tensor(out=sdi, in0=magic[:],
                                         in1=sdi, op=ALU.subtract))
                return e.tensor_copy(out=dst_ap, in_=sdi.bitcast(F32))
            s.op("dve", emit_seed, reads=(src_key, "magic"),
                 writes=(dst_key, sdi_key), cost=1.0, tag="nr_seed")

            def emit_nr(e, src_ap=src_ap, dst_ap=dst_ap, nr_ap=nr_ap):
                inst = None
                for it in range(iters):
                    stepp(e, e.tensor_tensor(out=nr_ap, in0=dst_ap,
                                             in1=dst_ap, op=ALU.mult))
                    stepp(e, e.tensor_tensor(out=nr_ap, in0=nr_ap,
                                             in1=src_ap, op=ALU.mult))
                    stepp(e, e.tensor_scalar(out=nr_ap, in0=nr_ap,
                                             scalar1=-0.5, scalar2=1.5,
                                             op0=ALU.mult, op1=ALU.add))
                    inst = e.tensor_tensor(out=dst_ap, in0=dst_ap,
                                           in1=nr_ap, op=ALU.mult)
                    if it < iters - 1 or clamp is not None:
                        stepp(e, inst)
                if clamp is not None:
                    inst = e.tensor_scalar(out=dst_ap, in0=dst_ap, scalar1=clamp,
                                           scalar2=None, op0=ALU.min)
                return inst
            return s.op("pool", emit_nr, reads=(src_key, dst_key),
                        writes=(dst_key, nr_key), cost=0.36 * (4 * iters + 1),
                        tag="nr_loop")

        def _ap(t):
            return t if isinstance(t, bass.AP) else t[:]
        for name, tile in [("bk2d", bk2d), ("bq2d", bq2d), ("lnb2d", lnb2d),
                           ("g2d", g2d), ("ps_bc", ps_bc),
                           ("bp_bc", bp_bc), ("bp2d", bp2d), ("bo_bc", bo_bc),
                           ("iota4", iota4),
                           ("sim2d", sim2d), ("temp", temp), ("ones_f", ones_f),
                           ("ones_c", ones_c), ("consts", consts)]:
            dma_in(_ap(tile), h[name], writes=(name,))
        dma_in(LTs[:], h["LTb"], writes=("LT",))
        dma_in(ones2[:], h["ones2b"], writes=("ones2",))

        dma_in(I128[:], h["I128b"], writes=("I128",))
        dma_in(magic[:], h["magic"], writes=("magic",))
        dma_in(x_in[:], hx[0:128, :], writes=("x_in",))

        stage = [(A, "scrA"), (B, "scrB")]
        cast_engines = ["dve", "act", "pool"]
        ei = 0
        for wi, wname in enumerate(W_NAMES + ["WpT", "WdT"]):
            src = h[wname]
            ncols = {"WpT": PROJ, "WdT": RTOT}.get(wname, D)
            dstt = {"WpT": WpTb, "WdT": WdTb}.get(wname, wb.get(wname))
            for c in range(NCH):
                stg, skey = stage[(wi * NCH + c) % 2]
                dma_in(stg[:, 0:ncols], src[c * 128:(c + 1) * 128, :],
                       writes=(skey,))
                eng = cast_engines[ei % 3]
                ei += 1
                dst_ap = dstt[:, c * ncols:(c + 1) * ncols]
                if wname == "WoT":
                    def emit(e, stg=stg, dst_ap=dst_ap, gc=g2d[:, c:c + 1], n=ncols):
                        return e.tensor_scalar(out=dst_ap, in0=stg[:, 0:n],
                                               scalar1=gc, scalar2=None,
                                               op0=ALU.mult)
                    s.op("dve", emit, reads=(skey, "g2d"), writes=((wname, c),))
                elif eng == "act":
                    def emit(e, stg=stg, dst_ap=dst_ap, n=ncols):
                        return e.activation(out=dst_ap, in_=stg[:, 0:n],
                                            func=AF.Identity)
                    s.op("act", emit, reads=(skey,), writes=((wname, c),))
                else:
                    def emit(e, stg=stg, dst_ap=dst_ap, n=ncols):
                        return e.tensor_copy(out=dst_ap, in_=stg[:, 0:n])
                    s.op(eng, emit, reads=(skey,), writes=((wname, c),))
        dma_in(C[0:RTOT, 0:D], h["WuT"], writes=("scrC",))
        s.op("dve", lambda e: e.tensor_copy(out=WuTb[:], in_=C[0:RTOT, 0:D]),
             reads=("scrC",), writes=("WuT", 0))
        wuk = (("WuT", 0),)

        def small_cast(e):
            e.tensor_copy(out=b2db[:, 0:8], in_=bk2d[:])
            e.tensor_copy(out=b2db[:, 8:16], in_=bq2d[:])
            return e.tensor_copy(out=b2db[:, 16:24], in_=lnb2d[:])
        s.op("pool", small_cast, reads=("bk2d", "bq2d", "lnb2d"), writes=("b2db",))

        def hilo_from_row(dst, dkey):
            h0 = KQT[0][0:1, 0:D]
            h1 = KQT[1][0:1, 0:D]

            def emit(e):
                stepd(e, e.tensor_copy(out=h0, in_=rowtmp))
                stepd(e, e.tensor_copy(out=rowbuf[:], in_=h0))
                stepd(e, e.tensor_tensor(out=rowbuf[:], in0=rowtmp,
                                         in1=rowbuf[:], op=ALU.subtract))
                return e.tensor_copy(out=h1, in_=rowbuf[:])
            s.op("dve", emit, reads=("rowtmp",),
                 writes=("kqT_0", "kqT_1", "rowbuf"), cost=0.5, tag="hilo")
            s.op("out", lambda e: e.dma_start(out=hlscr[0:1, 0:D], in_=h0),
                 reads=("kqT_0",), writes=("hlscr",))
            s.op("out", lambda e: e.dma_start(out=hlscr[0:1, D:2 * D], in_=h1),
                 reads=("kqT_1",), writes=("hlscr",))
            s.op("in", lambda e, dst=dst: e.dma_start(
                out=dst[:], in_=hlscr.rearrange("a (b c) -> (a b) c", b=2)),
                reads=("hlscr",), writes=(dkey,))

        def row_fold(boff, wname, wkey_t, add_ap, addkey, dst, dkey, slot_row):
            jobs = []
            for c in range(NCH):
                for nh in range(2):
                    jobs.append((PS[slot_row][nh][0:1, 0:512],
                                 b2db[:, boff + c:boff + c + 1],
                                 wb[wname][:, c * D + nh * 512:c * D + (nh + 1) * 512],
                                 c == 0, c == NCH - 1))
            mmseq(jobs, reads=("b2db",) + wkey_t, writes=(("ps", slot_row),))

            def ev(e, sl=slot_row, ar=add_ap):
                stepd(e, e.tensor_tensor(out=rowtmp[0:1, 0:512],
                                         in0=PS[sl][0][0:1, :],
                                         in1=ar[0:1, 0:512], op=ALU.add))
                return e.tensor_tensor(out=rowtmp[0:1, 512:1024],
                                       in0=PS[sl][1][0:1, :],
                                       in1=ar[0:1, 512:1024], op=ALU.add)
            s.op("dve", ev, reads=(("ps", slot_row), addkey), writes=("rowtmp",))
            hilo_from_row(dst, dkey)

        dma_in(rowbuf[:], h["bkp_row"], writes=("rowbuf",))
        row_fold(0, "WkpT", wkeys("WkpT"), rowbuf[:], "rowbuf",
                 kp_hl, "kp_hl", 0)
        dma_in(rowbuf[:], h["bqp_row"], writes=("rowbuf",))
        row_fold(8, "WqpT", wkeys("WqpT"), rowbuf[:], "rowbuf",
                 qp_hl, "qp_hl", 2)
        dma_in(rowtmp, h["bv_row"], writes=("rowtmp",))
        hilo_from_row(v_hl, "v_hl")

        for c in range(NCH):
            stg, skey = stage[c % 2]
            dma_in(stg[:, 0:D], h["WoT"][c * 128:(c + 1) * 128, :], writes=(skey,))

            def cem(e, stg=stg):
                return e.tensor_copy(out=bsh[:, 0:D], in_=stg[:, 0:D])
            s.op("dve", cem, reads=(skey,), writes=("bsh",))
            jb = []
            for nh in range(2):
                jb.append((PS[0][nh][0:1, 0:512], b2db[:, 16 + c:17 + c],
                           bsh[:, nh * 512:(nh + 1) * 512],
                           c == 0, c == NCH - 1))
            mmseq(jb, reads=("bsh", "b2db"), writes=(("ps", 0),))

        def bo_ev(e):
            e.tensor_copy(out=rowtmp[0:1, 0:512], in_=PS[0][0][0:1, :])
            return e.tensor_copy(out=rowtmp[0:1, 512:1024], in_=PS[0][1][0:1, :])
        s.op("dve", bo_ev, reads=(("ps", 0),), writes=("rowtmp",))
        jobs2 = [(PS[1][nh][:, 0:512], ones_f[:], rowtmp[0:1, nh * 512:(nh + 1) * 512],
                  True, True) for nh in range(2)]
        mmseq(jobs2, reads=("rowtmp", "ones_f"), writes=(("ps", 1),))

        def bo_bc_ev(e):
            e.tensor_tensor(out=bo_bc[:, 0:512], in0=PS[1][0][:, :],
                            in1=bo_bc[:, 0:512], op=ALU.add)
            return e.tensor_tensor(out=bo_bc[:, 512:1024], in0=PS[1][1][:, :],
                                   in1=bo_bc[:, 512:1024], op=ALU.add)
        s.op("dve", bo_bc_ev, reads=(("ps", 1), "bo_bc"), writes=("bo_bc",))

        s.op("act", lambda e: e.activation(out=gsm[:, 0:8], in_=sim2d[:],
                                           func=AF.Square),
             reads=("sim2d",), writes=("gsm",))
        jb = [(PS[2][0][0:1, 0:8], ones_c[:], gsm[:, 0:8], True, True)]
        mmseq(jb, reads=("gsm", "ones_c"), writes=(("ps", 2),))

        def sim_norm(e):
            stepd(e, e.tensor_copy(out=srow[0:1, 8:16], in_=PS[2][0][0:1, 0:8]))
            stepd(e, e.tensor_tensor(out=srow[0:1, 0:4], in0=srow[0:1, 8:12],
                                     in1=srow[0:1, 12:16], op=ALU.add))
            sbits = srow[0:1, 0:4].bitcast(I32)
            stepd(e, e.tensor_scalar(out=seed4[0:1, 0:4], in0=sbits, scalar1=1,
                                     scalar2=None, op0=ALU.arith_shift_right))
            stepd(e, e.tensor_scalar(out=seed4[0:1, 4:8], in0=seed4[0:1, 0:4],
                                     scalar1=-1, scalar2=0x5F3759DF,
                                     op0=ALU.mult, op1=ALU.add))
            stepd(e, e.tensor_copy(out=srow[0:1, 4:8],
                                   in_=seed4[0:1, 4:8].bitcast(F32)))
            for _ in range(4):
                stepd(e, e.tensor_tensor(out=srow[0:1, 8:12], in0=srow[0:1, 4:8],
                                         in1=srow[0:1, 4:8], op=ALU.mult))
                stepd(e, e.tensor_tensor(out=srow[0:1, 8:12], in0=srow[0:1, 8:12],
                                         in1=srow[0:1, 0:4], op=ALU.mult))
                stepd(e, e.tensor_scalar(out=srow[0:1, 8:12], in0=srow[0:1, 8:12],
                                         scalar1=-0.5, scalar2=1.5, op0=ALU.mult,
                                         op1=ALU.add))
                stepd(e, e.tensor_tensor(out=srow[0:1, 4:8], in0=srow[0:1, 4:8],
                                         in1=srow[0:1, 8:12], op=ALU.mult))
            return e.tensor_scalar(out=srow[0:1, 4:8], in0=srow[0:1, 4:8],
                                   scalar1=1e12, scalar2=None, op0=ALU.min)
        s.op("dve", sim_norm, reads=(("ps", 2),), writes=("srow", "seed4"))
        jb = [(PS[2][0][:, 0:4], ones_f[:], srow[0:1, 4:8], True, True)]
        mmseq(jb, reads=("srow", "ones_f"), writes=(("ps", 2),))

        def simn_apply(e):
            stepd(e, e.tensor_copy(out=gsm[:, 8:12], in_=PS[2][0][:, 0:4]))
            stepd(e, e.tensor_tensor(out=gsm[:, 0:4], in0=sim2d[:, 0:4],
                                     in1=gsm[:, 8:12], op=ALU.mult))
            stepd(e, e.tensor_tensor(out=gsm[:, 4:8], in0=sim2d[:, 4:8],
                                     in1=gsm[:, 8:12], op=ALU.mult))
            return e.tensor_copy(out=simnb[:], in_=gsm[:, 0:8])
        s.op("dve", simn_apply, reads=(("ps", 2), "sim2d"),
             writes=("simnb", "gsm"))
        dma_in(gsm[:], h["gpad"], writes=("gsm",))
        s.op("pool", lambda e: e.tensor_copy(out=bp2db[:], in_=bp2d[:]),
             reads=("bp2d",), writes=("bp2db",))
        jb = [(PS[3][0][0:1, 0:4], bp2db[:, c:c + 1], simnb[:, c * 4:(c + 1) * 4],
               c == 0, c == 1) for c in range(2)]
        mmseq(jb, reads=("bp2db", "simnb"), writes=(("ps", 3),))
        s.op("dve", lambda e: e.tensor_copy(out=srow[0:1, 0:4],
                                            in_=PS[3][0][0:1, 0:4]),
             reads=(("ps", 3),), writes=("srow",))
        jb = [(PS[3][0][:, 8:12], ones_f[:], srow[0:1, 0:4], True, True)]
        mmseq(jb, reads=("srow", "ones_f"), writes=(("ps", 3),))
        s.op("dve", lambda e: e.tensor_copy(out=bps4[:], in_=PS[3][0][:, 8:12]),
             reads=(("ps", 3),), writes=("bps4",))
        s.op("dve", lambda e: e.scalar_tensor_tensor(
            out=junkb[:, 0:PROJ], in0=bp_bc[:], scalar=1.0, in1=bp_bc[:],
            op0=ALU.mult, op1=ALU.mult, accum_out=cbc[:]),
            reads=("bp_bc",), writes=("junkb", "cbc"))

        s.op("dve", lambda e: e.tensor_scalar(
            out=srow[0:1, 12:13], in0=temp[:], scalar1=CLAMP_MAX, scalar2=0.5,
            op0=ALU.min, op1=ALU.mult),
            reads=("temp",), writes=("srow",))
        s.op("act", lambda e: e.activation(out=srow[0:1, 13:14],
                                           in_=srow[0:1, 12:13], func=AF.Tanh),
             reads=("srow",), writes=("srow",))

        def sc_fin(e):
            stepd(e, e.tensor_scalar(out=srow[0:1, 14:15], in0=srow[0:1, 13:14],
                                     scalar1=-1.0, scalar2=1.0, op0=ALU.mult,
                                     op1=ALU.add))
            stepd(e, e.reciprocal(out=srow[0:1, 14:15], in_=srow[0:1, 14:15]))
            stepd(e, e.tensor_scalar(out=srow[0:1, 13:14], in0=srow[0:1, 13:14],
                                     scalar1=1.0, scalar2=None, op0=ALU.add))
            return e.tensor_tensor(out=srow[0:1, 15:16], in0=srow[0:1, 13:14],
                                   in1=srow[0:1, 14:15], op=ALU.mult)
        s.op("dve", sc_fin, reads=("srow",), writes=("srow",))
        jb = [(PS[2][0][:, 4:5], ones_f[:], srow[0:1, 15:16], True, True)]
        mmseq(jb, reads=("srow", "ones_f"), writes=(("ps", 2),))
        s.op("dve", lambda e: e.tensor_copy(out=sc_bc[:], in_=PS[2][0][:, 4:5]),
             reads=(("ps", 2),), writes=("sc_bc",))

        c_S = consts[:, 0:1]
        c_HPI = consts[:, 1:2]
        c_HALF = consts[:, 2:3]
        c_NS = consts[:, 3:4]


        def tctx(ti):
            pp = ti % 2
            stp = STAT[pp]
            ks = (f"st1_{pp}", f"st2_{pp}", f"st3_{pp}", f"st4_{pp}")
            return dict(pp=pp, stp=stp, ks=ks, nrp=NRS[pp], knr=f"nrs_{pp}",
                        sdp=SEEDI[pp][:, 0:1], ksd=f"seedi_{pp}",
                        sdp2=SEEDI[pp][:, 1:2], ksd2=f"seedi2_{pp}",
                        pst=PSTAT[pp], kps=f"pstat_{pp}",
                        xin=XIN[pp], kx=f"x_in_{pp}",
                        kqt=KQT[pp], kk=f"kqT_{pp}")

        def ln_negmu(t):
            stp1, stp3 = t["stp"][0], t["stp"][2]
            k1, k3 = t["ks"][0], t["ks"][2]
            s.op("dve", lambda e, stp1=stp1, stp3=stp3: e.tensor_scalar(
                out=stp1[:], in0=stp3[:], scalar1=-1.0 / D, scalar2=None,
                op0=ALU.mult),
                reads=(k3,), writes=(k1,))

        def ln_rstd(t, eps):
            stp1, stp2, stp3, stp4 = t["stp"]
            k1, k2, k3, k4 = t["ks"]
            nrp, knr = t["nrp"], t["knr"]

            def emit(e):
                stepp(e, e.tensor_scalar(out=nrp[:, 1:2], in0=stp4[:],
                                         scalar1=1.0 / D, scalar2=eps,
                                         op0=ALU.mult, op1=ALU.add))
                stepp(e, e.tensor_tensor(out=nrp[:, 2:3], in0=stp1[:],
                                         in1=stp1[:], op=ALU.mult))
                return e.tensor_tensor(out=stp2[:], in0=nrp[:, 1:2],
                                       in1=nrp[:, 2:3], op=ALU.subtract)
            s.op("pool", emit, reads=(k1, k4), writes=(k2, knr), cost=1.1,
                 tag="rstd_w")
            rsqrt_nr(stp2, k2, stp3, k3, t["sdp"], t["ksd"], nrp, knr, iters=3)

        def ln_stats(t, eps, sum_i=2):
            stp1, stp2, stp3, stp4 = t["stp"]
            k1, k2, k3, k4 = t["ks"]
            sums, ksum = t["stp"][sum_i], t["ks"][sum_i]
            nrp, knr = t["nrp"], t["knr"]

            def emit(e, sums=sums):
                stepp(e, e.tensor_scalar(out=stp1[:], in0=sums[:],
                                         scalar1=1.0 / D, scalar2=None,
                                         op0=ALU.mult))
                e.tensor_scalar(out=nrp[:, 1:2], in0=stp4[:], scalar1=1.0 / D,
                                scalar2=eps, op0=ALU.mult, op1=ALU.add)
                stepp(e, e.tensor_tensor(out=nrp[:, 2:3], in0=stp1[:],
                                         in1=stp1[:], op=ALU.mult))
                return e.tensor_tensor(out=stp2[:], in0=nrp[:, 1:2],
                                       in1=nrp[:, 2:3], op=ALU.subtract)
            s.op("pool", emit, reads=(ksum, k4), writes=(k1, k2, knr),
                 cost=1.45, tag="lnst_w")
            rsqrt_nr(stp2, k2, stp3, k3, t["sdp"], t["ksd"], nrp, knr, iters=3)

            def emit_nb(e):
                stepp(e, e.tensor_tensor(out=stp4[:], in0=stp1[:], in1=stp3[:],
                                         op=ALU.mult))
                return e.tensor_scalar(out=stp4[:], in0=stp4[:], scalar1=-1.0,
                                       scalar2=None, op0=ALU.mult)
            s.op("pool", emit_nb, reads=(k1, k3), writes=(k4,), cost=0.72,
                 tag="ln_nb")

        def wrap_and_trig(src_key, src_ap):
            s.op("act", lambda e, sa=src_ap: e.activation(
                out=A[:], in_=sa, func=AF.Tanh),
                reads=(src_key,), writes=("scrA",), cost=1.25, tag="wr_tanh")
            s.op("dve", lambda e: e.scalar_tensor_tensor(
                out=A[:], in0=A[:], scalar=INV_2PI, in1=ps_bc[:],
                op0=ALU.mult, op1=ALU.mult),
                reads=("scrA", "ps_bc"), writes=("scrA",), cost=1.28, tag="wr_t")
            s.op("dve", lambda e: e.tensor_scalar(
                out=U[:], in0=A[:], scalar1=KOFF, scalar2=None, op0=ALU.add),
                reads=("scrA",), writes=("scrU",), cost=0.7, tag="wr_u")
            s.op("dve", lambda e: e.scalar_tensor_tensor(
                out=C[:], in0=U[:], scalar=-KOFF, in1=A[:],
                op0=ALU.add, op1=ALU.subtract),
                reads=("scrU", "scrA"), writes=("scrC",), cost=1.28, tag="wr_nv")
            s.op("dve", lambda e: e.scalar_tensor_tensor(
                out=C[:], in0=C[:], scalar=-0.5, in1=C[:], op0=ALU.is_lt,
                op1=ALU.add),
                reads=("scrC",), writes=("scrC",), cost=1.28, tag="wr_w")
            s.op("act", lambda e: e.activation(
                out=SKb[:], in_=C[:], func=AF.Sin, scale=c_NS),
                reads=("scrC", "consts"), writes=("SK",), cost=1.25, tag="wr_sin")
            s.op("dve", lambda e: e.scalar_tensor_tensor(
                out=A[:], in0=C[:], scalar=-0.25, in1=C[:], op0=ALU.is_le,
                op1=ALU.add),
                reads=("scrC",), writes=("scrA",), cost=1.28, tag="wr_vc")
            s.op("act", lambda e: e.activation(
                out=CKb[:], in_=A[:], func=AF.Sin, scale=c_NS, bias=c_HPI),
                reads=("scrA", "consts"), writes=("CK",), cost=1.25, tag="wr_cos")

        def f_xkey(ti):
            t = tctx(ti)
            s.op("act", lambda e, x=t["xin"]: e.activation(
                out=xbb[:], in_=x[:], func=AF.Identity),
                reads=(t["kx"],), writes=("xbb",), cost=1.2, tag="xb")
            transpose8(3, "xbb", xbb[:])
            evac_pair(3, xTb[:, 0:1024], "xTb")
            jobs = []
            for ec in range(NCH):
                for dc in range(NCH):
                    jobs.append((P[1][:, ec * 128:(ec + 1) * 128],
                                 wb["WkT"][:, dc * D + ec * 128:dc * D + (ec + 1) * 128],
                                 xTb[:, dc * 128:(dc + 1) * 128],
                                 dc == 0, dc == NCH - 1))
            mmseq(jobs, reads=("xTb",) + wkeys("WkT"), writes=(("ps", 1),))
            evac_pair(1, t["kqt"][:, 0:1024], t["kk"])

        def f_val_kp(ti):
            t = tctx(ti)
            kq = t["kqt"]
            jobs = []
            for nh in range(2):
                for dc in range(NCH):
                    jobs.append((PS[0][nh][:, 0:512],
                                 xTb[:, dc * 128:(dc + 1) * 128],
                                 wb["WvT"][:, dc * D + nh * 512:dc * D + (nh + 1) * 512],
                                 dc == 0, False))
                jobs.append((PS[0][nh][:, 0:512], ones2[:],
                             v_hl[0:2, nh * 512:(nh + 1) * 512], False, True))
            mmseq(jobs, reads=("xTb", "v_hl") + wkeys("WvT"),
                  writes=(("ps", 0),), tag="value")
            s.op("act", lambda e: e.activation(out=VAL[:], in_=P[0][:, :],
                                               func=AF.Identity),
                 reads=(("ps", 0),), writes=("VAL",), cost=1.2, tag="val_ev")
            jobs = []
            for nh in range(2):
                for ec in range(NCH):
                    jobs.append((PS[1][nh][:, 0:512],
                                 kq[:, ec * 128:(ec + 1) * 128],
                                 wb["WkpT"][:, ec * D + nh * 512:ec * D + (nh + 1) * 512],
                                 ec == 0, False))
                jobs.append((PS[1][nh][:, 0:512], ones2[:],
                             kp_hl[0:2, nh * 512:(nh + 1) * 512], False, True))
            mmseq(jobs, reads=(t["kk"], "kp_hl") + wkeys("WkpT"),
                  writes=(("ps", 1),), tag="kp")
            wrap_and_trig(("ps", 1), P[1][:, :])

        def f_qbm(ti):
            t = tctx(ti)
            jobs = []
            for ec in range(NCH):
                for dc in range(NCH):
                    jobs.append((P[1][:, ec * 128:(ec + 1) * 128],
                                 wb["WqT"][:, dc * D + ec * 128:dc * D + (ec + 1) * 128],
                                 xTb[:, dc * 128:(dc + 1) * 128],
                                 dc == 0, dc == NCH - 1))
            mmseq(jobs, reads=("xTb",) + wkeys("WqT"), writes=(("ps", 1),))
            s.op("dve", lambda e: e.tensor_tensor(out=xbb[:], in0=VAL[:],
                                                  in1=CKb[:], op=ALU.mult),
                 reads=("VAL", "CK"), writes=("xbb",))
            s.op("dve", lambda e: e.tensor_tensor(out=bsh[:], in0=VAL[:],
                                                  in1=SKb[:], op=ALU.mult),
                 reads=("VAL", "SK"), writes=("bsh",))
            evac_pair(1, t["kqt"][:, 0:1024], t["kk"])
            jobs = [(PS[0][nh][:, 0:512], LTs[:], xbb[:, nh * 512:(nh + 1) * 512],
                     True, True) for nh in range(2)]
            mmseq(jobs, reads=("xbb", "LT"), writes=(("ps", 0),))
            jobs = [(PS[1][nh][:, 0:512], LTs[:], bsh[:, nh * 512:(nh + 1) * 512],
                     True, True) for nh in range(2)]
            mmseq(jobs, reads=("bsh", "LT"), writes=(("ps", 1),))
            s.op("dve", lambda e: e.tensor_copy(out=VAL[:], in_=P[0][:, :]),
                 reads=(("ps", 0),), writes=("VAL",), cost=1.2, tag="mr_ev")
            s.op("dve", lambda e: e.tensor_copy(out=MIB[:], in_=P[1][:, :]),
                 reads=(("ps", 1),), writes=("MIB",), cost=1.2, tag="mi_ev")

        def f_qp(ti):
            t = tctx(ti)
            kq = t["kqt"]
            jobs = []
            for nh in range(2):
                for ec in range(NCH):
                    jobs.append((PS[2][nh][:, 0:512],
                                 kq[:, ec * 128:(ec + 1) * 128],
                                 wb["WqpT"][:, ec * D + nh * 512:ec * D + (nh + 1) * 512],
                                 ec == 0, False))
                jobs.append((PS[2][nh][:, 0:512], ones2[:],
                             qp_hl[0:2, nh * 512:(nh + 1) * 512], False, True))
            mmseq(jobs, reads=(t["kk"], "qp_hl") + wkeys("WqpT"),
                  writes=(("ps", 2),), tag="qp")
            wrap_and_trig(("ps", 2), P[2][:, :])

        def f_retr(ti):
            t = tctx(ti)
            stp3, stp4 = t["stp"][2], t["stp"][3]
            k3, k4 = t["ks"][2], t["ks"][3]
            s.op("dve", lambda e: e.tensor_tensor(
                out=VAL[:], in0=VAL[:], in1=CKb[:], op=ALU.mult),
                reads=("VAL", "CK"), writes=("VAL",))
            s.op("dve", lambda e: e.tensor_tensor(
                out=MIB[:], in0=MIB[:], in1=SKb[:], op=ALU.mult),
                reads=("MIB", "SK"), writes=("MIB",))
            s.op("dve", lambda e, stp3=stp3: e.scalar_tensor_tensor(
                out=MIB[:], in0=VAL[:], scalar=1.0, in1=MIB[:], op0=ALU.mult,
                op1=ALU.add, accum_out=stp3[:]),
                reads=("VAL", "MIB"), writes=("MIB", k3))
            ln_negmu(t)
            s.op("act", lambda e, stp4=stp4: e.activation(
                out=junkb[:], in_=MIB[:], func=AF.Square, accum_out=stp4[:]),
                reads=("MIB",), writes=("junkb", k4))
            ln_rstd(t, EPS_IN)

        def f_lnT(ti):
            t = tctx(ti)
            stp1 = t["stp"][0]
            k1 = t["ks"][0]
            s.op("dve", lambda e, stp1=stp1: e.tensor_scalar(
                out=bsh[:], in0=MIB[:], scalar1=stp1[:], scalar2=None,
                op0=ALU.add),
                reads=("MIB", k1), writes=("bsh",), cost=0.7, tag="lncast")
            transpose8(3, "bsh", bsh[:])
            evac_pair(3, tsh[:, 0:1024], "tsh")

        def f_wo(ti):
            t = tctx(ti)
            stp3 = t["stp"][2]
            k3 = t["ks"][2]
            proj_tok(2, "tsh", tsh[:], wb["WoT"], wkeys("WoT"))
            s.op("dve", lambda e, stp3=stp3: e.scalar_tensor_tensor(
                out=C[:], in0=P[2][:, :], scalar=stp3[:], in1=bo_bc[:, 0:1024],
                op0=ALU.mult, op1=ALU.add),
                reads=(("ps", 2), k3, "bo_bc"), writes=("scrC",))
            s.op("dve", lambda e, x=t["xin"], stp3=stp3: e.scalar_tensor_tensor(
                out=B[:], in0=x[:], scalar=2.0, in1=C[:], op0=ALU.mult,
                op1=ALU.add, accum_out=stp3[:]),
                reads=(t["kx"], "scrC"), writes=("scrB", k3))

        def f_ln1x2(ti):
            t = tctx(ti)
            stp1, stp2, stp3, stp4 = t["stp"]
            k1, k2, k3, k4 = t["ks"]
            s.op("dve", lambda e, stp1=stp1, stp3=stp3: e.tensor_scalar(
                out=stp1[:], in0=stp3[:], scalar1=-1.0 / D, scalar2=None,
                op0=ALU.mult),
                reads=(k3,), writes=(k1,), cost=0.2, tag="negmu1")
            s.op("act", lambda e, stp4=stp4: e.activation(
                out=junkb[:], in_=B[:], func=AF.Square, accum_out=stp4[:]),
                reads=("scrB",), writes=("junkb", k4), cost=1.25, tag="sq1")
            s.op("dve", lambda e, stp1=stp1: e.tensor_scalar(
                out=bsh[:], in0=B[:], scalar1=stp1[:], scalar2=None,
                op0=ALU.add),
                reads=("scrB", k1), writes=("bsh",), cost=0.7, tag="x2b")
            transpose8(3, "bsh", bsh[:])
            evac_pair(3, tsh[:, 0:1024], "tsh")
            ln_rstd(t, EPS)
            s.op("pool", lambda e, stp1=stp1, stp3=stp3, stp4=stp4:
                 e.tensor_tensor(out=stp4[:], in0=stp1[:], in1=stp3[:],
                                 op=ALU.mult),
                 reads=(k1, k3), writes=(k4,), cost=0.36, tag="nb1")

        def f_ffn1(ti):
            t = tctx(ti)
            stp3 = t["stp"][2]
            k3 = t["ks"][2]
            pst, kps = t["pst"], t["kps"]
            base_k = ("ps", 2)
            proj_tok(2, "tsh", tsh[:], wb["WfT"], wkeys("WfT"), stop_last=False)
            jobs = [(PS[3][0][:, 0:PROJ], tsh[:, dc * 128:(dc + 1) * 128],
                     WpTb[:, dc * PROJ:(dc + 1) * PROJ], dc == 0, dc == NCH - 1)
                    for dc in range(NCH)]
            mmseq(jobs, reads=("tsh",) + wkeys("WpT"), writes=(("ps", 3),),
                  tag="Wp")
            jobs = [(PS[3][1][:, 0:RTOT], tsh[:, dc * 128:(dc + 1) * 128],
                     WdTb[:, dc * RTOT:(dc + 1) * RTOT], dc == 0, dc == NCH - 1)
                    for dc in range(NCH)]
            mmseq(jobs, reads=("tsh",) + wkeys("WdT"), writes=(("ps", 3),),
                  tag="Wd")
            s.op("act", lambda e: e.activation(out=projb[:],
                                               in_=PS[3][0][:, 0:PROJ],
                                               func=AF.Identity),
                 reads=(("ps", 3),), writes=("projb",), cost=0.6, tag="projb")
            s.op("act", lambda e, pst=pst: e.activation(
                out=junkb[:, 0:PROJ], in_=PS[3][0][:, 0:PROJ], func=AF.Square,
                accum_out=pst[:, 0:1]),
                reads=(("ps", 3),), writes=("junkb", kps), cost=0.6, tag="sqp")
            s.op("dve", lambda e, pst=pst: e.scalar_tensor_tensor(
                out=C[:, 0:PROJ], in0=PS[3][0][:, 0:PROJ], scalar=1.0,
                in1=bp_bc[:], op0=ALU.mult, op1=ALU.mult,
                accum_out=pst[:, 1:2]),
                reads=(("ps", 3), "bp_bc"), writes=("scrC", kps), cost=0.4,
                tag="bxp")
            s.op("dve", lambda e: e.tensor_copy(out=hsml[:],
                                                in_=PS[3][1][:, 0:RTOT]),
                 reads=(("ps", 3),), writes=("hsml",), cost=0.25, tag="h_ev")
            jobs = [(PS[3][0][:, PROJ + c * 128:PROJ + (c + 1) * 128],
                     projb[:, c * 128:(c + 1) * 128], I128[:], True, True)
                    for c in range(2)]
            mmseq(jobs, reads=("projb", "I128"), writes=(("ps", 3),),
                  tag="projT")
            s.op("dve", lambda e: e.tensor_copy(out=projT[:],
                                                in_=PS[3][0][:, PROJ:PROJ + 256]),
                 reads=(("ps", 3),), writes=("projT",), cost=0.4, tag="projT_ev")
            jobs = [(PS[0][0][:, 0:4], projT[:, c * 128:(c + 1) * 128],
                     simnb[:, c * 4:(c + 1) * 4], c == 0, c == 1)
                    for c in range(2)]
            mmseq(jobs, reads=("projT", "simnb"), writes=(("ps", 0),),
                  tag="logits")
            def smalls(e, pst=pst, stp3=stp3):
                stepp(e, e.tensor_tensor(out=pst[:, 2:3], in0=stp3[:],
                                         in1=stp3[:], op=ALU.mult))
                stepp(e, e.tensor_tensor(out=pst[:, 2:3], in0=pst[:, 2:3],
                                         in1=pst[:, 0:1], op=ALU.mult))
                stepp(e, e.tensor_tensor(out=pst[:, 4:5], in0=stp3[:],
                                         in1=pst[:, 1:2], op=ALU.mult))
                stepp(e, e.tensor_scalar(out=pst[:, 4:5], in0=pst[:, 4:5],
                                         scalar1=2.0, scalar2=None,
                                         op0=ALU.mult))
                stepp(e, e.tensor_tensor(out=pst[:, 4:5], in0=pst[:, 4:5],
                                         in1=cbc[:], op=ALU.add))
                return e.tensor_tensor(out=pst[:, 2:3], in0=pst[:, 2:3],
                                       in1=pst[:, 4:5], op=ALU.add)
            s.op("pool", smalls, reads=(kps, k3, "cbc"), writes=(kps,),
                 cost=1.8, tag="ssqp")
            rsqrt_nr(None, kps, None, kps, t["sdp2"], t["ksd2"],
                     pst, kps, clamp=1e12, iters=3,
                     src_ap=pst[:, 2:3], dst_ap=pst[:, 3:4],
                     nr_ap=pst[:, 4:5])
            def vsmalls(e, pst=pst, stp3=stp3):
                stepp(e, e.tensor_tensor(out=pst[:, 3:4], in0=pst[:, 3:4],
                                         in1=sc_bc[:], op=ALU.mult))
                return e.tensor_tensor(out=pst[:, 5:6], in0=pst[:, 3:4],
                                       in1=stp3[:], op=ALU.mult)
            s.op("pool", vsmalls, reads=(kps, k3, "sc_bc"), writes=(kps,),
                 cost=0.75, tag="vsm")
            def lgev(e, pst=pst):
                stepd(e, e.tensor_scalar(out=gsm[:, 0:4], in0=PS[0][0][:, 0:4],
                                         scalar1=pst[:, 5:6], scalar2=None,
                                         op0=ALU.mult))
                return e.scalar_tensor_tensor(out=gsm[:, 0:4], in0=bps4[:],
                                              scalar=pst[:, 3:4],
                                              in1=gsm[:, 0:4], op0=ALU.mult,
                                              op1=ALU.add)
            s.op("dve", lgev, reads=(("ps", 0), kps, "bps4"), writes=("gsm",),
                 cost=0.6, tag="lg")

        def f_gate(ti):
            def gate1(e):
                stepd(e, e.max(gsm[:, 8:16], gsm[:, 0:8]))
                stepd(e, e.max_index(gidx[:], gsm[:, 8:16], gsm[:, 0:8]))
                return e.tensor_scalar(out=gsm[:, 24:28], in0=gsm[:, 0:4],
                                       scalar1=-1.0, scalar2=gsm[:, 8:9],
                                       op0=ALU.mult, op1=ALU.add)
            s.op("dve", gate1, reads=("gsm",), writes=("gsm",), cost=1.1,
                 tag="gate1")
            s.op("act", lambda e: e.activation(out=gsm[:, 24:28],
                                               in_=gsm[:, 24:28],
                                               func=AF.Tanh, scale=c_HALF),
                 reads=("gsm", "consts"), writes=("gsm",), cost=0.45,
                 tag="gate_tanh")

            def gate2(e):
                stepd(e, e.tensor_scalar(out=gsm[:, 0:4], in0=gsm[:, 24:28],
                                         scalar1=-1.0, scalar2=1.0,
                                         op0=ALU.mult, op1=ALU.add))
                stepd(e, e.tensor_scalar(out=gsm[:, 24:28], in0=gsm[:, 24:28],
                                         scalar1=1.0, scalar2=None, op0=ALU.add))
                stepd(e, e.reciprocal(out=gsm[:, 24:28], in_=gsm[:, 24:28]))
                stepd(e, e.tensor_tensor(out=gsm[:, 0:4], in0=gsm[:, 0:4],
                                         in1=gsm[:, 24:28], op=ALU.mult))
                stepd(e, e.tensor_reduce(gsm[:, 28:29], gsm[:, 0:4], AX.X,
                                         ALU.add))
                stepd(e, e.reciprocal(out=gsm[:, 28:29], in_=gsm[:, 28:29]))
                stepd(e, e.tensor_copy(out=gsm[:, 29:30], in_=gidx[:, 0:1]))
                stepd(e, e.tensor_scalar(out=gsm[:, 24:28], in0=iota4[:],
                                         scalar1=gsm[:, 29:30], scalar2=None,
                                         op0=ALU.is_equal))
                return e.tensor_scalar(out=gsm[:, 24:28], in0=gsm[:, 24:28],
                                       scalar1=gsm[:, 28:29], scalar2=None,
                                       op0=ALU.mult)
            s.op("dve", gate2, reads=("gsm", "iota4"), writes=("gsm",),
                 cost=2.9, tag="gate2")

            def hgate(e):
                inst = None
                for i, (off, r) in enumerate(zip(ROFF, RANKS)):
                    inst = e.tensor_scalar(out=hpb[:, off:off + r],
                                           in0=hsml[:, off:off + r],
                                           scalar1=gsm[:, 24 + i:25 + i],
                                           scalar2=None, op0=ALU.mult)
                return inst
            s.op("dve", hgate, reads=("hsml", "gsm"), writes=("hpb",), cost=0.8, tag="hgate")

        def f_ffn2(ti):
            t = tctx(ti)
            stp3, stp4 = t["stp"][2], t["stp"][3]
            k3, k4 = t["ks"][2], t["ks"][3]
            nrp, knr = t["nrp"], t["knr"]
            s.op("dve", lambda e, stp3=stp3, stp4=stp4: e.tensor_scalar(
                out=X2[:], in0=B[:], scalar1=stp3[:], scalar2=stp4[:],
                op0=ALU.mult, op1=ALU.add),
                reads=("scrB", k3, k4), writes=("X2",), cost=0.7, tag="x2f")
            jobs = [(PS[0][1][0:RTOT, 128:256], hpb[:], I128[:], True, True)]
            mmseq(jobs, reads=("hpb", "I128"), writes=(("ps", 0),), tag="hT")
            s.op("dve", lambda e: e.tensor_copy(out=hpT[:],
                                                in_=PS[0][1][0:RTOT, 128:256]),
                 reads=(("ps", 0),), writes=("hpT",), cost=0.3, tag="hpT_ev")
            jobs = [(PS[2][nh][:, 0:512], hpT[:],
                     WuTb[:, nh * 512:(nh + 1) * 512], False, True)
                    for nh in range(2)]
            mmseq(jobs, reads=("hpT",) + wuk, writes=(("ps", 2),))

            s.op("dve", lambda e, stp1=t["stp"][0], stp3=stp3:
                 e.scalar_tensor_tensor(
                     out=B[:], in0=P[2][:, :], scalar=stp3[:], in1=X2[:],
                     op0=ALU.mult, op1=ALU.add, accum_out=stp1[:]),
                reads=(("ps", 2), "X2", k3), writes=("scrB", t["ks"][0]),
                cost=1.3, tag="pre3")
            s.op("act", lambda e, stp4=stp4: e.activation(
                out=junkb[:], in_=B[:], func=AF.Square, accum_out=stp4[:]),
                reads=("scrB",), writes=("junkb", k4), cost=1.25, tag="sq3")
            ln_stats(t, EPS, sum_i=0)

        def f_out(ti):
            t = tctx(ti)
            stp3, stp4 = t["stp"][2], t["stp"][3]
            k3, k4 = t["ks"][2], t["ks"][3]
            s.op("dve", lambda e, stp3=stp3, stp4=stp4: e.tensor_scalar(
                out=OUTB[:], in0=B[:], scalar1=stp3[:], scalar2=stp4[:],
                op0=ALU.mult, op1=ALU.add),
                reads=("scrB", k3, k4), writes=("OUTB",), cost=0.7, tag="outap")
            s.op("out", lambda e, r0=ti * 128: e.dma_start(
                out=hy[r0:r0 + 128, :], in_=OUTB[:]),
                reads=("OUTB",), writes=())

        f_xkey(0)
        f_val_kp(0)
        f_qbm(0)
        if NT > 1:
            tn = tctx(1)
            dma_in(tn["xin"][:], hx[128:256, :], writes=(tn["kx"],))
            f_xkey(1)
        for i in range(NT):
            f_qp(i)
            f_retr(i)
            if i > 0:
                f_out(i - 1)
            if i + 1 < NT:
                f_val_kp(i + 1)
            f_lnT(i)
            f_wo(i)
            if i + 1 < NT:
                f_qbm(i + 1)
            f_ln1x2(i)
            f_ffn1(i)
            if i + 2 < NT:
                tn = tctx(i + 2)
                dma_in(tn["xin"][:], hx[(i + 2) * 128:(i + 3) * 128, :],
                       writes=(tn["kx"],))
                f_xkey(i + 2)
            f_gate(i)
            f_ffn2(i)
        f_out(NT - 1)

        s.op("pool", lambda e: e.nop(),
             extra_waits=tuple((ch, s.cnt[ch]) for ch in DMA_OUT if s.cnt[ch]))

        qmap = {"pe": "tensor", "act": "scalar", "dve": "vector",
                "pool": "gpsimd", "sp": "sync"}
        with nc.Block() as block:
            s.emit(sems, block, qmap)

    return nc


_PROG = None


def kernel(**inputs):
    global _PROG
    x = np.ascontiguousarray(np.asarray(inputs["L0_attn_x"], np.float32))
    assert x.shape == (8, S, D)
    hp = host_prep(inputs)
    if _PROG is None:
        _PROG = build_core_program()
    in_maps = []
    for b in range(8):
        m = {"hx": x[b]}
        m.update(hp)
        in_maps.append(m)
    res = run_bass_kernel_spmd(_PROG, in_maps, core_ids=list(range(8)))
    out = np.stack([np.asarray(r["hy"]) for r in res.results], axis=0)
    return out.astype(np.float32)


# version 09
# speedup vs baseline: 1.4626x; 1.0032x over previous
import os
import numpy as np
import ml_dtypes
import concourse.bass as bass
import concourse.mybir as mybir
from concourse.bass_utils import run_bass_kernel_spmd

F32 = mybir.dt.float32
BF16 = mybir.dt.bfloat16
I32 = mybir.dt.int32
AF = mybir.ActivationFunctionType
ALU = mybir.AluOpType
AX = mybir.AxisListType

D = 1024
S = 4096
NCH = 8
NT = int(os.environ.get("KNT", "32"))
CHUNK = 64
EPS = 1e-5
EPS_IN = 1024 * 1e-5
CLAMP_MAX = float(np.log(1.0 / 0.01))
RANKS = [1, 2, 4, 8]
ROFF = [0, 1, 3, 7]
RTOT = 15
PROJ = 256

S_SCALE = float(np.nextafter(np.float32(2 * np.pi), np.float32(0)))
HALF_PI = float(np.float32(np.pi / 2))
INV_2PI = 1.0 / (2.0 * np.pi)
KOFF = 64.0

SC_C2 = float(np.float32(16.0) - np.float32(S_SCALE))
assert SC_C2 + S_SCALE == 16.0
_d = np.float32(-(2.0 * np.pi - S_SCALE))
while (16.0 - float(_d)) - SC_C2 >= 2.0 * np.pi:
    _d = np.nextafter(_d, np.float32(0))
SC_DELTA = float(_d)
assert float(np.float32(np.float32(SC_DELTA) + np.float32(-16.0))) == -16.0
B_C3 = float(np.float32(2.0) - np.float32(HALF_PI))
assert B_C3 + HALF_PI == 2.0
_db = np.float32(np.pi / 2 - HALF_PI)
while (2.0 + float(_db)) - B_C3 > np.pi / 2:
    _db = np.nextafter(_db, -np.inf)
B_DELTA = float(_db)
assert float(np.float32(np.float32(B_DELTA) + np.float32(2.0))) == 2.0
assert 0.0 <= np.pi / 2 - ((2.0 + B_DELTA) - B_C3) < 1e-13
assert 0.0 < 2.0 * np.pi - ((16.0 - SC_DELTA) - SC_C2) < 1e-12

ENGINES = ("pe", "act", "dve", "pool", "sp")
DMA_IN = tuple(f"in{i}" for i in range(4))
DMA_OUT = tuple(f"out{i}" for i in range(2))
DMA_CH = DMA_IN + DMA_OUT


class Sched:
    def __init__(self):
        self.ops = []
        self.cnt = {e: 0 for e in ENGINES + DMA_CH}
        self.writer = {}
        self.readers = {}
        self.need_inc = set()
        self.meta = []
        self.din = 0
        self.dout = 0

    def op(self, eng, emit, reads=(), writes=(), extra_waits=(), cost=None,
           tag=""):
        if eng == "in":
            eng = DMA_IN[self.din % len(DMA_IN)]
            self.din += 1
        elif eng == "out":
            eng = DMA_OUT[self.dout % len(DMA_OUT)]
            self.dout += 1
        step = 16 if eng in DMA_CH else 1
        waits = set(extra_waits)
        if eng in DMA_CH and self.cnt[eng] > 0:
            waits.add((eng, self.cnt[eng]))
        for k in reads:
            w = self.writer.get(k)
            if w is not None:
                waits.add(w)
        for k in writes:
            w = self.writer.get(k)
            if w is not None:
                waits.add(w)
            for r in self.readers.get(k, ()):
                waits.add(r)
        self.cnt[eng] += step
        me = (eng, self.cnt[eng])
        for k in reads:
            self.readers.setdefault(k, []).append(me)
        for k in writes:
            self.writer[k] = me
            self.readers[k] = []
        for w in waits:
            self.need_inc.add(w)
        if eng in DMA_CH:
            self.need_inc.add(me)
        if cost is None:
            cost = {"pe": 0.3, "act": 1.2, "dve": 1.2, "pool": 0.25,
                    "sp": 0.05}.get(eng, 2.5)
            if eng.startswith("in") or eng.startswith("out"):
                cost = 2.5
        self.meta.append((cost, tag))
        self.ops.append((eng, emit, sorted(waits), me))
        return me

    def simulate(self, gap=0.15, xgap=0.12):
        qfree = {}
        done = {}
        rows = []
        for idx, (eng, _, waits, me) in enumerate(self.ops):
            cost, tag = self.meta[idx]
            q = eng if eng in ENGINES else ("dma" + eng[-1] if eng[-1].isdigit()
                                            else eng)
            t0 = qfree.get(q, 0.0)
            src_max = 0.0
            for w in waits:
                src_max = max(src_max, done.get(w, 0.0) + xgap)
            t0 = max(t0, src_max)
            t1 = t0 + cost
            qfree[q] = t1 + gap
            done[me] = t1
            rows.append((t0, t1, eng, tag, me, waits))
        busy = {}
        for idx, (eng, _, _, _) in enumerate(self.ops):
            busy[eng] = busy.get(eng, 0.0) + self.meta[idx][0]
        return rows, busy

    def emit(self, sems, block, qmap):
        import bisect
        inc_list = {}
        for (e, c) in self.need_inc:
            inc_list.setdefault(e, []).append(c)
        for e in inc_list:
            inc_list[e].sort()

        def remap(src, val):
            lst = inc_list.get(src, ())
            r = bisect.bisect_right(lst, val)
            assert r > 0 and lst[r - 1] == val, (src, val)
            return r * (16 if src in DMA_CH else 1)

        per_q = {e: [] for e in ENGINES}
        for eng, emitfn, waits, me in self.ops:
            q = eng if eng in per_q else ("sp" if eng.startswith("in") else "pool")
            per_q[q].append((emitfn, waits, me if me in self.need_inc else None, eng))
        for q, items in per_q.items():
            if not items:
                continue

            def make_body(items):
                def body(e):
                    last = {}
                    for emitfn, waits, inc, semeng in items:
                        for (src, val) in waits:
                            v = remap(src, val)
                            if last.get(src, -1) >= v:
                                continue
                            e.wait_ge(sems[src], v)
                            last[src] = v
                        inst = emitfn(e)
                        if inc is not None:
                            inst.then_inc(sems[semeng],
                                          16 if semeng in DMA_CH else 1)
                return body
            getattr(block, qmap[q])(make_body(items))


def host_prep(inp):
    f = np.float32
    T = lambda a: np.ascontiguousarray(np.asarray(a, f).T)
    chunk2d = lambda v: np.ascontiguousarray(np.asarray(v, f).reshape(NCH, 128).T)
    bcast = lambda v, n: np.ascontiguousarray(
        np.broadcast_to(np.asarray(v, f).reshape(1, n), (128, n)))

    d = {}
    d["WkT"] = T(inp["L0_attn_Wk"])
    d["WvT"] = T(inp["L0_attn_Wv"])
    d["WqT"] = T(inp["L0_attn_Wq"])
    d["WkpT"] = T(inp["L0_attn_Wkp"])
    d["WqpT"] = T(inp["L0_attn_Wqp"])
    d["WoT"] = T(inp["L0_attn_Wo"])
    d["WfT"] = T(inp["L0_ffn_W"])
    d["WpT"] = T(inp["L0_ffn_Wp"])
    wd = np.concatenate([np.asarray(inp[f"L0_ffn_Wd{i}"], f) for i in range(4)], 0)
    d["WdT"] = T(wd)
    wu = np.concatenate([np.asarray(inp[f"L0_ffn_Wu{i}"], f) for i in range(4)], 1)
    d["WuT"] = T(wu)

    d["bk2d"] = chunk2d(inp["L0_attn_bk"])
    d["bq2d"] = chunk2d(inp["L0_attn_bq"])
    d["lnb2d"] = chunk2d(inp["L0_attn_ln_b"])
    d["g2d"] = chunk2d(inp["L0_attn_ln_g"])
    d["bv_row"] = np.asarray(inp["L0_attn_bv"], f).reshape(1, D).copy()
    d["ps_bc"] = bcast(inp["L0_attn_phase_scale"], D)
    d["bp_bc"] = bcast(inp["L0_ffn_bp"], PROJ)
    d["bp2d"] = np.ascontiguousarray(np.asarray(inp["L0_ffn_bp"], f).reshape(2, 128).T)
    d["bkp_row"] = np.asarray(inp["L0_attn_bkp"], f).reshape(1, D).copy()
    d["bqp_row"] = np.asarray(inp["L0_attn_bqp"], f).reshape(1, D).copy()
    d["bo_bc"] = bcast(inp["L0_attn_bo"], D)
    sim = np.asarray(inp["L0_ffn_sim"], f).reshape(2, 128, 4).transpose(1, 0, 2)
    d["sim2d"] = np.ascontiguousarray(sim.reshape(128, 8))
    d["temp"] = np.asarray(inp["L0_ffn_temp"], f).reshape(1, 1).copy()

    L = np.zeros((128, 128), f)
    for b in range(2):
        L[b * CHUNK:(b + 1) * CHUNK, b * CHUNK:(b + 1) * CHUNK] = \
            np.tril(np.ones((CHUNK, CHUNK), f))
    d["LTb"] = np.ascontiguousarray(L.T).astype(ml_dtypes.bfloat16)
    d["I128b"] = np.eye(128, dtype=ml_dtypes.bfloat16)
    d["ones_f"] = np.ones((1, 128), f)
    d["ones2b"] = np.ones((2, 128), dtype=ml_dtypes.bfloat16)
    d["ones_c"] = np.ones((128, 1), f)
    d["magic"] = np.full((128, 1), 0x5F3759DF, np.int32)
    consts = np.zeros((128, 8), f)
    consts[:, 0] = S_SCALE
    consts[:, 1] = HALF_PI
    consts[:, 2] = 0.5
    consts[:, 3] = -S_SCALE
    consts[:, 4] = SC_DELTA
    consts[:, 5] = B_DELTA
    d["consts"] = consts
    d["iota4"] = np.ascontiguousarray(
        np.broadcast_to(np.arange(4, dtype=f), (128, 4)))
    gpad = np.full((128, 32), 0.0, f)
    gpad[:, 4:8] = -1e30
    d["gpad"] = gpad
    return d


def build_core_program():
    nc = bass.Bass("TRN2", target_bir_lowering=False)

    hx = nc.dram_tensor("hx", [S, D], F32, kind="ExternalInput").ap()
    hy = nc.dram_tensor("hy", [S, D], F32, kind="ExternalOutput").ap()
    W_NAMES = ["WkT", "WvT", "WqT", "WkpT", "WqpT", "WoT", "WfT"]
    h = {}
    for n in W_NAMES:
        h[n] = nc.dram_tensor(n, [D, D], F32, kind="ExternalInput").ap()
    h["WpT"] = nc.dram_tensor("WpT", [D, PROJ], F32, kind="ExternalInput").ap()
    h["WdT"] = nc.dram_tensor("WdT", [D, RTOT], F32, kind="ExternalInput").ap()
    h["WuT"] = nc.dram_tensor("WuT", [RTOT, D], F32, kind="ExternalInput").ap()
    for n, shp in [("bk2d", [128, 8]), ("bq2d", [128, 8]), ("lnb2d", [128, 8]),
                   ("g2d", [128, 8]), ("bv_row", [1, D]), ("ps_bc", [128, D]),
                   ("bp_bc", [128, PROJ]), ("bp2d", [128, 2]), ("bkp_row", [1, D]),
                   ("bqp_row", [1, D]), ("bo_bc", [128, D]), ("sim2d", [128, 8]),
                   ("temp", [1, 1]), ("ones_f", [1, 128]),
                   ("ones_c", [128, 1]), ("consts", [128, 8]),
                   ("gpad", [128, 32])]:
        h[n] = nc.dram_tensor(n, shp, F32, kind="ExternalInput").ap()
    h["iota4"] = nc.dram_tensor("iota4", [128, 4], F32,
                                kind="ExternalInput").ap()
    h["LTb"] = nc.dram_tensor("LTb", [128, 128], BF16, kind="ExternalInput").ap()
    h["ones2b"] = nc.dram_tensor("ones2b", [2, 128], BF16, kind="ExternalInput").ap()
    h["I128b"] = nc.dram_tensor("I128b", [128, 128], BF16, kind="ExternalInput").ap()
    h["magic"] = nc.dram_tensor("magic", [128, 1], I32, kind="ExternalInput").ap()
    hlscr = nc.dram_tensor("hlscr", [1, 2 * D], BF16, kind="Internal").ap()

    from contextlib import ExitStack
    ctx = ExitStack()
    sb = lambda name, shape, dt: ctx.enter_context(nc.sbuf_tensor(name, shape, dt))
    with ctx:
        wb = {n: sb(n + "_b", [128, NCH * D], BF16) for n in W_NAMES}
        WpTb = sb("WpT_b", [128, NCH * PROJ], BF16)
        WdTb = sb("WdT_b", [128, NCH * RTOT], BF16)
        WuTb = sb("WuT_b", [RTOT, D], BF16)
        LTs = sb("LT_b", [128, 128], BF16)
        I128 = sb("I128_b", [128, 128], BF16)

        bk2d = sb("bk2d_s", [128, 8], F32)
        bq2d = sb("bq2d_s", [128, 8], F32)
        lnb2d = sb("lnb2d_s", [128, 8], F32)
        g2d = sb("g2d_s", [128, 8], F32)
        b2db = sb("b2d_bb", [128, 24], BF16)
        ps_bc = sb("ps_bc_s", [128, D], F32)
        bp_bc = sb("bp_bc_s", [128, PROJ], F32)
        bp2d = sb("bp2d_s", [128, 2], F32)
        bp2db = sb("bp2d_bb", [128, 2], BF16)
        bps4 = sb("bps4_s", [128, 4], F32)
        cbc = sb("cbc_s", [128, 1], F32)
        rowtmp_t = sb("rowtmp", [2, D], F32)
        rowbuf = sb("rowbuf", [1, D], F32)
        kp_hl = sb("kp_hl", [2, D], BF16)
        qp_hl = sb("qp_hl", [2, D], BF16)
        v_hl = sb("v_hl", [2, D], BF16)
        ones2 = sb("ones2_s", [2, 128], BF16)
        iota4 = sb("iota4_s", [128, 4], F32)
        gidx = sb("gidx_s", [128, 8], mybir.dt.uint32)
        bo_bc = sb("bo_bc_s", [128, D], F32)
        sim2d = sb("sim2d_s", [128, 8], F32)
        simnb = sb("simn_b", [128, 8], BF16)
        temp = sb("temp_s", [1, 1], F32)
        ones_f = sb("ones_f_s", [1, 128], F32)
        ones_c = sb("ones_c_s", [128, 1], F32)
        consts = sb("consts_s", [128, 8], F32)
        magic = sb("magic_s", [128, 1], I32)
        sc_bc = sb("sc_bc_s", [128, 1], F32)

        x_in = sb("x_in", [128, D], F32)
        A = sb("scrA", [128, D], F32)
        B = sb("scrB", [128, D], F32)
        C = sb("scrC", [128, D], F32)
        U = sb("scrU", [128, D], I32)
        SKb = sb("sinK", [128, D], BF16)
        CKb = sb("cosK", [128, D], BF16)
        VAL = sb("val", [128, D], F32)
        X2 = sb("x2", [128, D], F32)
        xbb = sb("xb_b", [128, D], BF16)
        bsh = sb("bshared", [128, D], BF16)
        tsh = sb("tshared", [128, D], BF16)
        kqT = sb("kqT", [128, D], BF16)
        junkb = sb("junk_b", [128, D], BF16)
        proj = sb("proj_s", [128, PROJ], F32)
        projb = sb("proj_bb", [128, PROJ], BF16)
        projT = sb("projT_b", [128, 2 * 128], BF16)
        hsml = sb("h_small", [128, RTOT], F32)
        hpb = sb("hp_b", [128, RTOT], BF16)
        hpT = sb("hpT_b", [RTOT, 128], BF16)
        gsm = sb("g_small", [128, 32], F32)
        lg = gsm[:, 0:4]
        STAT = [[sb(f"stat{p}_{i}", [128, 1], F32) for i in range(4)]
                for p in range(2)]
        NRS = [sb(f"nr_small{p}", [128, 4], F32) for p in range(2)]
        PSTAT = [sb(f"pstat{p}", [128, 6], F32) for p in range(2)]
        SEEDI = [sb(f"seed_i{p}", [128, 2], I32) for p in range(2)]
        seed4 = sb("seed4_i", [1, 8], I32)
        rowtmp = rowtmp_t[0:1, :]
        srow = sb("srow", [1, 16], F32)
        st1, st2, st3, st4 = STAT[0]
        nrs = NRS[0]
        seedi = SEEDI[0]

        xTb = sb("xT_b", [128, D], BF16)
        OUTB = sb("out_b", [128, D], F32)
        MIB = sb("mi_b", [128, D], F32)
        KQT = [kqT, sb("kqT2", [128, D], BF16)]
        XIN = [x_in, sb("x_in2", [128, D], F32)]

        P = [ctx.enter_context(nc.psum_tensor(f"P{i}", [128, 1024], F32))
             for i in range(4)]
        PS = [[P[i][:, 0:512], P[i][:, 512:1024]] for i in range(4)]

        sems = {}
        for e in ENGINES + DMA_CH:
            sems[e] = ctx.enter_context(nc.semaphore("sem_" + e))
        chain_sems = {e: ctx.enter_context(nc.semaphore("chain_" + e))
                      for e in ("dve", "act", "pool")}
        chain_cnt = {e: 0 for e in chain_sems}

        def mkstep(eng):
            def step(e, inst):
                chain_cnt[eng] += 1
                inst.then_inc(chain_sems[eng], 1)
                e.wait_ge(chain_sems[eng], chain_cnt[eng])
            return step
        stepd = mkstep("dve")
        stepp = mkstep("pool")

        s = Sched()
        wkeys = lambda n: tuple((n, c) for c in range(NCH))

        def dma_in(dst_ap, src_ap, writes, reads=()):
            return s.op("in", lambda e, d=dst_ap, sr=src_ap: e.dma_start(out=d, in_=sr),
                        reads=reads, writes=writes)

        def mmseq(jobs, reads, writes, stop_last=True, tag="mm"):
            jobs = list(jobs)
            cost = sum(max(j[2].shape[-1], 70) / 2400.0 + 0.015 for j in jobs)

            def emit(e, jobs=jobs):
                inst = None
                for (o, l, r, st, sp) in jobs:
                    inst = e.matmul(out=o, lhsT=l, rhs=r, start=st, stop=sp)
                return inst
            return s.op("pe", emit, reads=reads, writes=writes, cost=cost,
                        tag=tag)

        def proj_tok(slot, lhsT_key, lhsT_ap, wtile, wkey_t, stop_last=True):
            jobs = []
            for nh in range(2):
                for dc in range(NCH):
                    jobs.append((PS[slot][nh][:, 0:512],
                                 lhsT_ap[:, dc * 128:(dc + 1) * 128],
                                 wtile[:, dc * D + nh * 512:dc * D + (nh + 1) * 512],
                                 dc == 0, (dc == NCH - 1) and stop_last))
            return mmseq(jobs, reads=(lhsT_key,) + wkey_t,
                         writes=(("ps", slot),))

        def feat_proj(slot, wtile, wkey_t):
            jobs = []
            for ec in range(NCH):
                bank = PS[slot][ec // 4]
                for dc in range(NCH):
                    jobs.append((bank[:, (ec % 4) * 128:(ec % 4 + 1) * 128],
                                 wtile[:, dc * D + ec * 128:dc * D + (ec + 1) * 128],
                                 tsh[:, dc * 128:(dc + 1) * 128],
                                 dc == 0, dc == NCH - 1))
            return mmseq(jobs, reads=("tsh",) + wkey_t, writes=(("ps", slot),))

        def transpose8(slot, src_key, src_ap):
            jobs = []
            for c in range(NCH):
                bank = PS[slot][c // 4]
                jobs.append((bank[:, (c % 4) * 128:(c % 4 + 1) * 128],
                             src_ap[:, c * 128:(c + 1) * 128], I128[:], True, True))
            return mmseq(jobs, reads=(src_key, "I128"), writes=(("ps", slot),))

        def evac_pair(slot, dst_ap, dst_key, eng="act"):
            if eng == "act":
                def emit(e, sl=slot, d=dst_ap):
                    return e.activation(out=d, in_=P[sl][:, :], func=AF.Identity)
            else:
                def emit(e, sl=slot, d=dst_ap):
                    return e.tensor_copy(out=d, in_=P[sl][:, :])
            return s.op(eng, emit, reads=(("ps", slot),), writes=(dst_key,),
                        cost=1.2, tag=f"evac{slot}")

        def evac_pair_add(slot, bcast_tile, bkey, dst, dst_key):
            def emit(e, sl=slot, bc=bcast_tile, d=dst):
                return e.tensor_tensor(out=d[:, 0:1024], in0=P[sl][:, :],
                                       in1=bc[:, 0:1024], op=ALU.add)
            return s.op("dve", emit, reads=(("ps", slot), bkey),
                        writes=(dst_key,), cost=1.25, tag=f"evadd{slot}")

        def rsqrt_nr(src, src_key, dst, dst_key, sdi, sdi_key, nr, nr_key,
                     clamp=None, iters=4, src_ap=None, dst_ap=None, nr_ap=None):
            if src_ap is None:
                src_ap = src[:]
            if dst_ap is None:
                dst_ap = dst[:]
            if nr_ap is None:
                nr_ap = nr[:, 0:1]

            def emit_seed(e, src_ap=src_ap, dst_ap=dst_ap, sdi=sdi):
                sbits = src_ap.bitcast(I32)
                stepd(e, e.tensor_scalar(out=sdi, in0=sbits, scalar1=1,
                                         scalar2=None,
                                         op0=ALU.arith_shift_right))
                stepd(e, e.tensor_tensor(out=sdi, in0=magic[:],
                                         in1=sdi, op=ALU.subtract))
                return e.tensor_copy(out=dst_ap, in_=sdi.bitcast(F32))
            s.op("dve", emit_seed, reads=(src_key, "magic"),
                 writes=(dst_key, sdi_key), cost=1.0, tag="nr_seed")

            def emit_nr(e, src_ap=src_ap, dst_ap=dst_ap, nr_ap=nr_ap):
                inst = None
                for it in range(iters):
                    stepp(e, e.tensor_tensor(out=nr_ap, in0=dst_ap,
                                             in1=dst_ap, op=ALU.mult))
                    stepp(e, e.tensor_tensor(out=nr_ap, in0=nr_ap,
                                             in1=src_ap, op=ALU.mult))
                    stepp(e, e.tensor_scalar(out=nr_ap, in0=nr_ap,
                                             scalar1=-0.5, scalar2=1.5,
                                             op0=ALU.mult, op1=ALU.add))
                    inst = e.tensor_tensor(out=dst_ap, in0=dst_ap,
                                           in1=nr_ap, op=ALU.mult)
                    if it < iters - 1 or clamp is not None:
                        stepp(e, inst)
                if clamp is not None:
                    inst = e.tensor_scalar(out=dst_ap, in0=dst_ap, scalar1=clamp,
                                           scalar2=None, op0=ALU.min)
                return inst
            return s.op("pool", emit_nr, reads=(src_key, dst_key),
                        writes=(dst_key, nr_key), cost=0.36 * (4 * iters + 1),
                        tag="nr_loop")

        def _ap(t):
            return t if isinstance(t, bass.AP) else t[:]
        for name, tile in [("bk2d", bk2d), ("bq2d", bq2d), ("lnb2d", lnb2d),
                           ("g2d", g2d), ("ps_bc", ps_bc),
                           ("bp_bc", bp_bc), ("bp2d", bp2d), ("bo_bc", bo_bc),
                           ("iota4", iota4),
                           ("sim2d", sim2d), ("temp", temp), ("ones_f", ones_f),
                           ("ones_c", ones_c), ("consts", consts)]:
            dma_in(_ap(tile), h[name], writes=(name,))
        s.op("dve", lambda e: e.tensor_scalar(
            out=consts[:, 3:4], in0=consts[:, 4:5], scalar1=-16.0,
            scalar2=-SC_C2, op0=ALU.add, op1=ALU.subtract),
            reads=("consts",), writes=("consts",), cost=0.2, tag="fixc")
        s.op("dve", lambda e: e.tensor_scalar(
            out=consts[:, 1:2], in0=consts[:, 5:6], scalar1=2.0,
            scalar2=B_C3, op0=ALU.add, op1=ALU.subtract),
            reads=("consts",), writes=("consts",), cost=0.2, tag="fixb")
        dma_in(LTs[:], h["LTb"], writes=("LT",))
        dma_in(ones2[:], h["ones2b"], writes=("ones2",))

        dma_in(I128[:], h["I128b"], writes=("I128",))
        dma_in(magic[:], h["magic"], writes=("magic",))
        dma_in(x_in[:], hx[0:128, :], writes=("x_in",))

        stage = [(A, "scrA"), (B, "scrB")]
        cast_engines = ["dve", "act", "pool"]
        ei = 0
        for wi, wname in enumerate(W_NAMES + ["WpT", "WdT"]):
            src = h[wname]
            ncols = {"WpT": PROJ, "WdT": RTOT}.get(wname, D)
            dstt = {"WpT": WpTb, "WdT": WdTb}.get(wname, wb.get(wname))
            for c in range(NCH):
                stg, skey = stage[(wi * NCH + c) % 2]
                dma_in(stg[:, 0:ncols], src[c * 128:(c + 1) * 128, :],
                       writes=(skey,))
                eng = cast_engines[ei % 3]
                ei += 1
                dst_ap = dstt[:, c * ncols:(c + 1) * ncols]
                if wname == "WoT":
                    def emit(e, stg=stg, dst_ap=dst_ap, gc=g2d[:, c:c + 1], n=ncols):
                        return e.tensor_scalar(out=dst_ap, in0=stg[:, 0:n],
                                               scalar1=gc, scalar2=None,
                                               op0=ALU.mult)
                    s.op("dve", emit, reads=(skey, "g2d"), writes=((wname, c),))
                elif eng == "act":
                    def emit(e, stg=stg, dst_ap=dst_ap, n=ncols):
                        return e.activation(out=dst_ap, in_=stg[:, 0:n],
                                            func=AF.Identity)
                    s.op("act", emit, reads=(skey,), writes=((wname, c),))
                else:
                    def emit(e, stg=stg, dst_ap=dst_ap, n=ncols):
                        return e.tensor_copy(out=dst_ap, in_=stg[:, 0:n])
                    s.op(eng, emit, reads=(skey,), writes=((wname, c),))
        dma_in(C[0:RTOT, 0:D], h["WuT"], writes=("scrC",))
        s.op("dve", lambda e: e.tensor_copy(out=WuTb[:], in_=C[0:RTOT, 0:D]),
             reads=("scrC",), writes=("WuT", 0))
        wuk = (("WuT", 0),)

        def small_cast(e):
            e.tensor_copy(out=b2db[:, 0:8], in_=bk2d[:])
            e.tensor_copy(out=b2db[:, 8:16], in_=bq2d[:])
            return e.tensor_copy(out=b2db[:, 16:24], in_=lnb2d[:])
        s.op("pool", small_cast, reads=("bk2d", "bq2d", "lnb2d"), writes=("b2db",))

        def hilo_from_row(dst, dkey):
            h0 = KQT[0][0:1, 0:D]
            h1 = KQT[1][0:1, 0:D]

            def emit(e):
                stepd(e, e.tensor_copy(out=h0, in_=rowtmp))
                stepd(e, e.tensor_copy(out=rowbuf[:], in_=h0))
                stepd(e, e.tensor_tensor(out=rowbuf[:], in0=rowtmp,
                                         in1=rowbuf[:], op=ALU.subtract))
                return e.tensor_copy(out=h1, in_=rowbuf[:])
            s.op("dve", emit, reads=("rowtmp",),
                 writes=("kqT_0", "kqT_1", "rowbuf"), cost=0.5, tag="hilo")
            s.op("out", lambda e: e.dma_start(out=hlscr[0:1, 0:D], in_=h0),
                 reads=("kqT_0",), writes=("hlscr",))
            s.op("out", lambda e: e.dma_start(out=hlscr[0:1, D:2 * D], in_=h1),
                 reads=("kqT_1",), writes=("hlscr",))
            s.op("in", lambda e, dst=dst: e.dma_start(
                out=dst[:], in_=hlscr.rearrange("a (b c) -> (a b) c", b=2)),
                reads=("hlscr",), writes=(dkey,))

        def row_fold(boff, wname, wkey_t, add_ap, addkey, dst, dkey, slot_row):
            jobs = []
            for c in range(NCH):
                for nh in range(2):
                    jobs.append((PS[slot_row][nh][0:1, 0:512],
                                 b2db[:, boff + c:boff + c + 1],
                                 wb[wname][:, c * D + nh * 512:c * D + (nh + 1) * 512],
                                 c == 0, c == NCH - 1))
            mmseq(jobs, reads=("b2db",) + wkey_t, writes=(("ps", slot_row),))

            def ev(e, sl=slot_row, ar=add_ap):
                stepd(e, e.tensor_tensor(out=rowtmp[0:1, 0:512],
                                         in0=PS[sl][0][0:1, :],
                                         in1=ar[0:1, 0:512], op=ALU.add))
                return e.tensor_tensor(out=rowtmp[0:1, 512:1024],
                                       in0=PS[sl][1][0:1, :],
                                       in1=ar[0:1, 512:1024], op=ALU.add)
            s.op("dve", ev, reads=(("ps", slot_row), addkey), writes=("rowtmp",))
            hilo_from_row(dst, dkey)

        dma_in(rowbuf[:], h["bkp_row"], writes=("rowbuf",))
        row_fold(0, "WkpT", wkeys("WkpT"), rowbuf[:], "rowbuf",
                 kp_hl, "kp_hl", 0)
        dma_in(rowbuf[:], h["bqp_row"], writes=("rowbuf",))
        row_fold(8, "WqpT", wkeys("WqpT"), rowbuf[:], "rowbuf",
                 qp_hl, "qp_hl", 2)
        dma_in(rowtmp, h["bv_row"], writes=("rowtmp",))
        hilo_from_row(v_hl, "v_hl")

        for c in range(NCH):
            stg, skey = stage[c % 2]
            dma_in(stg[:, 0:D], h["WoT"][c * 128:(c + 1) * 128, :], writes=(skey,))

            def cem(e, stg=stg):
                return e.tensor_copy(out=bsh[:, 0:D], in_=stg[:, 0:D])
            s.op("dve", cem, reads=(skey,), writes=("bsh",))
            jb = []
            for nh in range(2):
                jb.append((PS[0][nh][0:1, 0:512], b2db[:, 16 + c:17 + c],
                           bsh[:, nh * 512:(nh + 1) * 512],
                           c == 0, c == NCH - 1))
            mmseq(jb, reads=("bsh", "b2db"), writes=(("ps", 0),))

        def bo_ev(e):
            e.tensor_copy(out=rowtmp[0:1, 0:512], in_=PS[0][0][0:1, :])
            return e.tensor_copy(out=rowtmp[0:1, 512:1024], in_=PS[0][1][0:1, :])
        s.op("dve", bo_ev, reads=(("ps", 0),), writes=("rowtmp",))
        jobs2 = [(PS[1][nh][:, 0:512], ones_f[:], rowtmp[0:1, nh * 512:(nh + 1) * 512],
                  True, True) for nh in range(2)]
        mmseq(jobs2, reads=("rowtmp", "ones_f"), writes=(("ps", 1),))

        def bo_bc_ev(e):
            e.tensor_tensor(out=bo_bc[:, 0:512], in0=PS[1][0][:, :],
                            in1=bo_bc[:, 0:512], op=ALU.add)
            return e.tensor_tensor(out=bo_bc[:, 512:1024], in0=PS[1][1][:, :],
                                   in1=bo_bc[:, 512:1024], op=ALU.add)
        s.op("dve", bo_bc_ev, reads=(("ps", 1), "bo_bc"), writes=("bo_bc",))

        s.op("act", lambda e: e.activation(out=gsm[:, 0:8], in_=sim2d[:],
                                           func=AF.Square),
             reads=("sim2d",), writes=("gsm",))
        jb = [(PS[2][0][0:1, 0:8], ones_c[:], gsm[:, 0:8], True, True)]
        mmseq(jb, reads=("gsm", "ones_c"), writes=(("ps", 2),))

        def sim_norm(e):
            stepd(e, e.tensor_copy(out=srow[0:1, 8:16], in_=PS[2][0][0:1, 0:8]))
            stepd(e, e.tensor_tensor(out=srow[0:1, 0:4], in0=srow[0:1, 8:12],
                                     in1=srow[0:1, 12:16], op=ALU.add))
            sbits = srow[0:1, 0:4].bitcast(I32)
            stepd(e, e.tensor_scalar(out=seed4[0:1, 0:4], in0=sbits, scalar1=1,
                                     scalar2=None, op0=ALU.arith_shift_right))
            stepd(e, e.tensor_scalar(out=seed4[0:1, 4:8], in0=seed4[0:1, 0:4],
                                     scalar1=-1, scalar2=0x5F3759DF,
                                     op0=ALU.mult, op1=ALU.add))
            stepd(e, e.tensor_copy(out=srow[0:1, 4:8],
                                   in_=seed4[0:1, 4:8].bitcast(F32)))
            for _ in range(4):
                stepd(e, e.tensor_tensor(out=srow[0:1, 8:12], in0=srow[0:1, 4:8],
                                         in1=srow[0:1, 4:8], op=ALU.mult))
                stepd(e, e.tensor_tensor(out=srow[0:1, 8:12], in0=srow[0:1, 8:12],
                                         in1=srow[0:1, 0:4], op=ALU.mult))
                stepd(e, e.tensor_scalar(out=srow[0:1, 8:12], in0=srow[0:1, 8:12],
                                         scalar1=-0.5, scalar2=1.5, op0=ALU.mult,
                                         op1=ALU.add))
                stepd(e, e.tensor_tensor(out=srow[0:1, 4:8], in0=srow[0:1, 4:8],
                                         in1=srow[0:1, 8:12], op=ALU.mult))
            return e.tensor_scalar(out=srow[0:1, 4:8], in0=srow[0:1, 4:8],
                                   scalar1=1e12, scalar2=None, op0=ALU.min)
        s.op("dve", sim_norm, reads=(("ps", 2),), writes=("srow", "seed4"))
        jb = [(PS[2][0][:, 0:4], ones_f[:], srow[0:1, 4:8], True, True)]
        mmseq(jb, reads=("srow", "ones_f"), writes=(("ps", 2),))

        def simn_apply(e):
            stepd(e, e.tensor_copy(out=gsm[:, 8:12], in_=PS[2][0][:, 0:4]))
            stepd(e, e.tensor_tensor(out=gsm[:, 0:4], in0=sim2d[:, 0:4],
                                     in1=gsm[:, 8:12], op=ALU.mult))
            stepd(e, e.tensor_tensor(out=gsm[:, 4:8], in0=sim2d[:, 4:8],
                                     in1=gsm[:, 8:12], op=ALU.mult))
            return e.tensor_copy(out=simnb[:], in_=gsm[:, 0:8])
        s.op("dve", simn_apply, reads=(("ps", 2), "sim2d"),
             writes=("simnb", "gsm"))
        dma_in(gsm[:], h["gpad"], writes=("gsm",))
        s.op("pool", lambda e: e.tensor_copy(out=bp2db[:], in_=bp2d[:]),
             reads=("bp2d",), writes=("bp2db",))
        jb = [(PS[3][0][0:1, 0:4], bp2db[:, c:c + 1], simnb[:, c * 4:(c + 1) * 4],
               c == 0, c == 1) for c in range(2)]
        mmseq(jb, reads=("bp2db", "simnb"), writes=(("ps", 3),))
        s.op("dve", lambda e: e.tensor_copy(out=srow[0:1, 0:4],
                                            in_=PS[3][0][0:1, 0:4]),
             reads=(("ps", 3),), writes=("srow",))
        jb = [(PS[3][0][:, 8:12], ones_f[:], srow[0:1, 0:4], True, True)]
        mmseq(jb, reads=("srow", "ones_f"), writes=(("ps", 3),))
        s.op("dve", lambda e: e.tensor_copy(out=bps4[:], in_=PS[3][0][:, 8:12]),
             reads=(("ps", 3),), writes=("bps4",))
        s.op("dve", lambda e: e.scalar_tensor_tensor(
            out=junkb[:, 0:PROJ], in0=bp_bc[:], scalar=1.0, in1=bp_bc[:],
            op0=ALU.mult, op1=ALU.mult, accum_out=cbc[:]),
            reads=("bp_bc",), writes=("junkb", "cbc"))

        s.op("dve", lambda e: e.tensor_scalar(
            out=srow[0:1, 12:13], in0=temp[:], scalar1=CLAMP_MAX, scalar2=0.5,
            op0=ALU.min, op1=ALU.mult),
            reads=("temp",), writes=("srow",))
        s.op("act", lambda e: e.activation(out=srow[0:1, 13:14],
                                           in_=srow[0:1, 12:13], func=AF.Tanh),
             reads=("srow",), writes=("srow",))

        def sc_fin(e):
            stepd(e, e.tensor_scalar(out=srow[0:1, 14:15], in0=srow[0:1, 13:14],
                                     scalar1=-1.0, scalar2=1.0, op0=ALU.mult,
                                     op1=ALU.add))
            stepd(e, e.reciprocal(out=srow[0:1, 14:15], in_=srow[0:1, 14:15]))
            stepd(e, e.tensor_scalar(out=srow[0:1, 13:14], in0=srow[0:1, 13:14],
                                     scalar1=1.0, scalar2=None, op0=ALU.add))
            return e.tensor_tensor(out=srow[0:1, 15:16], in0=srow[0:1, 13:14],
                                   in1=srow[0:1, 14:15], op=ALU.mult)
        s.op("dve", sc_fin, reads=("srow",), writes=("srow",))
        jb = [(PS[2][0][:, 4:5], ones_f[:], srow[0:1, 15:16], True, True)]
        mmseq(jb, reads=("srow", "ones_f"), writes=(("ps", 2),))
        s.op("dve", lambda e: e.tensor_copy(out=sc_bc[:], in_=PS[2][0][:, 4:5]),
             reads=(("ps", 2),), writes=("sc_bc",))

        c_S = consts[:, 0:1]
        c_HPI = consts[:, 1:2]
        c_HALF = consts[:, 2:3]
        c_NS = consts[:, 3:4]


        def tctx(ti):
            pp = ti % 2
            stp = STAT[pp]
            ks = (f"st1_{pp}", f"st2_{pp}", f"st3_{pp}", f"st4_{pp}")
            return dict(pp=pp, stp=stp, ks=ks, nrp=NRS[pp], knr=f"nrs_{pp}",
                        sdp=SEEDI[pp][:, 0:1], ksd=f"seedi_{pp}",
                        sdp2=SEEDI[pp][:, 1:2], ksd2=f"seedi2_{pp}",
                        pst=PSTAT[pp], kps=f"pstat_{pp}",
                        xin=XIN[pp], kx=f"x_in_{pp}",
                        kqt=KQT[pp], kk=f"kqT_{pp}")

        def ln_negmu(t):
            stp1, stp3 = t["stp"][0], t["stp"][2]
            k1, k3 = t["ks"][0], t["ks"][2]
            s.op("dve", lambda e, stp1=stp1, stp3=stp3: e.tensor_scalar(
                out=stp1[:], in0=stp3[:], scalar1=-1.0 / D, scalar2=None,
                op0=ALU.mult),
                reads=(k3,), writes=(k1,))

        def ln_rstd(t, eps):
            stp1, stp2, stp3, stp4 = t["stp"]
            k1, k2, k3, k4 = t["ks"]
            nrp, knr = t["nrp"], t["knr"]

            def emit(e):
                stepp(e, e.tensor_scalar(out=nrp[:, 1:2], in0=stp4[:],
                                         scalar1=1.0 / D, scalar2=eps,
                                         op0=ALU.mult, op1=ALU.add))
                stepp(e, e.tensor_tensor(out=nrp[:, 2:3], in0=stp1[:],
                                         in1=stp1[:], op=ALU.mult))
                return e.tensor_tensor(out=stp2[:], in0=nrp[:, 1:2],
                                       in1=nrp[:, 2:3], op=ALU.subtract)
            s.op("pool", emit, reads=(k1, k4), writes=(k2, knr), cost=1.1,
                 tag="rstd_w")
            rsqrt_nr(stp2, k2, stp3, k3, t["sdp"], t["ksd"], nrp, knr, iters=3)

        def ln_stats(t, eps, sum_i=2):
            stp1, stp2, stp3, stp4 = t["stp"]
            k1, k2, k3, k4 = t["ks"]
            sums, ksum = t["stp"][sum_i], t["ks"][sum_i]
            nrp, knr = t["nrp"], t["knr"]

            def emit(e, sums=sums):
                stepp(e, e.tensor_scalar(out=stp1[:], in0=sums[:],
                                         scalar1=1.0 / D, scalar2=None,
                                         op0=ALU.mult))
                e.tensor_scalar(out=nrp[:, 1:2], in0=stp4[:], scalar1=1.0 / D,
                                scalar2=eps, op0=ALU.mult, op1=ALU.add)
                stepp(e, e.tensor_tensor(out=nrp[:, 2:3], in0=stp1[:],
                                         in1=stp1[:], op=ALU.mult))
                return e.tensor_tensor(out=stp2[:], in0=nrp[:, 1:2],
                                       in1=nrp[:, 2:3], op=ALU.subtract)
            s.op("pool", emit, reads=(ksum, k4), writes=(k1, k2, knr),
                 cost=1.45, tag="lnst_w")
            rsqrt_nr(stp2, k2, stp3, k3, t["sdp"], t["ksd"], nrp, knr, iters=3)

            def emit_nb(e):
                stepp(e, e.tensor_tensor(out=stp4[:], in0=stp1[:], in1=stp3[:],
                                         op=ALU.mult))
                return e.tensor_scalar(out=stp4[:], in0=stp4[:], scalar1=-1.0,
                                       scalar2=None, op0=ALU.mult)
            s.op("pool", emit_nb, reads=(k1, k3), writes=(k4,), cost=0.72,
                 tag="ln_nb")

        def wrap_and_trig(src_key, src_ap):
            s.op("act", lambda e, sa=src_ap: e.activation(
                out=A[:], in_=sa, func=AF.Tanh),
                reads=(src_key,), writes=("scrA",), cost=1.25, tag="wr_tanh")
            s.op("dve", lambda e: e.scalar_tensor_tensor(
                out=A[:], in0=A[:], scalar=INV_2PI, in1=ps_bc[:],
                op0=ALU.mult, op1=ALU.mult),
                reads=("scrA", "ps_bc"), writes=("scrA",), cost=1.28, tag="wr_t")
            s.op("dve", lambda e: e.tensor_scalar(
                out=U[:], in0=A[:], scalar1=KOFF, scalar2=None, op0=ALU.add),
                reads=("scrA",), writes=("scrU",), cost=0.7, tag="wr_u")
            s.op("dve", lambda e: e.scalar_tensor_tensor(
                out=C[:], in0=U[:], scalar=-KOFF, in1=A[:],
                op0=ALU.add, op1=ALU.subtract),
                reads=("scrU", "scrA"), writes=("scrC",), cost=1.28, tag="wr_nv")
            s.op("dve", lambda e: e.scalar_tensor_tensor(
                out=C[:], in0=C[:], scalar=-0.5, in1=C[:], op0=ALU.is_lt,
                op1=ALU.add),
                reads=("scrC",), writes=("scrC",), cost=1.28, tag="wr_w")
            s.op("act", lambda e: e.activation(
                out=SKb[:], in_=C[:], func=AF.Sin, scale=c_NS),
                reads=("scrC", "consts"), writes=("SK",), cost=1.25, tag="wr_sin")
            s.op("dve", lambda e: e.scalar_tensor_tensor(
                out=A[:], in0=C[:], scalar=-0.25, in1=C[:], op0=ALU.is_le,
                op1=ALU.add),
                reads=("scrC",), writes=("scrA",), cost=1.28, tag="wr_vc")
            s.op("act", lambda e: e.activation(
                out=CKb[:], in_=A[:], func=AF.Sin, scale=c_NS, bias=c_HPI),
                reads=("scrA", "consts"), writes=("CK",), cost=1.25, tag="wr_cos")

        def f_xkey(ti):
            t = tctx(ti)
            s.op("act", lambda e, x=t["xin"]: e.activation(
                out=xbb[:], in_=x[:], func=AF.Identity),
                reads=(t["kx"],), writes=("xbb",), cost=1.2, tag="xb")
            transpose8(3, "xbb", xbb[:])
            evac_pair(3, xTb[:, 0:1024], "xTb")
            jobs = []
            for ec in range(NCH):
                for dc in range(NCH):
                    jobs.append((P[1][:, ec * 128:(ec + 1) * 128],
                                 wb["WkT"][:, dc * D + ec * 128:dc * D + (ec + 1) * 128],
                                 xTb[:, dc * 128:(dc + 1) * 128],
                                 dc == 0, dc == NCH - 1))
            mmseq(jobs, reads=("xTb",) + wkeys("WkT"), writes=(("ps", 1),))
            evac_pair(1, t["kqt"][:, 0:1024], t["kk"])

        def f_val_kp(ti):
            t = tctx(ti)
            kq = t["kqt"]
            jobs = []
            for nh in range(2):
                for dc in range(NCH):
                    jobs.append((PS[0][nh][:, 0:512],
                                 xTb[:, dc * 128:(dc + 1) * 128],
                                 wb["WvT"][:, dc * D + nh * 512:dc * D + (nh + 1) * 512],
                                 dc == 0, False))
                jobs.append((PS[0][nh][:, 0:512], ones2[:],
                             v_hl[0:2, nh * 512:(nh + 1) * 512], False, True))
            mmseq(jobs, reads=("xTb", "v_hl") + wkeys("WvT"),
                  writes=(("ps", 0),), tag="value")
            s.op("act", lambda e: e.activation(out=VAL[:], in_=P[0][:, :],
                                               func=AF.Identity),
                 reads=(("ps", 0),), writes=("VAL",), cost=1.2, tag="val_ev")
            jobs = []
            for nh in range(2):
                for ec in range(NCH):
                    jobs.append((PS[1][nh][:, 0:512],
                                 kq[:, ec * 128:(ec + 1) * 128],
                                 wb["WkpT"][:, ec * D + nh * 512:ec * D + (nh + 1) * 512],
                                 ec == 0, False))
                jobs.append((PS[1][nh][:, 0:512], ones2[:],
                             kp_hl[0:2, nh * 512:(nh + 1) * 512], False, True))
            mmseq(jobs, reads=(t["kk"], "kp_hl") + wkeys("WkpT"),
                  writes=(("ps", 1),), tag="kp")
            wrap_and_trig(("ps", 1), P[1][:, :])

        def f_qbm(ti):
            t = tctx(ti)
            jobs = []
            for ec in range(NCH):
                for dc in range(NCH):
                    jobs.append((P[1][:, ec * 128:(ec + 1) * 128],
                                 wb["WqT"][:, dc * D + ec * 128:dc * D + (ec + 1) * 128],
                                 xTb[:, dc * 128:(dc + 1) * 128],
                                 dc == 0, dc == NCH - 1))
            mmseq(jobs, reads=("xTb",) + wkeys("WqT"), writes=(("ps", 1),))
            s.op("dve", lambda e: e.tensor_tensor(out=xbb[:], in0=VAL[:],
                                                  in1=CKb[:], op=ALU.mult),
                 reads=("VAL", "CK"), writes=("xbb",))
            s.op("dve", lambda e: e.tensor_tensor(out=bsh[:], in0=VAL[:],
                                                  in1=SKb[:], op=ALU.mult),
                 reads=("VAL", "SK"), writes=("bsh",))
            evac_pair(1, t["kqt"][:, 0:1024], t["kk"])
            jobs = [(PS[0][nh][:, 0:512], LTs[:], xbb[:, nh * 512:(nh + 1) * 512],
                     True, True) for nh in range(2)]
            mmseq(jobs, reads=("xbb", "LT"), writes=(("ps", 0),))
            jobs = [(PS[1][nh][:, 0:512], LTs[:], bsh[:, nh * 512:(nh + 1) * 512],
                     True, True) for nh in range(2)]
            mmseq(jobs, reads=("bsh", "LT"), writes=(("ps", 1),))
            s.op("dve", lambda e: e.tensor_copy(out=VAL[:], in_=P[0][:, :]),
                 reads=(("ps", 0),), writes=("VAL",), cost=1.2, tag="mr_ev")
            s.op("dve", lambda e: e.tensor_copy(out=MIB[:], in_=P[1][:, :]),
                 reads=(("ps", 1),), writes=("MIB",), cost=1.2, tag="mi_ev")

        def f_qp(ti):
            t = tctx(ti)
            kq = t["kqt"]
            jobs = []
            for nh in range(2):
                for ec in range(NCH):
                    jobs.append((PS[2][nh][:, 0:512],
                                 kq[:, ec * 128:(ec + 1) * 128],
                                 wb["WqpT"][:, ec * D + nh * 512:ec * D + (nh + 1) * 512],
                                 ec == 0, False))
                jobs.append((PS[2][nh][:, 0:512], ones2[:],
                             qp_hl[0:2, nh * 512:(nh + 1) * 512], False, True))
            mmseq(jobs, reads=(t["kk"], "qp_hl") + wkeys("WqpT"),
                  writes=(("ps", 2),), tag="qp")
            wrap_and_trig(("ps", 2), P[2][:, :])

        def f_retr(ti):
            t = tctx(ti)
            stp3, stp4 = t["stp"][2], t["stp"][3]
            k3, k4 = t["ks"][2], t["ks"][3]
            s.op("dve", lambda e: e.tensor_tensor(
                out=VAL[:], in0=VAL[:], in1=CKb[:], op=ALU.mult),
                reads=("VAL", "CK"), writes=("VAL",))
            s.op("dve", lambda e: e.tensor_tensor(
                out=MIB[:], in0=MIB[:], in1=SKb[:], op=ALU.mult),
                reads=("MIB", "SK"), writes=("MIB",))
            s.op("dve", lambda e, stp3=stp3: e.scalar_tensor_tensor(
                out=MIB[:], in0=VAL[:], scalar=1.0, in1=MIB[:], op0=ALU.mult,
                op1=ALU.add, accum_out=stp3[:]),
                reads=("VAL", "MIB"), writes=("MIB", k3))
            ln_negmu(t)
            s.op("act", lambda e, stp4=stp4: e.activation(
                out=junkb[:], in_=MIB[:], func=AF.Square, accum_out=stp4[:]),
                reads=("MIB",), writes=("junkb", k4))
            ln_rstd(t, EPS_IN)

        def f_lnT(ti):
            t = tctx(ti)
            stp1 = t["stp"][0]
            k1 = t["ks"][0]
            s.op("dve", lambda e, stp1=stp1: e.tensor_scalar(
                out=bsh[:], in0=MIB[:], scalar1=stp1[:], scalar2=None,
                op0=ALU.add),
                reads=("MIB", k1), writes=("bsh",), cost=0.7, tag="lncast")
            transpose8(3, "bsh", bsh[:])
            evac_pair(3, tsh[:, 0:1024], "tsh")

        def f_wo(ti):
            t = tctx(ti)
            stp3 = t["stp"][2]
            k3 = t["ks"][2]
            proj_tok(2, "tsh", tsh[:], wb["WoT"], wkeys("WoT"))
            s.op("dve", lambda e, stp3=stp3: e.scalar_tensor_tensor(
                out=C[:], in0=P[2][:, :], scalar=stp3[:], in1=bo_bc[:, 0:1024],
                op0=ALU.mult, op1=ALU.add),
                reads=(("ps", 2), k3, "bo_bc"), writes=("scrC",))
            s.op("dve", lambda e, x=t["xin"], stp3=stp3: e.scalar_tensor_tensor(
                out=B[:], in0=x[:], scalar=2.0, in1=C[:], op0=ALU.mult,
                op1=ALU.add, accum_out=stp3[:]),
                reads=(t["kx"], "scrC"), writes=("scrB", k3))

        def f_ln1x2(ti):
            t = tctx(ti)
            stp1, stp2, stp3, stp4 = t["stp"]
            k1, k2, k3, k4 = t["ks"]
            s.op("dve", lambda e, stp1=stp1, stp3=stp3: e.tensor_scalar(
                out=stp1[:], in0=stp3[:], scalar1=-1.0 / D, scalar2=None,
                op0=ALU.mult),
                reads=(k3,), writes=(k1,), cost=0.2, tag="negmu1")
            s.op("act", lambda e, stp4=stp4: e.activation(
                out=junkb[:], in_=B[:], func=AF.Square, accum_out=stp4[:]),
                reads=("scrB",), writes=("junkb", k4), cost=1.25, tag="sq1")
            s.op("dve", lambda e, stp1=stp1: e.tensor_scalar(
                out=bsh[:], in0=B[:], scalar1=stp1[:], scalar2=None,
                op0=ALU.add),
                reads=("scrB", k1), writes=("bsh",), cost=0.7, tag="x2b")
            transpose8(3, "bsh", bsh[:])
            evac_pair(3, tsh[:, 0:1024], "tsh")
            ln_rstd(t, EPS)
            s.op("pool", lambda e, stp1=stp1, stp3=stp3, stp4=stp4:
                 e.tensor_tensor(out=stp4[:], in0=stp1[:], in1=stp3[:],
                                 op=ALU.mult),
                 reads=(k1, k3), writes=(k4,), cost=0.36, tag="nb1")

        def f_ffn1(ti):
            t = tctx(ti)
            stp3 = t["stp"][2]
            k3 = t["ks"][2]
            pst, kps = t["pst"], t["kps"]
            base_k = ("ps", 2)
            proj_tok(2, "tsh", tsh[:], wb["WfT"], wkeys("WfT"), stop_last=False)
            jobs = [(PS[3][0][:, 0:PROJ], tsh[:, dc * 128:(dc + 1) * 128],
                     WpTb[:, dc * PROJ:(dc + 1) * PROJ], dc == 0, dc == NCH - 1)
                    for dc in range(NCH)]
            mmseq(jobs, reads=("tsh",) + wkeys("WpT"), writes=(("ps", 3),),
                  tag="Wp")
            jobs = [(PS[3][1][:, 0:RTOT], tsh[:, dc * 128:(dc + 1) * 128],
                     WdTb[:, dc * RTOT:(dc + 1) * RTOT], dc == 0, dc == NCH - 1)
                    for dc in range(NCH)]
            mmseq(jobs, reads=("tsh",) + wkeys("WdT"), writes=(("ps", 3),),
                  tag="Wd")
            s.op("act", lambda e: e.activation(out=projb[:],
                                               in_=PS[3][0][:, 0:PROJ],
                                               func=AF.Identity),
                 reads=(("ps", 3),), writes=("projb",), cost=0.6, tag="projb")
            s.op("act", lambda e, pst=pst: e.activation(
                out=junkb[:, 0:PROJ], in_=PS[3][0][:, 0:PROJ], func=AF.Square,
                accum_out=pst[:, 0:1]),
                reads=(("ps", 3),), writes=("junkb", kps), cost=0.6, tag="sqp")
            s.op("dve", lambda e, pst=pst: e.scalar_tensor_tensor(
                out=C[:, 0:PROJ], in0=PS[3][0][:, 0:PROJ], scalar=1.0,
                in1=bp_bc[:], op0=ALU.mult, op1=ALU.mult,
                accum_out=pst[:, 1:2]),
                reads=(("ps", 3), "bp_bc"), writes=("scrC", kps), cost=0.4,
                tag="bxp")
            s.op("dve", lambda e: e.tensor_copy(out=hsml[:],
                                                in_=PS[3][1][:, 0:RTOT]),
                 reads=(("ps", 3),), writes=("hsml",), cost=0.25, tag="h_ev")
            jobs = [(PS[3][0][:, PROJ + c * 128:PROJ + (c + 1) * 128],
                     projb[:, c * 128:(c + 1) * 128], I128[:], True, True)
                    for c in range(2)]
            mmseq(jobs, reads=("projb", "I128"), writes=(("ps", 3),),
                  tag="projT")
            s.op("dve", lambda e: e.tensor_copy(out=projT[:],
                                                in_=PS[3][0][:, PROJ:PROJ + 256]),
                 reads=(("ps", 3),), writes=("projT",), cost=0.4, tag="projT_ev")
            jobs = [(PS[0][0][:, 0:4], projT[:, c * 128:(c + 1) * 128],
                     simnb[:, c * 4:(c + 1) * 4], c == 0, c == 1)
                    for c in range(2)]
            mmseq(jobs, reads=("projT", "simnb"), writes=(("ps", 0),),
                  tag="logits")
            def smalls(e, pst=pst, stp3=stp3):
                stepp(e, e.tensor_tensor(out=pst[:, 2:3], in0=stp3[:],
                                         in1=stp3[:], op=ALU.mult))
                stepp(e, e.tensor_tensor(out=pst[:, 2:3], in0=pst[:, 2:3],
                                         in1=pst[:, 0:1], op=ALU.mult))
                stepp(e, e.tensor_tensor(out=pst[:, 4:5], in0=stp3[:],
                                         in1=pst[:, 1:2], op=ALU.mult))
                stepp(e, e.tensor_scalar(out=pst[:, 4:5], in0=pst[:, 4:5],
                                         scalar1=2.0, scalar2=None,
                                         op0=ALU.mult))
                stepp(e, e.tensor_tensor(out=pst[:, 4:5], in0=pst[:, 4:5],
                                         in1=cbc[:], op=ALU.add))
                return e.tensor_tensor(out=pst[:, 2:3], in0=pst[:, 2:3],
                                       in1=pst[:, 4:5], op=ALU.add)
            s.op("pool", smalls, reads=(kps, k3, "cbc"), writes=(kps,),
                 cost=1.8, tag="ssqp")
            rsqrt_nr(None, kps, None, kps, t["sdp2"], t["ksd2"],
                     pst, kps, clamp=1e12, iters=3,
                     src_ap=pst[:, 2:3], dst_ap=pst[:, 3:4],
                     nr_ap=pst[:, 4:5])
            def vsmalls(e, pst=pst, stp3=stp3):
                stepp(e, e.tensor_tensor(out=pst[:, 3:4], in0=pst[:, 3:4],
                                         in1=sc_bc[:], op=ALU.mult))
                return e.tensor_tensor(out=pst[:, 5:6], in0=pst[:, 3:4],
                                       in1=stp3[:], op=ALU.mult)
            s.op("pool", vsmalls, reads=(kps, k3, "sc_bc"), writes=(kps,),
                 cost=0.75, tag="vsm")
            def lgev(e, pst=pst):
                stepd(e, e.tensor_scalar(out=gsm[:, 0:4], in0=PS[0][0][:, 0:4],
                                         scalar1=pst[:, 5:6], scalar2=None,
                                         op0=ALU.mult))
                return e.scalar_tensor_tensor(out=gsm[:, 0:4], in0=bps4[:],
                                              scalar=pst[:, 3:4],
                                              in1=gsm[:, 0:4], op0=ALU.mult,
                                              op1=ALU.add)
            s.op("dve", lgev, reads=(("ps", 0), kps, "bps4"), writes=("gsm",),
                 cost=0.6, tag="lg")

        def f_gate(ti):
            def gate1(e):
                stepd(e, e.max(gsm[:, 8:16], gsm[:, 0:8]))
                stepd(e, e.tensor_scalar(out=gsm[:, 16:20], in0=gsm[:, 0:4],
                                         scalar1=gsm[:, 8:9], scalar2=1e9,
                                         op0=ALU.is_lt, op1=ALU.mult))
                stepd(e, e.tensor_tensor(out=gsm[:, 16:20], in0=gsm[:, 16:20],
                                         in1=iota4[:], op=ALU.add))
                stepd(e, e.tensor_reduce(gsm[:, 20:21], gsm[:, 16:20], AX.X,
                                         ALU.min))
                return e.tensor_scalar(out=gsm[:, 24:28], in0=gsm[:, 0:4],
                                       scalar1=-1.0, scalar2=gsm[:, 8:9],
                                       op0=ALU.mult, op1=ALU.add)
            s.op("dve", gate1, reads=("gsm", "iota4"), writes=("gsm",),
                 cost=1.6, tag="gate1")
            s.op("act", lambda e: e.activation(out=gsm[:, 24:28],
                                               in_=gsm[:, 24:28],
                                               func=AF.Tanh, scale=c_HALF),
                 reads=("gsm", "consts"), writes=("gsm",), cost=0.45,
                 tag="gate_tanh")

            def gate2(e):
                stepd(e, e.tensor_scalar(out=gsm[:, 0:4], in0=gsm[:, 24:28],
                                         scalar1=-1.0, scalar2=1.0,
                                         op0=ALU.mult, op1=ALU.add))
                stepd(e, e.tensor_scalar(out=gsm[:, 24:28], in0=gsm[:, 24:28],
                                         scalar1=1.0, scalar2=None, op0=ALU.add))
                stepd(e, e.reciprocal(out=gsm[:, 24:28], in_=gsm[:, 24:28]))
                stepd(e, e.tensor_tensor(out=gsm[:, 0:4], in0=gsm[:, 0:4],
                                         in1=gsm[:, 24:28], op=ALU.mult))
                stepd(e, e.tensor_reduce(gsm[:, 28:29], gsm[:, 0:4], AX.X,
                                         ALU.add))
                stepd(e, e.reciprocal(out=gsm[:, 28:29], in_=gsm[:, 28:29]))
                stepd(e, e.tensor_scalar(out=gsm[:, 24:28], in0=iota4[:],
                                         scalar1=gsm[:, 20:21], scalar2=None,
                                         op0=ALU.is_equal))
                return e.tensor_scalar(out=gsm[:, 24:28], in0=gsm[:, 24:28],
                                       scalar1=gsm[:, 28:29], scalar2=None,
                                       op0=ALU.mult)
            s.op("dve", gate2, reads=("gsm", "iota4"), writes=("gsm",),
                 cost=2.9, tag="gate2")

            def hgate(e):
                inst = None
                for i, (off, r) in enumerate(zip(ROFF, RANKS)):
                    inst = e.tensor_scalar(out=hpb[:, off:off + r],
                                           in0=hsml[:, off:off + r],
                                           scalar1=gsm[:, 24 + i:25 + i],
                                           scalar2=None, op0=ALU.mult)
                return inst
            s.op("dve", hgate, reads=("hsml", "gsm"), writes=("hpb",), cost=0.8, tag="hgate")

        def f_ffn2(ti):
            t = tctx(ti)
            stp3, stp4 = t["stp"][2], t["stp"][3]
            k3, k4 = t["ks"][2], t["ks"][3]
            nrp, knr = t["nrp"], t["knr"]
            s.op("dve", lambda e, stp3=stp3, stp4=stp4: e.tensor_scalar(
                out=X2[:], in0=B[:], scalar1=stp3[:], scalar2=stp4[:],
                op0=ALU.mult, op1=ALU.add),
                reads=("scrB", k3, k4), writes=("X2",), cost=0.7, tag="x2f")
            jobs = [(PS[0][1][0:RTOT, 128:256], hpb[:], I128[:], True, True)]
            mmseq(jobs, reads=("hpb", "I128"), writes=(("ps", 0),), tag="hT")
            s.op("dve", lambda e: e.tensor_copy(out=hpT[:],
                                                in_=PS[0][1][0:RTOT, 128:256]),
                 reads=(("ps", 0),), writes=("hpT",), cost=0.3, tag="hpT_ev")
            jobs = [(PS[2][nh][:, 0:512], hpT[:],
                     WuTb[:, nh * 512:(nh + 1) * 512], False, True)
                    for nh in range(2)]
            mmseq(jobs, reads=("hpT",) + wuk, writes=(("ps", 2),))

            s.op("dve", lambda e, stp1=t["stp"][0], stp3=stp3:
                 e.scalar_tensor_tensor(
                     out=B[:], in0=P[2][:, :], scalar=stp3[:], in1=X2[:],
                     op0=ALU.mult, op1=ALU.add, accum_out=stp1[:]),
                reads=(("ps", 2), "X2", k3), writes=("scrB", t["ks"][0]),
                cost=1.3, tag="pre3")
            s.op("act", lambda e, stp4=stp4: e.activation(
                out=junkb[:], in_=B[:], func=AF.Square, accum_out=stp4[:]),
                reads=("scrB",), writes=("junkb", k4), cost=1.25, tag="sq3")
            ln_stats(t, EPS, sum_i=0)

        def f_out(ti):
            t = tctx(ti)
            stp3, stp4 = t["stp"][2], t["stp"][3]
            k3, k4 = t["ks"][2], t["ks"][3]
            s.op("dve", lambda e, stp3=stp3, stp4=stp4: e.tensor_scalar(
                out=OUTB[:], in0=B[:], scalar1=stp3[:], scalar2=stp4[:],
                op0=ALU.mult, op1=ALU.add),
                reads=("scrB", k3, k4), writes=("OUTB",), cost=0.7, tag="outap")
            s.op("out", lambda e, r0=ti * 128: e.dma_start(
                out=hy[r0:r0 + 128, :], in_=OUTB[:]),
                reads=("OUTB",), writes=())

        f_xkey(0)
        f_val_kp(0)
        f_qbm(0)
        if NT > 1:
            tn = tctx(1)
            dma_in(tn["xin"][:], hx[128:256, :], writes=(tn["kx"],))
            f_xkey(1)
        for i in range(NT):
            f_qp(i)
            f_retr(i)
            if i > 0:
                f_out(i - 1)
            if i + 1 < NT:
                f_val_kp(i + 1)
            f_lnT(i)
            f_wo(i)
            if i + 1 < NT:
                f_qbm(i + 1)
            f_ln1x2(i)
            f_ffn1(i)
            if i + 2 < NT:
                tn = tctx(i + 2)
                dma_in(tn["xin"][:], hx[(i + 2) * 128:(i + 3) * 128, :],
                       writes=(tn["kx"],))
                f_xkey(i + 2)
            f_gate(i)
            f_ffn2(i)
        f_out(NT - 1)

        s.op("pool", lambda e: e.nop(),
             extra_waits=tuple((ch, s.cnt[ch]) for ch in DMA_OUT if s.cnt[ch]))

        qmap = {"pe": "tensor", "act": "scalar", "dve": "vector",
                "pool": "gpsimd", "sp": "sync"}
        with nc.Block() as block:
            s.emit(sems, block, qmap)

    return nc


_PROG = None


def kernel(**inputs):
    global _PROG
    x = np.ascontiguousarray(np.asarray(inputs["L0_attn_x"], np.float32))
    assert x.shape == (8, S, D)
    hp = host_prep(inputs)
    if _PROG is None:
        _PROG = build_core_program()
    in_maps = []
    for b in range(8):
        m = {"hx": x[b]}
        m.update(hp)
        in_maps.append(m)
    res = run_bass_kernel_spmd(_PROG, in_maps, core_ids=list(range(8)))
    out = np.stack([np.asarray(r["hy"]) for r in res.results], axis=0)
    return out.astype(np.float32)


# version 11
# speedup vs baseline: 1.5602x; 1.0667x over previous
import os
import numpy as np
import ml_dtypes
import concourse.bass as bass
import concourse.mybir as mybir
from concourse.bass_utils import run_bass_kernel_spmd

F32 = mybir.dt.float32
BF16 = mybir.dt.bfloat16
I32 = mybir.dt.int32
AF = mybir.ActivationFunctionType
ALU = mybir.AluOpType
AX = mybir.AxisListType

D = 1024
S = 4096
NCH = 8
NT = int(os.environ.get("KNT", "32"))
CHUNK = 64
EPS = 1e-5
EPS_IN = 1024 * 1e-5
CLAMP_MAX = float(np.log(1.0 / 0.01))
RANKS = [1, 2, 4, 8]
ROFF = [0, 1, 3, 7]
RTOT = 15
PROJ = 256

S_SCALE = float(np.nextafter(np.float32(2 * np.pi), np.float32(0)))
HALF_PI = float(np.float32(np.pi / 2))
INV_2PI = 1.0 / (2.0 * np.pi)
KOFF = 64.0

SC_C2 = float(np.float32(16.0) - np.float32(S_SCALE))
assert SC_C2 + S_SCALE == 16.0
_d = np.float32(-(2.0 * np.pi - S_SCALE))
while (16.0 - float(_d)) - SC_C2 >= 2.0 * np.pi:
    _d = np.nextafter(_d, np.float32(0))
SC_DELTA = float(_d)
assert float(np.float32(np.float32(SC_DELTA) + np.float32(-16.0))) == -16.0
B_C3 = float(np.float32(2.0) - np.float32(HALF_PI))
assert B_C3 + HALF_PI == 2.0
_db = np.float32(np.pi / 2 - HALF_PI)
while (2.0 + float(_db)) - B_C3 > np.pi / 2:
    _db = np.nextafter(_db, -np.inf)
B_DELTA = float(_db)
assert float(np.float32(np.float32(B_DELTA) + np.float32(2.0))) == 2.0
assert 0.0 <= np.pi / 2 - ((2.0 + B_DELTA) - B_C3) < 1e-13
assert 0.0 < 2.0 * np.pi - ((16.0 - SC_DELTA) - SC_C2) < 1e-12

ENGINES = ("pe", "act", "dve", "pool", "sp")
DMA_IN = tuple(f"in{i}" for i in range(4))
DMA_OUT = tuple(f"out{i}" for i in range(2))
DMA_CH = DMA_IN + DMA_OUT


class Sched:
    def __init__(self):
        self.ops = []
        self.cnt = {e: 0 for e in ENGINES + DMA_CH}
        self.writer = {}
        self.readers = {}
        self.need_inc = set()
        self.meta = []
        self.din = 0
        self.dout = 0

    def op(self, eng, emit, reads=(), writes=(), extra_waits=(), cost=None,
           tag=""):
        if eng == "in":
            eng = DMA_IN[self.din % len(DMA_IN)]
            self.din += 1
        elif eng == "out":
            eng = DMA_OUT[self.dout % len(DMA_OUT)]
            self.dout += 1
        step = 16 if eng in DMA_CH else 1
        waits = set(extra_waits)
        if eng in DMA_CH and self.cnt[eng] > 0:
            waits.add((eng, self.cnt[eng]))
        for k in reads:
            w = self.writer.get(k)
            if w is not None:
                waits.add(w)
        for k in writes:
            w = self.writer.get(k)
            if w is not None:
                waits.add(w)
            for r in self.readers.get(k, ()):
                waits.add(r)
        self.cnt[eng] += step
        me = (eng, self.cnt[eng])
        for k in reads:
            self.readers.setdefault(k, []).append(me)
        for k in writes:
            self.writer[k] = me
            self.readers[k] = []
        for w in waits:
            self.need_inc.add(w)
        if eng in DMA_CH:
            self.need_inc.add(me)
        if cost is None:
            cost = {"pe": 0.3, "act": 1.2, "dve": 1.2, "pool": 0.25,
                    "sp": 0.05}.get(eng, 2.5)
            if eng.startswith("in") or eng.startswith("out"):
                cost = 2.5
        self.meta.append((cost, tag))
        self.ops.append((eng, emit, sorted(waits), me))
        return me

    def reorder(self, gap=0.15, xgap=0.12, keep_order=()):
        n = len(self.ops)
        byme = {op[3]: i for i, op in enumerate(self.ops)}
        succ = [[] for _ in range(n)]
        pred = []
        last_on = {}
        import os as _os
        _spec = _os.environ.get("KDVECUT", "86")
        if ":" in _spec:
            _lo, _hi = _spec.split(":")
            dve_lo, dve_hi = int(_lo), int(_hi)
        else:
            dve_lo, dve_hi = 0, int(_spec)
        dve_rank = 0
        for i, (eng, emitfn, waits, me) in enumerate(self.ops):
            ws = [byme[w] for w in waits]
            grp = eng
            if grp.startswith("in") or grp.startswith("out"):
                grp = "dma"
            chain = grp in keep_order
            if grp == "dve":
                if not (dve_lo <= dve_rank < dve_hi):
                    chain = True
                    grp = "dvechain"
                dve_rank += 1
            if chain:
                p = last_on.get(grp)
                if p is not None and p not in ws:
                    ws.append(p)
                last_on[grp] = i
            pred.append(ws)
            for w in ws:
                succ[w].append(i)
        cost = [self.meta[i][0] for i in range(n)]
        h = [0.0] * n
        for i in range(n - 1, -1, -1):
            m = 0.0
            for t in succ[i]:
                if h[t] > m:
                    m = h[t]
            h[i] = cost[i] + m
        qfree = {}
        nd = [0] * n
        dep_done = [0.0] * n
        readyset = [i for i in range(n) if not pred[i]]
        order = []
        while readyset:
            best = None
            bestk = None
            for i in readyset:
                est = max(qfree.get(self.ops[i][0], 0.0), dep_done[i])
                k = (est, -h[i])
                if bestk is None or k < bestk:
                    bestk = k
                    best = i
            i = best
            readyset.remove(i)
            t1 = bestk[0] + cost[i]
            qfree[self.ops[i][0]] = t1 + gap
            order.append(i)
            for t in succ[i]:
                nd[t] += 1
                if dep_done[t] < t1 + xgap:
                    dep_done[t] = t1 + xgap
                if nd[t] == len(pred[t]):
                    readyset.append(t)
        assert len(order) == n, (len(order), n)
        self.ops = [self.ops[i] for i in order]
        self.meta = [self.meta[i] for i in order]
        remap_me = {}
        cnt = {e: 0 for e in self.cnt}
        for (eng, emitfn, waits, me) in self.ops:
            step = 16 if eng in DMA_CH else 1
            cnt[eng] += step
            remap_me[me] = (eng, cnt[eng])
        self.ops = [(eng, emitfn, sorted(remap_me[w] for w in waits),
                     remap_me[me]) for (eng, emitfn, waits, me) in self.ops]
        self.need_inc = {remap_me[m] for m in self.need_inc}
        self.cnt = cnt

    def simulate(self, gap=0.15, xgap=0.12):
        qfree = {}
        done = {}
        rows = []
        for idx, (eng, _, waits, me) in enumerate(self.ops):
            cost, tag = self.meta[idx]
            q = eng if eng in ENGINES else ("dma" + eng[-1] if eng[-1].isdigit()
                                            else eng)
            t0 = qfree.get(q, 0.0)
            src_max = 0.0
            for w in waits:
                src_max = max(src_max, done.get(w, 0.0) + xgap)
            t0 = max(t0, src_max)
            t1 = t0 + cost
            qfree[q] = t1 + gap
            done[me] = t1
            rows.append((t0, t1, eng, tag, me, waits))
        busy = {}
        for idx, (eng, _, _, _) in enumerate(self.ops):
            busy[eng] = busy.get(eng, 0.0) + self.meta[idx][0]
        return rows, busy

    def emit(self, sems, block, qmap):
        import bisect
        inc_list = {}
        for (e, c) in self.need_inc:
            inc_list.setdefault(e, []).append(c)
        for e in inc_list:
            inc_list[e].sort()

        def remap(src, val):
            lst = inc_list.get(src, ())
            r = bisect.bisect_right(lst, val)
            assert r > 0 and lst[r - 1] == val, (src, val)
            return r * (16 if src in DMA_CH else 1)

        per_q = {e: [] for e in ENGINES}
        for eng, emitfn, waits, me in self.ops:
            q = eng if eng in per_q else ("sp" if eng.startswith("in") else "pool")
            per_q[q].append((emitfn, waits, me if me in self.need_inc else None, eng))
        for q, items in per_q.items():
            if not items:
                continue

            def make_body(items):
                def body(e):
                    last = {}
                    for emitfn, waits, inc, semeng in items:
                        for (src, val) in waits:
                            v = remap(src, val)
                            if last.get(src, -1) >= v:
                                continue
                            e.wait_ge(sems[src], v)
                            last[src] = v
                        inst = emitfn(e)
                        if inc is not None:
                            inst.then_inc(sems[semeng],
                                          16 if semeng in DMA_CH else 1)
                return body
            getattr(block, qmap[q])(make_body(items))


def host_prep(inp):
    f = np.float32
    T = lambda a: np.ascontiguousarray(np.asarray(a, f).T)
    chunk2d = lambda v: np.ascontiguousarray(np.asarray(v, f).reshape(NCH, 128).T)
    bcast = lambda v, n: np.ascontiguousarray(
        np.broadcast_to(np.asarray(v, f).reshape(1, n), (128, n)))

    d = {}
    d["WkT"] = T(inp["L0_attn_Wk"])
    d["WvT"] = T(inp["L0_attn_Wv"])
    d["WqT"] = T(inp["L0_attn_Wq"])
    d["WkpT"] = T(inp["L0_attn_Wkp"])
    d["WqpT"] = T(inp["L0_attn_Wqp"])
    d["WoT"] = T(inp["L0_attn_Wo"])
    d["WfT"] = T(inp["L0_ffn_W"])
    d["WpT"] = T(inp["L0_ffn_Wp"])
    wd = np.concatenate([np.asarray(inp[f"L0_ffn_Wd{i}"], f) for i in range(4)], 0)
    d["WdT"] = T(wd)
    wu = np.concatenate([np.asarray(inp[f"L0_ffn_Wu{i}"], f) for i in range(4)], 1)
    d["WuT"] = T(wu)

    d["bk2d"] = chunk2d(inp["L0_attn_bk"])
    d["bq2d"] = chunk2d(inp["L0_attn_bq"])
    d["lnb2d"] = chunk2d(inp["L0_attn_ln_b"])
    d["g2d"] = chunk2d(inp["L0_attn_ln_g"])
    d["bv_row"] = np.asarray(inp["L0_attn_bv"], f).reshape(1, D).copy()
    d["ps_bc"] = bcast(inp["L0_attn_phase_scale"], D)
    d["bp_bc"] = bcast(inp["L0_ffn_bp"], PROJ)
    d["bp2d"] = np.ascontiguousarray(np.asarray(inp["L0_ffn_bp"], f).reshape(2, 128).T)
    d["bkp_row"] = np.asarray(inp["L0_attn_bkp"], f).reshape(1, D).copy()
    d["bqp_row"] = np.asarray(inp["L0_attn_bqp"], f).reshape(1, D).copy()
    d["bo_bc"] = bcast(inp["L0_attn_bo"], D)
    sim = np.asarray(inp["L0_ffn_sim"], f).reshape(2, 128, 4).transpose(1, 0, 2)
    d["sim2d"] = np.ascontiguousarray(sim.reshape(128, 8))
    d["temp"] = np.asarray(inp["L0_ffn_temp"], f).reshape(1, 1).copy()

    L = np.zeros((128, 128), f)
    for b in range(2):
        L[b * CHUNK:(b + 1) * CHUNK, b * CHUNK:(b + 1) * CHUNK] = \
            np.tril(np.ones((CHUNK, CHUNK), f))
    d["LTb"] = np.ascontiguousarray(L.T).astype(ml_dtypes.bfloat16)
    d["I128b"] = np.eye(128, dtype=ml_dtypes.bfloat16)
    d["ones_f"] = np.ones((1, 128), f)
    d["ones2b"] = np.ones((2, 128), dtype=ml_dtypes.bfloat16)
    d["ones_c"] = np.ones((128, 1), f)
    d["magic"] = np.full((128, 1), 0x5F3759DF, np.int32)
    consts = np.zeros((128, 8), f)
    consts[:, 0] = S_SCALE
    consts[:, 1] = HALF_PI
    consts[:, 2] = 0.5
    consts[:, 3] = -S_SCALE
    consts[:, 4] = SC_DELTA
    consts[:, 5] = B_DELTA
    d["consts"] = consts
    d["iota4"] = np.ascontiguousarray(
        np.broadcast_to(np.arange(4, dtype=f), (128, 4)))
    gpad = np.full((128, 32), 0.0, f)
    gpad[:, 4:8] = -1e30
    d["gpad"] = gpad
    return d


def build_core_program():
    nc = bass.Bass("TRN2", target_bir_lowering=False)

    hx = nc.dram_tensor("hx", [S, D], F32, kind="ExternalInput").ap()
    hy = nc.dram_tensor("hy", [S, D], F32, kind="ExternalOutput").ap()
    W_NAMES = ["WkT", "WvT", "WqT", "WkpT", "WqpT", "WoT", "WfT"]
    h = {}
    for n in W_NAMES:
        h[n] = nc.dram_tensor(n, [D, D], F32, kind="ExternalInput").ap()
    h["WpT"] = nc.dram_tensor("WpT", [D, PROJ], F32, kind="ExternalInput").ap()
    h["WdT"] = nc.dram_tensor("WdT", [D, RTOT], F32, kind="ExternalInput").ap()
    h["WuT"] = nc.dram_tensor("WuT", [RTOT, D], F32, kind="ExternalInput").ap()
    for n, shp in [("bk2d", [128, 8]), ("bq2d", [128, 8]), ("lnb2d", [128, 8]),
                   ("g2d", [128, 8]), ("bv_row", [1, D]), ("ps_bc", [128, D]),
                   ("bp_bc", [128, PROJ]), ("bp2d", [128, 2]), ("bkp_row", [1, D]),
                   ("bqp_row", [1, D]), ("bo_bc", [128, D]), ("sim2d", [128, 8]),
                   ("temp", [1, 1]), ("ones_f", [1, 128]),
                   ("ones_c", [128, 1]), ("consts", [128, 8]),
                   ("gpad", [128, 32])]:
        h[n] = nc.dram_tensor(n, shp, F32, kind="ExternalInput").ap()
    h["iota4"] = nc.dram_tensor("iota4", [128, 4], F32,
                                kind="ExternalInput").ap()
    h["LTb"] = nc.dram_tensor("LTb", [128, 128], BF16, kind="ExternalInput").ap()
    h["ones2b"] = nc.dram_tensor("ones2b", [2, 128], BF16, kind="ExternalInput").ap()
    h["I128b"] = nc.dram_tensor("I128b", [128, 128], BF16, kind="ExternalInput").ap()
    h["magic"] = nc.dram_tensor("magic", [128, 1], I32, kind="ExternalInput").ap()
    hlscr = nc.dram_tensor("hlscr", [1, 2 * D], BF16, kind="Internal").ap()

    from contextlib import ExitStack
    ctx = ExitStack()
    sb = lambda name, shape, dt: ctx.enter_context(nc.sbuf_tensor(name, shape, dt))
    with ctx:
        wb = {n: sb(n + "_b", [128, NCH * D], BF16) for n in W_NAMES}
        WpTb = sb("WpT_b", [128, NCH * PROJ], BF16)
        WdTb = sb("WdT_b", [128, NCH * RTOT], BF16)
        WuTb = sb("WuT_b", [RTOT, D], BF16)
        LTs = sb("LT_b", [128, 128], BF16)
        I128 = sb("I128_b", [128, 128], BF16)

        bk2d = sb("bk2d_s", [128, 8], F32)
        bq2d = sb("bq2d_s", [128, 8], F32)
        lnb2d = sb("lnb2d_s", [128, 8], F32)
        g2d = sb("g2d_s", [128, 8], F32)
        b2db = sb("b2d_bb", [128, 24], BF16)
        ps_bc = sb("ps_bc_s", [128, D], F32)
        bp_bc = sb("bp_bc_s", [128, PROJ], F32)
        bp2d = sb("bp2d_s", [128, 2], F32)
        bp2db = sb("bp2d_bb", [128, 2], BF16)
        bps4 = sb("bps4_s", [128, 4], F32)
        cbc = sb("cbc_s", [128, 1], F32)
        rowtmp_t = None
        rowbuf = None
        kp_hl = sb("kp_hl", [2, D], BF16)
        qp_hl = sb("qp_hl", [2, D], BF16)
        v_hl = sb("v_hl", [2, D], BF16)
        ones2 = sb("ones2_s", [2, 128], BF16)
        iota4 = sb("iota4_s", [128, 4], F32)
        gidx = sb("gidx_s", [128, 8], mybir.dt.uint32)
        bo_bc = sb("bo_bc_s", [128, D], F32)
        sim2d = sb("sim2d_s", [128, 8], F32)
        simnb = sb("simn_b", [128, 8], BF16)
        temp = sb("temp_s", [1, 1], F32)
        ones_f = sb("ones_f_s", [1, 128], F32)
        ones_c = sb("ones_c_s", [128, 1], F32)
        consts = sb("consts_s", [128, 8], F32)
        magic = sb("magic_s", [128, 1], I32)
        sc_bc = sb("sc_bc_s", [128, 1], F32)

        x_in = sb("x_in", [128, D], F32)
        A = sb("scrA", [128, D], F32)
        B = sb("scrB", [128, D], F32)
        C = sb("scrC", [128, D], F32)
        U = sb("scrU", [128, D], I32)
        SKb = sb("sinK", [128, D], BF16)
        CKb = sb("cosK", [128, D], BF16)
        SKq = sb("sinQ", [128, D], BF16)
        CKq = sb("cosQ", [128, D], BF16)
        WFS = sb("wfs", [128, D], F32)
        VAL = sb("val", [128, D], F32)
        X2 = sb("x2", [128, D], F32)
        xbb = sb("xb_b", [128, D], BF16)
        bsh = sb("bshared", [128, D], BF16)
        tsh = sb("tshared", [128, D], BF16)
        tsh2 = sb("tshared2", [128, D], BF16)
        kqT = sb("kqT", [128, D], BF16)
        junkb = sb("junk_b", [128, D], BF16)
        proj = sb("proj_s", [128, PROJ], F32)
        projb = sb("proj_bb", [128, PROJ], BF16)
        projT = sb("projT_b", [128, 2 * 128], BF16)
        hsml = sb("h_small", [128, RTOT], F32)
        hpb = sb("hp_b", [128, RTOT], BF16)
        hpT = sb("hpT_b", [RTOT, 128], BF16)
        gsm = sb("g_small", [128, 32], F32)
        lg = gsm[:, 0:4]
        STAT = [[sb(f"stat{p}_{i}", [128, 1], F32) for i in range(4)]
                for p in range(2)]
        NRS = [sb(f"nr_small{p}", [128, 4], F32) for p in range(2)]
        PSTAT = [sb(f"pstat{p}", [128, 8], F32) for p in range(2)]
        SEEDI = [sb(f"seed_i{p}", [128, 2], I32) for p in range(2)]
        seed4 = sb("seed4_i", [1, 8], I32)
        srow = sb("srow", [1, 16], F32)
        st1, st2, st3, st4 = STAT[0]
        nrs = NRS[0]
        seedi = SEEDI[0]

        xTb = sb("xT_b", [128, D], BF16)
        OUTB = sb("out_b", [128, D], F32)
        rowtmp = OUTB[0:1, :]
        rowbuf = X2[0:1, :]
        MIB = sb("mi_b", [128, D], F32)
        KQT = [kqT, sb("kqT2", [128, D], BF16)]
        XIN = [x_in, sb("x_in2", [128, D], F32)]

        P = [ctx.enter_context(nc.psum_tensor(f"P{i}", [128, 1024], F32))
             for i in range(4)]
        PS = [[P[i][:, 0:512], P[i][:, 512:1024]] for i in range(4)]

        sems = {}
        for e in ENGINES + DMA_CH:
            sems[e] = ctx.enter_context(nc.semaphore("sem_" + e))
        chain_sems = {e: ctx.enter_context(nc.semaphore("chain_" + e))
                      for e in ("dve", "act", "pool")}
        chain_cnt = {e: 0 for e in chain_sems}

        def mkstep(eng):
            def step(e, inst):
                chain_cnt[eng] += 1
                inst.then_inc(chain_sems[eng], 1)
                e.wait_ge(chain_sems[eng], chain_cnt[eng])
            return step
        stepd = mkstep("dve")
        stepp = mkstep("pool")

        s = Sched()
        wkeys = lambda n: tuple((n, c) for c in range(NCH))

        def dma_in(dst_ap, src_ap, writes, reads=()):
            return s.op("in", lambda e, d=dst_ap, sr=src_ap: e.dma_start(out=d, in_=sr),
                        reads=reads, writes=writes)

        def mmseq(jobs, reads, writes, stop_last=True, tag="mm"):
            jobs = list(jobs)
            cost = sum(max(j[2].shape[-1], 70) / 2400.0 + 0.015 for j in jobs)

            def emit(e, jobs=jobs):
                inst = None
                for (o, l, r, st, sp) in jobs:
                    inst = e.matmul(out=o, lhsT=l, rhs=r, start=st, stop=sp)
                return inst
            return s.op("pe", emit, reads=reads, writes=writes, cost=cost,
                        tag=tag)

        def proj_tok(slot, lhsT_key, lhsT_ap, wtile, wkey_t, stop_last=True):
            jobs = []
            for nh in range(2):
                for dc in range(NCH):
                    jobs.append((PS[slot][nh][:, 0:512],
                                 lhsT_ap[:, dc * 128:(dc + 1) * 128],
                                 wtile[:, dc * D + nh * 512:dc * D + (nh + 1) * 512],
                                 dc == 0, (dc == NCH - 1) and stop_last))
            return mmseq(jobs, reads=(lhsT_key,) + wkey_t,
                         writes=(("ps", slot),))

        def feat_proj(slot, wtile, wkey_t):
            jobs = []
            for ec in range(NCH):
                bank = PS[slot][ec // 4]
                for dc in range(NCH):
                    jobs.append((bank[:, (ec % 4) * 128:(ec % 4 + 1) * 128],
                                 wtile[:, dc * D + ec * 128:dc * D + (ec + 1) * 128],
                                 tsh[:, dc * 128:(dc + 1) * 128],
                                 dc == 0, dc == NCH - 1))
            return mmseq(jobs, reads=("tsh",) + wkey_t, writes=(("ps", slot),))

        def transpose8(slot, src_key, src_ap):
            jobs = []
            for c in range(NCH):
                bank = PS[slot][c // 4]
                jobs.append((bank[:, (c % 4) * 128:(c % 4 + 1) * 128],
                             src_ap[:, c * 128:(c + 1) * 128], I128[:], True, True))
            return mmseq(jobs, reads=(src_key, "I128"), writes=(("ps", slot),))

        def evac_pair(slot, dst_ap, dst_key, eng="act"):
            if eng == "act":
                def emit(e, sl=slot, d=dst_ap):
                    return e.activation(out=d, in_=P[sl][:, :], func=AF.Identity)
            else:
                def emit(e, sl=slot, d=dst_ap):
                    return e.tensor_copy(out=d, in_=P[sl][:, :])
            return s.op(eng, emit, reads=(("ps", slot),), writes=(dst_key,),
                        cost=1.2, tag=f"evac{slot}")

        def evac_pair_add(slot, bcast_tile, bkey, dst, dst_key):
            def emit(e, sl=slot, bc=bcast_tile, d=dst):
                return e.tensor_tensor(out=d[:, 0:1024], in0=P[sl][:, :],
                                       in1=bc[:, 0:1024], op=ALU.add)
            return s.op("dve", emit, reads=(("ps", slot), bkey),
                        writes=(dst_key,), cost=1.25, tag=f"evadd{slot}")

        def rsqrt_nr(src, src_key, dst, dst_key, sdi, sdi_key, nr, nr_key,
                     clamp=None, iters=4, src_ap=None, dst_ap=None, nr_ap=None):
            if src_ap is None:
                src_ap = src[:]
            if dst_ap is None:
                dst_ap = dst[:]
            if nr_ap is None:
                nr_ap = nr[:, 0:1]

            def emit_seed(e, src_ap=src_ap, dst_ap=dst_ap, sdi=sdi):
                sbits = src_ap.bitcast(I32)
                stepd(e, e.tensor_scalar(out=sdi, in0=sbits, scalar1=1,
                                         scalar2=None,
                                         op0=ALU.arith_shift_right))
                stepd(e, e.tensor_tensor(out=sdi, in0=magic[:],
                                         in1=sdi, op=ALU.subtract))
                return e.tensor_copy(out=dst_ap, in_=sdi.bitcast(F32))
            s.op("dve", emit_seed, reads=(src_key, "magic"),
                 writes=(dst_key, sdi_key), cost=1.0, tag="nr_seed")

            def emit_nr(e, src_ap=src_ap, dst_ap=dst_ap, nr_ap=nr_ap):
                inst = None
                for it in range(iters):
                    stepp(e, e.tensor_tensor(out=nr_ap, in0=dst_ap,
                                             in1=dst_ap, op=ALU.mult))
                    stepp(e, e.tensor_tensor(out=nr_ap, in0=nr_ap,
                                             in1=src_ap, op=ALU.mult))
                    stepp(e, e.tensor_scalar(out=nr_ap, in0=nr_ap,
                                             scalar1=-0.5, scalar2=1.5,
                                             op0=ALU.mult, op1=ALU.add))
                    inst = e.tensor_tensor(out=dst_ap, in0=dst_ap,
                                           in1=nr_ap, op=ALU.mult)
                    if it < iters - 1 or clamp is not None:
                        stepp(e, inst)
                if clamp is not None:
                    inst = e.tensor_scalar(out=dst_ap, in0=dst_ap, scalar1=clamp,
                                           scalar2=None, op0=ALU.min)
                return inst
            return s.op("pool", emit_nr, reads=(src_key, dst_key),
                        writes=(dst_key, nr_key), cost=0.36 * (4 * iters + 1),
                        tag="nr_loop")

        def _ap(t):
            return t if isinstance(t, bass.AP) else t[:]
        for name, tile in [("bk2d", bk2d), ("bq2d", bq2d), ("lnb2d", lnb2d),
                           ("g2d", g2d), ("ps_bc", ps_bc),
                           ("bp_bc", bp_bc), ("bp2d", bp2d), ("bo_bc", bo_bc),
                           ("iota4", iota4),
                           ("sim2d", sim2d), ("temp", temp), ("ones_f", ones_f),
                           ("ones_c", ones_c), ("consts", consts)]:
            dma_in(_ap(tile), h[name], writes=(name,))
        s.op("dve", lambda e: e.tensor_scalar(
            out=consts[:, 3:4], in0=consts[:, 4:5], scalar1=-16.0,
            scalar2=-SC_C2, op0=ALU.add, op1=ALU.subtract),
            reads=("consts",), writes=("consts",), cost=0.2, tag="fixc")
        s.op("dve", lambda e: e.tensor_scalar(
            out=consts[:, 1:2], in0=consts[:, 5:6], scalar1=2.0,
            scalar2=B_C3, op0=ALU.add, op1=ALU.subtract),
            reads=("consts",), writes=("consts",), cost=0.2, tag="fixb")
        dma_in(LTs[:], h["LTb"], writes=("LT",))
        dma_in(ones2[:], h["ones2b"], writes=("ones2",))

        dma_in(I128[:], h["I128b"], writes=("I128",))
        dma_in(magic[:], h["magic"], writes=("magic",))
        dma_in(x_in[:], hx[0:128, :], writes=("x_in_0",))

        stage = [(A, "scrA"), (B, "scrB")]
        cast_engines = ["dve", "act", "pool"]
        ei = 0
        for wi, wname in enumerate(W_NAMES + ["WpT", "WdT"]):
            src = h[wname]
            ncols = {"WpT": PROJ, "WdT": RTOT}.get(wname, D)
            dstt = {"WpT": WpTb, "WdT": WdTb}.get(wname, wb.get(wname))
            for c in range(NCH):
                stg, skey = stage[(wi * NCH + c) % 2]
                dma_in(stg[:, 0:ncols], src[c * 128:(c + 1) * 128, :],
                       writes=(skey,))
                eng = cast_engines[ei % 3]
                ei += 1
                dst_ap = dstt[:, c * ncols:(c + 1) * ncols]
                if wname == "WoT":
                    def emit(e, stg=stg, dst_ap=dst_ap, gc=g2d[:, c:c + 1], n=ncols):
                        return e.tensor_scalar(out=dst_ap, in0=stg[:, 0:n],
                                               scalar1=gc, scalar2=None,
                                               op0=ALU.mult)
                    s.op("dve", emit, reads=(skey, "g2d"), writes=((wname, c),))
                elif eng == "act":
                    def emit(e, stg=stg, dst_ap=dst_ap, n=ncols):
                        return e.activation(out=dst_ap, in_=stg[:, 0:n],
                                            func=AF.Identity)
                    s.op("act", emit, reads=(skey,), writes=((wname, c),))
                else:
                    def emit(e, stg=stg, dst_ap=dst_ap, n=ncols):
                        return e.tensor_copy(out=dst_ap, in_=stg[:, 0:n])
                    s.op(eng, emit, reads=(skey,), writes=((wname, c),))
        dma_in(C[0:RTOT, 0:D], h["WuT"], writes=("scrC",))
        s.op("dve", lambda e: e.tensor_copy(out=WuTb[:], in_=C[0:RTOT, 0:D]),
             reads=("scrC",), writes=("WuT", 0))
        wuk = (("WuT", 0),)

        def small_cast(e):
            e.tensor_copy(out=b2db[:, 0:8], in_=bk2d[:])
            e.tensor_copy(out=b2db[:, 8:16], in_=bq2d[:])
            return e.tensor_copy(out=b2db[:, 16:24], in_=lnb2d[:])
        s.op("pool", small_cast, reads=("bk2d", "bq2d", "lnb2d"), writes=("b2db",))

        def hilo_from_row(dst, dkey):
            h0 = KQT[0][0:1, 0:D]
            h1 = KQT[1][0:1, 0:D]

            def emit(e):
                stepd(e, e.tensor_copy(out=h0, in_=rowtmp))
                stepd(e, e.tensor_copy(out=rowbuf[:], in_=h0))
                stepd(e, e.tensor_tensor(out=rowbuf[:], in0=rowtmp,
                                         in1=rowbuf[:], op=ALU.subtract))
                return e.tensor_copy(out=h1, in_=rowbuf[:])
            s.op("dve", emit, reads=("OUTB",),
                 writes=("kqT_0", "kqT_1", "X2"), cost=0.5, tag="hilo")
            s.op("out", lambda e: e.dma_start(out=hlscr[0:1, 0:D], in_=h0),
                 reads=("kqT_0",), writes=("hlscr",))
            s.op("out", lambda e: e.dma_start(out=hlscr[0:1, D:2 * D], in_=h1),
                 reads=("kqT_1",), writes=("hlscr",))
            s.op("in", lambda e, dst=dst: e.dma_start(
                out=dst[:], in_=hlscr.rearrange("a (b c) -> (a b) c", b=2)),
                reads=("hlscr",), writes=(dkey,))

        def row_fold(boff, wname, wkey_t, add_ap, addkey, dst, dkey, slot_row):
            jobs = []
            for c in range(NCH):
                for nh in range(2):
                    jobs.append((PS[slot_row][nh][0:1, 0:512],
                                 b2db[:, boff + c:boff + c + 1],
                                 wb[wname][:, c * D + nh * 512:c * D + (nh + 1) * 512],
                                 c == 0, c == NCH - 1))
            mmseq(jobs, reads=("b2db",) + wkey_t, writes=(("ps", slot_row),))

            def ev(e, sl=slot_row, ar=add_ap):
                stepd(e, e.tensor_tensor(out=rowtmp[0:1, 0:512],
                                         in0=PS[sl][0][0:1, :],
                                         in1=ar[0:1, 0:512], op=ALU.add))
                return e.tensor_tensor(out=rowtmp[0:1, 512:1024],
                                       in0=PS[sl][1][0:1, :],
                                       in1=ar[0:1, 512:1024], op=ALU.add)
            s.op("dve", ev, reads=(("ps", slot_row), addkey), writes=("OUTB",))
            hilo_from_row(dst, dkey)

        dma_in(rowbuf[:], h["bkp_row"], writes=("X2",))
        row_fold(0, "WkpT", wkeys("WkpT"), rowbuf[:], "X2",
                 kp_hl, "kp_hl", 0)
        dma_in(rowbuf[:], h["bqp_row"], writes=("X2",))
        row_fold(8, "WqpT", wkeys("WqpT"), rowbuf[:], "X2",
                 qp_hl, "qp_hl", 2)
        dma_in(rowtmp, h["bv_row"], writes=("OUTB",))
        hilo_from_row(v_hl, "v_hl")

        for c in range(NCH):
            stg, skey = stage[c % 2]
            dma_in(stg[:, 0:D], h["WoT"][c * 128:(c + 1) * 128, :], writes=(skey,))

            def cem(e, stg=stg):
                return e.tensor_copy(out=bsh[:, 0:D], in_=stg[:, 0:D])
            s.op("dve", cem, reads=(skey,), writes=("bsh",))
            jb = []
            for nh in range(2):
                jb.append((PS[0][nh][0:1, 0:512], b2db[:, 16 + c:17 + c],
                           bsh[:, nh * 512:(nh + 1) * 512],
                           c == 0, c == NCH - 1))
            mmseq(jb, reads=("bsh", "b2db"), writes=(("ps", 0),))

        def bo_ev(e):
            e.tensor_copy(out=rowtmp[0:1, 0:512], in_=PS[0][0][0:1, :])
            return e.tensor_copy(out=rowtmp[0:1, 512:1024], in_=PS[0][1][0:1, :])
        s.op("dve", bo_ev, reads=(("ps", 0),), writes=("OUTB",))
        jobs2 = [(PS[1][nh][:, 0:512], ones_f[:], rowtmp[0:1, nh * 512:(nh + 1) * 512],
                  True, True) for nh in range(2)]
        mmseq(jobs2, reads=("OUTB", "ones_f"), writes=(("ps", 1),))

        def bo_bc_ev(e):
            e.tensor_tensor(out=bo_bc[:, 0:512], in0=PS[1][0][:, :],
                            in1=bo_bc[:, 0:512], op=ALU.add)
            return e.tensor_tensor(out=bo_bc[:, 512:1024], in0=PS[1][1][:, :],
                                   in1=bo_bc[:, 512:1024], op=ALU.add)
        s.op("dve", bo_bc_ev, reads=(("ps", 1), "bo_bc"), writes=("bo_bc",))

        s.op("act", lambda e: e.activation(out=gsm[:, 0:8], in_=sim2d[:],
                                           func=AF.Square),
             reads=("sim2d",), writes=("gsm",))
        jb = [(PS[2][0][0:1, 0:8], ones_c[:], gsm[:, 0:8], True, True)]
        mmseq(jb, reads=("gsm", "ones_c"), writes=(("ps", 2),))

        def sim_norm(e):
            stepd(e, e.tensor_copy(out=srow[0:1, 8:16], in_=PS[2][0][0:1, 0:8]))
            stepd(e, e.tensor_tensor(out=srow[0:1, 0:4], in0=srow[0:1, 8:12],
                                     in1=srow[0:1, 12:16], op=ALU.add))
            sbits = srow[0:1, 0:4].bitcast(I32)
            stepd(e, e.tensor_scalar(out=seed4[0:1, 0:4], in0=sbits, scalar1=1,
                                     scalar2=None, op0=ALU.arith_shift_right))
            stepd(e, e.tensor_scalar(out=seed4[0:1, 4:8], in0=seed4[0:1, 0:4],
                                     scalar1=-1, scalar2=0x5F3759DF,
                                     op0=ALU.mult, op1=ALU.add))
            stepd(e, e.tensor_copy(out=srow[0:1, 4:8],
                                   in_=seed4[0:1, 4:8].bitcast(F32)))
            for _ in range(4):
                stepd(e, e.tensor_tensor(out=srow[0:1, 8:12], in0=srow[0:1, 4:8],
                                         in1=srow[0:1, 4:8], op=ALU.mult))
                stepd(e, e.tensor_tensor(out=srow[0:1, 8:12], in0=srow[0:1, 8:12],
                                         in1=srow[0:1, 0:4], op=ALU.mult))
                stepd(e, e.tensor_scalar(out=srow[0:1, 8:12], in0=srow[0:1, 8:12],
                                         scalar1=-0.5, scalar2=1.5, op0=ALU.mult,
                                         op1=ALU.add))
                stepd(e, e.tensor_tensor(out=srow[0:1, 4:8], in0=srow[0:1, 4:8],
                                         in1=srow[0:1, 8:12], op=ALU.mult))
            return e.tensor_scalar(out=srow[0:1, 4:8], in0=srow[0:1, 4:8],
                                   scalar1=1e12, scalar2=None, op0=ALU.min)
        s.op("dve", sim_norm, reads=(("ps", 2),), writes=("srow", "seed4"))
        jb = [(PS[2][0][:, 0:4], ones_f[:], srow[0:1, 4:8], True, True)]
        mmseq(jb, reads=("srow", "ones_f"), writes=(("ps", 2),))

        def simn_apply(e):
            stepd(e, e.tensor_copy(out=gsm[:, 8:12], in_=PS[2][0][:, 0:4]))
            stepd(e, e.tensor_tensor(out=gsm[:, 0:4], in0=sim2d[:, 0:4],
                                     in1=gsm[:, 8:12], op=ALU.mult))
            stepd(e, e.tensor_tensor(out=gsm[:, 4:8], in0=sim2d[:, 4:8],
                                     in1=gsm[:, 8:12], op=ALU.mult))
            return e.tensor_copy(out=simnb[:], in_=gsm[:, 0:8])
        s.op("dve", simn_apply, reads=(("ps", 2), "sim2d"),
             writes=("simnb", "gsm"))
        dma_in(gsm[:], h["gpad"], writes=("gsm",))
        s.op("pool", lambda e: e.tensor_copy(out=bp2db[:], in_=bp2d[:]),
             reads=("bp2d",), writes=("bp2db",))
        jb = [(PS[3][0][0:1, 0:4], bp2db[:, c:c + 1], simnb[:, c * 4:(c + 1) * 4],
               c == 0, c == 1) for c in range(2)]
        mmseq(jb, reads=("bp2db", "simnb"), writes=(("ps", 3),))
        s.op("dve", lambda e: e.tensor_copy(out=srow[0:1, 0:4],
                                            in_=PS[3][0][0:1, 0:4]),
             reads=(("ps", 3),), writes=("srow",))
        jb = [(PS[3][0][:, 8:12], ones_f[:], srow[0:1, 0:4], True, True)]
        mmseq(jb, reads=("srow", "ones_f"), writes=(("ps", 3),))
        s.op("dve", lambda e: e.tensor_copy(out=bps4[:], in_=PS[3][0][:, 8:12]),
             reads=(("ps", 3),), writes=("bps4",))
        s.op("dve", lambda e: e.scalar_tensor_tensor(
            out=junkb[:, 0:PROJ], in0=bp_bc[:], scalar=1.0, in1=bp_bc[:],
            op0=ALU.mult, op1=ALU.mult, accum_out=cbc[:]),
            reads=("bp_bc",), writes=("junkb", "cbc"))

        s.op("dve", lambda e: e.tensor_scalar(
            out=srow[0:1, 12:13], in0=temp[:], scalar1=CLAMP_MAX, scalar2=0.5,
            op0=ALU.min, op1=ALU.mult),
            reads=("temp",), writes=("srow",))
        s.op("act", lambda e: e.activation(out=srow[0:1, 13:14],
                                           in_=srow[0:1, 12:13], func=AF.Tanh),
             reads=("srow",), writes=("srow",))

        def sc_fin(e):
            stepd(e, e.tensor_scalar(out=srow[0:1, 14:15], in0=srow[0:1, 13:14],
                                     scalar1=-1.0, scalar2=1.0, op0=ALU.mult,
                                     op1=ALU.add))
            stepd(e, e.reciprocal(out=srow[0:1, 14:15], in_=srow[0:1, 14:15]))
            stepd(e, e.tensor_scalar(out=srow[0:1, 13:14], in0=srow[0:1, 13:14],
                                     scalar1=1.0, scalar2=None, op0=ALU.add))
            return e.tensor_tensor(out=srow[0:1, 15:16], in0=srow[0:1, 13:14],
                                   in1=srow[0:1, 14:15], op=ALU.mult)
        s.op("dve", sc_fin, reads=("srow",), writes=("srow",))
        jb = [(PS[2][0][:, 4:5], ones_f[:], srow[0:1, 15:16], True, True)]
        mmseq(jb, reads=("srow", "ones_f"), writes=(("ps", 2),))
        s.op("dve", lambda e: e.tensor_copy(out=sc_bc[:], in_=PS[2][0][:, 4:5]),
             reads=(("ps", 2),), writes=("sc_bc",))

        c_S = consts[:, 0:1]
        c_HPI = consts[:, 1:2]
        c_HALF = consts[:, 2:3]
        c_NS = consts[:, 3:4]


        def tctx(ti):
            pp = ti % 2
            stp = STAT[pp]
            ks = (f"st1_{pp}", f"st2_{pp}", f"st3_{pp}", f"st4_{pp}")
            return dict(pp=pp, stp=stp, ks=ks, nrp=NRS[pp], knr=f"nrs_{pp}",
                        sdp=SEEDI[pp][:, 0:1], ksd=f"seedi_{pp}",
                        sdp2=SEEDI[pp][:, 1:2], ksd2=f"seedi2_{pp}",
                        pst=PSTAT[pp], kps=f"pstat_{pp}",
                        xin=XIN[pp], kx=f"x_in_{pp}",
                        kqt=KQT[pp], kk=f"kqT_{pp}")

        def ln_negmu(t):
            stp1, stp3 = t["stp"][0], t["stp"][2]
            k1, k3 = t["ks"][0], t["ks"][2]
            s.op("dve", lambda e, stp1=stp1, stp3=stp3: e.tensor_scalar(
                out=stp1[:], in0=stp3[:], scalar1=-1.0 / D, scalar2=None,
                op0=ALU.mult),
                reads=(k3,), writes=(k1,))

        def ln_rstd(t, eps):
            stp1, stp2, stp3, stp4 = t["stp"]
            k1, k2, k3, k4 = t["ks"]
            nrp, knr = t["nrp"], t["knr"]

            def emit(e):
                stepp(e, e.tensor_scalar(out=nrp[:, 1:2], in0=stp4[:],
                                         scalar1=1.0 / D, scalar2=eps,
                                         op0=ALU.mult, op1=ALU.add))
                stepp(e, e.tensor_tensor(out=nrp[:, 2:3], in0=stp1[:],
                                         in1=stp1[:], op=ALU.mult))
                return e.tensor_tensor(out=stp2[:], in0=nrp[:, 1:2],
                                       in1=nrp[:, 2:3], op=ALU.subtract)
            s.op("pool", emit, reads=(k1, k4), writes=(k2, knr), cost=1.1,
                 tag="rstd_w")
            rsqrt_nr(stp2, k2, stp3, k3, t["sdp"], t["ksd"], nrp, knr, iters=3)

        def ln_stats(t, eps, sum_i=2):
            stp1, stp2, stp3, stp4 = t["stp"]
            k1, k2, k3, k4 = t["ks"]
            sums, ksum = t["stp"][sum_i], t["ks"][sum_i]
            nrp, knr = t["nrp"], t["knr"]

            def emit(e, sums=sums):
                stepp(e, e.tensor_scalar(out=stp1[:], in0=sums[:],
                                         scalar1=1.0 / D, scalar2=None,
                                         op0=ALU.mult))
                e.tensor_scalar(out=nrp[:, 1:2], in0=stp4[:], scalar1=1.0 / D,
                                scalar2=eps, op0=ALU.mult, op1=ALU.add)
                stepp(e, e.tensor_tensor(out=nrp[:, 2:3], in0=stp1[:],
                                         in1=stp1[:], op=ALU.mult))
                return e.tensor_tensor(out=stp2[:], in0=nrp[:, 1:2],
                                       in1=nrp[:, 2:3], op=ALU.subtract)
            s.op("pool", emit, reads=(ksum, k4), writes=(k1, k2, knr),
                 cost=1.45, tag="lnst_w")
            rsqrt_nr(stp2, k2, stp3, k3, t["sdp"], t["ksd"], nrp, knr, iters=3)

            def emit_nb(e):
                stepp(e, e.tensor_tensor(out=stp4[:], in0=stp1[:], in1=stp3[:],
                                         op=ALU.mult))
                return e.tensor_scalar(out=stp4[:], in0=stp4[:], scalar1=-1.0,
                                       scalar2=None, op0=ALU.mult)
            s.op("pool", emit_nb, reads=(k1, k3), writes=(k4,), cost=0.72,
                 tag="ln_nb")

        def wrap_and_trig(src_key, src_ap, SKb=SKb, CKb=CKb, skk="SK",
                          ckk="CK"):
            s.op("act", lambda e, sa=src_ap: e.activation(
                out=A[:], in_=sa, func=AF.Tanh),
                reads=(src_key,), writes=("scrA",), cost=1.25, tag="wr_tanh")
            s.op("dve", lambda e: e.scalar_tensor_tensor(
                out=A[:], in0=A[:], scalar=INV_2PI, in1=ps_bc[:],
                op0=ALU.mult, op1=ALU.mult),
                reads=("scrA", "ps_bc"), writes=("scrA",), cost=1.28, tag="wr_t")
            s.op("dve", lambda e: e.tensor_scalar(
                out=U[:], in0=A[:], scalar1=KOFF, scalar2=None, op0=ALU.add),
                reads=("scrA",), writes=("scrU",), cost=0.7, tag="wr_u")
            s.op("dve", lambda e: e.scalar_tensor_tensor(
                out=C[:], in0=U[:], scalar=-KOFF, in1=A[:],
                op0=ALU.add, op1=ALU.subtract),
                reads=("scrU", "scrA"), writes=("scrC",), cost=1.28,
                tag="wr_nv")
            s.op("dve", lambda e: e.scalar_tensor_tensor(
                out=C[:], in0=C[:], scalar=-0.5, in1=C[:], op0=ALU.is_lt,
                op1=ALU.add),
                reads=("scrC",), writes=("scrC",), cost=1.28, tag="wr_w")
            s.op("act", lambda e, SKb=SKb: e.activation(
                out=SKb[:], in_=C[:], func=AF.Sin, scale=c_NS),
                reads=("scrC", "consts"), writes=(skk,), cost=1.25,
                tag="wr_sin")
            s.op("dve", lambda e: e.scalar_tensor_tensor(
                out=A[:], in0=C[:], scalar=-0.25, in1=C[:], op0=ALU.is_le,
                op1=ALU.add),
                reads=("scrC",), writes=("scrA",), cost=1.28, tag="wr_vc")
            s.op("act", lambda e, CKb=CKb: e.activation(
                out=CKb[:], in_=A[:], func=AF.Sin, scale=c_NS, bias=c_HPI),
                reads=("scrA", "consts"), writes=(ckk,), cost=1.25,
                tag="wr_cos")

        def f_xkey(ti):
            t = tctx(ti)
            s.op("act", lambda e, x=t["xin"]: e.activation(
                out=xbb[:], in_=x[:], func=AF.Identity),
                reads=(t["kx"],), writes=("xbb",), cost=1.2, tag="xb")
            transpose8(3, "xbb", xbb[:])
            evac_pair(3, xTb[:, 0:1024], "xTb")
            jobs = []
            for ec in range(NCH):
                for dc in range(NCH):
                    jobs.append((P[1][:, ec * 128:(ec + 1) * 128],
                                 wb["WkT"][:, dc * D + ec * 128:dc * D + (ec + 1) * 128],
                                 xTb[:, dc * 128:(dc + 1) * 128],
                                 dc == 0, dc == NCH - 1))
            mmseq(jobs, reads=("xTb",) + wkeys("WkT"), writes=(("ps", 1),))
            evac_pair(1, t["kqt"][:, 0:1024], t["kk"])

        def f_val_kp(ti):
            t = tctx(ti)
            kq = t["kqt"]
            jobs = []
            for nh in range(2):
                for dc in range(NCH):
                    jobs.append((PS[0][nh][:, 0:512],
                                 xTb[:, dc * 128:(dc + 1) * 128],
                                 wb["WvT"][:, dc * D + nh * 512:dc * D + (nh + 1) * 512],
                                 dc == 0, False))
                jobs.append((PS[0][nh][:, 0:512], ones2[:],
                             v_hl[0:2, nh * 512:(nh + 1) * 512], False, True))
            mmseq(jobs, reads=("xTb", "v_hl") + wkeys("WvT"),
                  writes=(("ps", 0),), tag="value")
            s.op("act", lambda e: e.activation(out=VAL[:], in_=P[0][:, :],
                                               func=AF.Identity),
                 reads=(("ps", 0),), writes=("VAL",), cost=1.2, tag="val_ev")
            jobs = []
            for nh in range(2):
                for ec in range(NCH):
                    jobs.append((PS[1][nh][:, 0:512],
                                 kq[:, ec * 128:(ec + 1) * 128],
                                 wb["WkpT"][:, ec * D + nh * 512:ec * D + (nh + 1) * 512],
                                 ec == 0, False))
                jobs.append((PS[1][nh][:, 0:512], ones2[:],
                             kp_hl[0:2, nh * 512:(nh + 1) * 512], False, True))
            mmseq(jobs, reads=(t["kk"], "kp_hl") + wkeys("WkpT"),
                  writes=(("ps", 1),), tag="kp")
            wrap_and_trig(("ps", 1), P[1][:, :])

        def f_qbm(ti):
            t = tctx(ti)
            jobs = []
            for ec in range(NCH):
                for dc in range(NCH):
                    jobs.append((P[1][:, ec * 128:(ec + 1) * 128],
                                 wb["WqT"][:, dc * D + ec * 128:dc * D + (ec + 1) * 128],
                                 xTb[:, dc * 128:(dc + 1) * 128],
                                 dc == 0, dc == NCH - 1))
            mmseq(jobs, reads=("xTb",) + wkeys("WqT"), writes=(("ps", 1),))
            s.op("dve", lambda e: e.tensor_tensor(out=xbb[:], in0=VAL[:],
                                                  in1=CKb[:], op=ALU.mult),
                 reads=("VAL", "CK"), writes=("xbb",))
            s.op("dve", lambda e: e.tensor_tensor(out=bsh[:], in0=VAL[:],
                                                  in1=SKb[:], op=ALU.mult),
                 reads=("VAL", "SK"), writes=("bsh",))
            evac_pair(1, t["kqt"][:, 0:1024], t["kk"])
            jobs = [(PS[0][nh][:, 0:512], LTs[:], xbb[:, nh * 512:(nh + 1) * 512],
                     True, True) for nh in range(2)]
            mmseq(jobs, reads=("xbb", "LT"), writes=(("ps", 0),))
            jobs = [(PS[1][nh][:, 0:512], LTs[:], bsh[:, nh * 512:(nh + 1) * 512],
                     True, True) for nh in range(2)]
            mmseq(jobs, reads=("bsh", "LT"), writes=(("ps", 1),))
            s.op("act", lambda e: e.activation(out=VAL[:], in_=P[0][:, :],
                                               func=AF.Identity),
                 reads=(("ps", 0),), writes=("VAL",), cost=1.15, tag="mr_ev")
            s.op("act", lambda e: e.activation(out=MIB[:], in_=P[1][:, :],
                                               func=AF.Identity),
                 reads=(("ps", 1),), writes=("MIB",), cost=1.15, tag="mi_ev")

        def f_qp(ti):
            t = tctx(ti)
            kq = t["kqt"]
            jobs = []
            for nh in range(2):
                for ec in range(NCH):
                    jobs.append((PS[3][nh][:, 0:512],
                                 kq[:, ec * 128:(ec + 1) * 128],
                                 wb["WqpT"][:, ec * D + nh * 512:ec * D + (nh + 1) * 512],
                                 ec == 0, False))
                jobs.append((PS[3][nh][:, 0:512], ones2[:],
                             qp_hl[0:2, nh * 512:(nh + 1) * 512], False, True))
            mmseq(jobs, reads=(t["kk"], "qp_hl") + wkeys("WqpT"),
                  writes=(("ps", 3),), tag="qp")
            wrap_and_trig(("ps", 3), P[3][:, :], SKq, CKq, "SKq", "CKq")

        def f_retr(ti):
            t = tctx(ti)
            stp3, stp4 = t["stp"][2], t["stp"][3]
            k3, k4 = t["ks"][2], t["ks"][3]
            s.op("dve", lambda e: e.tensor_tensor(
                out=VAL[:], in0=VAL[:], in1=CKq[:], op=ALU.mult),
                reads=("VAL", "CKq"), writes=("VAL",))
            s.op("dve", lambda e: e.tensor_tensor(
                out=MIB[:], in0=MIB[:], in1=SKq[:], op=ALU.mult),
                reads=("MIB", "SKq"), writes=("MIB",))
            s.op("dve", lambda e, stp3=stp3: e.scalar_tensor_tensor(
                out=MIB[:], in0=VAL[:], scalar=1.0, in1=MIB[:], op0=ALU.mult,
                op1=ALU.add, accum_out=stp3[:]),
                reads=("VAL", "MIB"), writes=("MIB", k3))
            ln_negmu(t)
            s.op("act", lambda e, stp4=stp4: e.activation(
                out=junkb[:], in_=MIB[:], func=AF.Square, accum_out=stp4[:]),
                reads=("MIB",), writes=("junkb", k4))
            ln_rstd(t, EPS_IN)

        def f_lnT(ti):
            t = tctx(ti)
            stp1 = t["stp"][0]
            k1 = t["ks"][0]
            s.op("dve", lambda e, stp1=stp1: e.tensor_scalar(
                out=bsh[:], in0=MIB[:], scalar1=stp1[:], scalar2=None,
                op0=ALU.add),
                reads=("MIB", k1), writes=("bsh",), cost=0.7, tag="lncast")
            transpose8(3, "bsh", bsh[:])
            evac_pair(3, tsh[:, 0:1024], "tsh")

        def f_wo(ti):
            t = tctx(ti)
            stp3 = t["stp"][2]
            k3 = t["ks"][2]
            proj_tok(2, "tsh", tsh[:], wb["WoT"], wkeys("WoT"))
            s.op("dve", lambda e, stp3=stp3: e.scalar_tensor_tensor(
                out=C[:], in0=P[2][:, :], scalar=stp3[:], in1=bo_bc[:, 0:1024],
                op0=ALU.mult, op1=ALU.add),
                reads=(("ps", 2), k3, "bo_bc"), writes=("scrC",))
            s.op("dve", lambda e, x=t["xin"], stp3=stp3: e.scalar_tensor_tensor(
                out=B[:], in0=x[:], scalar=2.0, in1=C[:], op0=ALU.mult,
                op1=ALU.add, accum_out=stp3[:]),
                reads=(t["kx"], "scrC"), writes=("scrB", k3))

        def f_ln1x2(ti):
            t = tctx(ti)
            stp1, stp2, stp3, stp4 = t["stp"]
            k1, k2, k3, k4 = t["ks"]
            s.op("dve", lambda e, stp1=stp1, stp3=stp3: e.tensor_scalar(
                out=stp1[:], in0=stp3[:], scalar1=-1.0 / D, scalar2=None,
                op0=ALU.mult),
                reads=(k3,), writes=(k1,), cost=0.2, tag="negmu1")
            s.op("act", lambda e, stp4=stp4: e.activation(
                out=junkb[:], in_=B[:], func=AF.Square, accum_out=stp4[:]),
                reads=("scrB",), writes=("junkb", k4), cost=1.25, tag="sq1")
            s.op("dve", lambda e, stp1=stp1: e.tensor_scalar(
                out=bsh[:], in0=B[:], scalar1=stp1[:], scalar2=None,
                op0=ALU.add),
                reads=("scrB", k1), writes=("bsh",), cost=0.7, tag="x2b")
            transpose8(3, "bsh", bsh[:])
            evac_pair(3, tsh2[:, 0:1024], "tsh2")
            ln_rstd(t, EPS)
            s.op("pool", lambda e, stp1=stp1, stp3=stp3, stp4=stp4:
                 e.tensor_tensor(out=stp4[:], in0=stp1[:], in1=stp3[:],
                                 op=ALU.mult),
                 reads=(k1, k3), writes=(k4,), cost=0.36, tag="nb1")

        def f_ffn1(ti):
            t = tctx(ti)
            stp3 = t["stp"][2]
            k3 = t["ks"][2]
            pst, kps = t["pst"], t["kps"]
            base_k = ("ps", 2)
            proj_tok(2, "tsh2", tsh2[:], wb["WfT"], wkeys("WfT"))
            jobs = [(PS[3][0][:, 0:PROJ], tsh2[:, dc * 128:(dc + 1) * 128],
                     WpTb[:, dc * PROJ:(dc + 1) * PROJ], dc == 0, dc == NCH - 1)
                    for dc in range(NCH)]
            mmseq(jobs, reads=("tsh2",) + wkeys("WpT"), writes=(("ps", 3),),
                  tag="Wp")
            jobs = [(PS[3][1][:, 0:RTOT], tsh2[:, dc * 128:(dc + 1) * 128],
                     WdTb[:, dc * RTOT:(dc + 1) * RTOT], dc == 0, dc == NCH - 1)
                    for dc in range(NCH)]
            mmseq(jobs, reads=("tsh2",) + wkeys("WdT"), writes=(("ps", 3),),
                  tag="Wd")
            s.op("act", lambda e: e.activation(out=projb[:],
                                               in_=PS[3][0][:, 0:PROJ],
                                               func=AF.Identity),
                 reads=(("ps", 3),), writes=("projb",), cost=0.6, tag="projb")
            s.op("act", lambda e, pst=pst: e.activation(
                out=junkb[:, 0:PROJ], in_=PS[3][0][:, 0:PROJ], func=AF.Square,
                accum_out=pst[:, 0:1]),
                reads=(("ps", 3),), writes=("junkb", kps), cost=0.6, tag="sqp")
            s.op("dve", lambda e, pst=pst: e.scalar_tensor_tensor(
                out=C[:, 0:PROJ], in0=PS[3][0][:, 0:PROJ], scalar=1.0,
                in1=bp_bc[:], op0=ALU.mult, op1=ALU.mult,
                accum_out=pst[:, 1:2]),
                reads=(("ps", 3), "bp_bc"), writes=("scrC", kps), cost=0.4,
                tag="bxp")
            s.op("dve", lambda e: e.tensor_copy(out=hsml[:],
                                                in_=PS[3][1][:, 0:RTOT]),
                 reads=(("ps", 3),), writes=("hsml",), cost=0.25, tag="h_ev")
            jobs = [(PS[3][0][:, PROJ + c * 128:PROJ + (c + 1) * 128],
                     projb[:, c * 128:(c + 1) * 128], I128[:], True, True)
                    for c in range(2)]
            mmseq(jobs, reads=("projb", "I128"), writes=(("ps", 3),),
                  tag="projT")
            s.op("dve", lambda e: e.tensor_copy(out=projT[:],
                                                in_=PS[3][0][:, PROJ:PROJ + 256]),
                 reads=(("ps", 3),), writes=("projT",), cost=0.4, tag="projT_ev")
            jobs = [(PS[0][0][:, 0:4], projT[:, c * 128:(c + 1) * 128],
                     simnb[:, c * 4:(c + 1) * 4], c == 0, c == 1)
                    for c in range(2)]
            mmseq(jobs, reads=("projT", "simnb"), writes=(("ps", 0),),
                  tag="logits")
            def smalls(e, pst=pst, stp3=stp3):
                stepp(e, e.tensor_tensor(out=pst[:, 2:3], in0=stp3[:],
                                         in1=stp3[:], op=ALU.mult))
                stepp(e, e.tensor_tensor(out=pst[:, 2:3], in0=pst[:, 2:3],
                                         in1=pst[:, 0:1], op=ALU.mult))
                stepp(e, e.tensor_tensor(out=pst[:, 4:5], in0=stp3[:],
                                         in1=pst[:, 1:2], op=ALU.mult))
                stepp(e, e.tensor_scalar(out=pst[:, 4:5], in0=pst[:, 4:5],
                                         scalar1=2.0, scalar2=None,
                                         op0=ALU.mult))
                stepp(e, e.tensor_tensor(out=pst[:, 4:5], in0=pst[:, 4:5],
                                         in1=cbc[:], op=ALU.add))
                return e.tensor_tensor(out=pst[:, 2:3], in0=pst[:, 2:3],
                                       in1=pst[:, 4:5], op=ALU.add)
            s.op("pool", smalls, reads=(kps, k3, "cbc"), writes=(kps,),
                 cost=1.8, tag="ssqp")
            rsqrt_nr(None, kps, None, kps, t["sdp2"], t["ksd2"],
                     pst, kps, clamp=1e12, iters=3,
                     src_ap=pst[:, 2:3], dst_ap=pst[:, 3:4],
                     nr_ap=pst[:, 4:5])
            s.op("pool", lambda e, pst=pst: e.tensor_tensor(
                out=pst[:, 3:4], in0=pst[:, 3:4], in1=sc_bc[:], op=ALU.mult),
                reads=(kps, "sc_bc"), writes=(kps,), cost=0.4, tag="vsm")
            s.op("dve", lambda e, stp3=stp3: e.scalar_tensor_tensor(
                out=gsm[:, 0:4], in0=PS[0][0][:, 0:4], scalar=stp3[:],
                in1=bps4[:], op0=ALU.mult, op1=ALU.add),
                reads=(("ps", 0), k3, "bps4"), writes=("gsm",),
                cost=0.3, tag="lg")

        def f_gate(ti):
            t = tctx(ti)
            stp3 = t["stp"][2]
            k3 = t["ks"][2]
            pst, kps = t["pst"], t["kps"]

            def gate1(e):
                stepd(e, e.max(gsm[:, 8:16], gsm[:, 0:8]))
                stepd(e, e.tensor_scalar(out=gsm[:, 16:20], in0=gsm[:, 0:4],
                                         scalar1=gsm[:, 8:9], scalar2=1e9,
                                         op0=ALU.is_lt, op1=ALU.mult))
                stepd(e, e.tensor_tensor(out=gsm[:, 16:20], in0=gsm[:, 16:20],
                                         in1=iota4[:], op=ALU.add))
                stepd(e, e.tensor_reduce(gsm[:, 20:21], gsm[:, 16:20], AX.X,
                                         ALU.min))
                return e.tensor_scalar(out=gsm[:, 24:28], in0=iota4[:],
                                       scalar1=gsm[:, 20:21], scalar2=None,
                                       op0=ALU.is_equal)
            s.op("dve", gate1, reads=("gsm", "iota4"), writes=("gsm",),
                 cost=1.8, tag="gate1")

            def hgate(e):
                inst = None
                for i, (off, r) in enumerate(zip(ROFF, RANKS)):
                    inst = e.tensor_scalar(out=hpb[:, off:off + r],
                                           in0=hsml[:, off:off + r],
                                           scalar1=gsm[:, 24 + i:25 + i],
                                           scalar2=None, op0=ALU.mult)
                return inst
            s.op("dve", hgate, reads=("hsml", "gsm"), writes=("hpb",), cost=0.8, tag="hgate")

            def dprep(e, pst=pst):
                stepd(e, e.tensor_scalar(out=gsm[:, 16:20], in0=gsm[:, 0:4],
                                         scalar1=-1.0, scalar2=gsm[:, 8:9],
                                         op0=ALU.mult, op1=ALU.add))
                return e.tensor_scalar(out=gsm[:, 16:20], in0=gsm[:, 16:20],
                                       scalar1=pst[:, 3:4], scalar2=None,
                                       op0=ALU.mult)
            s.op("dve", dprep, reads=("gsm", kps), writes=("gsm",),
                 cost=0.6, tag="dprep")
            s.op("act", lambda e: e.activation(out=gsm[:, 16:20],
                                               in_=gsm[:, 16:20],
                                               func=AF.Tanh, scale=c_HALF),
                 reads=("gsm", "consts"), writes=("gsm",), cost=0.45,
                 tag="gate_tanh")

            def gate2(e):
                stepd(e, e.tensor_scalar(out=gsm[:, 0:4], in0=gsm[:, 16:20],
                                         scalar1=-1.0, scalar2=1.0,
                                         op0=ALU.mult, op1=ALU.add))
                stepd(e, e.tensor_scalar(out=gsm[:, 16:20], in0=gsm[:, 16:20],
                                         scalar1=1.0, scalar2=None, op0=ALU.add))
                stepd(e, e.reciprocal(out=gsm[:, 16:20], in_=gsm[:, 16:20]))
                stepd(e, e.tensor_tensor(out=gsm[:, 0:4], in0=gsm[:, 0:4],
                                         in1=gsm[:, 16:20], op=ALU.mult))
                stepd(e, e.tensor_reduce(gsm[:, 28:29], gsm[:, 0:4], AX.X,
                                         ALU.add))
                return e.reciprocal(out=gsm[:, 28:29], in_=gsm[:, 28:29])
            s.op("dve", gate2, reads=("gsm",), writes=("gsm",), cost=2.2,
                 tag="gate2")
            s.op("pool", lambda e, pst=pst, stp3=stp3: e.tensor_tensor(
                out=pst[:, 6:7], in0=stp3[:], in1=gsm[:, 28:29], op=ALU.mult),
                reads=(k3, "gsm"), writes=(kps,), cost=0.36, tag="wmul")

        def f_ffn2(ti):
            t = tctx(ti)
            stp3, stp4 = t["stp"][2], t["stp"][3]
            k3, k4 = t["ks"][2], t["ks"][3]
            nrp, knr = t["nrp"], t["knr"]
            s.op("dve", lambda e, stp3=stp3, stp4=stp4: e.tensor_scalar(
                out=X2[:], in0=B[:], scalar1=stp3[:], scalar2=stp4[:],
                op0=ALU.mult, op1=ALU.add),
                reads=("scrB", k3, k4), writes=("X2",), cost=0.7, tag="x2f")
            jobs = [(PS[0][1][0:RTOT, 128:256], hpb[:], I128[:], True, True)]
            mmseq(jobs, reads=("hpb", "I128"), writes=(("ps", 0),), tag="hT")
            s.op("dve", lambda e: e.tensor_copy(out=hpT[:],
                                                in_=PS[0][1][0:RTOT, 128:256]),
                 reads=(("ps", 0),), writes=("hpT",), cost=0.3, tag="hpT_ev")
            s.op("dve", lambda e, stp3=stp3: e.scalar_tensor_tensor(
                out=WFS[:], in0=P[2][:, :], scalar=stp3[:], in1=X2[:],
                op0=ALU.mult, op1=ALU.add),
                reads=(("ps", 2), k3, "X2"), writes=("WFS",), cost=1.3,
                tag="wfs")
            jobs = [(PS[2][nh][:, 0:512], hpT[:],
                     WuTb[:, nh * 512:(nh + 1) * 512], True, True)
                    for nh in range(2)]
            mmseq(jobs, reads=("hpT",) + wuk, writes=(("ps", 2),))

            pst, kps = t["pst"], t["kps"]
            s.op("dve", lambda e, stp1=t["stp"][0], pst=pst:
                 e.scalar_tensor_tensor(
                     out=B[:], in0=P[2][:, :], scalar=pst[:, 6:7], in1=WFS[:],
                     op0=ALU.mult, op1=ALU.add, accum_out=stp1[:]),
                reads=(("ps", 2), "WFS", kps), writes=("scrB", t["ks"][0]),
                cost=1.3, tag="pre3")
            s.op("act", lambda e, stp4=stp4: e.activation(
                out=junkb[:], in_=B[:], func=AF.Square, accum_out=stp4[:]),
                reads=("scrB",), writes=("junkb", k4), cost=1.25, tag="sq3")
            ln_stats(t, EPS, sum_i=0)

        def f_out(ti):
            t = tctx(ti)
            stp3, stp4 = t["stp"][2], t["stp"][3]
            k3, k4 = t["ks"][2], t["ks"][3]
            s.op("dve", lambda e, stp3=stp3, stp4=stp4: e.tensor_scalar(
                out=OUTB[:], in0=B[:], scalar1=stp3[:], scalar2=stp4[:],
                op0=ALU.mult, op1=ALU.add),
                reads=("scrB", k3, k4), writes=("OUTB",), cost=0.7, tag="outap")
            s.op("out", lambda e, r0=ti * 128: e.dma_start(
                out=hy[r0:r0 + 128, :], in_=OUTB[:]),
                reads=("OUTB",), writes=())

        f_xkey(0)
        f_val_kp(0)
        f_qbm(0)
        if NT > 1:
            tn = tctx(1)
            dma_in(tn["xin"][:], hx[128:256, :], writes=(tn["kx"],))
            f_xkey(1)
        for i in range(NT):
            f_qp(i)
            f_retr(i)
            if i > 0:
                f_out(i - 1)
            if i + 1 < NT:
                f_val_kp(i + 1)
            f_lnT(i)
            f_wo(i)
            if i + 1 < NT:
                f_qbm(i + 1)
            f_ln1x2(i)
            f_ffn1(i)
            if i + 2 < NT:
                tn = tctx(i + 2)
                dma_in(tn["xin"][:], hx[(i + 2) * 128:(i + 3) * 128, :],
                       writes=(tn["kx"],))
                f_xkey(i + 2)
            f_gate(i)
            f_ffn2(i)
        f_out(NT - 1)

        s.op("pool", lambda e: e.nop(),
             extra_waits=tuple((ch, s.cnt[ch]) for ch in DMA_OUT if s.cnt[ch]))

        s.reorder(keep_order=tuple(
            os.environ.get("KRO", "pe,dma,sp").split(",")))
        qmap = {"pe": "tensor", "act": "scalar", "dve": "vector",
                "pool": "gpsimd", "sp": "sync"}
        with nc.Block() as block:
            s.emit(sems, block, qmap)

    return nc


_PROG = None


def kernel(**inputs):
    global _PROG
    x = np.ascontiguousarray(np.asarray(inputs["L0_attn_x"], np.float32))
    assert x.shape == (8, S, D)
    hp = host_prep(inputs)
    if _PROG is None:
        _PROG = build_core_program()
    in_maps = []
    for b in range(8):
        m = {"hx": x[b]}
        m.update(hp)
        in_maps.append(m)
    res = run_bass_kernel_spmd(_PROG, in_maps, core_ids=list(range(8)))
    out = np.stack([np.asarray(r["hy"]) for r in res.results], axis=0)
    return out.astype(np.float32)


# version 13 (final)
# speedup vs baseline: 1.6553x; 1.0610x over previous
import os
import numpy as np
import ml_dtypes
import concourse.bass as bass
import concourse.mybir as mybir
from concourse.bass_utils import run_bass_kernel_spmd

F32 = mybir.dt.float32
BF16 = mybir.dt.bfloat16
I32 = mybir.dt.int32
AF = mybir.ActivationFunctionType
ALU = mybir.AluOpType
AX = mybir.AxisListType

D = 1024
S = 4096
NCH = 8
NT = int(os.environ.get("KNT", "32"))
CHUNK = 64
EPS = 1e-5
EPS_IN = 1024 * 1e-5
CLAMP_MAX = float(np.log(1.0 / 0.01))
RANKS = [1, 2, 4, 8]
ROFF = [0, 1, 3, 7]
RTOT = 15
PROJ = 256

S_SCALE = float(np.nextafter(np.float32(2 * np.pi), np.float32(0)))
HALF_PI = float(np.float32(np.pi / 2))
INV_2PI = 1.0 / (2.0 * np.pi)
KOFF = 64.0

SC_C2 = float(np.float32(16.0) - np.float32(S_SCALE))
assert SC_C2 + S_SCALE == 16.0
_d = np.float32(-(2.0 * np.pi - S_SCALE))
while (16.0 - float(_d)) - SC_C2 >= 2.0 * np.pi:
    _d = np.nextafter(_d, np.float32(0))
SC_DELTA = float(_d)
assert float(np.float32(np.float32(SC_DELTA) + np.float32(-16.0))) == -16.0
B_C3 = float(np.float32(2.0) - np.float32(HALF_PI))
assert B_C3 + HALF_PI == 2.0
_db = np.float32(np.pi / 2 - HALF_PI)
while (2.0 + float(_db)) - B_C3 > np.pi / 2:
    _db = np.nextafter(_db, -np.inf)
B_DELTA = float(_db)
assert float(np.float32(np.float32(B_DELTA) + np.float32(2.0))) == 2.0
assert 0.0 <= np.pi / 2 - ((2.0 + B_DELTA) - B_C3) < 1e-13
assert 0.0 < 2.0 * np.pi - ((16.0 - SC_DELTA) - SC_C2) < 1e-12

ENGINES = ("pe", "act", "dve", "pool", "sp")
DMA_IN = tuple(f"in{i}" for i in range(4))
DMA_OUT = tuple(f"out{i}" for i in range(2))
DMA_CH = DMA_IN + DMA_OUT


class Sched:
    def __init__(self):
        self.ops = []
        self.cnt = {e: 0 for e in ENGINES + DMA_CH}
        self.writer = {}
        self.readers = {}
        self.need_inc = set()
        self.meta = []
        self.din = 0
        self.dout = 0

    def op(self, eng, emit, reads=(), writes=(), extra_waits=(), cost=None,
           tag=""):
        if eng == "in":
            eng = DMA_IN[self.din % len(DMA_IN)]
            self.din += 1
        elif eng == "out":
            eng = DMA_OUT[self.dout % len(DMA_OUT)]
            self.dout += 1
        step = 16 if eng in DMA_CH else 1
        waits = set(extra_waits)
        if eng in DMA_CH and self.cnt[eng] > 0:
            waits.add((eng, self.cnt[eng]))
        for k in reads:
            w = self.writer.get(k)
            if w is not None:
                waits.add(w)
        for k in writes:
            w = self.writer.get(k)
            if w is not None:
                waits.add(w)
            for r in self.readers.get(k, ()):
                waits.add(r)
        self.cnt[eng] += step
        me = (eng, self.cnt[eng])
        for k in reads:
            self.readers.setdefault(k, []).append(me)
        for k in writes:
            self.writer[k] = me
            self.readers[k] = []
        for w in waits:
            self.need_inc.add(w)
        if eng in DMA_CH:
            self.need_inc.add(me)
        if cost is None:
            cost = {"pe": 0.3, "act": 1.2, "dve": 1.2, "pool": 0.25,
                    "sp": 0.05}.get(eng, 2.5)
            if eng.startswith("in") or eng.startswith("out"):
                cost = 2.5
        self.meta.append((cost, tag))
        self.ops.append((eng, emit, sorted(waits), me))
        return me

    def reorder(self, gap=0.15, xgap=0.12, keep_order=()):
        n = len(self.ops)
        byme = {op[3]: i for i, op in enumerate(self.ops)}
        succ = [[] for _ in range(n)]
        pred = []
        last_on = {}
        import os as _os
        _spec = _os.environ.get("KDVECUT", "0")
        if ":" in _spec:
            _lo, _hi = _spec.split(":")
            dve_lo, dve_hi = int(_lo), int(_hi)
        else:
            dve_lo, dve_hi = 0, int(_spec)
        dve_rank = 0
        for i, (eng, emitfn, waits, me) in enumerate(self.ops):
            ws = [byme[w] for w in waits]
            grp = eng
            if grp.startswith("in") or grp.startswith("out"):
                grp = "dma"
            chain = grp in keep_order
            if grp == "dve":
                if not (dve_lo <= dve_rank < dve_hi):
                    chain = True
                    grp = "dvechain"
                dve_rank += 1
            if chain:
                p = last_on.get(grp)
                if p is not None and p not in ws:
                    ws.append(p)
                last_on[grp] = i
            pred.append(ws)
            for w in ws:
                succ[w].append(i)
        cost = [self.meta[i][0] for i in range(n)]
        h = [0.0] * n
        for i in range(n - 1, -1, -1):
            m = 0.0
            for t in succ[i]:
                if h[t] > m:
                    m = h[t]
            h[i] = cost[i] + m
        qfree = {}
        nd = [0] * n
        dep_done = [0.0] * n
        readyset = [i for i in range(n) if not pred[i]]
        order = []
        while readyset:
            best = None
            bestk = None
            for i in readyset:
                est = max(qfree.get(self.ops[i][0], 0.0), dep_done[i])
                k = (est, -h[i])
                if bestk is None or k < bestk:
                    bestk = k
                    best = i
            i = best
            readyset.remove(i)
            t1 = bestk[0] + cost[i]
            qfree[self.ops[i][0]] = t1 + gap
            order.append(i)
            for t in succ[i]:
                nd[t] += 1
                if dep_done[t] < t1 + xgap:
                    dep_done[t] = t1 + xgap
                if nd[t] == len(pred[t]):
                    readyset.append(t)
        assert len(order) == n, (len(order), n)
        self.ops = [self.ops[i] for i in order]
        self.meta = [self.meta[i] for i in order]
        remap_me = {}
        cnt = {e: 0 for e in self.cnt}
        for (eng, emitfn, waits, me) in self.ops:
            step = 16 if eng in DMA_CH else 1
            cnt[eng] += step
            remap_me[me] = (eng, cnt[eng])
        self.ops = [(eng, emitfn, sorted(remap_me[w] for w in waits),
                     remap_me[me]) for (eng, emitfn, waits, me) in self.ops]
        self.need_inc = {remap_me[m] for m in self.need_inc}
        self.cnt = cnt

    def simulate(self, gap=0.15, xgap=0.12):
        qfree = {}
        done = {}
        rows = []
        for idx, (eng, _, waits, me) in enumerate(self.ops):
            cost, tag = self.meta[idx]
            q = eng if eng in ENGINES else ("dma" + eng[-1] if eng[-1].isdigit()
                                            else eng)
            t0 = qfree.get(q, 0.0)
            src_max = 0.0
            for w in waits:
                src_max = max(src_max, done.get(w, 0.0) + xgap)
            t0 = max(t0, src_max)
            t1 = t0 + cost
            qfree[q] = t1 + gap
            done[me] = t1
            rows.append((t0, t1, eng, tag, me, waits))
        busy = {}
        for idx, (eng, _, _, _) in enumerate(self.ops):
            busy[eng] = busy.get(eng, 0.0) + self.meta[idx][0]
        return rows, busy

    def emit(self, sems, block, qmap):
        import bisect
        inc_list = {}
        for (e, c) in self.need_inc:
            inc_list.setdefault(e, []).append(c)
        for e in inc_list:
            inc_list[e].sort()

        def remap(src, val):
            lst = inc_list.get(src, ())
            r = bisect.bisect_right(lst, val)
            assert r > 0 and lst[r - 1] == val, (src, val)
            return r * (16 if src in DMA_CH else 1)

        per_q = {e: [] for e in ENGINES}
        for eng, emitfn, waits, me in self.ops:
            q = eng if eng in per_q else ("sp" if eng.startswith("in") else "pool")
            per_q[q].append((emitfn, waits, me if me in self.need_inc else None, eng))
        for q, items in per_q.items():
            if not items:
                continue

            def make_body(items):
                def body(e):
                    last = {}
                    for emitfn, waits, inc, semeng in items:
                        for (src, val) in waits:
                            v = remap(src, val)
                            if last.get(src, -1) >= v:
                                continue
                            e.wait_ge(sems[src], v)
                            last[src] = v
                        inst = emitfn(e)
                        if inc is not None:
                            inst.then_inc(sems[semeng],
                                          16 if semeng in DMA_CH else 1)
                return body
            getattr(block, qmap[q])(make_body(items))


def host_prep(inp):
    f = np.float32
    T = lambda a: np.ascontiguousarray(np.asarray(a, f).T)
    chunk2d = lambda v: np.ascontiguousarray(np.asarray(v, f).reshape(NCH, 128).T)
    bcast = lambda v, n: np.ascontiguousarray(
        np.broadcast_to(np.asarray(v, f).reshape(1, n), (128, n)))

    d = {}
    d["WkT"] = T(inp["L0_attn_Wk"])
    d["WvT"] = T(inp["L0_attn_Wv"])
    d["WqT"] = T(inp["L0_attn_Wq"])
    d["WkpT"] = T(inp["L0_attn_Wkp"])
    d["WqpT"] = T(inp["L0_attn_Wqp"])
    d["WoT"] = T(inp["L0_attn_Wo"])
    d["WfT"] = T(inp["L0_ffn_W"])
    d["WpT"] = T(inp["L0_ffn_Wp"])
    wd = np.concatenate([np.asarray(inp[f"L0_ffn_Wd{i}"], f) for i in range(4)], 0)
    d["WdT"] = T(wd)
    wu = np.concatenate([np.asarray(inp[f"L0_ffn_Wu{i}"], f) for i in range(4)], 1)
    d["WuT"] = T(wu)

    d["bk2d"] = chunk2d(inp["L0_attn_bk"])
    d["bq2d"] = chunk2d(inp["L0_attn_bq"])
    d["lnb2d"] = chunk2d(inp["L0_attn_ln_b"])
    d["g2d"] = chunk2d(inp["L0_attn_ln_g"])
    d["bv_row"] = np.asarray(inp["L0_attn_bv"], f).reshape(1, D).copy()
    d["ps_bc"] = bcast(inp["L0_attn_phase_scale"], D)
    d["bp_bc"] = bcast(inp["L0_ffn_bp"], PROJ)
    d["bp2d"] = np.ascontiguousarray(np.asarray(inp["L0_ffn_bp"], f).reshape(2, 128).T)
    d["bkp_row"] = np.asarray(inp["L0_attn_bkp"], f).reshape(1, D).copy()
    d["bqp_row"] = np.asarray(inp["L0_attn_bqp"], f).reshape(1, D).copy()
    d["bo_bc"] = bcast(inp["L0_attn_bo"], D)
    sim = np.asarray(inp["L0_ffn_sim"], f).reshape(2, 128, 4).transpose(1, 0, 2)
    d["sim2d"] = np.ascontiguousarray(sim.reshape(128, 8))
    d["temp"] = np.asarray(inp["L0_ffn_temp"], f).reshape(1, 1).copy()

    L = np.zeros((128, 128), f)
    for b in range(2):
        L[b * CHUNK:(b + 1) * CHUNK, b * CHUNK:(b + 1) * CHUNK] = \
            np.tril(np.ones((CHUNK, CHUNK), f))
    d["LTb"] = np.ascontiguousarray(L.T).astype(ml_dtypes.bfloat16)
    d["I128b"] = np.eye(128, dtype=ml_dtypes.bfloat16)
    d["ones_f"] = np.ones((1, 128), f)
    d["ones2b"] = np.ones((2, 128), dtype=ml_dtypes.bfloat16)
    d["ones_c"] = np.ones((128, 1), f)
    d["magic"] = np.full((128, 1), 0x5F3759DF, np.int32)
    consts = np.zeros((128, 8), f)
    consts[:, 0] = S_SCALE
    consts[:, 1] = HALF_PI
    consts[:, 2] = 0.5
    consts[:, 3] = -S_SCALE
    consts[:, 4] = SC_DELTA
    consts[:, 5] = B_DELTA
    d["consts"] = consts
    d["iota4"] = np.ascontiguousarray(
        np.broadcast_to(np.arange(4, dtype=f), (128, 4)))
    gpad = np.full((128, 32), 0.0, f)
    gpad[:, 4:8] = -1e30
    d["gpad"] = gpad
    return d


def build_core_program():
    nc = bass.Bass("TRN2", target_bir_lowering=False)

    hx = nc.dram_tensor("hx", [S, D], F32, kind="ExternalInput").ap()
    hy = nc.dram_tensor("hy", [S, D], F32, kind="ExternalOutput").ap()
    W_NAMES = ["WkT", "WvT", "WqT", "WkpT", "WqpT", "WoT", "WfT"]
    h = {}
    for n in W_NAMES:
        h[n] = nc.dram_tensor(n, [D, D], F32, kind="ExternalInput").ap()
    h["WpT"] = nc.dram_tensor("WpT", [D, PROJ], F32, kind="ExternalInput").ap()
    h["WdT"] = nc.dram_tensor("WdT", [D, RTOT], F32, kind="ExternalInput").ap()
    h["WuT"] = nc.dram_tensor("WuT", [RTOT, D], F32, kind="ExternalInput").ap()
    for n, shp in [("bk2d", [128, 8]), ("bq2d", [128, 8]), ("lnb2d", [128, 8]),
                   ("g2d", [128, 8]), ("bv_row", [1, D]), ("ps_bc", [128, D]),
                   ("bp_bc", [128, PROJ]), ("bp2d", [128, 2]), ("bkp_row", [1, D]),
                   ("bqp_row", [1, D]), ("bo_bc", [128, D]), ("sim2d", [128, 8]),
                   ("temp", [1, 1]), ("ones_f", [1, 128]),
                   ("ones_c", [128, 1]), ("consts", [128, 8]),
                   ("gpad", [128, 32])]:
        h[n] = nc.dram_tensor(n, shp, F32, kind="ExternalInput").ap()
    h["iota4"] = nc.dram_tensor("iota4", [128, 4], F32,
                                kind="ExternalInput").ap()
    h["LTb"] = nc.dram_tensor("LTb", [128, 128], BF16, kind="ExternalInput").ap()
    h["ones2b"] = nc.dram_tensor("ones2b", [2, 128], BF16, kind="ExternalInput").ap()
    h["I128b"] = nc.dram_tensor("I128b", [128, 128], BF16, kind="ExternalInput").ap()
    h["magic"] = nc.dram_tensor("magic", [128, 1], I32, kind="ExternalInput").ap()
    hlscr = nc.dram_tensor("hlscr", [1, 2 * D], BF16, kind="Internal").ap()

    from contextlib import ExitStack
    ctx = ExitStack()
    sb = lambda name, shape, dt: ctx.enter_context(nc.sbuf_tensor(name, shape, dt))
    with ctx:
        wb = {n: sb(n + "_b", [128, NCH * D], BF16) for n in W_NAMES}
        WpTb = sb("WpT_b", [128, NCH * PROJ], BF16)
        WdTb = sb("WdT_b", [128, NCH * RTOT], BF16)
        WuTb = sb("WuT_b", [RTOT, D], BF16)
        LTs = sb("LT_b", [128, 128], BF16)
        I128 = sb("I128_b", [128, 128], BF16)

        bk2d = sb("bk2d_s", [128, 8], F32)
        bq2d = sb("bq2d_s", [128, 8], F32)
        lnb2d = sb("lnb2d_s", [128, 8], F32)
        g2d = sb("g2d_s", [128, 8], F32)
        b2db = sb("b2d_bb", [128, 24], BF16)
        ps_bc = sb("ps_bc_s", [128, D], F32)
        bp_bc = sb("bp_bc_s", [128, PROJ], F32)
        bp2d = sb("bp2d_s", [128, 2], F32)
        bp2db = sb("bp2d_bb", [128, 2], BF16)
        bps4 = sb("bps4_s", [128, 4], F32)
        cbc = sb("cbc_s", [128, 1], F32)
        rowtmp_t = None
        rowbuf = None
        kp_hl = sb("kp_hl", [2, D], BF16)
        qp_hl = sb("qp_hl", [2, D], BF16)
        v_hl = sb("v_hl", [2, D], BF16)
        ones2 = sb("ones2_s", [2, 128], BF16)
        iota4 = sb("iota4_s", [128, 4], F32)
        gidx = sb("gidx_s", [128, 8], mybir.dt.uint32)
        bo_bc = sb("bo_bc_s", [128, D], F32)
        sim2d = sb("sim2d_s", [128, 8], F32)
        simnb = sb("simn_b", [128, 8], BF16)
        temp = sb("temp_s", [1, 1], F32)
        ones_f = sb("ones_f_s", [1, 128], F32)
        ones_c = sb("ones_c_s", [128, 1], F32)
        consts = sb("consts_s", [128, 8], F32)
        magic = sb("magic_s", [128, 1], I32)
        sc_bc = sb("sc_bc_s", [128, 1], F32)

        x_in = sb("x_in", [128, D], F32)
        A = sb("scrA", [128, D], F32)
        B = sb("scrB", [128, D], F32)
        C = sb("scrC", [128, D], F32)
        U = sb("scrU", [128, D], I32)
        SKb = sb("sinK", [128, D], BF16)
        CKb = sb("cosK", [128, D], BF16)
        SKq = sb("sinQ", [128, D], BF16)
        CKq = sb("cosQ", [128, D], BF16)
        WFS = sb("wfs", [128, D], F32)
        VAL = sb("val", [128, D], F32)
        X2 = sb("x2", [128, D], F32)
        xbb = sb("xb_b", [128, D], BF16)
        bsh = sb("bshared", [128, D], BF16)
        tsh = sb("tshared", [128, D], BF16)
        tsh2 = sb("tshared2", [128, D], BF16)
        kqT = sb("kqT", [128, D], BF16)
        junkb = sb("junk_b", [128, D], BF16)
        proj = sb("proj_s", [128, PROJ], F32)
        projb = sb("proj_bb", [128, PROJ], BF16)
        projT = sb("projT_b", [128, 2 * 128], BF16)
        hsml = sb("h_small", [128, RTOT], F32)
        hpb = sb("hp_b", [128, RTOT], BF16)
        hpT = sb("hpT_b", [RTOT, 128], BF16)
        gsm = sb("g_small", [128, 32], F32)
        lg = gsm[:, 0:4]
        STAT = [[sb(f"stat{p}_{i}", [128, 1], F32) for i in range(4)]
                for p in range(2)]
        NRS = [sb(f"nr_small{p}", [128, 4], F32) for p in range(2)]
        PSTAT = [sb(f"pstat{p}", [128, 8], F32) for p in range(2)]
        SEEDI = [sb(f"seed_i{p}", [128, 2], I32) for p in range(2)]
        seed4 = sb("seed4_i", [1, 8], I32)
        srow = sb("srow", [1, 16], F32)
        st1, st2, st3, st4 = STAT[0]
        nrs = NRS[0]
        seedi = SEEDI[0]

        xTb = sb("xT_b", [128, D], BF16)
        OUTB = sb("out_b", [128, D], F32)
        rowtmp = OUTB[0:1, :]
        rowbuf = X2[0:1, :]
        MIB = sb("mi_b", [128, D], F32)
        KQT = [kqT, sb("kqT2", [128, D], BF16)]
        XIN = [x_in, sb("x_in2", [128, D], F32)]

        P = [ctx.enter_context(nc.psum_tensor(f"P{i}", [128, 1024], F32))
             for i in range(4)]
        PS = [[P[i][:, 0:512], P[i][:, 512:1024]] for i in range(4)]

        sems = {}
        for e in ENGINES + DMA_CH:
            sems[e] = ctx.enter_context(nc.semaphore("sem_" + e))
        chain_sems = {e: ctx.enter_context(nc.semaphore("chain_" + e))
                      for e in ("dve", "act", "pool")}
        chain_cnt = {e: 0 for e in chain_sems}

        def mkstep(eng):
            def step(e, inst):
                chain_cnt[eng] += 1
                inst.then_inc(chain_sems[eng], 1)
                e.wait_ge(chain_sems[eng], chain_cnt[eng])
            return step
        stepd = mkstep("dve")
        stepp = mkstep("pool")

        s = Sched()
        wkeys = lambda n: tuple((n, c) for c in range(NCH))

        def dma_in(dst_ap, src_ap, writes, reads=()):
            return s.op("in", lambda e, d=dst_ap, sr=src_ap: e.dma_start(out=d, in_=sr),
                        reads=reads, writes=writes)

        def mmseq(jobs, reads, writes, stop_last=True, tag="mm"):
            jobs = list(jobs)
            cost = sum(max(j[2].shape[-1], 70) / 2400.0 + 0.015 for j in jobs)

            def emit(e, jobs=jobs):
                inst = None
                for (o, l, r, st, sp) in jobs:
                    inst = e.matmul(out=o, lhsT=l, rhs=r, start=st, stop=sp)
                return inst
            return s.op("pe", emit, reads=reads, writes=writes, cost=cost,
                        tag=tag)

        def proj_tok(slot, lhsT_key, lhsT_ap, wtile, wkey_t, stop_last=True):
            jobs = []
            for nh in range(2):
                for dc in range(NCH):
                    jobs.append((PS[slot][nh][:, 0:512],
                                 lhsT_ap[:, dc * 128:(dc + 1) * 128],
                                 wtile[:, dc * D + nh * 512:dc * D + (nh + 1) * 512],
                                 dc == 0, (dc == NCH - 1) and stop_last))
            return mmseq(jobs, reads=(lhsT_key,) + wkey_t,
                         writes=(("ps", slot),))

        def feat_proj(slot, wtile, wkey_t):
            jobs = []
            for ec in range(NCH):
                bank = PS[slot][ec // 4]
                for dc in range(NCH):
                    jobs.append((bank[:, (ec % 4) * 128:(ec % 4 + 1) * 128],
                                 wtile[:, dc * D + ec * 128:dc * D + (ec + 1) * 128],
                                 tsh[:, dc * 128:(dc + 1) * 128],
                                 dc == 0, dc == NCH - 1))
            return mmseq(jobs, reads=("tsh",) + wkey_t, writes=(("ps", slot),))

        def transpose8(slot, src_key, src_ap):
            jobs = []
            for c in range(NCH):
                bank = PS[slot][c // 4]
                jobs.append((bank[:, (c % 4) * 128:(c % 4 + 1) * 128],
                             src_ap[:, c * 128:(c + 1) * 128], I128[:], True, True))
            return mmseq(jobs, reads=(src_key, "I128"), writes=(("ps", slot),))

        def evac_pair(slot, dst_ap, dst_key, eng="act"):
            if eng == "act":
                def emit(e, sl=slot, d=dst_ap):
                    return e.activation(out=d, in_=P[sl][:, :], func=AF.Identity)
            else:
                def emit(e, sl=slot, d=dst_ap):
                    return e.tensor_copy(out=d, in_=P[sl][:, :])
            return s.op(eng, emit, reads=(("ps", slot),), writes=(dst_key,),
                        cost=1.2, tag=f"evac{slot}")

        def evac_pair_add(slot, bcast_tile, bkey, dst, dst_key):
            def emit(e, sl=slot, bc=bcast_tile, d=dst):
                return e.tensor_tensor(out=d[:, 0:1024], in0=P[sl][:, :],
                                       in1=bc[:, 0:1024], op=ALU.add)
            return s.op("dve", emit, reads=(("ps", slot), bkey),
                        writes=(dst_key,), cost=1.25, tag=f"evadd{slot}")

        def rsqrt_nr(src, src_key, dst, dst_key, sdi, sdi_key, nr, nr_key,
                     clamp=None, iters=4, src_ap=None, dst_ap=None, nr_ap=None):
            if src_ap is None:
                src_ap = src[:]
            if dst_ap is None:
                dst_ap = dst[:]
            if nr_ap is None:
                nr_ap = nr[:, 0:1]

            def emit_seed(e, src_ap=src_ap, dst_ap=dst_ap, sdi=sdi):
                sbits = src_ap.bitcast(I32)
                stepd(e, e.tensor_scalar(out=sdi, in0=sbits, scalar1=1,
                                         scalar2=None,
                                         op0=ALU.arith_shift_right))
                stepd(e, e.tensor_tensor(out=sdi, in0=magic[:],
                                         in1=sdi, op=ALU.subtract))
                return e.tensor_copy(out=dst_ap, in_=sdi.bitcast(F32))
            s.op("dve", emit_seed, reads=(src_key, "magic"),
                 writes=(dst_key, sdi_key), cost=1.0, tag="nr_seed")

            def emit_nr(e, src_ap=src_ap, dst_ap=dst_ap, nr_ap=nr_ap):
                inst = None
                for it in range(iters):
                    stepp(e, e.tensor_tensor(out=nr_ap, in0=dst_ap,
                                             in1=dst_ap, op=ALU.mult))
                    stepp(e, e.tensor_tensor(out=nr_ap, in0=nr_ap,
                                             in1=src_ap, op=ALU.mult))
                    stepp(e, e.tensor_scalar(out=nr_ap, in0=nr_ap,
                                             scalar1=-0.5, scalar2=1.5,
                                             op0=ALU.mult, op1=ALU.add))
                    inst = e.tensor_tensor(out=dst_ap, in0=dst_ap,
                                           in1=nr_ap, op=ALU.mult)
                    if it < iters - 1 or clamp is not None:
                        stepp(e, inst)
                if clamp is not None:
                    inst = e.tensor_scalar(out=dst_ap, in0=dst_ap, scalar1=clamp,
                                           scalar2=None, op0=ALU.min)
                return inst
            return s.op("pool", emit_nr, reads=(src_key, dst_key),
                        writes=(dst_key, nr_key), cost=0.36 * (4 * iters + 1),
                        tag="nr_loop")

        def _ap(t):
            return t if isinstance(t, bass.AP) else t[:]
        for name, tile in [("bk2d", bk2d), ("bq2d", bq2d), ("lnb2d", lnb2d),
                           ("g2d", g2d), ("ps_bc", ps_bc),
                           ("bp_bc", bp_bc), ("bp2d", bp2d), ("bo_bc", bo_bc),
                           ("iota4", iota4),
                           ("sim2d", sim2d), ("temp", temp), ("ones_f", ones_f),
                           ("ones_c", ones_c), ("consts", consts)]:
            dma_in(_ap(tile), h[name], writes=(name,))
        s.op("dve", lambda e: e.tensor_scalar(
            out=consts[:, 3:4], in0=consts[:, 4:5], scalar1=-16.0,
            scalar2=-SC_C2, op0=ALU.add, op1=ALU.subtract),
            reads=("consts",), writes=("consts",), cost=0.2, tag="fixc")
        s.op("dve", lambda e: e.tensor_scalar(
            out=consts[:, 1:2], in0=consts[:, 5:6], scalar1=2.0,
            scalar2=B_C3, op0=ALU.add, op1=ALU.subtract),
            reads=("consts",), writes=("consts",), cost=0.2, tag="fixb")
        dma_in(LTs[:], h["LTb"], writes=("LT",))
        dma_in(ones2[:], h["ones2b"], writes=("ones2",))

        dma_in(I128[:], h["I128b"], writes=("I128",))
        dma_in(magic[:], h["magic"], writes=("magic",))
        dma_in(x_in[:], hx[0:128, :], writes=("x_in_0",))

        stage = [(A, "scrA"), (B, "scrB")]
        cast_engines = ["dve", "act", "pool"]
        ei = 0
        for wi, wname in enumerate(W_NAMES + ["WpT", "WdT"]):
            src = h[wname]
            ncols = {"WpT": PROJ, "WdT": RTOT}.get(wname, D)
            dstt = {"WpT": WpTb, "WdT": WdTb}.get(wname, wb.get(wname))
            for c in range(NCH):
                stg, skey = stage[(wi * NCH + c) % 2]
                dma_in(stg[:, 0:ncols], src[c * 128:(c + 1) * 128, :],
                       writes=(skey,))
                eng = cast_engines[ei % 3]
                ei += 1
                dst_ap = dstt[:, c * ncols:(c + 1) * ncols]
                if wname == "WoT":
                    def emit(e, stg=stg, dst_ap=dst_ap, gc=g2d[:, c:c + 1], n=ncols):
                        return e.tensor_scalar(out=dst_ap, in0=stg[:, 0:n],
                                               scalar1=gc, scalar2=None,
                                               op0=ALU.mult)
                    s.op("dve", emit, reads=(skey, "g2d"), writes=((wname, c),))
                elif eng == "act":
                    def emit(e, stg=stg, dst_ap=dst_ap, n=ncols):
                        return e.activation(out=dst_ap, in_=stg[:, 0:n],
                                            func=AF.Identity)
                    s.op("act", emit, reads=(skey,), writes=((wname, c),))
                else:
                    def emit(e, stg=stg, dst_ap=dst_ap, n=ncols):
                        return e.tensor_copy(out=dst_ap, in_=stg[:, 0:n])
                    s.op(eng, emit, reads=(skey,), writes=((wname, c),))
        dma_in(C[0:RTOT, 0:D], h["WuT"], writes=("scrC",))
        s.op("dve", lambda e: e.tensor_copy(out=WuTb[:], in_=C[0:RTOT, 0:D]),
             reads=("scrC",), writes=("WuT", 0))
        wuk = (("WuT", 0),)

        def small_cast(e):
            e.tensor_copy(out=b2db[:, 0:8], in_=bk2d[:])
            e.tensor_copy(out=b2db[:, 8:16], in_=bq2d[:])
            return e.tensor_copy(out=b2db[:, 16:24], in_=lnb2d[:])
        s.op("pool", small_cast, reads=("bk2d", "bq2d", "lnb2d"), writes=("b2db",))

        def hilo_from_row(dst, dkey):
            h0 = KQT[0][0:1, 0:D]
            h1 = KQT[1][0:1, 0:D]

            def emit(e):
                stepd(e, e.tensor_copy(out=h0, in_=rowtmp))
                stepd(e, e.tensor_copy(out=rowbuf[:], in_=h0))
                stepd(e, e.tensor_tensor(out=rowbuf[:], in0=rowtmp,
                                         in1=rowbuf[:], op=ALU.subtract))
                return e.tensor_copy(out=h1, in_=rowbuf[:])
            s.op("dve", emit, reads=("OUTB",),
                 writes=("kqT_0", "kqT_1", "X2"), cost=0.5, tag="hilo")
            s.op("out", lambda e: e.dma_start(out=hlscr[0:1, 0:D], in_=h0),
                 reads=("kqT_0",), writes=("hlscr",))
            s.op("out", lambda e: e.dma_start(out=hlscr[0:1, D:2 * D], in_=h1),
                 reads=("kqT_1",), writes=("hlscr",))
            s.op("in", lambda e, dst=dst: e.dma_start(
                out=dst[:], in_=hlscr.rearrange("a (b c) -> (a b) c", b=2)),
                reads=("hlscr",), writes=(dkey,))

        def row_fold(boff, wname, wkey_t, add_ap, addkey, dst, dkey, slot_row):
            jobs = []
            for c in range(NCH):
                for nh in range(2):
                    jobs.append((PS[slot_row][nh][0:1, 0:512],
                                 b2db[:, boff + c:boff + c + 1],
                                 wb[wname][:, c * D + nh * 512:c * D + (nh + 1) * 512],
                                 c == 0, c == NCH - 1))
            mmseq(jobs, reads=("b2db",) + wkey_t, writes=(("ps", slot_row),))

            def ev(e, sl=slot_row, ar=add_ap):
                stepd(e, e.tensor_tensor(out=rowtmp[0:1, 0:512],
                                         in0=PS[sl][0][0:1, :],
                                         in1=ar[0:1, 0:512], op=ALU.add))
                return e.tensor_tensor(out=rowtmp[0:1, 512:1024],
                                       in0=PS[sl][1][0:1, :],
                                       in1=ar[0:1, 512:1024], op=ALU.add)
            s.op("dve", ev, reads=(("ps", slot_row), addkey), writes=("OUTB",))
            hilo_from_row(dst, dkey)

        dma_in(rowbuf[:], h["bkp_row"], writes=("X2",))
        row_fold(0, "WkpT", wkeys("WkpT"), rowbuf[:], "X2",
                 kp_hl, "kp_hl", 0)
        dma_in(rowbuf[:], h["bqp_row"], writes=("X2",))
        row_fold(8, "WqpT", wkeys("WqpT"), rowbuf[:], "X2",
                 qp_hl, "qp_hl", 2)
        dma_in(rowtmp, h["bv_row"], writes=("OUTB",))
        hilo_from_row(v_hl, "v_hl")

        for c in range(NCH):
            stg, skey = stage[c % 2]
            dma_in(stg[:, 0:D], h["WoT"][c * 128:(c + 1) * 128, :], writes=(skey,))

            def cem(e, stg=stg):
                return e.tensor_copy(out=bsh[:, 0:D], in_=stg[:, 0:D])
            s.op("dve", cem, reads=(skey,), writes=("bsh",))
            jb = []
            for nh in range(2):
                jb.append((PS[0][nh][0:1, 0:512], b2db[:, 16 + c:17 + c],
                           bsh[:, nh * 512:(nh + 1) * 512],
                           c == 0, c == NCH - 1))
            mmseq(jb, reads=("bsh", "b2db"), writes=(("ps", 0),))

        def bo_ev(e):
            e.tensor_copy(out=rowtmp[0:1, 0:512], in_=PS[0][0][0:1, :])
            return e.tensor_copy(out=rowtmp[0:1, 512:1024], in_=PS[0][1][0:1, :])
        s.op("dve", bo_ev, reads=(("ps", 0),), writes=("OUTB",))
        jobs2 = [(PS[1][nh][:, 0:512], ones_f[:], rowtmp[0:1, nh * 512:(nh + 1) * 512],
                  True, True) for nh in range(2)]
        mmseq(jobs2, reads=("OUTB", "ones_f"), writes=(("ps", 1),))

        def bo_bc_ev(e):
            e.tensor_tensor(out=bo_bc[:, 0:512], in0=PS[1][0][:, :],
                            in1=bo_bc[:, 0:512], op=ALU.add)
            return e.tensor_tensor(out=bo_bc[:, 512:1024], in0=PS[1][1][:, :],
                                   in1=bo_bc[:, 512:1024], op=ALU.add)
        s.op("dve", bo_bc_ev, reads=(("ps", 1), "bo_bc"), writes=("bo_bc",))

        s.op("act", lambda e: e.activation(out=gsm[:, 0:8], in_=sim2d[:],
                                           func=AF.Square),
             reads=("sim2d",), writes=("gsm",))
        jb = [(PS[2][0][0:1, 0:8], ones_c[:], gsm[:, 0:8], True, True)]
        mmseq(jb, reads=("gsm", "ones_c"), writes=(("ps", 2),))

        def sim_norm(e):
            stepd(e, e.tensor_copy(out=srow[0:1, 8:16], in_=PS[2][0][0:1, 0:8]))
            stepd(e, e.tensor_tensor(out=srow[0:1, 0:4], in0=srow[0:1, 8:12],
                                     in1=srow[0:1, 12:16], op=ALU.add))
            sbits = srow[0:1, 0:4].bitcast(I32)
            stepd(e, e.tensor_scalar(out=seed4[0:1, 0:4], in0=sbits, scalar1=1,
                                     scalar2=None, op0=ALU.arith_shift_right))
            stepd(e, e.tensor_scalar(out=seed4[0:1, 4:8], in0=seed4[0:1, 0:4],
                                     scalar1=-1, scalar2=0x5F3759DF,
                                     op0=ALU.mult, op1=ALU.add))
            stepd(e, e.tensor_copy(out=srow[0:1, 4:8],
                                   in_=seed4[0:1, 4:8].bitcast(F32)))
            for _ in range(4):
                stepd(e, e.tensor_tensor(out=srow[0:1, 8:12], in0=srow[0:1, 4:8],
                                         in1=srow[0:1, 4:8], op=ALU.mult))
                stepd(e, e.tensor_tensor(out=srow[0:1, 8:12], in0=srow[0:1, 8:12],
                                         in1=srow[0:1, 0:4], op=ALU.mult))
                stepd(e, e.tensor_scalar(out=srow[0:1, 8:12], in0=srow[0:1, 8:12],
                                         scalar1=-0.5, scalar2=1.5, op0=ALU.mult,
                                         op1=ALU.add))
                stepd(e, e.tensor_tensor(out=srow[0:1, 4:8], in0=srow[0:1, 4:8],
                                         in1=srow[0:1, 8:12], op=ALU.mult))
            return e.tensor_scalar(out=srow[0:1, 4:8], in0=srow[0:1, 4:8],
                                   scalar1=1e12, scalar2=None, op0=ALU.min)
        s.op("dve", sim_norm, reads=(("ps", 2),), writes=("srow", "seed4"))
        jb = [(PS[2][0][:, 0:4], ones_f[:], srow[0:1, 4:8], True, True)]
        mmseq(jb, reads=("srow", "ones_f"), writes=(("ps", 2),))

        def simn_apply(e):
            stepd(e, e.tensor_copy(out=gsm[:, 8:12], in_=PS[2][0][:, 0:4]))
            stepd(e, e.tensor_tensor(out=gsm[:, 0:4], in0=sim2d[:, 0:4],
                                     in1=gsm[:, 8:12], op=ALU.mult))
            stepd(e, e.tensor_tensor(out=gsm[:, 4:8], in0=sim2d[:, 4:8],
                                     in1=gsm[:, 8:12], op=ALU.mult))
            return e.tensor_copy(out=simnb[:], in_=gsm[:, 0:8])
        s.op("dve", simn_apply, reads=(("ps", 2), "sim2d"),
             writes=("simnb", "gsm"))
        dma_in(gsm[:], h["gpad"], writes=("gsm",))
        s.op("pool", lambda e: e.tensor_copy(out=bp2db[:], in_=bp2d[:]),
             reads=("bp2d",), writes=("bp2db",))
        jb = [(PS[3][0][0:1, 0:4], bp2db[:, c:c + 1], simnb[:, c * 4:(c + 1) * 4],
               c == 0, c == 1) for c in range(2)]
        mmseq(jb, reads=("bp2db", "simnb"), writes=(("ps", 3),))
        s.op("dve", lambda e: e.tensor_copy(out=srow[0:1, 0:4],
                                            in_=PS[3][0][0:1, 0:4]),
             reads=(("ps", 3),), writes=("srow",))
        jb = [(PS[3][0][:, 8:12], ones_f[:], srow[0:1, 0:4], True, True)]
        mmseq(jb, reads=("srow", "ones_f"), writes=(("ps", 3),))
        s.op("dve", lambda e: e.tensor_copy(out=bps4[:], in_=PS[3][0][:, 8:12]),
             reads=(("ps", 3),), writes=("bps4",))
        s.op("dve", lambda e: e.scalar_tensor_tensor(
            out=junkb[:, 0:PROJ], in0=bp_bc[:], scalar=1.0, in1=bp_bc[:],
            op0=ALU.mult, op1=ALU.mult, accum_out=cbc[:]),
            reads=("bp_bc",), writes=("junkb", "cbc"))

        s.op("dve", lambda e: e.tensor_scalar(
            out=srow[0:1, 12:13], in0=temp[:], scalar1=CLAMP_MAX, scalar2=0.5,
            op0=ALU.min, op1=ALU.mult),
            reads=("temp",), writes=("srow",))
        s.op("act", lambda e: e.activation(out=srow[0:1, 13:14],
                                           in_=srow[0:1, 12:13], func=AF.Tanh),
             reads=("srow",), writes=("srow",))

        def sc_fin(e):
            stepd(e, e.tensor_scalar(out=srow[0:1, 14:15], in0=srow[0:1, 13:14],
                                     scalar1=-1.0, scalar2=1.0, op0=ALU.mult,
                                     op1=ALU.add))
            stepd(e, e.reciprocal(out=srow[0:1, 14:15], in_=srow[0:1, 14:15]))
            stepd(e, e.tensor_scalar(out=srow[0:1, 13:14], in0=srow[0:1, 13:14],
                                     scalar1=1.0, scalar2=None, op0=ALU.add))
            return e.tensor_tensor(out=srow[0:1, 15:16], in0=srow[0:1, 13:14],
                                   in1=srow[0:1, 14:15], op=ALU.mult)
        s.op("dve", sc_fin, reads=("srow",), writes=("srow",))
        jb = [(PS[2][0][:, 4:5], ones_f[:], srow[0:1, 15:16], True, True)]
        mmseq(jb, reads=("srow", "ones_f"), writes=(("ps", 2),))
        s.op("dve", lambda e: e.tensor_copy(out=sc_bc[:], in_=PS[2][0][:, 4:5]),
             reads=(("ps", 2),), writes=("sc_bc",))

        c_S = consts[:, 0:1]
        c_HPI = consts[:, 1:2]
        c_HALF = consts[:, 2:3]
        c_NS = consts[:, 3:4]


        def tctx(ti):
            pp = ti % 2
            stp = STAT[pp]
            ks = (f"st1_{pp}", f"st2_{pp}", f"st3_{pp}", f"st4_{pp}")
            return dict(pp=pp, stp=stp, ks=ks, nrp=NRS[pp], knr=f"nrs_{pp}",
                        sdp=SEEDI[pp][:, 0:1], ksd=f"seedi_{pp}",
                        sdp2=SEEDI[pp][:, 1:2], ksd2=f"seedi2_{pp}",
                        pst=PSTAT[pp], kps=f"pstat_{pp}",
                        xin=XIN[pp], kx=f"x_in_{pp}",
                        kqt=KQT[pp], kk=f"kqT_{pp}")

        def ln_negmu(t):
            stp1, stp3 = t["stp"][0], t["stp"][2]
            k1, k3 = t["ks"][0], t["ks"][2]
            s.op("dve", lambda e, stp1=stp1, stp3=stp3: e.tensor_scalar(
                out=stp1[:], in0=stp3[:], scalar1=-1.0 / D, scalar2=None,
                op0=ALU.mult),
                reads=(k3,), writes=(k1,))

        def ln_rstd(t, eps):
            stp1, stp2, stp3, stp4 = t["stp"]
            k1, k2, k3, k4 = t["ks"]
            nrp, knr = t["nrp"], t["knr"]

            def emit(e):
                stepp(e, e.tensor_scalar(out=nrp[:, 1:2], in0=stp4[:],
                                         scalar1=1.0 / D, scalar2=eps,
                                         op0=ALU.mult, op1=ALU.add))
                stepp(e, e.tensor_tensor(out=nrp[:, 2:3], in0=stp1[:],
                                         in1=stp1[:], op=ALU.mult))
                return e.tensor_tensor(out=stp2[:], in0=nrp[:, 1:2],
                                       in1=nrp[:, 2:3], op=ALU.subtract)
            s.op("pool", emit, reads=(k1, k4), writes=(k2, knr), cost=1.1,
                 tag="rstd_w")
            rsqrt_nr(stp2, k2, stp3, k3, t["sdp"], t["ksd"], nrp, knr, iters=3)

        def ln_stats(t, eps, sum_i=2):
            stp1, stp2, stp3, stp4 = t["stp"]
            k1, k2, k3, k4 = t["ks"]
            sums, ksum = t["stp"][sum_i], t["ks"][sum_i]
            nrp, knr = t["nrp"], t["knr"]

            def emit(e, sums=sums):
                stepp(e, e.tensor_scalar(out=stp1[:], in0=sums[:],
                                         scalar1=1.0 / D, scalar2=None,
                                         op0=ALU.mult))
                e.tensor_scalar(out=nrp[:, 1:2], in0=stp4[:], scalar1=1.0 / D,
                                scalar2=eps, op0=ALU.mult, op1=ALU.add)
                stepp(e, e.tensor_tensor(out=nrp[:, 2:3], in0=stp1[:],
                                         in1=stp1[:], op=ALU.mult))
                return e.tensor_tensor(out=stp2[:], in0=nrp[:, 1:2],
                                       in1=nrp[:, 2:3], op=ALU.subtract)
            s.op("pool", emit, reads=(ksum, k4), writes=(k1, k2, knr),
                 cost=1.45, tag="lnst_w")
            rsqrt_nr(stp2, k2, stp3, k3, t["sdp"], t["ksd"], nrp, knr, iters=3)

            def emit_nb(e):
                stepp(e, e.tensor_tensor(out=stp4[:], in0=stp1[:], in1=stp3[:],
                                         op=ALU.mult))
                return e.tensor_scalar(out=stp4[:], in0=stp4[:], scalar1=-1.0,
                                       scalar2=None, op0=ALU.mult)
            s.op("pool", emit_nb, reads=(k1, k3), writes=(k4,), cost=0.72,
                 tag="ln_nb")

        def wrap_and_trig(src_key, src_ap, SKb=SKb, CKb=CKb, skk="SK",
                          ckk="CK"):
            s.op("act", lambda e, sa=src_ap: e.activation(
                out=A[:], in_=sa, func=AF.Tanh),
                reads=(src_key,), writes=("scrA",), cost=1.25, tag="wr_tanh")
            s.op("dve", lambda e: e.scalar_tensor_tensor(
                out=A[:], in0=A[:], scalar=INV_2PI, in1=ps_bc[:],
                op0=ALU.mult, op1=ALU.mult),
                reads=("scrA", "ps_bc"), writes=("scrA",), cost=1.28, tag="wr_t")
            s.op("dve", lambda e: e.tensor_scalar(
                out=U[:], in0=A[:], scalar1=KOFF, scalar2=None, op0=ALU.add),
                reads=("scrA",), writes=("scrU",), cost=0.7, tag="wr_u")
            s.op("dve", lambda e: e.scalar_tensor_tensor(
                out=C[:], in0=U[:], scalar=-KOFF, in1=A[:],
                op0=ALU.add, op1=ALU.subtract),
                reads=("scrU", "scrA"), writes=("scrC",), cost=1.28,
                tag="wr_nv")
            s.op("dve", lambda e: e.scalar_tensor_tensor(
                out=C[:], in0=C[:], scalar=-0.5, in1=C[:], op0=ALU.is_lt,
                op1=ALU.add),
                reads=("scrC",), writes=("scrC",), cost=1.28, tag="wr_w")
            s.op("act", lambda e, SKb=SKb: e.activation(
                out=SKb[:], in_=C[:], func=AF.Sin, scale=c_NS),
                reads=("scrC", "consts"), writes=(skk,), cost=1.25,
                tag="wr_sin")
            s.op("dve", lambda e: e.scalar_tensor_tensor(
                out=A[:], in0=C[:], scalar=-0.25, in1=C[:], op0=ALU.is_le,
                op1=ALU.add),
                reads=("scrC",), writes=("scrA",), cost=1.28, tag="wr_vc")
            s.op("act", lambda e, CKb=CKb: e.activation(
                out=CKb[:], in_=A[:], func=AF.Sin, scale=c_NS, bias=c_HPI),
                reads=("scrA", "consts"), writes=(ckk,), cost=1.25,
                tag="wr_cos")

        def f_xkey(ti):
            t = tctx(ti)
            s.op("act", lambda e, x=t["xin"]: e.activation(
                out=xbb[:], in_=x[:], func=AF.Identity),
                reads=(t["kx"],), writes=("xbb",), cost=1.2, tag="xb")
            transpose8(3, "xbb", xbb[:])
            evac_pair(3, xTb[:, 0:1024], "xTb")
            jobs = []
            for ec in range(NCH):
                for dc in range(NCH):
                    jobs.append((P[1][:, ec * 128:(ec + 1) * 128],
                                 wb["WkT"][:, dc * D + ec * 128:dc * D + (ec + 1) * 128],
                                 xTb[:, dc * 128:(dc + 1) * 128],
                                 dc == 0, dc == NCH - 1))
            mmseq(jobs, reads=("xTb",) + wkeys("WkT"), writes=(("ps", 1),))
            evac_pair(1, t["kqt"][:, 0:1024], t["kk"])

        def f_val_kp(ti):
            t = tctx(ti)
            kq = t["kqt"]
            jobs = []
            for nh in range(2):
                for dc in range(NCH):
                    jobs.append((PS[0][nh][:, 0:512],
                                 xTb[:, dc * 128:(dc + 1) * 128],
                                 wb["WvT"][:, dc * D + nh * 512:dc * D + (nh + 1) * 512],
                                 dc == 0, False))
                jobs.append((PS[0][nh][:, 0:512], ones2[:],
                             v_hl[0:2, nh * 512:(nh + 1) * 512], False, True))
            mmseq(jobs, reads=("xTb", "v_hl") + wkeys("WvT"),
                  writes=(("ps", 0),), tag="value")
            s.op("act", lambda e: e.activation(out=VAL[:], in_=P[0][:, :],
                                               func=AF.Identity),
                 reads=(("ps", 0),), writes=("VAL",), cost=1.2, tag="val_ev")
            jobs = []
            for nh in range(2):
                for ec in range(NCH):
                    jobs.append((PS[1][nh][:, 0:512],
                                 kq[:, ec * 128:(ec + 1) * 128],
                                 wb["WkpT"][:, ec * D + nh * 512:ec * D + (nh + 1) * 512],
                                 ec == 0, False))
                jobs.append((PS[1][nh][:, 0:512], ones2[:],
                             kp_hl[0:2, nh * 512:(nh + 1) * 512], False, True))
            mmseq(jobs, reads=(t["kk"], "kp_hl") + wkeys("WkpT"),
                  writes=(("ps", 1),), tag="kp")
            wrap_and_trig(("ps", 1), P[1][:, :])

        def f_qbm(ti):
            t = tctx(ti)
            jobs = []
            for ec in range(NCH):
                for dc in range(NCH):
                    jobs.append((P[1][:, ec * 128:(ec + 1) * 128],
                                 wb["WqT"][:, dc * D + ec * 128:dc * D + (ec + 1) * 128],
                                 xTb[:, dc * 128:(dc + 1) * 128],
                                 dc == 0, dc == NCH - 1))
            mmseq(jobs, reads=("xTb",) + wkeys("WqT"), writes=(("ps", 1),))
            s.op("dve", lambda e: e.tensor_tensor(out=xbb[:], in0=VAL[:],
                                                  in1=CKb[:], op=ALU.mult),
                 reads=("VAL", "CK"), writes=("xbb",))
            s.op("dve", lambda e: e.tensor_tensor(out=bsh[:], in0=VAL[:],
                                                  in1=SKb[:], op=ALU.mult),
                 reads=("VAL", "SK"), writes=("bsh",))
            evac_pair(1, t["kqt"][:, 0:1024], t["kk"])
            jobs = [(PS[0][nh][:, 0:512], LTs[:], xbb[:, nh * 512:(nh + 1) * 512],
                     True, True) for nh in range(2)]
            mmseq(jobs, reads=("xbb", "LT"), writes=(("ps", 0),))
            jobs = [(PS[1][nh][:, 0:512], LTs[:], bsh[:, nh * 512:(nh + 1) * 512],
                     True, True) for nh in range(2)]
            mmseq(jobs, reads=("bsh", "LT"), writes=(("ps", 1),))
            s.op("act", lambda e: e.activation(out=VAL[:], in_=P[0][:, :],
                                               func=AF.Identity),
                 reads=(("ps", 0),), writes=("VAL",), cost=1.15, tag="mr_ev")
            s.op("act", lambda e: e.activation(out=MIB[:], in_=P[1][:, :],
                                               func=AF.Identity),
                 reads=(("ps", 1),), writes=("MIB",), cost=1.15, tag="mi_ev")

        def f_qp(ti):
            t = tctx(ti)
            kq = t["kqt"]
            jobs = []
            for nh in range(2):
                for ec in range(NCH):
                    jobs.append((PS[3][nh][:, 0:512],
                                 kq[:, ec * 128:(ec + 1) * 128],
                                 wb["WqpT"][:, ec * D + nh * 512:ec * D + (nh + 1) * 512],
                                 ec == 0, False))
                jobs.append((PS[3][nh][:, 0:512], ones2[:],
                             qp_hl[0:2, nh * 512:(nh + 1) * 512], False, True))
            mmseq(jobs, reads=(t["kk"], "qp_hl") + wkeys("WqpT"),
                  writes=(("ps", 3),), tag="qp")
            wrap_and_trig(("ps", 3), P[3][:, :], SKq, CKq, "SKq", "CKq")

        def f_retr(ti):
            t = tctx(ti)
            stp3, stp4 = t["stp"][2], t["stp"][3]
            k3, k4 = t["ks"][2], t["ks"][3]
            s.op("dve", lambda e: e.tensor_tensor(
                out=VAL[:], in0=VAL[:], in1=CKq[:], op=ALU.mult),
                reads=("VAL", "CKq"), writes=("VAL",))
            s.op("dve", lambda e: e.tensor_tensor(
                out=MIB[:], in0=MIB[:], in1=SKq[:], op=ALU.mult),
                reads=("MIB", "SKq"), writes=("MIB",))
            s.op("dve", lambda e, stp3=stp3: e.scalar_tensor_tensor(
                out=MIB[:], in0=VAL[:], scalar=1.0, in1=MIB[:], op0=ALU.mult,
                op1=ALU.add, accum_out=stp3[:]),
                reads=("VAL", "MIB"), writes=("MIB", k3))
            ln_negmu(t)
            s.op("act", lambda e, stp4=stp4: e.activation(
                out=junkb[:], in_=MIB[:], func=AF.Square, accum_out=stp4[:]),
                reads=("MIB",), writes=("junkb", k4))
            ln_rstd(t, EPS_IN)

        def f_lnT(ti):
            t = tctx(ti)
            stp1 = t["stp"][0]
            k1 = t["ks"][0]
            s.op("act", lambda e, stp1=stp1: e.activation(
                out=bsh[:], in_=MIB[:], func=AF.Identity, bias=stp1[:]),
                reads=("MIB", k1), writes=("bsh",), cost=1.15, tag="lncast")
            transpose8(3, "bsh", bsh[:])
            evac_pair(3, tsh[:, 0:1024], "tsh")

        def f_wo(ti):
            t = tctx(ti)
            stp3 = t["stp"][2]
            k3 = t["ks"][2]
            proj_tok(2, "tsh", tsh[:], wb["WoT"], wkeys("WoT"))
            s.op("dve", lambda e, stp3=stp3: e.scalar_tensor_tensor(
                out=C[:], in0=P[2][:, :], scalar=stp3[:], in1=bo_bc[:, 0:1024],
                op0=ALU.mult, op1=ALU.add),
                reads=(("ps", 2), k3, "bo_bc"), writes=("scrC",))
            s.op("dve", lambda e, x=t["xin"], stp3=stp3: e.scalar_tensor_tensor(
                out=B[:], in0=x[:], scalar=2.0, in1=C[:], op0=ALU.mult,
                op1=ALU.add, accum_out=stp3[:]),
                reads=(t["kx"], "scrC"), writes=("scrB", k3))

        def f_ln1x2(ti):
            t = tctx(ti)
            stp1, stp2, stp3, stp4 = t["stp"]
            k1, k2, k3, k4 = t["ks"]
            s.op("dve", lambda e, stp1=stp1, stp3=stp3: e.tensor_scalar(
                out=stp1[:], in0=stp3[:], scalar1=-1.0 / D, scalar2=None,
                op0=ALU.mult),
                reads=(k3,), writes=(k1,), cost=0.2, tag="negmu1")
            s.op("act", lambda e, stp4=stp4: e.activation(
                out=junkb[:], in_=B[:], func=AF.Square, accum_out=stp4[:]),
                reads=("scrB",), writes=("junkb", k4), cost=1.25, tag="sq1")
            s.op("act", lambda e, stp1=stp1: e.activation(
                out=bsh[:], in_=B[:], func=AF.Identity, bias=stp1[:]),
                reads=("scrB", k1), writes=("bsh",), cost=1.15, tag="x2b")
            transpose8(3, "bsh", bsh[:])
            evac_pair(3, tsh2[:, 0:1024], "tsh2")
            ln_rstd(t, EPS)
            s.op("pool", lambda e, stp1=stp1, stp3=stp3, stp4=stp4:
                 e.tensor_tensor(out=stp4[:], in0=stp1[:], in1=stp3[:],
                                 op=ALU.mult),
                 reads=(k1, k3), writes=(k4,), cost=0.36, tag="nb1")

        def f_ffn1(ti):
            t = tctx(ti)
            stp3 = t["stp"][2]
            k3 = t["ks"][2]
            pst, kps = t["pst"], t["kps"]
            base_k = ("ps", 2)
            proj_tok(2, "tsh2", tsh2[:], wb["WfT"], wkeys("WfT"))
            jobs = [(PS[3][0][:, 0:PROJ], tsh2[:, dc * 128:(dc + 1) * 128],
                     WpTb[:, dc * PROJ:(dc + 1) * PROJ], dc == 0, dc == NCH - 1)
                    for dc in range(NCH)]
            mmseq(jobs, reads=("tsh2",) + wkeys("WpT"), writes=(("ps", 3),),
                  tag="Wp")
            jobs = [(PS[3][1][:, 0:RTOT], tsh2[:, dc * 128:(dc + 1) * 128],
                     WdTb[:, dc * RTOT:(dc + 1) * RTOT], dc == 0, dc == NCH - 1)
                    for dc in range(NCH)]
            mmseq(jobs, reads=("tsh2",) + wkeys("WdT"), writes=(("ps", 3),),
                  tag="Wd")
            s.op("act", lambda e: e.activation(out=projb[:],
                                               in_=PS[3][0][:, 0:PROJ],
                                               func=AF.Identity),
                 reads=(("ps", 3),), writes=("projb",), cost=0.6, tag="projb")
            s.op("act", lambda e, pst=pst: e.activation(
                out=junkb[:, 0:PROJ], in_=PS[3][0][:, 0:PROJ], func=AF.Square,
                accum_out=pst[:, 0:1]),
                reads=(("ps", 3),), writes=("junkb", kps), cost=0.6, tag="sqp")
            s.op("dve", lambda e, pst=pst: e.scalar_tensor_tensor(
                out=C[:, 0:PROJ], in0=PS[3][0][:, 0:PROJ], scalar=1.0,
                in1=bp_bc[:], op0=ALU.mult, op1=ALU.mult,
                accum_out=pst[:, 1:2]),
                reads=(("ps", 3), "bp_bc"), writes=("scrC", kps), cost=0.4,
                tag="bxp")
            s.op("dve", lambda e: e.tensor_copy(out=hsml[:],
                                                in_=PS[3][1][:, 0:RTOT]),
                 reads=(("ps", 3),), writes=("hsml",), cost=0.25, tag="h_ev")
            jobs = [(PS[3][0][:, PROJ + c * 128:PROJ + (c + 1) * 128],
                     projb[:, c * 128:(c + 1) * 128], I128[:], True, True)
                    for c in range(2)]
            mmseq(jobs, reads=("projb", "I128"), writes=(("ps", 3),),
                  tag="projT")
            s.op("dve", lambda e: e.tensor_copy(out=projT[:],
                                                in_=PS[3][0][:, PROJ:PROJ + 256]),
                 reads=(("ps", 3),), writes=("projT",), cost=0.4, tag="projT_ev")
            jobs = [(PS[0][0][:, 0:4], projT[:, c * 128:(c + 1) * 128],
                     simnb[:, c * 4:(c + 1) * 4], c == 0, c == 1)
                    for c in range(2)]
            mmseq(jobs, reads=("projT", "simnb"), writes=(("ps", 0),),
                  tag="logits")
            def smalls(e, pst=pst, stp3=stp3):
                stepp(e, e.tensor_tensor(out=pst[:, 2:3], in0=stp3[:],
                                         in1=stp3[:], op=ALU.mult))
                stepp(e, e.tensor_tensor(out=pst[:, 2:3], in0=pst[:, 2:3],
                                         in1=pst[:, 0:1], op=ALU.mult))
                stepp(e, e.tensor_tensor(out=pst[:, 4:5], in0=stp3[:],
                                         in1=pst[:, 1:2], op=ALU.mult))
                stepp(e, e.tensor_scalar(out=pst[:, 4:5], in0=pst[:, 4:5],
                                         scalar1=2.0, scalar2=None,
                                         op0=ALU.mult))
                stepp(e, e.tensor_tensor(out=pst[:, 4:5], in0=pst[:, 4:5],
                                         in1=cbc[:], op=ALU.add))
                return e.tensor_tensor(out=pst[:, 2:3], in0=pst[:, 2:3],
                                       in1=pst[:, 4:5], op=ALU.add)
            s.op("pool", smalls, reads=(kps, k3, "cbc"), writes=(kps,),
                 cost=1.8, tag="ssqp")
            rsqrt_nr(None, kps, None, kps, t["sdp2"], t["ksd2"],
                     pst, kps, clamp=1e12, iters=3,
                     src_ap=pst[:, 2:3], dst_ap=pst[:, 3:4],
                     nr_ap=pst[:, 4:5])
            s.op("pool", lambda e, pst=pst: e.tensor_tensor(
                out=pst[:, 3:4], in0=pst[:, 3:4], in1=sc_bc[:], op=ALU.mult),
                reads=(kps, "sc_bc"), writes=(kps,), cost=0.4, tag="vsm")
            s.op("dve", lambda e, stp3=stp3: e.scalar_tensor_tensor(
                out=gsm[:, 0:4], in0=PS[0][0][:, 0:4], scalar=stp3[:],
                in1=bps4[:], op0=ALU.mult, op1=ALU.add),
                reads=(("ps", 0), k3, "bps4"), writes=("gsm",),
                cost=0.3, tag="lg")

        def f_gate(ti):
            t = tctx(ti)
            stp3 = t["stp"][2]
            k3 = t["ks"][2]
            pst, kps = t["pst"], t["kps"]

            def gate1(e):
                stepd(e, e.max(gsm[:, 8:16], gsm[:, 0:8]))
                stepd(e, e.tensor_scalar(out=gsm[:, 16:20], in0=gsm[:, 0:4],
                                         scalar1=gsm[:, 8:9], scalar2=1e9,
                                         op0=ALU.is_lt, op1=ALU.mult))
                stepd(e, e.tensor_tensor(out=gsm[:, 16:20], in0=gsm[:, 16:20],
                                         in1=iota4[:], op=ALU.add))
                stepd(e, e.tensor_reduce(gsm[:, 20:21], gsm[:, 16:20], AX.X,
                                         ALU.min))
                return e.tensor_scalar(out=gsm[:, 24:28], in0=iota4[:],
                                       scalar1=gsm[:, 20:21], scalar2=None,
                                       op0=ALU.is_equal)
            s.op("dve", gate1, reads=("gsm", "iota4"), writes=("gsm",),
                 cost=1.8, tag="gate1")

            def hgate(e):
                inst = None
                for i, (off, r) in enumerate(zip(ROFF, RANKS)):
                    inst = e.tensor_scalar(out=hpb[:, off:off + r],
                                           in0=hsml[:, off:off + r],
                                           scalar1=gsm[:, 24 + i:25 + i],
                                           scalar2=None, op0=ALU.mult)
                return inst
            s.op("dve", hgate, reads=("hsml", "gsm"), writes=("hpb",), cost=0.8, tag="hgate")

            def dprep(e, pst=pst):
                stepd(e, e.tensor_scalar(out=gsm[:, 16:20], in0=gsm[:, 0:4],
                                         scalar1=-1.0, scalar2=gsm[:, 8:9],
                                         op0=ALU.mult, op1=ALU.add))
                return e.tensor_scalar(out=gsm[:, 16:20], in0=gsm[:, 16:20],
                                       scalar1=pst[:, 3:4], scalar2=None,
                                       op0=ALU.mult)
            s.op("dve", dprep, reads=("gsm", kps), writes=("gsm",),
                 cost=0.6, tag="dprep")
            s.op("act", lambda e: e.activation(out=gsm[:, 16:20],
                                               in_=gsm[:, 16:20],
                                               func=AF.Tanh, scale=c_HALF),
                 reads=("gsm", "consts"), writes=("gsm",), cost=0.45,
                 tag="gate_tanh")

            def gate2(e):
                stepd(e, e.tensor_scalar(out=gsm[:, 0:4], in0=gsm[:, 16:20],
                                         scalar1=-1.0, scalar2=1.0,
                                         op0=ALU.mult, op1=ALU.add))
                stepd(e, e.tensor_scalar(out=gsm[:, 16:20], in0=gsm[:, 16:20],
                                         scalar1=1.0, scalar2=None, op0=ALU.add))
                stepd(e, e.reciprocal(out=gsm[:, 16:20], in_=gsm[:, 16:20]))
                stepd(e, e.tensor_tensor(out=gsm[:, 0:4], in0=gsm[:, 0:4],
                                         in1=gsm[:, 16:20], op=ALU.mult))
                stepd(e, e.tensor_reduce(gsm[:, 28:29], gsm[:, 0:4], AX.X,
                                         ALU.add))
                return e.reciprocal(out=gsm[:, 28:29], in_=gsm[:, 28:29])
            s.op("dve", gate2, reads=("gsm",), writes=("gsm",), cost=2.2,
                 tag="gate2")
            s.op("pool", lambda e, pst=pst, stp3=stp3: e.tensor_tensor(
                out=pst[:, 6:7], in0=stp3[:], in1=gsm[:, 28:29], op=ALU.mult),
                reads=(k3, "gsm"), writes=(kps,), cost=0.36, tag="wmul")

        def f_ffn2(ti):
            t = tctx(ti)
            stp3, stp4 = t["stp"][2], t["stp"][3]
            k3, k4 = t["ks"][2], t["ks"][3]
            nrp, knr = t["nrp"], t["knr"]
            s.op("act", lambda e, stp3=stp3, stp4=stp4: e.activation(
                out=X2[:], in_=B[:], func=AF.Identity, scale=stp3[:],
                bias=stp4[:]),
                reads=("scrB", k3, k4), writes=("X2",), cost=1.15, tag="x2f")
            jobs = [(PS[0][1][0:RTOT, 128:256], hpb[:], I128[:], True, True)]
            mmseq(jobs, reads=("hpb", "I128"), writes=(("ps", 0),), tag="hT")
            s.op("dve", lambda e: e.tensor_copy(out=hpT[:],
                                                in_=PS[0][1][0:RTOT, 128:256]),
                 reads=(("ps", 0),), writes=("hpT",), cost=0.3, tag="hpT_ev")
            s.op("dve", lambda e, stp3=stp3: e.scalar_tensor_tensor(
                out=WFS[:], in0=P[2][:, :], scalar=stp3[:], in1=X2[:],
                op0=ALU.mult, op1=ALU.add),
                reads=(("ps", 2), k3, "X2"), writes=("WFS",), cost=1.3,
                tag="wfs")
            jobs = [(PS[2][nh][:, 0:512], hpT[:],
                     WuTb[:, nh * 512:(nh + 1) * 512], True, True)
                    for nh in range(2)]
            mmseq(jobs, reads=("hpT",) + wuk, writes=(("ps", 2),))

            pst, kps = t["pst"], t["kps"]
            s.op("dve", lambda e, stp1=t["stp"][0], pst=pst:
                 e.scalar_tensor_tensor(
                     out=B[:], in0=P[2][:, :], scalar=pst[:, 6:7], in1=WFS[:],
                     op0=ALU.mult, op1=ALU.add, accum_out=stp1[:]),
                reads=(("ps", 2), "WFS", kps), writes=("scrB", t["ks"][0]),
                cost=1.3, tag="pre3")
            s.op("act", lambda e, stp4=stp4: e.activation(
                out=junkb[:], in_=B[:], func=AF.Square, accum_out=stp4[:]),
                reads=("scrB",), writes=("junkb", k4), cost=1.25, tag="sq3")
            ln_stats(t, EPS, sum_i=0)

        def f_out(ti):
            t = tctx(ti)
            stp3, stp4 = t["stp"][2], t["stp"][3]
            k3, k4 = t["ks"][2], t["ks"][3]
            s.op("act", lambda e, stp3=stp3, stp4=stp4: e.activation(
                out=OUTB[:], in_=B[:], func=AF.Identity, scale=stp3[:],
                bias=stp4[:]),
                reads=("scrB", k3, k4), writes=("OUTB",), cost=1.15,
                tag="outap")
            s.op("out", lambda e, r0=ti * 128: e.dma_start(
                out=hy[r0:r0 + 128, :], in_=OUTB[:]),
                reads=("OUTB",), writes=())

        f_xkey(0)
        f_val_kp(0)
        f_qbm(0)
        if NT > 1:
            tn = tctx(1)
            dma_in(tn["xin"][:], hx[128:256, :], writes=(tn["kx"],))
            f_xkey(1)
        for i in range(NT):
            f_qp(i)
            f_retr(i)
            if i > 0:
                f_out(i - 1)
            if i + 1 < NT:
                f_val_kp(i + 1)
            f_lnT(i)
            f_wo(i)
            if i + 1 < NT:
                f_qbm(i + 1)
            f_ln1x2(i)
            f_ffn1(i)
            if i + 2 < NT:
                tn = tctx(i + 2)
                dma_in(tn["xin"][:], hx[(i + 2) * 128:(i + 3) * 128, :],
                       writes=(tn["kx"],))
                f_xkey(i + 2)
            f_gate(i)
            f_ffn2(i)
        f_out(NT - 1)

        s.op("pool", lambda e: e.nop(),
             extra_waits=tuple((ch, s.cnt[ch]) for ch in DMA_OUT if s.cnt[ch]))

        s.reorder(keep_order=tuple(
            os.environ.get("KRO", "pe,dma,sp").split(",")))
        qmap = {"pe": "tensor", "act": "scalar", "dve": "vector",
                "pool": "gpsimd", "sp": "sync"}
        with nc.Block() as block:
            s.emit(sems, block, qmap)

    return nc


_PROG = None


def kernel(**inputs):
    global _PROG
    x = np.ascontiguousarray(np.asarray(inputs["L0_attn_x"], np.float32))
    assert x.shape == (8, S, D)
    hp = host_prep(inputs)
    if _PROG is None:
        _PROG = build_core_program()
    in_maps = []
    for b in range(8):
        m = {"hx": x[b]}
        m.update(hp)
        in_maps.append(m)
    res = run_bass_kernel_spmd(_PROG, in_maps, core_ids=list(range(8)))
    out = np.stack([np.asarray(r["hy"]) for r in res.results], axis=0)
    return out.astype(np.float32)
